# version 06
# baseline (speedup 1.0000x reference)
from contextlib import ExitStack
import math
import numpy as np
import ml_dtypes
import concourse.bass as bass
import concourse.mybir as mybir

F32 = mybir.dt.float32
BF16 = mybir.dt.bfloat16
AF = mybir.ActivationFunctionType
ALU = mybir.AluOpType
AX = mybir.AxisListType

ENGMAP = {"pe": "tensor", "dve": "vector", "act": "scalar", "pool": "gpsimd", "sp": "sync"}
import os
DBG_L1_SIMPLE = os.environ.get("DBG_L1_SIMPLE")
C = 1024
NCH = 8


class Buf:
    __slots__ = ("w", "rs", "name")

    def __init__(self, name=""):
        self.w = None
        self.rs = []
        self.name = name


class Prog:
    def __init__(self, nc, ctx: ExitStack, n_sp_sems=6, n_pool_sems=6):
        self.nc, self.ctx = nc, ctx
        self.items = {k: [] for k in ENGMAP}
        self.csem = {k: ctx.enter_context(nc.semaphore("c_" + k))
                     for k in ("pe", "dve", "act", "pool")}
        self.ccnt = {k: 0 for k in ("pe", "dve", "act", "pool")}
        self.dsems = {"sp": [ctx.enter_context(nc.semaphore(f"d_sp{i}")) for i in range(n_sp_sems)],
                      "pool": [ctx.enter_context(nc.semaphore(f"d_pl{i}")) for i in range(n_pool_sems)]}
        self.dcnt = {("sp", i): 0 for i in range(n_sp_sems)}
        self.dcnt.update({("pool", i): 0 for i in range(n_pool_sems)})
        self.drr = {"sp": 0, "pool": 0}

    def _wire(self, stream, entry_token, reads, writes, deps):
        alldeps = list(deps)
        for r in reads:
            if r.w is not None:
                alldeps.append(r.w)
        for w in writes:
            if w.w is not None:
                alldeps.append(w.w)
            alldeps.extend(w.rs)
        for r in reads:
            r.rs.append(entry_token)
        for w in writes:
            w.w = entry_token
            w.rs = []
        return alldeps

    def op(self, stream, fn, reads=(), writes=(), deps=()):
        self.ccnt[stream] += 1
        tok = (self.csem[stream], self.ccnt[stream])
        alldeps = self._wire(stream, tok, reads, writes, deps)
        self.items[stream].append((fn, alldeps, tok[0], tok[1]))
        return tok

    def dma(self, stream, out, in_, reads=(), writes=(), deps=(), sem=None, **kw):
        i = sem if sem is not None else self.drr[stream] % len(self.dsems[stream])
        if sem is None:
            self.drr[stream] += 1
        prev = self.dcnt[(stream, i)]
        self.dcnt[(stream, i)] += 16
        tok = (self.dsems[stream][i], self.dcnt[(stream, i)])
        deps = list(deps)
        if prev > 0:
            deps.append((self.dsems[stream][i], prev))
        alldeps = self._wire(stream, tok, reads, writes, deps)
        fn = lambda eng, out=out, in_=in_, kw=kw: eng.dma_start(out=out, in_=in_, **kw)
        self.items[stream].append((fn, alldeps, tok[0], tok[1]))
        return tok

    def collective(self, fn, reads=(), writes=(), deps=()):
        i = self.drr["pool"] % len(self.dsems["pool"])
        self.drr["pool"] += 1
        prev = self.dcnt[("pool", i)]
        self.dcnt[("pool", i)] += 16
        tok = (self.dsems["pool"][i], self.dcnt[("pool", i)])
        deps = list(deps)
        if prev > 0:
            deps.append((self.dsems["pool"][i], prev))
        alldeps = self._wire("pool", tok, reads, writes, deps)
        self.items["pool"].append((fn, alldeps, tok[0], tok[1]))
        return tok

    def all_tokens(self):
        toks = [(self.csem[k], self.ccnt[k]) for k in self.csem if self.ccnt[k] > 0]
        for (s, i), c in self.dcnt.items():
            if c > 0:
                toks.append((self.dsems[s][i], c))
        return toks

    def barrier(self):
        toks = self.all_tokens()
        for stream in self.items:
            self.items[stream].append((None, toks, None, None))

    def wait(self, stream, deps):
        self.items[stream].append((None, list(deps), None, None))

    def emit(self):
        dma_semset = set()
        for lst in self.dsems.values():
            dma_semset.update(id(s) for s in lst)
        with self.nc.Block() as block:
            for stream, items in self.items.items():
                if not items:
                    continue
                dec = getattr(block, ENGMAP[stream])

                def body(eng, _items=items):
                    lw = {}
                    for fn, deps, sem, val in _items:
                        for (dsem, dval) in deps:
                            key = id(dsem)
                            if lw.get(key, 0) < dval:
                                eng.wait_ge(dsem, dval)
                                lw[key] = dval
                        if fn is not None:
                            inst = fn(eng)
                            if isinstance(inst, (list, tuple)):
                                inst = inst[-1]
                            inst.then_inc(sem, 16 if id(sem) in dma_semset else 1)

                dec(body)


class Env:

    def __init__(self, nc, ctx, p: Prog, psum_spec=None):
        self.nc, self.ctx, self.p = nc, ctx, p
        self.host_consts = {}
        self.const_aps = {}
        self.const_bufs = {}
        self.prefix = ""
        self.psum = {}
        if psum_spec is not False:
            for name, wid in (psum_spec or [("pA", 1024), ("pB", 1024), ("pC", 1024), ("pD", 1024)]):
                h = ctx.enter_context(nc.psum_tensor(name, [128, wid], F32))
                self.psum[name] = h
                self.const_bufs[name] = Buf(name)
        self.sb = {}
        self.sbuf_bufs = {}

    def begin_block(self, prefix, psum_spec, block_ctx):
        self.prefix = prefix
        self.psum = {}
        for name, wid in psum_spec:
            h = block_ctx.enter_context(self.nc.psum_tensor(prefix + name, [128, wid], F32))
            self.psum[name] = h
            self.const_bufs[name] = Buf(prefix + name)

    def add_const(self, name, arr):
        if name in self.const_aps:
            return self.const_aps[name]
        self.host_consts[name] = arr
        dt = {np.dtype(np.float32): F32, np.dtype(ml_dtypes.bfloat16): BF16,
              np.dtype(np.int32): mybir.dt.int32}[arr.dtype]
        ap = self.nc.dram_tensor("cst_" + name, list(arr.shape), dt, kind="ExternalInput").ap()
        self.const_aps[name] = ap
        return ap

    SHARED_NAMES = {"identb", "onesb", "onescol", "identf", "m_onesf", "cmask"}

    def sbuf(self, name, shape, dt, ctx=None):
        pn = name if name in self.SHARED_NAMES else self.prefix + name
        h = (ctx or self.ctx).enter_context(self.nc.sbuf_tensor(pn, shape, dt))
        self.sb[name] = h
        self.sbuf_bufs[name] = Buf(pn)
        return h

    def set_block_ctx(self, ctx):
        self._saved_ctx = self.ctx
        self.ctx = ctx

    def restore_ctx(self):
        self.ctx = self._saved_ctx

    def buf(self, name):
        return self.sbuf_bufs.get(name) or self.const_bufs[name]

    def alias_buf(self, src_buf, name):
        b = Buf(name)
        b.w = src_buf.w
        b.rs = list(src_buf.rs)
        return b


def np_bf16(a):
    return np.asarray(a, dtype=ml_dtypes.bfloat16)



def setup_common_consts(env: Env):
    nc, p = env.nc, env.p
    idb = env.add_const("identb", np_bf16(np.eye(128)))
    ones = env.add_const("onesb", np_bf16(np.ones((1, 128))))
    onesc = env.add_const("onescol", np_bf16(np.ones((128, 1))))
    idf = env.add_const("identf", np.eye(128, dtype=np.float32))
    cm = env.add_const("cmask128", np.triu(np.full((128, 128), -1e30, np.float32), 1))
    env.sbuf("identb", [128, 128], BF16)
    env.sbuf("onesb", [1, 128], BF16)
    env.sbuf("onescol", [128, 1], BF16)
    env.sbuf("identf", [128, 128], F32)
    env.sbuf("cmask", [128, 128], F32)
    env.sbuf("m_onesf", [1, 128], F32)
    p.dma("sp", env.sb["identb"][:], idb, writes=[env.buf("identb")])
    p.dma("sp", env.sb["onesb"][:], ones, writes=[env.buf("onesb")])
    p.dma("sp", env.sb["onescol"][:], onesc, writes=[env.buf("onescol")])
    p.dma("sp", env.sb["identf"][:], idf, writes=[env.buf("identf")])
    p.dma("sp", env.sb["cmask"][:], cm, writes=[env.buf("cmask")])
    p.op("dve", lambda v: v.memset(env.sb["m_onesf"][:], 1.0), writes=[env.buf("m_onesf")])


def emit_ln_transpose(p, env, xt_ap, xt_buf, stats_ap, stats_buf, diag_ap, diag_buf,
                      xcb_ap, xcb_buf, sqd_ap, sqd_buf, pT_name, xlnT_ap, xlnT_buf,
                      n_t=128, out_f32_ap=None, out_f32_buf=None):
    nc = env.nc
    ident, identb_buf = env.sb["identb"], env.buf("identb")
    p.op("dve", lambda v: v.tensor_reduce(stats_ap[:, 0:1], xt_ap, AX.X, ALU.add, negate=True),
         reads=[xt_buf], writes=[stats_buf])
    p.op("dve", lambda v: v.tensor_scalar(stats_ap[:, 1:2], stats_ap[:, 0:1], 1.0 / C, None, ALU.mult),
         reads=[stats_buf], writes=[stats_buf])
    p.op("dve", lambda v: v.tensor_scalar(xcb_ap, xt_ap, stats_ap[:, 1:2], None, ALU.add),
         reads=[xt_buf, stats_buf], writes=[xcb_buf])
    p.op("act", lambda s: s.activation(sqd_ap, xcb_ap, AF.Square, accum_out=stats_ap[:, 2:3]),
         reads=[xcb_buf], writes=[sqd_buf, stats_buf])
    p.op("dve", lambda v: v.tensor_scalar(stats_ap[:, 3:4], stats_ap[:, 2:3], 1.0 / C, 1e-5, ALU.mult, ALU.add),
         reads=[stats_buf], writes=[stats_buf])
    p.op("act", lambda s: s.activation(stats_ap[:, 4:5], stats_ap[:, 3:4], AF.Sqrt),
         reads=[stats_buf], writes=[stats_buf])
    p.op("dve", lambda v: v.reciprocal(stats_ap[:, 5:6], stats_ap[:, 4:5]),
         reads=[stats_buf], writes=[stats_buf])
    p.op("dve", lambda v: v.tensor_scalar(diag_ap, ident[:n_t, :n_t], stats_ap[:, 5:6], None, ALU.mult),
         reads=[identb_buf, stats_buf], writes=[diag_buf])
    pT = env.psum[pT_name]
    pT_buf = env.buf(pT_name)

    def mm(t, xcb_ap=xcb_ap, diag_ap=diag_ap, pT=pT, n_t=n_t):
        last = None
        for c2 in range(NCH):
            last = t.matmul(pT[:, c2 * n_t:(c2 + 1) * n_t],
                            xcb_ap[:, c2 * 128:(c2 + 1) * 128], diag_ap,
                            start=True, stop=True)
        return last
    p.op("pe", mm, reads=[xcb_buf, diag_buf], writes=[pT_buf])
    if isinstance(xlnT_ap, list):
        def evac(v, pT=pT, n_t=n_t, xlnT_ap=xlnT_ap):
            last = None
            for c2 in range(NCH):
                last = v.tensor_copy(xlnT_ap[c2], pT[:, c2 * n_t:(c2 + 1) * n_t])
            return last
        p.op("dve", evac, reads=[pT_buf], writes=[xlnT_buf])
    else:
        p.op("dve", lambda v: v.tensor_copy(xlnT_ap, pT[:, 0:NCH * n_t]),
             reads=[pT_buf], writes=[xlnT_buf])
    if out_f32_ap is not None:
        p.op("act", lambda s: s.activation(out_f32_ap, xcb_ap, AF.Copy, scale=stats_ap[:, 5:6]),
             reads=[xcb_buf, stats_buf], writes=[out_f32_buf])


def emit_weight_T(p, env, w_hbm_ap, rows, cols, wb_name, wT_ap, wT_buf, pname="pC",
                  n_row_chunks=None, accum_into=None):
    nc = env.nc
    assert rows % 128 == 0 or rows <= 128
    assert cols % 128 == 0
    ncc = cols // 128
    nrc = (rows + 127) // 128
    wtmp = env.sbuf(wb_name, [128, cols], BF16)
    wtmp_bufs = [Buf(wb_name + str(i)) for i in range(2)]
    ident, identb_buf = env.sb["identb"], env.buf("identb")
    pT = env.psum[pname]
    pT_buf = env.buf(pname)
    for rc in range(nrc):
        r0 = rc * 128
        nr = min(128, rows - r0)
        slot = rc % 2
        b = wtmp_bufs[0]
        p.dma("pool", wtmp[:nr, :], w_hbm_ap[r0:r0 + nr, :], writes=[b])
        for cc in range(ncc):
            def mm(t, nr=nr, cc=cc, wtmp=wtmp):
                return t.matmul(pT[:, 0:nr], wtmp[:nr, cc * 128:(cc + 1) * 128],
                                ident[:nr, :nr], start=True, stop=True)
            p.op("pe", mm, reads=[b, identb_buf], writes=[pT_buf])
            dst = wT_ap[:, cc * rows + r0: cc * rows + r0 + nr]
            p.op("dve", lambda v, dst=dst, nr=nr: v.tensor_copy(dst, pT[:, 0:nr]),
                 reads=[pT_buf], writes=[wT_buf])
    return wtmp


def emit_fvec128(p, env, vec_hbm_ap, F, name, scale=None, bias=None):
    nc = env.nc
    ncv = F // 128
    tmp = env.sbuf(name + "_r", [max(ncv, 1), 128], F32)
    tmp_buf = Buf(name + "_r")
    out = env.sbuf(name, [128, ncv], F32)
    out_buf = env.buf(name)
    if "identf" not in env.sb:
        idf = env.add_const("identf", np.eye(128, dtype=np.float32))
        env.sbuf("identf", [128, 128], F32)
        p.dma("sp", env.sb["identf"][:], idf, writes=[env.buf("identf")])
    identf, identf_buf = env.sb["identf"], env.buf("identf")
    p.dma("sp", tmp[:, :], vec_hbm_ap.rearrange("(a p) -> a p", p=128), writes=[tmp_buf])
    pname = "pD" if "pD" in env.psum else list(env.psum)[-1]
    pT = env.psum[pname]
    pT_buf = env.buf(pname)
    p.op("pe", lambda t: t.matmul(pT[:, 0:ncv], tmp[:ncv, :], identf[:ncv, :ncv],
                                  start=True, stop=True),
         reads=[tmp_buf, identf_buf], writes=[pT_buf])
    if scale is None and bias is None:
        p.op("dve", lambda v: v.tensor_copy(out[:, :], pT[:, 0:ncv]),
             reads=[pT_buf], writes=[out_buf])
    else:
        s = 1.0 if scale is None else scale
        b = 0.0 if bias is None else bias
        p.op("dve", lambda v: v.tensor_scalar(out[:, :], pT[:, 0:ncv], s, b, ALU.mult, ALU.add),
             reads=[pT_buf], writes=[out_buf])
    return out, out_buf


def emit_row_load(p, env, vec_hbm_ap, F, name, dt=BF16):
    out = env.sbuf(name, [1, F], dt)
    out_buf = env.buf(name)
    stream = "pool" if dt == BF16 else "sp"
    p.dma(stream, out[0:1, :], vec_hbm_ap[None, :] if len(vec_hbm_ap.shape) == 1 else vec_hbm_ap,
          writes=[out_buf])
    return out, out_buf



def emit_L0_ffn(p, env, x_in, x_out, W, ntok):
    nc = env.nc
    ntiles = ntok // 128
    weffT = env.sbuf("weffT", [128, NCH * C], BF16)
    weffT_buf = env.buf("weffT")
    emit_weight_T(p, env, W["W"], C, C, "l0w_tmp", weffT[:, :], weffT_buf, pname="pC")
    ld = env.sbuf("ld", [4, 100], BF16); ld_buf = env.buf("ld")
    lu = env.sbuf("lu", [50, 4], BF16); lu_buf = env.buf("lu")
    da = env.sbuf("da", [100, C], BF16); da_buf = env.buf("da")
    ua = env.sbuf("ua", [128, 8 * 50], BF16); ua_buf = env.buf("ua")
    p.dma("pool", ld[:, :], W["lora_down"], writes=[ld_buf])
    p.dma("pool", lu[:, :], W["lora_up"], writes=[lu_buf])
    p.dma("pool", da[:, :], W["down_aux"], writes=[da_buf])
    for cc in range(8):
        p.dma("pool", ua[:, cc * 50:(cc + 1) * 50], W["up_aux"][cc * 128:(cc + 1) * 128, :],
              writes=[ua_buf])
    ident, identb_buf = env.sb["identb"], env.buf("identb")
    pC, pC_buf = env.psum["pC"], env.buf("pC")
    pD, pD_buf = env.psum["pD"], env.buf("pD")
    ldT = env.sbuf("ldT", [100, 4], BF16); ldT_buf = env.buf("ldT")
    p.op("pe", lambda t: t.matmul(pC[0:100, 0:4], ld[:, 0:100], ident[0:4, 0:4], start=True, stop=True),
         reads=[ld_buf, identb_buf], writes=[pC_buf])
    p.op("dve", lambda v: v.tensor_copy(ldT[:, :], pC[0:100, 0:4]), reads=[pC_buf], writes=[ldT_buf])
    A = env.sbuf("A", [4, C], BF16); A_buf = env.buf("A")
    def mmA(t):
        t.matmul(pD[0:4, 0:512], ldT[:, :], da[:, 0:512], start=True, stop=True)
        return t.matmul(pD[0:4, 512:1024], ldT[:, :], da[:, 512:1024], start=True, stop=True)
    p.op("pe", mmA, reads=[ldT_buf, da_buf], writes=[pD_buf])
    p.op("dve", lambda v: v.tensor_copy(A[:, :], pD[0:4, 0:1024]), reads=[pD_buf], writes=[A_buf])
    luT = env.sbuf("luT", [4, 50], BF16); luT_buf = env.buf("luT")
    p.op("pe", lambda t: t.matmul(pC[0:4, 0:50], lu[:, 0:4], ident[0:50, 0:50], start=True, stop=True),
         reads=[lu_buf, identb_buf], writes=[pC_buf])
    p.op("dve", lambda v: v.tensor_copy(luT[:, :], pC[0:4, 0:50]), reads=[pC_buf], writes=[luT_buf])
    Bm = env.sbuf("Bm", [50, C], BF16); Bm_buf = env.buf("Bm")
    def mmB(t):
        t.matmul(pD[0:50, 0:512], luT[:, :], A[:, 0:512], start=True, stop=True)
        return t.matmul(pD[0:50, 512:1024], luT[:, :], A[:, 512:1024], start=True, stop=True)
    p.op("pe", mmB, reads=[luT_buf, A_buf], writes=[pD_buf])
    p.op("dve", lambda v: v.tensor_copy(Bm[:, :], pD[0:50, 0:1024]), reads=[pD_buf], writes=[Bm_buf])
    uaT = env.sbuf("uaT", [50, C], BF16); uaT_buf = env.buf("uaT")
    for cc2 in range(8):
        p.op("pe", lambda t, cc2=cc2: t.matmul(pC[0:50, 0:128], ua[:, cc2 * 50:(cc2 + 1) * 50],
                                               ident[:, :], start=True, stop=True),
             reads=[ua_buf, identb_buf], writes=[pC_buf])
        p.op("dve", lambda v, cc2=cc2: v.tensor_copy(uaT[:, cc2 * 128:(cc2 + 1) * 128], pC[0:50, 0:128]),
             reads=[pC_buf], writes=[uaT_buf])
    for ic in range(8):
        def mmCT(t, ic=ic):
            t.matmul(pD[:, 0:512], Bm[:, ic * 128:(ic + 1) * 128], uaT[:, 0:512], start=True, stop=True)
            return t.matmul(pD[:, 512:1024], Bm[:, ic * 128:(ic + 1) * 128], uaT[:, 512:1024], start=True, stop=True)
        p.op("pe", mmCT, reads=[Bm_buf, uaT_buf], writes=[pD_buf])
        dst = weffT[:, ic * C:(ic + 1) * C]
        p.op("dve", lambda v, dst=dst: v.tensor_tensor(dst, pD[:, 0:1024], dst, ALU.add),
             reads=[pD_buf, weffT_buf], writes=[weffT_buf])
    b_row, b_row_buf = emit_row_load(p, env, W["b"], C, "l0b_row")

    NS = 2
    xt = [env.sbuf(f"xt{i}", [128, C], F32) for i in range(NS)]
    xo = [env.sbuf(f"xo{i}", [128, C], F32) for i in range(NS)]
    stats = [env.sbuf(f"stats{i}", [128, 8], F32) for i in range(NS)]
    diag = [env.sbuf(f"diag{i}", [128, 128], BF16) for i in range(NS)]
    xcb = [env.sbuf(f"xcb{i}", [128, C], BF16) for i in range(NS)]
    sqd = env.sbuf("sqd", [128, C], BF16)
    xlnT = [env.sbuf(f"xlnT{i}", [128, C], BF16) for i in range(NS)]
    bufs = {n: [env.buf(f"{n}{i}") for i in range(NS)] for n in ("xt", "xo", "stats", "diag", "xcb", "xlnT")}
    sqd_buf = env.buf("sqd")
    ones, ones_buf = env.sb["onesb"], env.buf("onesb")

    for ti in range(ntiles):
        sl = ti % NS
        psl = "pA" if ti % 2 == 0 else "pB"
        pP, pP_buf = env.psum[psl], env.buf(psl)
        pTname = "pC" if ti % 2 == 0 else "pD"
        row = x_in[ti * 128:(ti + 1) * 128, :]
        p.dma("sp", xt[sl][:, :], row, writes=[bufs["xt"][sl]], sem=ti % 3)
        emit_ln_transpose(p, env, xt[sl][:, :], bufs["xt"][sl], stats[sl][:, :], bufs["stats"][sl],
                          diag[sl][:, :], bufs["diag"][sl], xcb[sl][:, :], bufs["xcb"][sl],
                          sqd[:, :], sqd_buf, pTname, xlnT[sl][:, :], bufs["xlnT"][sl])
        def mk_proj(half):
            def mm(t, half=half, sl=sl, pP=pP):
                for c2 in range(NCH):
                    t.matmul(pP[:, half * 512:half * 512 + 512],
                             xlnT[sl][:, c2 * 128:(c2 + 1) * 128],
                             weffT[:, c2 * C + half * 512: c2 * C + half * 512 + 512],
                             start=(c2 == 0), stop=False)
                return t.matmul(pP[:, half * 512:half * 512 + 512], ones[0:1, :],
                                b_row[0:1, half * 512:half * 512 + 512], start=False, stop=True)
            return mm
        p.op("pe", mk_proj(0), reads=[bufs["xlnT"][sl], weffT_buf, ones_buf, b_row_buf], writes=[pP_buf])
        p.op("pe", mk_proj(1), reads=[bufs["xlnT"][sl], weffT_buf, ones_buf, b_row_buf], writes=[pP_buf])
        p.op("dve", lambda v, sl=sl, pP=pP: v.tensor_add(xo[sl][:, :], pP[:, 0:C], xt[sl][:, :]),
             reads=[pP_buf, bufs["xt"][sl]], writes=[bufs["xo"][sl]])
        p.dma("sp", x_out[ti * 128:(ti + 1) * 128, :], xo[sl][:, :], reads=[bufs["xo"][sl]], sem=3 + ti % 3)



def emit_L1_ffn(p, env, x_in, x_out, W, ntok):
    nc = env.nc
    DF = 4096
    NFC = DF // 128
    TG = 512
    ngroups = ntok // TG
    ident, identb_buf = env.sb["identb"], env.buf("identb")
    ones, ones_buf = env.sb["onesb"], env.buf("onesb")

    w1 = env.sbuf("w1r", [128, NCH * DF], BF16)
    w1_buf = env.buf("w1r")
    w2t = [env.sbuf(f"w2t{i}", [128, C], BF16) for i in range(2)]
    w2t_bufs = [env.buf(f"w2t{i}") for i in range(2)]
    for c in range(NCH):
        p.dma("pool", w1[:, c * DF:(c + 1) * DF], W["W1"][c * 128:(c + 1) * 128, :], writes=[w1_buf])
    b2row, b2row_buf = emit_row_load(p, env, W["b2"], C, "l1b2_row")

    b1v, b1v_buf = emit_fvec128(p, env, W["b1"], DF, "l1b1")
    thv, thv_buf = emit_fvec128(p, env, W["quantum_phases"], DF, "l1th")
    wrv, wrv_buf = emit_fvec128(p, env, W["quantum_weights_real"], DF, "l1wr")
    wiv, wiv_buf = emit_fvec128(p, env, W["quantum_weights_imag"], DF, "l1wi")
    PI = float(np.pi)
    fx = env.sbuf("l1fx", [128, 8 * NFC], F32)
    fx_buf = env.buf("l1fx")
    V = NFC

    def colap(i):
        return fx[:, i * V:(i + 1) * V]

    def reduce_sin(src_ap, src_buf, dst_i, shift):
        t_i, g_i = 7, 5
        p.op("dve", lambda v: v.tensor_scalar(colap(t_i), src_ap, shift, None, ALU.add),
             reads=[src_buf], writes=[fx_buf])
        for _ in range(9):
            p.op("dve", lambda v: v.tensor_scalar(colap(dst_i), colap(t_i), PI, None, ALU.is_ge),
                 reads=[fx_buf], writes=[fx_buf])
            p.op("dve", lambda v: v.scalar_tensor_tensor(colap(t_i), colap(dst_i), -2 * PI, colap(t_i),
                                                         ALU.mult, ALU.add),
                 reads=[fx_buf], writes=[fx_buf])
            p.op("dve", lambda v: v.tensor_scalar(colap(dst_i), colap(t_i), -PI, None, ALU.is_lt),
                 reads=[fx_buf], writes=[fx_buf])
            p.op("dve", lambda v: v.scalar_tensor_tensor(colap(t_i), colap(dst_i), 2 * PI, colap(t_i),
                                                         ALU.mult, ALU.add),
                 reads=[fx_buf], writes=[fx_buf])
        p.op("dve", lambda v: v.tensor_copy(colap(dst_i), colap(t_i)),
             reads=[fx_buf], writes=[fx_buf])

    reduce_sin(thv[:, :], thv_buf, 0, 0.0)
    p.op("act", lambda s: s.activation(colap(0), colap(0), AF.Sin), reads=[fx_buf], writes=[fx_buf])
    reduce_sin(thv[:, :], thv_buf, 1, PI / 2)
    p.op("act", lambda s: s.activation(colap(1), colap(1), AF.Sin), reads=[fx_buf], writes=[fx_buf])
    p.op("dve", lambda v: v.tensor_mul(colap(2), colap(1), wrv[:, :]), reads=[fx_buf, wrv_buf], writes=[fx_buf])
    p.op("dve", lambda v: v.tensor_mul(colap(3), colap(0), wiv[:, :]), reads=[fx_buf, wiv_buf], writes=[fx_buf])
    p.op("dve", lambda v: v.tensor_add(colap(2), colap(2), colap(3)), reads=[fx_buf], writes=[fx_buf])
    p.op("dve", lambda v: v.tensor_mul(colap(2), colap(2), wiv[:, :]), reads=[fx_buf, wiv_buf], writes=[fx_buf])
    p.op("dve", lambda v: v.scalar_tensor_tensor(colap(3), colap(2), 0.1, colap(0), ALU.mult, ALU.mult),
         reads=[fx_buf], writes=[fx_buf])
    p.op("dve", lambda v: v.scalar_tensor_tensor(colap(4), colap(2), 0.1, colap(1), ALU.mult, ALU.mult),
         reads=[fx_buf], writes=[fx_buf])
    p.op("dve", lambda v: v.tensor_scalar(colap(5), b1v[:, :], 0.1, None, ALU.mult),
         reads=[b1v_buf], writes=[fx_buf])
    p.op("dve", lambda v: v.tensor_scalar(colap(6), b1v[:, :], 0.1, PI / 2, ALU.mult, ALU.add),
         reads=[b1v_buf], writes=[fx_buf])

    NS = 4
    xt = [env.sbuf(f"l1xt{i}", [128, C], F32) for i in range(NS)]
    xo = [env.sbuf(f"l1xo{i}", [128, C], F32) for i in range(NS)]
    stats = [env.sbuf(f"l1st{i}", [128, 8], F32) for i in range(NS)]
    diag = [env.sbuf(f"l1dg{i}", [128, 128], BF16) for i in range(NS)]
    xcb = [env.sbuf(f"l1xc{i}", [128, C], BF16) for i in range(NS)]
    sqd = env.sbuf("l1sq", [128, C], BF16)
    bufs = {n: [env.buf(f"l1{n}{i}") for i in range(NS)] for n in ("xt", "xo", "st", "dg", "xc")}
    sqd_buf = env.buf("l1sq")
    xlnT = env.sbuf("l1xlnT", [128, NCH * TG], BF16)
    xlnT_buf = env.buf("l1xlnT")
    slab = env.sbuf("l1slab", [128, NFC * TG], BF16)
    slab_buf = env.buf("l1slab")
    ch = [env.sbuf(f"l1ch{i}", [128, TG], F32) for i in range(2)]
    sh = [env.sbuf(f"l1sh{i}", [128, TG], F32) for i in range(2)]
    t1 = [env.sbuf(f"l1t1{i}", [128, TG], F32) for i in range(2)]
    chb = [env.buf(f"l1ch{i}") for i in range(2)]
    shb = [env.buf(f"l1sh{i}") for i in range(2)]
    t1b = [env.buf(f"l1t1{i}") for i in range(2)]

    for g in range(ngroups):
        for tt in range(TG // 128):
            ti = g * (TG // 128) + tt
            sl = ti % NS
            row = x_in[ti * 128:(ti + 1) * 128, :]
            p.dma("sp", xt[sl][:, :], row, writes=[bufs["xt"][sl]], sem=tt % 2)
            dsts = [xlnT[:, c * TG + tt * 128: c * TG + (tt + 1) * 128] for c in range(NCH)]
            emit_ln_transpose(p, env, xt[sl][:, :], bufs["xt"][sl], stats[sl][:, :], bufs["st"][sl],
                              diag[sl][:, :], bufs["dg"][sl], xcb[sl][:, :], bufs["xc"][sl],
                              sqd[:, :], sqd_buf, "pD", dsts, xlnT_buf)
        for fc in range(NFC):
            half = fc % 2
            pH, pH_buf = env.psum["pA"], env.buf("pA")
            hap = pH[:, half * 512:(half + 1) * 512]

            def mmh(t, fc=fc, hap=hap):
                last = None
                for c in range(NCH):
                    last = t.matmul(hap, w1[:, c * DF + fc * 128:c * DF + (fc + 1) * 128],
                                    xlnT[:, c * TG:(c + 1) * TG], start=(c == 0), stop=(c == NCH - 1))
                return last
            p.op("pe", mmh, reads=[w1_buf, xlnT_buf], writes=[pH_buf])
            fcs = slice(fc, fc + 1)
            p.op("act", lambda s, hap=hap, fc=fc: s.activation(ch[fc % 2][:, :], hap, AF.Sin,
                 bias=fx[:, 6 * V + fc:6 * V + fc + 1], scale=0.1),
                 reads=[pH_buf, fx_buf], writes=[chb[fc % 2]])
            p.op("act", lambda s, hap=hap, fc=fc: s.activation(sh[fc % 2][:, :], hap, AF.Sin,
                 bias=fx[:, 5 * V + fc:5 * V + fc + 1], scale=0.1),
                 reads=[pH_buf, fx_buf], writes=[shb[fc % 2]])
            p.op("dve", lambda v, fc=fc: v.tensor_scalar(
                 t1[fc % 2][:, :], ch[fc % 2][:, :], fx[:, 3 * V + fc:3 * V + fc + 1],
                 b1v[:, fc:fc + 1], ALU.mult, ALU.add),
                 reads=[chb[fc % 2], fx_buf, b1v_buf], writes=[t1b[fc % 2]])
            p.op("dve", lambda v, fc=fc: v.scalar_tensor_tensor(
                 t1[fc % 2][:, :], sh[fc % 2][:, :], fx[:, 4 * V + fc:4 * V + fc + 1], t1[fc % 2][:, :],
                 ALU.mult, ALU.add),
                 reads=[shb[fc % 2], fx_buf], writes=[t1b[fc % 2]])
            p.op("dve", lambda v, hap=hap, fc=fc: v.tensor_tensor(
                 t1[fc % 2][:, :], hap, t1[fc % 2][:, :], ALU.add),
                 reads=[pH_buf, t1b[fc % 2]], writes=[t1b[fc % 2]])
            p.op("dve", lambda v, fc=fc: v.tensor_scalar(
                 slab[:, fc * TG:(fc + 1) * TG], t1[fc % 2][:, :], 0.0, None, ALU.max),
                 reads=[t1b[fc % 2]], writes=[slab_buf])
        pos = [env.psum[n] for n in ("pA", "pB", "pC", "pD")]
        pos_bufs = [env.buf(n) for n in ("pA", "pB", "pC", "pD")]
        for fc in range(NFC):
            wsl = fc % 2
            p.dma("pool", w2t[wsl][:, :], W["W2"][fc * 128:(fc + 1) * 128, :], writes=[w2t_bufs[wsl]])

            def mmo(t, fc=fc, wsl=wsl):
                last = None
                for tt in range(TG // 128):
                    for half in range(2):
                        last = t.matmul(pos[tt][:, half * 512:(half + 1) * 512],
                                        slab[:, fc * TG + tt * 128:fc * TG + (tt + 1) * 128],
                                        w2t[wsl][:, half * 512:(half + 1) * 512],
                                        start=(fc == 0), stop=False)
                return last
            p.op("pe", mmo, reads=[slab_buf, w2t_bufs[wsl]], writes=pos_bufs)

        def mmbias(t):
            last = None
            for tt in range(TG // 128):
                for half in range(2):
                    last = t.matmul(pos[tt][:, half * 512:(half + 1) * 512], ones[0:1, :],
                                    b2row[0:1, half * 512:(half + 1) * 512], start=False, stop=True)
            return last
        p.op("pe", mmbias, reads=[ones_buf, b2row_buf], writes=pos_bufs)
        for tt in range(TG // 128):
            ti = g * (TG // 128) + tt
            sl = ti % NS
            pO, pO_buf = pos[tt], pos_bufs[tt]
            p.op("dve", lambda v, sl=sl, pO=pO: v.tensor_add(xo[sl][:, :], pO[:, 0:C], xt[sl][:, :]),
                 reads=[pO_buf, bufs["xt"][sl]], writes=[bufs["xo"][sl]])
            p.dma("sp", x_out[ti * 128:(ti + 1) * 128, :], xo[sl][:, :], reads=[bufs["xo"][sl]], sem=2 + tt % 2)



def emit_std_attention(p, env, x_in, x_out, W, ntok, cfg):
    nc = env.nc
    T = cfg["T"]
    nseq = ntok // T
    NT = T // 128
    H, HD = 16, 64
    ident, identb_buf = env.sb["identb"], env.buf("identb")
    ones, ones_buf = env.sb["onesb"], env.buf("onesb")
    pS, pS_buf = env.psum["pS"], env.buf("pS")
    pT, pT_buf = env.psum["pT"], env.buf("pT")
    pO, pO_buf = env.psum["pO"], env.buf("pO")

    cm = env.add_const("cmask128", np.triu(np.full((128, 128), -1e30, np.float32), 1))
    if "cmask" not in env.sb:
        cmask0 = env.sbuf("cmask", [128, 128], F32)
        p.dma("sp", cmask0[:, :], cm, writes=[env.buf("cmask")])
    cmask, cmask_buf = env.sb["cmask"], env.buf("cmask")

    wininfo = {}
    if cfg.get("kind") == "window":
        RAD = cfg.get("radius", 256)
        CHK = cfg.get("chunk", 256)
        mask_cache = {}
        for qt in range(T // 128):
            n = (qt * 128) // CHK
            k0 = max(0, n * CHK - CHK)
            k1 = min(T, n * CHK + 2 * CHK)
            width = k1 - k0
            qpos = qt * 128 + np.arange(128)[:, None]
            kpos = k0 + np.arange(width)[None, :]
            valid = np.abs(qpos - kpos) <= RAD
            marr = np.where(valid, 0.0, -1e30).astype(np.float32)
            key = marr.tobytes()
            if key not in mask_cache:
                mname = f"wmask{len(mask_cache)}"
                mc = env.add_const(mname, np_bf16(marr))
                msb = env.sbuf(mname, [128, width], BF16)
                p.dma("pool", msb[:, :], mc, writes=[env.buf(mname)])
                mask_cache[key] = mname
            wininfo[qt] = (k0, width, mask_cache[key])

    wsb = {}
    wbuf = {}
    for wn in ["wo"]:
        wsb[wn] = env.sbuf("a_" + wn, [128, NCH * C], BF16)
        wbuf[wn] = env.buf("a_" + wn)
        for c in range(NCH):
            p.dma("pool", wsb[wn][:, c * C:(c + 1) * C], cfg[wn][c * 128:(c + 1) * 128, :],
                  writes=[wbuf[wn]])
    wstr = [env.sbuf(f"a_ws{i}", [128, NCH * 512], BF16) for i in range(2)]
    wstr_buf = [env.buf(f"a_ws{i}") for i in range(2)]
    brow = {}
    brow_buf = {}
    fvecs = {}
    for bn in ("bv", "bo"):
        if cfg.get(bn) is not None:
            brow[bn], brow_buf[bn] = emit_row_load(p, env, cfg[bn], C, "a_" + bn)
    for bn, bscale in (("bq", cfg.get("qscale", 1.0)), ("bk", 1.0)):
        if cfg.get(bn) is not None:
            fvecs[bn] = emit_fvec128(p, env, cfg[bn], C, "a_fv" + bn, scale=bscale)

    xlnT = env.sbuf("a_xlnT", [128, NCH * T], BF16); xlnT_buf = env.buf("a_xlnT")
    qfm = env.sbuf("a_qfm", [128, NCH * T], BF16); qfm_buf = env.buf("a_qfm")
    kfm = env.sbuf("a_kfm", [128, NCH * T], BF16); kfm_buf = env.buf("a_kfm")
    vtok = env.sbuf("a_vtok", [128, NT * C], BF16); vtok_buf = env.buf("a_vtok")
    statsA = env.sbuf("a_stats", [128, 2 * NT], F32); statsA_buf = env.buf("a_stats")

    NS = 2
    xt = [env.sbuf(f"a_xt{i}", [128, C], F32) for i in range(NS)]
    stats = [env.sbuf(f"a_st{i}", [128, 8], F32) for i in range(NS)]
    diag = [env.sbuf(f"a_dg{i}", [128, 128], BF16) for i in range(NS)]
    xcb = [env.sbuf(f"a_xc{i}", [128, C], BF16) for i in range(NS)]
    sqd = env.sbuf("a_sq", [128, C], BF16)
    xo = [env.sbuf(f"a_xo{i}", [128, C], F32) for i in range(NS)]
    bufs = {n: [env.buf(f"a_{n}{i}") for i in range(NS)] for n in ("xt", "st", "dg", "xc", "xo")}
    sqd_buf = env.buf("a_sq")
    EW = 768 if cfg.get("kind") == "window" else T
    Es = [env.sbuf(f"a_E{i}", [128, EW], BF16) for i in range(2)]
    Es_buf = [env.buf(f"a_E{i}") for i in range(2)]
    ETs = env.sbuf("a_ET", [128, 1024], BF16); ETs_buf = env.buf("a_ET")
    st2 = [env.sbuf(f"a_s2_{i}", [128, 4], F32) for i in range(2)]
    st2_buf = [env.buf(f"a_s2_{i}") for i in range(2)]
    dgl = [env.sbuf(f"a_dl{i}", [128, 128], BF16) for i in range(2)]
    dgl_buf = [env.buf(f"a_dl{i}") for i in range(2)]
    ysb = env.sbuf("a_y", [128, NCH * 128], BF16); ysb_buf = env.buf("a_y")

    qscale = cfg.get("qscale", 1.0)

    for s in range(nseq):
        base = s * T
        for tt in range(NT):
            sl = tt % NS
            p.dma("sp", xt[sl][:, :], x_in[base + tt * 128: base + (tt + 1) * 128, :],
                  writes=[bufs["xt"][sl]], sem=tt % 2)
            dsts = [xlnT[:, c * T + tt * 128: c * T + (tt + 1) * 128] for c in range(NCH)]
            emit_ln_transpose(p, env, xt[sl][:, :], bufs["xt"][sl], stats[sl][:, :], bufs["st"][sl],
                              diag[sl][:, :], bufs["dg"][sl], xcb[sl][:, :], bufs["xc"][sl],
                              sqd[:, :], sqd_buf, "pT", dsts, xlnT_buf)
            if cfg.get("extra_ln_residual"):
                p.op("dve", lambda v, sl=sl, tt=tt: v.tensor_copy(
                     statsA[:, 2 * tt:2 * tt + 1], stats[sl][:, 1:2]),
                     reads=[bufs["st"][sl]], writes=[statsA_buf])
                p.op("dve", lambda v, sl=sl, tt=tt: v.tensor_copy(
                     statsA[:, 2 * tt + 1:2 * tt + 2], stats[sl][:, 5:6]),
                     reads=[bufs["st"][sl]], writes=[statsA_buf])
        for dst, dst_buf, wn, scale, tokmajor in (
                (qfm, qfm_buf, "wq", qscale, False),
                (kfm, kfm_buf, "wk", 1.0, False),
                (vtok, vtok_buf, "wv", 1.0, True)):
            bn = {"wq": "bq", "wk": "bk", "wv": "bv"}[wn]
            has_b = cfg.get(bn) is not None
            if not tokmajor:
                bfv = bfv_buf = None
                if has_b:
                    bfv, bfv_buf = fvecs[bn]
                wsrc = cfg[wn].rearrange("(c p) f -> p c f", p=128)
                for fc in range(NCH):
                    wslot = fc % 2
                    wt = wstr[wslot]
                    wt3 = wt[:, 0:NCH * 128].rearrange("p (c j) -> p c j", j=128)
                    p.dma("pool", wt3, wsrc[:, :, fc * 128:(fc + 1) * 128],
                          writes=[wstr_buf[wslot]])
                    for tc in range(T // 512):
                        half = (fc + tc) % 2
                        hap = pS[:, half * 512:(half + 1) * 512]

                        def mmq(t, fc=fc, tc=tc, hap=hap, wt=wt):
                            for c in range(NCH):
                                last = t.matmul(hap, wt[:, c * 128:(c + 1) * 128],
                                                xlnT[:, c * T + tc * 512:c * T + (tc + 1) * 512],
                                                start=(c == 0), stop=(c == NCH - 1))
                            return last
                        p.op("pe", mmq, reads=[wstr_buf[wslot], xlnT_buf], writes=[pS_buf])
                        if has_b:
                            p.op("dve", lambda v, fc=fc, tc=tc, hap=hap, dst=dst, scale=scale, bfv=bfv:
                                 v.tensor_scalar(dst[:, fc * T + tc * 512:fc * T + (tc + 1) * 512],
                                                 hap, scale, bfv[:, fc:fc + 1], ALU.mult, ALU.add),
                                 reads=[pS_buf, bfv_buf], writes=[dst_buf])
                        else:
                            p.op("dve", lambda v, fc=fc, tc=tc, hap=hap, dst=dst, scale=scale:
                                 v.tensor_scalar(dst[:, fc * T + tc * 512:fc * T + (tc + 1) * 512],
                                                 hap, scale, None, ALU.mult),
                                 reads=[pS_buf], writes=[dst_buf])
            else:
                wsrc = cfg[wn].rearrange("(c p) f -> p c f", p=128)
                for half in range(2):
                    wslot = half
                    wt = wstr[wslot]
                    wt3 = wt[:, :].rearrange("p (c j) -> p c j", j=512)
                    p.dma("pool", wt3, wsrc[:, :, half * 512:(half + 1) * 512],
                          writes=[wstr_buf[wslot]])
                    for tt in range(NT):
                        hap = pS[:, (tt % 2) * 512:(tt % 2 + 1) * 512]

                        def mmv(t, tt=tt, half=half, hap=hap, wt=wt, has_b=has_b, bn=bn):
                            for c in range(NCH):
                                last = t.matmul(hap, xlnT[:, c * T + tt * 128:c * T + (tt + 1) * 128],
                                                wt[:, c * 512:(c + 1) * 512],
                                                start=(c == 0), stop=(c == NCH - 1) and not has_b)
                            if has_b:
                                last = t.matmul(hap, ones[0:1, :],
                                                brow[bn][0:1, half * 512:(half + 1) * 512],
                                                start=False, stop=True)
                            return last
                        p.op("pe", mmv, reads=[wstr_buf[wslot], xlnT_buf, ones_buf] +
                             ([brow_buf[bn]] if has_b else []), writes=[pS_buf])
                        p.op("dve", lambda v, tt=tt, half=half, hap=hap, dst=dst:
                             v.tensor_copy(dst[:, tt * C + half * 512:tt * C + (half + 1) * 512], hap),
                             reads=[pS_buf], writes=[dst_buf])
                continue
        for qt in range(NT):
            if cfg.get("kind") == "window":
                kbase, kmax, mname = wininfo[qt]
            else:
                kbase, kmax, mname = 0, (qt + 1) * 128, None
            n_ks = kmax // 128
            par = qt % 2
            for h in range(H):
                hp = 64 * (h % 2)
                hc = h // 2
                def mms(t, qt=qt, h=h, hp=hp, hc=hc, kmax=kmax, kbase=kbase):
                    last = None
                    for k0 in range(0, kmax, 512):
                        kn = min(512, kmax - k0)
                        last = t.matmul(pS[:, k0:k0 + kn],
                                        qfm[hp:hp + 64, hc * T + qt * 128:hc * T + (qt + 1) * 128],
                                        kfm[hp:hp + 64, hc * T + kbase + k0:hc * T + kbase + k0 + kn],
                                        start=True, stop=True)
                    return last
                p.op("pe", mms, reads=[qfm_buf, kfm_buf], writes=[pS_buf])
                if mname is None:
                    p.op("dve", lambda v, qt=qt, kmax=kmax: v.tensor_tensor(
                         pS[:, kmax - 128:kmax], pS[:, kmax - 128:kmax], cmask[:, :], ALU.add),
                         reads=[cmask_buf], writes=[pS_buf])
                else:
                    p.op("dve", lambda v, kmax=kmax, mname=mname: v.tensor_tensor(
                         pS[:, 0:kmax], pS[:, 0:kmax], env.sb[mname][:, 0:kmax], ALU.add),
                         reads=[env.buf(mname)], writes=[pS_buf])
                p.op("dve", lambda v, kmax=kmax, par=par: v.tensor_reduce(
                     st2[par][:, 0:1], pS[:, 0:kmax], AX.X, ALU.max, negate=True),
                     reads=[pS_buf], writes=[st2_buf[par]])
                p.op("act", lambda s2, kmax=kmax, par=par: s2.activation(
                     Es[par][:, 0:kmax], pS[:, 0:kmax], AF.Exp, bias=st2[par][:, 0:1],
                     accum_out=st2[par][:, 1:2]),
                     reads=[pS_buf, st2_buf[par]], writes=[Es_buf[par], st2_buf[par]])
                p.op("dve", lambda v, par=par: v.reciprocal(st2[par][:, 2:3], st2[par][:, 1:2]),
                     reads=[st2_buf[par]], writes=[st2_buf[par]])
                p.op("dve", lambda v, par=par: v.tensor_scalar(
                     dgl[par][:, :], ident[:, :], st2[par][:, 2:3], None, ALU.mult),
                     reads=[identb_buf, st2_buf[par]], writes=[dgl_buf[par]])
                for ks0 in range(0, n_ks, 8):
                    kn = min(8, n_ks - ks0)

                    def mmt(t, ks0=ks0, kn=kn, par=par):
                        for j in range(kn):
                            last = t.matmul(pT[:, j * 128:(j + 1) * 128],
                                            Es[par][:, (ks0 + j) * 128:(ks0 + j + 1) * 128],
                                            dgl[par][:, :], start=True, stop=True)
                        return last
                    p.op("pe", mmt, reads=[Es_buf[par], dgl_buf[par]], writes=[pT_buf])
                    p.op("dve", lambda v, kn=kn: v.tensor_copy(ETs[:, 0:kn * 128], pT[:, 0:kn * 128]),
                         reads=[pT_buf], writes=[ETs_buf])

                    def mmpv(t, ks0=ks0, kn=kn, h=h, n_ks=n_ks, kbase=kbase):
                        for j in range(kn):
                            ks = ks0 + j
                            ksg = kbase // 128 + ks
                            last = t.matmul(pO[0:64, 0:128],
                                            vtok[:, ksg * C + h * 64:ksg * C + (h + 1) * 64],
                                            ETs[:, j * 128:(j + 1) * 128],
                                            start=(ks == 0), stop=(ks == n_ks - 1))
                        return last
                    p.op("pe", mmpv, reads=[vtok_buf, ETs_buf], writes=[pO_buf])
                p.op("dve", lambda v, hp=hp, hc=hc: v.tensor_copy(
                     ysb[hp:hp + 64, hc * 128:(hc + 1) * 128], pO[0:64, 0:128]),
                     reads=[pO_buf], writes=[ysb_buf])
            sl = qt % NS
            p.dma("sp", xt[sl][:, :], x_in[base + qt * 128: base + (qt + 1) * 128, :],
                  writes=[bufs["xt"][sl]], sem=2 + qt % 2)
            for half in range(2):
                def mmo(t, half=half):
                    for cc in range(NCH):
                        last = t.matmul(pT[:, half * 512:(half + 1) * 512],
                                        ysb[:, cc * 128:(cc + 1) * 128],
                                        wsb["wo"][:, cc * C + half * 512:cc * C + (half + 1) * 512],
                                        start=(cc == 0), stop=(cc == NCH - 1) and "bo" not in brow)
                    if "bo" in brow:
                        last = t.matmul(pT[:, half * 512:(half + 1) * 512], ones[0:1, :],
                                        brow["bo"][0:1, half * 512:(half + 1) * 512],
                                        start=False, stop=True)
                    return last
                p.op("pe", mmo, reads=[ysb_buf, wbuf["wo"], ones_buf] +
                     ([brow_buf["bo"]] if "bo" in brow else []), writes=[pT_buf])
            p.op("dve", lambda v, sl=sl: v.tensor_add(xo[sl][:, :], pT[:, 0:C], xt[sl][:, :]),
                 reads=[pT_buf, bufs["xt"][sl]], writes=[bufs["xo"][sl]])
            if cfg.get("extra_ln_residual"):
                p.op("dve", lambda v, sl=sl, qt=qt: v.tensor_scalar(
                     xcb[sl][:, :], xt[sl][:, :], statsA[:, 2 * qt:2 * qt + 1],
                     statsA[:, 2 * qt + 1:2 * qt + 2], ALU.add, ALU.mult),
                     reads=[bufs["xt"][sl], statsA_buf], writes=[bufs["xc"][sl]])
                p.op("dve", lambda v, sl=sl: v.tensor_tensor(
                     xo[sl][:, :], xo[sl][:, :], xcb[sl][:, :], ALU.add),
                     reads=[bufs["xc"][sl]], writes=[bufs["xo"][sl]])
            p.dma("sp", x_out[base + qt * 128: base + (qt + 1) * 128, :], xo[sl][:, :],
                  reads=[bufs["xo"][sl]], sem=4 + qt % 2)


def emit_L1_attn(p, env, x_in, x_out, W, ntok, T=2048):
    nc = env.nc
    nseq = ntok // T
    NT = T // 128
    H, HD = 16, 64
    EPS = 1e-6
    ident, identb_buf = env.sb["identb"], env.buf("identb")
    ones, ones_buf = env.sb["onesb"], env.buf("onesb")
    pS, pS_buf = env.psum["pS"], env.buf("pS")
    pT, pT_buf = env.psum["pT"], env.buf("pT")
    pO, pO_buf = env.psum["pO"], env.buf("pO")

    cm = env.add_const("cmask128", np.triu(np.full((128, 128), -1e30, np.float32), 1))
    if "cmask" not in env.sb:
        cmask = env.sbuf("cmask", [128, 128], F32)
        p.dma("sp", cmask[:, :], cm, writes=[env.buf("cmask")])
    cmask, cmask_buf = env.sb["cmask"], env.buf("cmask")
    onescol, onescol_buf = env.sb["onescol"], env.buf("onescol")

    wqkv = cfg_w = W["Wqkv"]
    wout = W["Wout"]
    wo_sb = env.sbuf("q_wo", [128, NCH * C], BF16)
    wo_buf = env.buf("q_wo")
    for c in range(NCH):
        p.dma("pool", wo_sb[:, c * C:(c + 1) * C], wout[c * 128:(c + 1) * 128, :], writes=[wo_buf])
    wstr = [env.sbuf(f"q_ws{i}", [128, NCH * 256], BF16) for i in range(2)]
    wstr_buf = [env.buf(f"q_ws{i}") for i in range(2)]

    xlnT = env.sbuf("q_xlnT", [128, NCH * T], BF16); xlnT_buf = env.buf("q_xlnT")
    qfm = env.sbuf("q_qfm", [128, NCH * T], BF16); qfm_buf = env.buf("q_qfm")
    kfm = env.sbuf("q_kfm", [128, NCH * T], BF16); kfm_buf = env.buf("q_kfm")
    vtok = env.sbuf("q_vtok", [128, NT * C], BF16); vtok_buf = env.buf("q_vtok")

    NS = 2
    xt = [env.sbuf(f"q_xt{i}", [128, C], F32) for i in range(NS)]
    stats = [env.sbuf(f"q_st{i}", [128, 8], F32) for i in range(NS)]
    diag = [env.sbuf(f"q_dg{i}", [128, 128], BF16) for i in range(NS)]
    xcb = [env.sbuf(f"q_xc{i}", [128, C], BF16) for i in range(NS)]
    sqd = env.sbuf("q_sq", [128, C], BF16)
    xo = [env.sbuf(f"q_xo{i}", [128, C], F32) for i in range(NS)]
    bufs = {n: [env.buf(f"q_{n}{i}") for i in range(NS)] for n in ("xt", "st", "dg", "xc", "xo")}
    sqd_buf = env.buf("q_sq")

    Es = [env.sbuf(f"q_E{i}", [128, T], BF16) for i in range(2)]
    Es_buf = [env.buf(f"q_E{i}") for i in range(2)]
    ETs = env.sbuf("q_ET", [128, 1024], BF16); ETs_buf = env.buf("q_ET")
    st2 = [env.sbuf(f"q_s2_{i}", [128, 4], F32) for i in range(2)]
    st2_buf = [env.buf(f"q_s2_{i}") for i in range(2)]
    dgl = [env.sbuf(f"q_dl{i}", [128, 128], BF16) for i in range(2)]
    dgl_buf = [env.buf(f"q_dl{i}") for i in range(2)]
    ssb = env.sbuf("q_ssb", [128, T], F32); ssb_buf = env.buf("q_ssb")
    den = env.sbuf("q_den", [128, 512], F32); den_buf = env.buf("q_den")
    rin = env.sbuf("q_rin", [128, 512], F32); rin_buf = env.buf("q_rin")
    d2, d2_buf = den, den_buf
    qaug = env.sbuf("q_qaug", [128, T], BF16); qaug_buf = env.buf("q_qaug")
    kaug = env.sbuf("q_kaug", [128, T], BF16); kaug_buf = env.buf("q_kaug")
    sqs, sqs_buf = Es[1], Es_buf[1]
    nrow = env.sb["q_ssb"].bitcast(BF16); nrow_buf = ssb_buf
    qnc = env.sbuf("q_qnc", [128, NT], F32); qnc_buf = env.buf("q_qnc")
    y_all = env.sb["q_xlnT"]

    for s in range(nseq):
        base = s * T
        for tt in range(NT):
            sl = tt % NS
            p.dma("sp", xt[sl][:, :], x_in[base + tt * 128: base + (tt + 1) * 128, :],
                  writes=[bufs["xt"][sl]], sem=tt % 2)
            dsts = [xlnT[:, c * T + tt * 128: c * T + (tt + 1) * 128] for c in range(NCH)]
            emit_ln_transpose(p, env, xt[sl][:, :], bufs["xt"][sl], stats[sl][:, :], bufs["st"][sl],
                              diag[sl][:, :], bufs["dg"][sl], xcb[sl][:, :], bufs["xc"][sl],
                              sqd[:, :], sqd_buf, "pT", dsts, xlnT_buf)
        for dst, dst_buf, w_ap, tokmajor in ((qfm, qfm_buf, wqkv[:, 0:C], False),
                                             (kfm, kfm_buf, wqkv[:, C:2 * C], False),
                                             (vtok, vtok_buf, wqkv[:, 2 * C:3 * C], True)):
            wsrc = w_ap.rearrange("(c p) f -> p c f", p=128)
            if not tokmajor:
                for fc in range(NCH):
                    wslot = fc % 2
                    wt = wstr[wslot]
                    wt3 = wt[:, 0:NCH * 128].rearrange("p (c j) -> p c j", j=128)
                    p.dma("pool", wt3, wsrc[:, :, fc * 128:(fc + 1) * 128], writes=[wstr_buf[wslot]])
                    for tc in range(T // 512):
                        half = (fc + tc) % 2
                        hap = pS[:, half * 512:(half + 1) * 512]

                        def mmq(t, fc=fc, tc=tc, hap=hap, wt=wt):
                            for c in range(NCH):
                                last = t.matmul(hap, wt[:, c * 128:(c + 1) * 128],
                                                xlnT[:, c * T + tc * 512:c * T + (tc + 1) * 512],
                                                start=(c == 0), stop=(c == NCH - 1))
                            return last
                        p.op("pe", mmq, reads=[wstr_buf[wslot], xlnT_buf], writes=[pS_buf])
                        p.op("dve", lambda v, fc=fc, tc=tc, hap=hap, dst=dst:
                             v.tensor_copy(dst[:, fc * T + tc * 512:fc * T + (tc + 1) * 512], hap),
                             reads=[pS_buf], writes=[dst_buf])
            else:
                for quar in range(4):
                    wslot = quar % 2
                    wt = wstr[wslot]
                    wt3 = wt[:, :].rearrange("p (c j) -> p c j", j=256)
                    p.dma("pool", wt3, wsrc[:, :, quar * 256:(quar + 1) * 256], writes=[wstr_buf[wslot]])
                    for tt in range(NT):
                        hap = pS[:, (tt % 2) * 512:(tt % 2) * 512 + 256]

                        def mmv(t, tt=tt, quar=quar, hap=hap, wt=wt):
                            for c in range(NCH):
                                last = t.matmul(hap, xlnT[:, c * T + tt * 128:c * T + (tt + 1) * 128],
                                                wt[:, c * 256:(c + 1) * 256],
                                                start=(c == 0), stop=(c == NCH - 1))
                            return last
                        p.op("pe", mmv, reads=[wstr_buf[wslot], xlnT_buf], writes=[pS_buf])
                        p.op("dve", lambda v, tt=tt, quar=quar, hap=hap, dst=dst:
                             v.tensor_copy(dst[:, tt * C + quar * 256:tt * C + (quar + 1) * 256], hap),
                             reads=[pS_buf], writes=[dst_buf])
        y_all_buf = env.alias_buf(xlnT_buf, "q_yall")
        p.op("dve", lambda v: v.memset(qaug[64:65, :], 1.0), writes=[qaug_buf])
        for h in range(H):
            hp = 64 * (h % 2)
            hc = h // 2
            qh = qfm[hp:hp + 64, hc * T:(hc + 1) * T]
            kh = kfm[hp:hp + 64, hc * T:(hc + 1) * T]
            p.op("dve", lambda v, qh=qh: v.tensor_scalar(qaug[0:64, :], qh, -2.0, None, ALU.mult),
                 reads=[qfm_buf], writes=[qaug_buf])
            p.op("act", lambda s2, kh=kh: s2.activation(sqs[0:64, :], kh, AF.Square),
                 reads=[kfm_buf], writes=[sqs_buf])

            def mmkn(t):
                for tc in range(T // 512):
                    last = t.matmul(pS[0:1, tc * 512:(tc + 1) * 512], onescol[0:64, 0:1],
                                    sqs[0:64, tc * 512:(tc + 1) * 512], start=True, stop=True)
                return last
            p.op("pe", mmkn, reads=[sqs_buf, onescol_buf], writes=[pS_buf])
            p.op("dve", lambda v: v.tensor_copy(nrow[0:1, 0:T], pS[0:1, 0:T]),
                 reads=[pS_buf], writes=[nrow_buf])
            p.op("dve", lambda v, kh=kh: v.tensor_copy(kaug[0:64, :], kh),
                 reads=[kfm_buf], writes=[kaug_buf])
            p.op("dve", lambda v: v.tensor_copy(kaug[64:65, :], nrow[0:1, 0:T]),
                 reads=[nrow_buf], writes=[kaug_buf])
            p.op("act", lambda s2, qh=qh: s2.activation(sqs[0:64, :], qh, AF.Square),
                 reads=[qfm_buf], writes=[sqs_buf])

            def mmqn(t):
                for tc in range(T // 512):
                    last = t.matmul(pS[0:1, tc * 512:(tc + 1) * 512], onescol[0:64, 0:1],
                                    sqs[0:64, tc * 512:(tc + 1) * 512], start=True, stop=True)
                return last
            p.op("pe", mmqn, reads=[sqs_buf, onescol_buf], writes=[pS_buf])
            p.op("dve", lambda v: v.tensor_copy(nrow[0:1, 0:T], pS[0:1, 0:T]),
                 reads=[pS_buf], writes=[nrow_buf])

            def mmqc(t):
                for qt in range(NT):
                    last = t.matmul(pO[:, qt:qt + 1], nrow[0:1, qt * 128:(qt + 1) * 128],
                                    ones[0:1, 0:1], start=True, stop=True)
                return last
            p.op("pe", mmqc, reads=[nrow_buf, ones_buf], writes=[pO_buf])
            p.op("dve", lambda v: v.tensor_scalar(qnc[:, 0:NT], pO[:, 0:NT], 1.0, EPS, ALU.mult, ALU.add),
                 reads=[pO_buf], writes=[qnc_buf])

            for qt in range(NT):
                kmax = (qt + 1) * 128
                n_ks = qt + 1
                par = qt % 2
                def mmdot(t, qt=qt, kmax=kmax, qh=qh, kh=kh):
                    for k0 in range(0, kmax, 512):
                        kn_ = min(512, kmax - k0)
                        last = t.matmul(pS[:, k0:k0 + kn_],
                                        qh[:, qt * 128:(qt + 1) * 128], kh[:, k0:k0 + kn_],
                                        start=True, stop=True)
                    return last
                p.op("pe", mmdot, reads=[qfm_buf, kfm_buf], writes=[pS_buf])
                for k0 in range(0, kmax, 512):
                    kn_ = min(512, kmax - k0)

                    def mmden(t, qt=qt, k0=k0, kn_=kn_):
                        return t.matmul(pT[:, 0:kn_], qaug[0:65, qt * 128:(qt + 1) * 128],
                                        kaug[0:65, k0:k0 + kn_], start=True, stop=True)
                    p.op("pe", mmden, reads=[qaug_buf, kaug_buf], writes=[pT_buf])
                    p.op("dve", lambda v, qt=qt, kn_=kn_: v.tensor_scalar(
                         den[:, 0:kn_], pT[:, 0:kn_], qnc[:, qt:qt + 1], EPS, ALU.add, ALU.max),
                         reads=[pT_buf, qnc_buf], writes=[den_buf])
                    p.op("act", lambda s2, kn_=kn_: s2.activation(rin[:, 0:kn_], den[:, 0:kn_], AF.Ln),
                         reads=[den_buf], writes=[rin_buf])
                    p.op("act", lambda s2, kn_=kn_: s2.activation(rin[:, 0:kn_], rin[:, 0:kn_],
                         AF.Exp, scale=-1.0),
                         reads=[rin_buf], writes=[rin_buf])
                    p.op("act", lambda s2, k0=k0, kn_=kn_: s2.activation(d2[:, 0:kn_], pS[:, k0:k0 + kn_],
                         AF.Square),
                         reads=[pS_buf], writes=[d2_buf])
                    p.op("dve", lambda v, k0=k0, kn_=kn_: v.tensor_mul(
                         ssb[:, k0:k0 + kn_], d2[:, 0:kn_], rin[:, 0:kn_]),
                         reads=[d2_buf, rin_buf], writes=[ssb_buf])
                p.op("dve", lambda v, kmax=kmax: v.tensor_tensor(
                     ssb[:, kmax - 128:kmax], ssb[:, kmax - 128:kmax], cmask[:, :], ALU.add),
                     reads=[cmask_buf], writes=[ssb_buf])
                p.op("dve", lambda v, kmax=kmax, par=par: v.tensor_reduce(
                     st2[par][:, 0:1], ssb[:, 0:kmax], AX.X, ALU.max, negate=True),
                     reads=[ssb_buf], writes=[st2_buf[par]])
                p.op("act", lambda s2, kmax=kmax, par=par: s2.activation(
                     Es[par][:, 0:kmax], ssb[:, 0:kmax], AF.Exp, bias=st2[par][:, 0:1],
                     accum_out=st2[par][:, 1:2]),
                     reads=[ssb_buf, st2_buf[par]], writes=[Es_buf[par], st2_buf[par]])
                p.op("dve", lambda v, par=par: v.reciprocal(st2[par][:, 2:3], st2[par][:, 1:2]),
                     reads=[st2_buf[par]], writes=[st2_buf[par]])
                p.op("dve", lambda v, par=par: v.tensor_scalar(
                     dgl[par][:, :], ident[:, :], st2[par][:, 2:3], None, ALU.mult),
                     reads=[identb_buf, st2_buf[par]], writes=[dgl_buf[par]])
                for ks0 in range(0, n_ks, 8):
                    kn_ = min(8, n_ks - ks0)

                    def mmt(t, ks0=ks0, kn_=kn_, par=par):
                        for j in range(kn_):
                            last = t.matmul(pT[:, j * 128:(j + 1) * 128],
                                            Es[par][:, (ks0 + j) * 128:(ks0 + j + 1) * 128],
                                            dgl[par][:, :], start=True, stop=True)
                        return last
                    p.op("pe", mmt, reads=[Es_buf[par], dgl_buf[par]], writes=[pT_buf])
                    p.op("dve", lambda v, kn_=kn_: v.tensor_copy(ETs[:, 0:kn_ * 128], pT[:, 0:kn_ * 128]),
                         reads=[pT_buf], writes=[ETs_buf])

                    def mmpv(t, ks0=ks0, kn_=kn_, h=h, n_ks=n_ks):
                        for j in range(kn_):
                            ks = ks0 + j
                            last = t.matmul(pO[0:64, 0:128],
                                            vtok[:, ks * C + h * 64:ks * C + (h + 1) * 64],
                                            ETs[:, j * 128:(j + 1) * 128],
                                            start=(ks == 0), stop=(ks == n_ks - 1))
                        return last
                    p.op("pe", mmpv, reads=[vtok_buf, ETs_buf], writes=[pO_buf])
                p.op("dve", lambda v, hp=hp, hc=hc, qt=qt: v.tensor_copy(
                     y_all[hp:hp + 64, qt * C + hc * 128:qt * C + (hc + 1) * 128], pO[0:64, 0:128]),
                     reads=[pO_buf], writes=[y_all_buf])
        for qt in range(NT):
            sl = qt % NS
            p.dma("sp", xt[sl][:, :], x_in[base + qt * 128: base + (qt + 1) * 128, :],
                  writes=[bufs["xt"][sl]], sem=2 + qt % 2)
            for half in range(2):
                def mmo(t, half=half, qt=qt):
                    for cc in range(NCH):
                        last = t.matmul(pT[:, half * 512:(half + 1) * 512],
                                        y_all[:, qt * C + cc * 128:qt * C + (cc + 1) * 128],
                                        wo_sb[:, cc * C + half * 512:cc * C + (half + 1) * 512],
                                        start=(cc == 0), stop=(cc == NCH - 1))
                    return last
                p.op("pe", mmo, reads=[y_all_buf, wo_buf], writes=[pT_buf])
            p.op("dve", lambda v, sl=sl: v.tensor_add(xo[sl][:, :], pT[:, 0:C], xt[sl][:, :]),
                 reads=[pT_buf, bufs["xt"][sl]], writes=[bufs["xo"][sl]])
            p.dma("sp", x_out[base + qt * 128: base + (qt + 1) * 128, :], xo[sl][:, :],
                  reads=[bufs["xo"][sl]], sem=4 + qt % 2)
        xlnT_buf.w = y_all_buf.w
        xlnT_buf.rs = list(y_all_buf.rs)


def emit_L2_attn(p, env, x_in, x_out, W, ntok, T=2048):
    cfg = {"T": T, "wq": W["Wq"], "wk": W["Wk"], "wv": W["Wv"], "wo": W["Wo"],
           "qscale": 0.125, "extra_ln_residual": True, "kind": "causal"}
    emit_std_attention(p, env, x_in, x_out, W, ntok, cfg)


def emit_L3_attn(p, env, x_in, x_out, W, ntok, T=2048):
    cfg = {"T": T, "wq": W["wq"], "wk": W["wk"], "wv": W["wv"], "wo": W["wo"],
           "bq": W["bq"], "bk": W["bk"], "bv": W["bv"], "bo": W["bo"],
           "qscale": 0.125, "kind": "window", "radius": 256, "chunk": 256}
    emit_std_attention(p, env, x_in, x_out, W, ntok, cfg)


def emit_L2_ffn(p, env, x_in, x_out, W, ntok):
    nc = env.nc
    NE = 34
    MG = min(2048, ntok)
    TGP = min(512, MG)
    nmg = ntok // MG
    ident, identb_buf = env.sb["identb"], env.buf("identb")
    ones, ones_buf = env.sb["onesb"], env.buf("onesb")
    pA, pA_buf = env.psum["pA"], env.buf("pA")
    pB, pB_buf = env.psum["pB"], env.buf("pB")
    pC, pC_buf = env.psum["pC"], env.buf("pC")
    pD, pD_buf = env.psum["pD"], env.buf("pD")

    if "identf" not in env.sb:
        idf = env.add_const("identf", np.eye(128, dtype=np.float32))
        env.sbuf("identf", [128, 128], F32)
        p.dma("sp", env.sb["identf"][:], idf, writes=[env.buf("identf")])
    identf, identf_buf = env.sb["identf"], env.buf("identf")
    if "m_onesf" not in env.sb:
        onesf0 = env.sbuf("m_onesf", [1, 128], F32)
        p.op("dve", lambda v: v.memset(onesf0[:, :], 1.0), writes=[env.buf("m_onesf")])
    onesf, onesf_buf = env.sb["m_onesf"], env.buf("m_onesf")

    guT_dram = nc.dram_tensor("m_guT", [NE, 128, NCH * C], BF16).ap()
    dnT_dram = nc.dram_tensor("m_dnT", [NE, 128, 4 * C], BF16).ap()
    guT_dram_buf = Buf("m_guT_d")
    dnT_dram_buf = Buf("m_dnT_d")

    def gu_rows(e, rc):
        if e < 32:
            return W["experts_gate_up"][e, rc * 128:(rc + 1) * 128, :]
        off = 0 if e == 32 else 512
        if rc < 4:
            return W["shared_gate_w"][off + rc * 128:off + (rc + 1) * 128, :]
        return W["shared_up_w"][off + (rc - 4) * 128:off + (rc - 3) * 128, :]

    def dn_rows(e, rc):
        if e < 32:
            return W["experts_down"][e, rc * 128:(rc + 1) * 128, :]
        off = 0 if e == 32 else 512
        return W["shared_down_w"][rc * 128:(rc + 1) * 128, off:off + 512]

    wld = [env.sbuf(f"m_wld{i}", [128, C], BF16) for i in range(2)]
    wld_buf = [env.buf(f"m_wld{i}") for i in range(2)]
    tsl = [env.sbuf(f"m_tsl{i}", [128, 1024], BF16) for i in range(2)]
    tsl_buf = [env.buf(f"m_tsl{i}") for i in range(2)]
    for e in range(NE):
        for kind, rows_fn, ncc, dram in (("gu", gu_rows, 8, guT_dram), ("dn", dn_rows, 8, dnT_dram)):
            nrc = 8 if kind == "gu" else 8
            ncols = 1024 if kind == "gu" else 512
            nccd = ncols // 128
            for rc in range(nrc):
                sl = rc % 2
                p.dma("pool", wld[sl][:, 0:ncols], rows_fn(e, rc), writes=[wld_buf[sl]])

                def mmtr(t, sl=sl, nccd=nccd):
                    for cc in range(nccd):
                        last = t.matmul(pA[:, cc * 128:(cc + 1) * 128],
                                        wld[sl][:, cc * 128:(cc + 1) * 128], ident[:, :],
                                        start=True, stop=True)
                    return last
                p.op("pe", mmtr, reads=[wld_buf[sl], identb_buf], writes=[pA_buf])
                p.op("dve", lambda v, sl=sl, nccd=nccd: v.tensor_copy(
                     tsl[sl][:, 0:nccd * 128], pA[:, 0:nccd * 128]),
                     reads=[pA_buf], writes=[tsl_buf[sl]])
                rows_total = C if kind == "gu" else C
                dbuf = guT_dram_buf if kind == "gu" else dnT_dram_buf
                for cc in range(nccd):
                    p.dma("sp", dram[e, :, cc * rows_total + rc * 128: cc * rows_total + (rc + 1) * 128],
                          tsl[sl][:, cc * 128:(cc + 1) * 128], reads=[tsl_buf[sl]], writes=[dbuf])

    gw_sb = env.sbuf("m_gw", [32, C], F32)
    gw_buf = env.buf("m_gw")
    p.dma("sp", gw_sb[:, :], W["gate_weight"], writes=[gw_buf])
    gwT = env.sbuf("m_gwT", [128, NCH * 32], F32)
    gwT_buf = env.buf("m_gwT")
    for c in range(NCH):
        p.op("pe", lambda t, c=c: t.matmul(pD[0:128, 0:32], gw_sb[:, c * 128:(c + 1) * 128],
                                           identf[0:32, 0:32], start=True, stop=True),
             reads=[gw_buf, identf_buf], writes=[pD_buf])
        p.op("dve", lambda v, c=c: v.tensor_copy(gwT[:, c * 32:(c + 1) * 32], pD[0:128, 0:32]),
             reads=[pD_buf], writes=[gwT_buf])
    eb_sb = env.sbuf("m_eb", [1, 32], F32)
    eb_buf = env.buf("m_eb")
    p.dma("sp", eb_sb[0:1, :], W["e_score_correction_bias"][None, :], writes=[eb_buf])
    ebb = env.sbuf("m_ebb", [128, 32], F32)
    ebb_buf = env.buf("m_ebb")
    p.op("pe", lambda t: t.matmul(pD[:, 0:32], onesf[0:1, :], eb_sb[0:1, :], start=True, stop=True),
         reads=[onesf_buf, eb_buf], writes=[pD_buf])
    p.op("dve", lambda v: v.tensor_copy(ebb[:, :], pD[:, 0:32]), reads=[pD_buf], writes=[ebb_buf])

    NTM = MG // 128
    xlnT = env.sbuf("m_xlnT", [128, NCH * MG], BF16); xlnT_buf = env.buf("m_xlnT")
    acc = env.sbuf("m_acc", [128, NTM * C], F32); acc_buf = env.buf("m_acc")
    guT = [env.sbuf(f"m_guT{i}", [128, NCH * C], BF16) for i in range(2)]
    guT_buf = [env.buf(f"m_guT{i}") for i in range(2)]
    dnT = [env.sbuf(f"m_dnT{i}", [128, 4 * C], BF16) for i in range(2)]
    dnT_buf = [env.buf(f"m_dnT{i}") for i in range(2)]
    slab = env.sbuf("m_slab", [128, 4 * TGP], BF16); slab_buf = env.buf("m_slab")
    silu_s = env.sbuf("m_silu", [128, TGP], F32); silu_buf = env.buf("m_silu")
    assert 2 * TGP <= 1024
    comb = env.sbuf("m_comb", [128, NTM * NE], F32); comb_buf = env.buf("m_comb")
    xcf = env.sbuf("m_xcf", [128, C], F32); xcf_buf = env.buf("m_xcf")
    xlnT32 = env.sbuf("m_xlnT32", [128, C], F32); xlnT32_buf = env.buf("m_xlnT32")
    diagf = env.sbuf("m_diagf", [128, 128], F32); diagf_buf = env.buf("m_diagf")
    lg = env.sbuf("m_lg", [128, 32], F32); lg_buf = env.buf("m_lg")
    rt = env.sbuf("m_rt", [128, 160], F32); rt_buf = env.buf("m_rt")

    NS = 2
    xt = [env.sbuf(f"m_xt{i}", [128, C], F32) for i in range(NS)]
    stats = [env.sbuf(f"m_st{i}", [128, 8], F32) for i in range(NS)]
    diag = [env.sbuf(f"m_dg{i}", [128, 128], BF16) for i in range(NS)]
    xcb = [env.sbuf(f"m_xc{i}", [128, C], BF16) for i in range(NS)]
    sqd = env.sbuf("m_sq", [128, C], BF16)
    xo = [env.sbuf(f"m_xo{i}", [128, C], F32) for i in range(NS)]
    bufs = {n: [env.buf(f"m_{n}{i}") for i in range(NS)] for n in ("xt", "st", "dg", "xc", "xo")}
    sqd_buf = env.buf("m_sq")

    for mg in range(nmg):
        base = mg * MG
        p.op("dve", lambda v: v.memset(acc[:, :], 0.0), writes=[acc_buf])
        for tt in range(NTM):
            sl = tt % NS
            p.dma("sp", xt[sl][:, :], x_in[base + tt * 128: base + (tt + 1) * 128, :],
                  writes=[bufs["xt"][sl]], sem=tt % 2)
            dsts = [xlnT[:, c * MG + tt * 128: c * MG + (tt + 1) * 128] for c in range(NCH)]
            emit_ln_transpose(p, env, xt[sl][:, :], bufs["xt"][sl], stats[sl][:, :], bufs["st"][sl],
                              diag[sl][:, :], bufs["dg"][sl], xcb[sl][:, :], bufs["xc"][sl],
                              sqd[:, :], sqd_buf, "pD", dsts, xlnT_buf)
            p.op("dve", lambda v, sl=sl: v.tensor_scalar(xcf[:, :], xt[sl][:, :],
                 stats[sl][:, 1:2], None, ALU.add),
                 reads=[bufs["xt"][sl], bufs["st"][sl]], writes=[xcf_buf])
            p.op("dve", lambda v, sl=sl: v.tensor_scalar(diagf[:, :], identf[:, :],
                 stats[sl][:, 5:6], None, ALU.mult),
                 reads=[identf_buf, bufs["st"][sl]], writes=[diagf_buf])

            def mmt32(t):
                for c in range(NCH):
                    last = t.matmul(pD[:, c * 128:(c + 1) * 128], xcf[:, c * 128:(c + 1) * 128],
                                    diagf[:, :], start=True, stop=True)
                return last
            p.op("pe", mmt32, reads=[xcf_buf, diagf_buf], writes=[pD_buf])
            p.op("dve", lambda v: v.tensor_copy(xlnT32[:, :], pD[:, 0:C]),
                 reads=[pD_buf], writes=[xlnT32_buf])

            def mmgate(t):
                for c in range(NCH):
                    last = t.matmul(pA[:, 0:32], xlnT32[:, c * 128:(c + 1) * 128],
                                    gwT[:, c * 32:(c + 1) * 32], start=(c == 0), stop=(c == NCH - 1))
                return last
            p.op("pe", mmgate, reads=[xlnT32_buf, gwT_buf], writes=[pA_buf])
            p.op("act", lambda s2: s2.activation(lg[:, :], pA[:, 0:32], AF.Sigmoid),
                 reads=[pA_buf], writes=[lg_buf])
            p.op("dve", lambda v: v.tensor_add(rt[:, 0:32], lg[:, :], ebb[:, :]),
                 reads=[lg_buf, ebb_buf], writes=[rt_buf])
            p.op("dve", lambda v: v.tensor_reduce(rt[:, 32:40],
                 rt[:, 0:32].rearrange("p (g e) -> p g e", e=4), AX.X, ALU.max),
                 reads=[rt_buf], writes=[rt_buf])
            p.op("dve", lambda v: v.max(out=rt[:, 40:48], in_=rt[:, 32:40]),
                 reads=[rt_buf], writes=[rt_buf])
            p.op("dve", lambda v: v.tensor_scalar(rt[:, 32:40], rt[:, 32:40], rt[:, 43:44],
                 None, ALU.is_ge),
                 reads=[rt_buf], writes=[rt_buf])
            for j in range(4):
                p.op("dve", lambda v, j=j: v.tensor_tensor(
                     rt[:, 48 + j:80:4], rt[:, j:32:4], rt[:, 32:40], ALU.mult),
                     reads=[rt_buf], writes=[rt_buf])
            p.op("dve", lambda v: v.max(out=rt[:, 80:88], in_=rt[:, 48:80]),
                 reads=[rt_buf], writes=[rt_buf])
            p.op("dve", lambda v: v.tensor_scalar(rt[:, 48:80], rt[:, 48:80], rt[:, 85:86],
                 None, ALU.is_ge),
                 reads=[rt_buf], writes=[rt_buf])
            p.op("dve", lambda v: v.tensor_mul(rt[:, 88:120], rt[:, 48:80], lg[:, :]),
                 reads=[rt_buf, lg_buf], writes=[rt_buf])
            p.op("dve", lambda v: v.tensor_reduce(rt[:, 120:121], rt[:, 88:120], AX.X, ALU.add),
                 reads=[rt_buf], writes=[rt_buf])
            p.op("dve", lambda v: v.tensor_scalar(rt[:, 121:122], rt[:, 120:121], 1e-20, None, ALU.add),
                 reads=[rt_buf], writes=[rt_buf])
            p.op("dve", lambda v: v.reciprocal(rt[:, 122:123], rt[:, 121:122]),
                 reads=[rt_buf], writes=[rt_buf])
            p.op("dve", lambda v, tt=tt: v.tensor_scalar(comb[:, tt * NE:tt * NE + 32],
                 rt[:, 88:120], rt[:, 122:123], 2.5, ALU.mult, ALU.mult),
                 reads=[rt_buf], writes=[comb_buf])
            p.op("dve", lambda v, tt=tt: v.memset(comb[:, tt * NE + 32:tt * NE + 34], 1.0),
                 writes=[comb_buf])
        for e in range(NE):
            sl = e % 2
            p.dma("pool", guT[sl][:, :], guT_dram[e], reads=[guT_dram_buf], writes=[guT_buf[sl]])
            p.dma("pool", dnT[sl][:, :], dnT_dram[e], reads=[dnT_dram_buf], writes=[dnT_buf[sl]])
            for tg in range(MG // TGP):
                for icp in range(4):
                    for gu_i, (lo, ic) in enumerate(((0, icp), (TGP, icp + 4))):

                        def mmgu(t, ic=ic, tg=tg, sl=sl, lo=lo):
                            for hc in range(NCH):
                                last = t.matmul(pA[:, lo:lo + TGP],
                                                guT[sl][:, hc * C + ic * 128:hc * C + (ic + 1) * 128],
                                                xlnT[:, hc * MG + tg * TGP:hc * MG + (tg + 1) * TGP],
                                                start=(hc == 0), stop=(hc == NCH - 1))
                            return last
                        p.op("pe", mmgu, reads=[guT_buf[sl], xlnT_buf], writes=[pA_buf])
                        if gu_i == 0:
                            p.op("act", lambda s2, lo=lo: s2.activation(silu_s[:, :], pA[:, lo:lo + TGP],
                                 AF.Silu),
                                 reads=[pA_buf], writes=[silu_buf])
                        else:
                            p.op("dve", lambda v, icp=icp, lo=lo: v.tensor_mul(
                                 slab[:, icp * TGP:(icp + 1) * TGP], silu_s[:, :], pA[:, lo:lo + TGP]),
                                 reads=[silu_buf, pA_buf], writes=[slab_buf])
                for tsub in range(TGP // 128):
                    pX, pX_buf = (pB, pB_buf) if tsub % 2 == 0 else (pC, pC_buf)

                    def mmdn(t, tsub=tsub, sl=sl, pX=pX):
                        for half in range(2):
                            for ic in range(4):
                                last = t.matmul(pX[:, half * 512:(half + 1) * 512],
                                                slab[:, ic * TGP + tsub * 128:ic * TGP + (tsub + 1) * 128],
                                                dnT[sl][:, ic * C + half * 512:ic * C + (half + 1) * 512],
                                                start=(ic == 0), stop=(ic == 3))
                        return last
                    p.op("pe", mmdn, reads=[slab_buf, dnT_buf[sl]], writes=[pX_buf])
                    ti = tg * 4 + tsub

                    def accop(v, e=e, ti=ti, pX=pX):
                        return v.scalar_tensor_tensor(acc[:, ti * C:(ti + 1) * C], pX[:, 0:C],
                                                      comb[:, ti * NE + e:ti * NE + e + 1],
                                                      acc[:, ti * C:(ti + 1) * C], ALU.mult, ALU.add)
                    p.op("dve", accop, reads=[pX_buf, comb_buf, acc_buf], writes=[acc_buf])
        for tt in range(NTM):
            sl = tt % NS
            p.dma("sp", xt[sl][:, :], x_in[base + tt * 128: base + (tt + 1) * 128, :],
                  writes=[bufs["xt"][sl]], sem=2 + tt % 2)
            p.op("dve", lambda v, sl=sl, tt=tt: v.tensor_add(xo[sl][:, :],
                 acc[:, tt * C:(tt + 1) * C], xt[sl][:, :]),
                 reads=[acc_buf, bufs["xt"][sl]], writes=[bufs["xo"][sl]])
            p.dma("sp", x_out[base + tt * 128: base + (tt + 1) * 128, :], xo[sl][:, :],
                  reads=[bufs["xo"][sl]], sem=4 + tt % 2)


def emit_L3_scores(p, env, x_in, W, s_out, ntok, s_t_out=None, carve=None):
    nc = env.nc
    ident, identb_buf = env.sb["identb"], env.buf("identb")
    if "identf" not in env.sb:
        idf = env.add_const("identf", np.eye(128, dtype=np.float32))
        env.sbuf("identf", [128, 128], F32)
        p.dma("sp", env.sb["identf"][:], idf, writes=[env.buf("identf")])
    identf, identf_buf = env.sb["identf"], env.buf("identf")
    NS = 2
    if carve is None:
        wg_sb = env.sbuf("s_wg", [128, NCH * 8], F32)
        wg_buf = env.buf("s_wg")
        xt = [env.sbuf(f"s_xt{i}", [128, C], F32) for i in range(NS)]
        stats = [env.sbuf(f"s_st{i}", [128, 8], F32) for i in range(NS)]
        xcf = [env.sbuf(f"s_xc{i}", [128, C], F32) for i in range(NS)]
        diagf = [env.sbuf(f"s_dg{i}", [128, 128], F32) for i in range(NS)]
        sqd = env.sbuf("s_sq", [128, C], BF16)[:, :]
        xlnT32 = env.sbuf("s_xT", [128, C], F32)
        sloc = [env.sbuf(f"s_sl{i}", [128, 8], F32) for i in range(NS)]
        slocT = [env.sbuf(f"s_slT{i}", [8, 128], F32) for i in range(NS)]
        bufs = {n: [env.buf(f"s_{n}{i}") for i in range(NS)]
                for n in ("xt", "st", "xc", "dg", "sl", "slT")}
        sqd_buf = env.buf("s_sq")
        xT_buf = env.buf("s_xT")
    else:
        reg, regbuf = carve
        off = [0]

        def cv(ncols):
            v = reg[:, off[0]:off[0] + ncols]
            off[0] += ncols
            return v
        wg_sb = cv(NCH * 8); wg_buf = regbuf
        xt = [cv(C) for i in range(NS)]
        stats = [cv(8) for i in range(NS)]
        xcf = [cv(C) for i in range(NS)]
        diagf = [cv(128) for i in range(NS)]
        sqd = cv(C)
        xlnT32 = cv(C)
        sloc = [cv(8) for i in range(NS)]
        slocT = [cv(128)[0:8, :] for i in range(NS)]
        bufs = {n: [regbuf, regbuf] for n in ("xt", "st", "xc", "dg", "sl", "slT")}
        sqd_buf = regbuf
        xT_buf = regbuf
    wg3 = wg_sb[:, 0:NCH * 8].rearrange("p (c j) -> p c j", j=8)
    p.dma("sp", wg3, W["Wg"].rearrange("(c p) e -> p c e", p=128), writes=[wg_buf])
    pD, pD_buf = env.psum[list(env.psum)[-1]], env.buf(list(env.psum)[-1])
    pA, pA_buf = env.psum[list(env.psum)[0]], env.buf(list(env.psum)[0])
    for tt in range(ntok // 128):
        sl = tt % NS
        p.dma("sp", xt[sl][:, 0:C], x_in[tt * 128:(tt + 1) * 128, :], writes=[bufs["xt"][sl]], sem=tt % 2)
        p.op("dve", lambda v, sl=sl: v.tensor_reduce(stats[sl][:, 0:1], xt[sl][:, 0:C], AX.X, ALU.add,
             negate=True), reads=[bufs["xt"][sl]], writes=[bufs["st"][sl]])
        p.op("dve", lambda v, sl=sl: v.tensor_scalar(stats[sl][:, 1:2], stats[sl][:, 0:1], 1.0 / C,
             None, ALU.mult), reads=[bufs["st"][sl]], writes=[bufs["st"][sl]])
        p.op("dve", lambda v, sl=sl: v.tensor_scalar(xcf[sl][:, 0:C], xt[sl][:, 0:C], stats[sl][:, 1:2],
             None, ALU.add), reads=[bufs["xt"][sl], bufs["st"][sl]], writes=[bufs["xc"][sl]])
        p.op("act", lambda s2, sl=sl: s2.activation(sqd[:, 0:C], xcf[sl][:, 0:C], AF.Square,
             accum_out=stats[sl][:, 2:3]), reads=[bufs["xc"][sl]], writes=[sqd_buf, bufs["st"][sl]])
        p.op("dve", lambda v, sl=sl: v.tensor_scalar(stats[sl][:, 3:4], stats[sl][:, 2:3], 1.0 / C,
             1e-5, ALU.mult, ALU.add), reads=[bufs["st"][sl]], writes=[bufs["st"][sl]])
        p.op("act", lambda s2, sl=sl: s2.activation(stats[sl][:, 4:5], stats[sl][:, 3:4], AF.Sqrt),
             reads=[bufs["st"][sl]], writes=[bufs["st"][sl]])
        p.op("dve", lambda v, sl=sl: v.reciprocal(stats[sl][:, 5:6], stats[sl][:, 4:5]),
             reads=[bufs["st"][sl]], writes=[bufs["st"][sl]])
        p.op("dve", lambda v, sl=sl: v.tensor_scalar(diagf[sl][:, 0:128], identf[:, :], stats[sl][:, 5:6],
             None, ALU.mult), reads=[identf_buf, bufs["st"][sl]], writes=[bufs["dg"][sl]])

        def mmt32(t, sl=sl):
            for c in range(NCH):
                last = t.matmul(pD[:, c * 128:(c + 1) * 128], xcf[sl][:, c * 128:(c + 1) * 128],
                                diagf[sl][:, 0:128], start=True, stop=True)
            return last
        p.op("pe", mmt32, reads=[bufs["xc"][sl], bufs["dg"][sl]], writes=[pD_buf])
        p.op("dve", lambda v: v.tensor_copy(xlnT32[:, 0:C], pD[:, 0:C]), reads=[pD_buf], writes=[xT_buf])

        def mmg(t):
            for c in range(NCH):
                last = t.matmul(pA[:, 0:8], xlnT32[:, c * 128:(c + 1) * 128],
                                wg_sb[:, c * 8:(c + 1) * 8], start=(c == 0), stop=(c == NCH - 1))
            return last
        p.op("pe", mmg, reads=[xT_buf, wg_buf], writes=[pA_buf])
        p.op("act", lambda s2, sl=sl: s2.activation(sloc[sl][:, 0:8], pA[:, 0:8], AF.Sigmoid),
             reads=[pA_buf], writes=[bufs["sl"][sl]])
        p.dma("sp", s_out[tt * 128:(tt + 1) * 128, :], sloc[sl][:, 0:8], reads=[bufs["sl"][sl]],
              sem=2 + tt % 2)
        if s_t_out is not None:
            p.op("pe", lambda t, sl=sl: t.matmul(pD[0:8, 0:128], sloc[sl][:, 0:8], identf[:, :],
                 start=True, stop=True), reads=[bufs["sl"][sl], identf_buf], writes=[pD_buf])
            p.op("dve", lambda v, sl=sl: v.tensor_copy(slocT[sl][0:8, 0:128], pD[0:8, 0:128]),
                 reads=[pD_buf], writes=[bufs["slT"][sl]])
            p.dma("sp", s_t_out[:, tt * 128:(tt + 1) * 128], slocT[sl][0:8, 0:128],
                  reads=[bufs["slT"][sl]], sem=5)


def emit_L3_ffn(p, env, x_in, x_out, W, ntok, S_t, S_tok, nglob, S_deps=()):
    nc = env.nc
    E, DF = 8, 4096
    NFC = DF // 128
    TGP = 512
    K_SEL = nglob // 4
    L = nglob // 8
    ident, identb_buf = env.sb["identb"], env.buf("identb")
    ones, ones_buf = env.sb["onesb"], env.buf("onesb")
    pA, pA_buf = env.psum["pA"], env.buf("pA")
    pB, pB_buf = env.psum["pB"], env.buf("pB")
    pC, pC_buf = env.psum["pC"], env.buf("pC")
    pD, pD_buf = env.psum["pD"], env.buf("pD")
    if "identf" not in env.sb:
        idf = env.add_const("identf", np.eye(128, dtype=np.float32))
        env.sbuf("identf", [128, 128], F32)
        p.dma("sp", env.sb["identf"][:], idf, writes=[env.buf("identf")])
    identf, identf_buf = env.sb["identf"], env.buf("identf")

    slab = env.sbuf("e_slab", [128, (DF // 128) * TGP], BF16); slab_buf = env.buf("e_slab")
    scnt = env.sbuf("e_scnt", [64, L], F32); scnt_buf = env.buf("e_scnt")
    gmode = (S_t is not None and getattr(S_t, "shape", [0])[0] == 64)
    if gmode:
        p.dma("sp", scnt[:, :], S_t, deps=list(S_deps), writes=[scnt_buf])
        p64 = env.add_const("p64m", (np.arange(64)[:, None] % 8 == np.arange(8)[None, :])
                            .astype(np.float32))
        p64T = env.add_const("p64Tm", (np.arange(64)[None, :] % 8 == np.arange(8)[:, None])
                             .astype(np.float32))
    else:
        p.dma("sp", scnt[:, :], S_t.rearrange("e (j n) -> (e j) n", j=8), writes=[scnt_buf])
        p64 = env.add_const("p64", (np.arange(64)[:, None] // 8 == np.arange(8)[None, :])
                            .astype(np.float32))
        p64T = env.add_const("p64T", (np.arange(64)[None, :] // 8 == np.arange(8)[:, None])
                             .astype(np.float32))
    p64_sb = env.sbuf("e_p64", [64, 8], F32); p64_buf = env.buf("e_p64")
    p64T_sb = env.sbuf("e_p64T", [8, 64], F32); p64T_buf = env.buf("e_p64T")
    p.dma("sp", p64_sb[:, :], p64, writes=[p64_buf])
    p.dma("sp", p64T_sb[:, :], p64T, writes=[p64T_buf])
    bs = env.sbuf("e_bs", [8, 16], F32); bs_buf = env.buf("e_bs")
    gem = slab.bitcast(F32)[0:64, 0:L]
    gem_buf = env.buf("e_slab")
    prt = env.sbuf("e_prt", [64, 1], F32); prt_buf = env.buf("e_prt")
    thr = env.sbuf("e_thr", [64, 1], F32); thr_buf = env.buf("e_thr")
    p.op("dve", lambda v: v.memset(bs[:, 0:1], 0.0), writes=[bs_buf])
    p.op("dve", lambda v: v.memset(bs[:, 1:2], 1.0), writes=[bs_buf])
    p.op("dve", lambda v: v.memset(bs[:, 3:4], -1e30), writes=[bs_buf])
    for it in range(40):
        p.op("dve", lambda v: v.tensor_scalar(bs[:, 2:3], bs[:, 0:1], bs[:, 1:2], 0.5,
             ALU.add, ALU.mult), reads=[bs_buf], writes=[bs_buf])
        p.op("pe", lambda t: t.matmul(pD[0:64, 0:1], p64T_sb[:, :], bs[:, 2:3], start=True, stop=True),
             reads=[p64T_buf, bs_buf], writes=[pD_buf])
        p.op("dve", lambda v: v.tensor_copy(thr[:, :], pD[0:64, 0:1]), reads=[pD_buf], writes=[thr_buf])
        p.op("dve", lambda v: v.tensor_scalar(gem[0:64, 0:L], scnt[:, :], thr[:, 0:1], None, ALU.is_ge),
             reads=[scnt_buf, thr_buf], writes=[gem_buf])
        p.op("dve", lambda v: v.tensor_reduce(prt[:, :], gem[0:64, 0:L], AX.X, ALU.add),
             reads=[gem_buf], writes=[prt_buf])
        p.op("pe", lambda t: t.matmul(pD[0:8, 0:1], p64_sb[:, :], prt[:, :], start=True, stop=True),
             reads=[p64_buf, prt_buf], writes=[pD_buf])
        p.op("dve", lambda v: v.tensor_copy(bs[:, 4:5], pD[0:8, 0:1]), reads=[pD_buf], writes=[bs_buf])
        p.op("dve", lambda v: v.tensor_scalar(bs[:, 5:6], bs[:, 4:5], float(K_SEL), None, ALU.is_ge),
             reads=[bs_buf], writes=[bs_buf])
        p.op("dve", lambda v: v.tensor_sub(bs[:, 6:7], bs[:, 2:3], bs[:, 0:1]),
             reads=[bs_buf], writes=[bs_buf])
        p.op("dve", lambda v: v.scalar_tensor_tensor(bs[:, 0:1], bs[:, 5:6], bs[:, 6:7], bs[:, 0:1],
             ALU.mult, ALU.add), reads=[bs_buf], writes=[bs_buf])
        p.op("dve", lambda v: v.tensor_scalar(bs[:, 5:6], bs[:, 5:6], -1.0, 1.0, ALU.mult, ALU.add),
             reads=[bs_buf], writes=[bs_buf])
        p.op("dve", lambda v: v.tensor_sub(bs[:, 6:7], bs[:, 2:3], bs[:, 1:2]),
             reads=[bs_buf], writes=[bs_buf])
        p.op("dve", lambda v: v.scalar_tensor_tensor(bs[:, 1:2], bs[:, 5:6], bs[:, 6:7], bs[:, 1:2],
             ALU.mult, ALU.add), reads=[bs_buf], writes=[bs_buf])
        p.op("dve", lambda v: v.tensor_scalar(bs[:, 5:6], bs[:, 4:5], float(K_SEL), None, ALU.is_equal),
             reads=[bs_buf], writes=[bs_buf])
        p.op("dve", lambda v: v.tensor_sub(bs[:, 6:7], bs[:, 2:3], bs[:, 3:4]),
             reads=[bs_buf], writes=[bs_buf])
        p.op("dve", lambda v: v.scalar_tensor_tensor(bs[:, 3:4], bs[:, 5:6], bs[:, 6:7], bs[:, 3:4],
             ALU.mult, ALU.add), reads=[bs_buf], writes=[bs_buf])
    p.op("dve", lambda v: v.tensor_scalar(bs[:, 5:6], bs[:, 3:4], -1e29, None, ALU.is_gt),
         reads=[bs_buf], writes=[bs_buf])
    p.op("dve", lambda v: v.tensor_sub(bs[:, 6:7], bs[:, 3:4], bs[:, 0:1]),
         reads=[bs_buf], writes=[bs_buf])
    p.op("dve", lambda v: v.scalar_tensor_tensor(bs[:, 7:8], bs[:, 5:6], bs[:, 6:7], bs[:, 0:1],
         ALU.mult, ALU.add), reads=[bs_buf], writes=[bs_buf])
    thb = env.sbuf("e_thb", [128, 8], F32); thb_buf = env.buf("e_thb")
    throw = env.sbuf("e_throw", [1, 8], F32); throw_buf = env.buf("e_throw")
    p.op("pe", lambda t: t.matmul(pD[0:1, 0:8], bs[:, 7:8], identf[0:8, 0:8], start=True, stop=True),
         reads=[bs_buf, identf_buf], writes=[pD_buf])
    p.op("dve", lambda v: v.tensor_copy(throw[:, :], pD[0:1, 0:8]), reads=[pD_buf], writes=[throw_buf])
    if "m_onesf" not in env.sb:
        onesf = env.sbuf("m_onesf", [1, 128], F32)
        p.op("dve", lambda v: v.memset(onesf[:, :], 1.0), writes=[env.buf("m_onesf")])
    onesf, onesf_buf = env.sb["m_onesf"], env.buf("m_onesf")
    p.op("pe", lambda t: t.matmul(pD[:, 0:8], onesf[0:1, :], throw[0:1, :], start=True, stop=True),
         reads=[onesf_buf, throw_buf], writes=[pD_buf])
    p.op("dve", lambda v: v.tensor_copy(thb[:, :], pD[:, 0:8]), reads=[pD_buf], writes=[thb_buf])

    NTC = ntok // 128
    coeff = env.sbuf("e_coeff", [128, NTC * 8], F32); coeff_buf = env.buf("e_coeff")
    stok = env.sbuf("e_stok", [128, 8], F32); stok_buf = env.buf("e_stok")
    xlnd = nc.dram_tensor("e_xlnd", [NTC, 128, C], BF16).ap()
    xlnd_buf = Buf("e_xlnd")
    accd = nc.dram_tensor("e_accd", [ntok, C], F32).ap()
    accd_buf = Buf("e_accd")
    NS = 2
    xt = [env.sbuf(f"e_xt{i}", [128, C], F32) for i in range(NS)]
    stats = [env.sbuf(f"e_st{i}", [128, 8], F32) for i in range(NS)]
    diag = [env.sbuf(f"e_dg{i}", [128, 128], BF16) for i in range(NS)]
    xcb = [env.sbuf(f"e_xc{i}", [128, C], BF16) for i in range(NS)]
    sqd = env.sbuf("e_sq", [128, C], BF16)
    xo = [env.sbuf(f"e_xo{i}", [128, C], F32) for i in range(NS)]
    bufs = {n: [env.buf(f"e_{n}{i}") for i in range(NS)] for n in ("xt", "st", "dg", "xc", "xo")}
    sqd_buf = env.buf("e_sq")
    xlt = [env.sbuf(f"e_xlt{i}", [128, C], BF16) for i in range(NS)]
    xlt_buf = [env.buf(f"e_xlt{i}") for i in range(NS)]
    for tt in range(NTC):
        sl = tt % NS
        p.dma("sp", xt[sl][:, :], x_in[tt * 128:(tt + 1) * 128, :], writes=[bufs["xt"][sl]], sem=tt % 2)
        dsts = [xlt[sl][:, c * 128:(c + 1) * 128] for c in range(NCH)]
        emit_ln_transpose(p, env, xt[sl][:, :], bufs["xt"][sl], stats[sl][:, :], bufs["st"][sl],
                          diag[sl][:, :], bufs["dg"][sl], xcb[sl][:, :], bufs["xc"][sl],
                          sqd[:, :], sqd_buf, "pD", dsts, xlt_buf[sl])
        p.dma("sp", xlnd[tt], xlt[sl][:, :], reads=[xlt_buf[sl]], writes=[xlnd_buf], sem=2 + tt % 2)
        p.dma("sp", stok[:, :], S_tok[tt * 128:(tt + 1) * 128, :], writes=[stok_buf], sem=4)
        p.op("dve", lambda v, tt=tt: v.tensor_tensor(coeff[:, tt * 8:(tt + 1) * 8], stok[:, :],
             thb[:, :], ALU.is_ge), reads=[stok_buf, thb_buf], writes=[coeff_buf])
        p.op("dve", lambda v, tt=tt: v.tensor_mul(coeff[:, tt * 8:(tt + 1) * 8],
             coeff[:, tt * 8:(tt + 1) * 8], stok[:, :]), reads=[stok_buf], writes=[coeff_buf])

    w1r = env.sbuf("e_w1r", [128, NCH * DF], BF16); w1r_buf = env.buf("e_w1r")
    w2t = [env.sbuf(f"e_w2t{i}", [128, C], BF16) for i in range(2)]
    w2t_bufs = [env.buf(f"e_w2t{i}") for i in range(2)]
    xlg0 = env.sbuf("e_xlg0", [128, NCH * TGP], BF16)
    xlg = [xlg0, xlg0]
    xlg_buf = [env.buf("e_xlg0"), env.buf("e_xlg0")]
    b2row = env.sbuf("e_b2r", [1, C], BF16); b2row_buf = env.buf("e_b2r")
    b1v = env.sbuf("e_b1v", [128, 32], F32); b1v_buf = env.buf("e_b1v")
    b1ld = env.sbuf("e_b1ld", [32, 128], F32); b1ld_buf = env.buf("e_b1ld")
    oacc0 = env.sbuf("e_oac0", [128, C], F32)
    oacc = [oacc0, oacc0]
    oacc_buf = [env.buf("e_oac0"), env.buf("e_oac0")]
    for e in range(E):
        for c in range(NCH):
            p.dma("pool", w1r[:, c * DF:(c + 1) * DF], W["W1"][e, c * 128:(c + 1) * 128, :],
                  writes=[w1r_buf])
        p.dma("pool", b2row[0:1, :], W["b2"][e][None, :], writes=[b2row_buf])
        p.dma("sp", b1ld[:, :], W["b1"][e].rearrange("(a p) -> a p", p=128), writes=[b1ld_buf])
        p.op("pe", lambda t: t.matmul(pD[:, 0:32], b1ld[:, :], identf[0:32, 0:32],
             start=True, stop=True), reads=[b1ld_buf, identf_buf], writes=[pD_buf])
        p.op("dve", lambda v: v.tensor_copy(b1v[:, :], pD[:, 0:32]), reads=[pD_buf], writes=[b1v_buf])
        for tg in range(ntok // TGP):
            gsl = tg % 2
            for tsub in range(TGP // 128):
                tt = tg * 4 + tsub
                dst3 = xlg[gsl][:, :].rearrange("p (c j) -> p c j", j=TGP)[:, :, tsub * 128:(tsub + 1) * 128]
                src3 = xlnd[tt].rearrange("p (c j) -> p c j", j=128)
                p.dma("pool", dst3, src3, reads=[xlnd_buf], writes=[xlg_buf[gsl]])
            for fc in range(NFC):
                half = fc % 2
                hap = pA[:, half * 512:half * 512 + TGP]

                def mmh(t, fc=fc, gsl=gsl, hap=hap):
                    for c in range(NCH):
                        last = t.matmul(hap, w1r[:, c * DF + fc * 128:c * DF + (fc + 1) * 128],
                                        xlg[gsl][:, c * TGP:(c + 1) * TGP],
                                        start=(c == 0), stop=(c == NCH - 1))
                    return last
                p.op("pe", mmh, reads=[w1r_buf, xlg_buf[gsl]], writes=[pA_buf])
                p.op("act", lambda s2, fc=fc, hap=hap: s2.activation(
                     slab[:, fc * TGP:(fc + 1) * TGP], hap, AF.Gelu, bias=b1v[:, fc:fc + 1]),
                     reads=[pA_buf, b1v_buf], writes=[slab_buf])
            pos = [pA, pB, pC, pD]
            pos_bufs = [pA_buf, pB_buf, pC_buf, pD_buf]
            for fc in range(NFC):
                wsl = fc % 2
                p.dma("pool", w2t[wsl][:, :], W["W2"][e, fc * 128:(fc + 1) * 128, :],
                      writes=[w2t_bufs[wsl]])

                def mmo(t, fc=fc, wsl=wsl):
                    last = None
                    for tsub in range(TGP // 128):
                        for half in range(2):
                            last = t.matmul(pos[tsub][:, half * 512:(half + 1) * 512],
                                            slab[:, fc * TGP + tsub * 128:fc * TGP + (tsub + 1) * 128],
                                            w2t[wsl][:, half * 512:(half + 1) * 512],
                                            start=(fc == 0), stop=False)
                    return last
                p.op("pe", mmo, reads=[slab_buf, w2t_bufs[wsl]], writes=pos_bufs)

            def mmbias(t):
                last = None
                for tsub in range(TGP // 128):
                    for half in range(2):
                        last = t.matmul(pos[tsub][:, half * 512:(half + 1) * 512], ones[0:1, :],
                                        b2row[0:1, half * 512:(half + 1) * 512], start=False, stop=True)
                return last
            p.op("pe", mmbias, reads=[ones_buf, b2row_buf], writes=pos_bufs)
            for tsub in range(TGP // 128):
                tt = tg * 4 + tsub
                osl = tsub % 2
                p.op("dve", lambda v, tsub=tsub, tt=tt, e=e, osl=osl: v.tensor_scalar(
                     oacc[osl][:, :], pos[tsub][:, 0:C], coeff[:, tt * 8 + e:tt * 8 + e + 1],
                     None, ALU.mult),
                     reads=[pos_bufs[tsub], coeff_buf], writes=[oacc_buf[osl]])
                if e == 0:
                    p.dma("pool", accd[tt * 128:(tt + 1) * 128, :], oacc[osl][:, :],
                          reads=[oacc_buf[osl]], writes=[accd_buf])
                else:
                    p.dma("pool", accd[tt * 128:(tt + 1) * 128, :], oacc[osl][:, :],
                          reads=[oacc_buf[osl]], writes=[accd_buf], accum_op=ALU.add)
    for tt in range(NTC):
        sl = tt % NS
        p.dma("sp", xt[sl][:, :], x_in[tt * 128:(tt + 1) * 128, :], writes=[bufs["xt"][sl]], sem=tt % 2)
        p.dma("sp", xo[sl][:, :], accd[tt * 128:(tt + 1) * 128, :],
              reads=[accd_buf], writes=[bufs["xo"][sl]], sem=2 + tt % 2)
        p.op("dve", lambda v, sl=sl: v.tensor_add(xo[sl][:, :], xo[sl][:, :], xt[sl][:, :]),
             reads=[bufs["xt"][sl]], writes=[bufs["xo"][sl]])
        p.dma("sp", x_out[tt * 128:(tt + 1) * 128, :], xo[sl][:, :],
              reads=[bufs["xo"][sl]], sem=4 + tt % 2)


def _l0_sinusoidal(n, d=64):
    pos = np.arange(n, dtype=np.float32)[:, None]
    div = np.exp(np.arange(0, d, 2, dtype=np.float32) * (-math.log(10000.0) / d))
    ang = pos * div
    return np.stack([np.sin(ang), np.cos(ang)], axis=-1).reshape(n, d).astype(np.float32)


def emit_L0_attn(p, env, x_in, x_out, W, ntok, T=2048):
    nc = env.nc
    H, HD, NCL, GS = 16, 64, 64, 256
    nseq = ntok // T
    NT = T // 128
    ntpc = T // NCL
    cpt = 128 // ntpc
    G = T // GS
    TPG = GS // 128
    ncpg = GS // ntpc
    c_max = (G - 1) * ncpg
    qscale = 1.0 / math.sqrt(HD)

    ident, identb_buf = env.sb["identb"], env.buf("identb")
    ones, ones_buf = env.sb["onesb"], env.buf("onesb")
    pS, pS_buf = env.psum["pS"], env.buf("pS")
    pT, pT_buf = env.psum["pT"], env.buf("pT")
    pO, pO_buf = env.psum["pO"], env.buf("pO")

    S = _l0_sinusoidal(c_max + GS)
    qsin = np.concatenate([S[pi * 128:(pi + 1) * 128, 0:32] for pi in range(TPG)], 1) * qscale
    qcos = np.concatenate([S[pi * 128:(pi + 1) * 128, 32:64] for pi in range(TPG)], 1) * qscale
    ks_l, kc_l = [], []
    for tt in range(NT):
        g, pi = tt // TPG, tt % TPG
        rows = np.arange(128) + g * ncpg + pi * 128
        ks_l.append(S[rows, 0:32]); kc_l.append(S[rows, 32:64])
    ksin, kcos = np.concatenate(ks_l, 1), np.concatenate(kc_l, 1)
    csin = np.zeros((64, 32), np.float32); ccos = np.zeros((64, 32), np.float32)
    csin[:c_max] = S[:c_max, 0:32]; ccos[:c_max] = S[:c_max, 32:64]
    seg = (np.arange(128)[:, None] // ntpc == np.arange(cpt)[None, :]).astype(np.float32)
    lm = np.tril(np.ones((GS, GS), np.float32))
    lmask = np.concatenate([np.where(lm[pi * 128:(pi + 1) * 128, :], 0, -1e30)
                            for pi in range(TPG)], 1).astype(np.float32)
    clm = np.zeros((128, NT * NCL), np.float32)
    for tt in range(NT):
        for pp in range(128):
            clm[pp, tt * NCL + tt * cpt + pp // ntpc] = 1.0
    consts = {}
    for nm, arr in (("l0qs", qsin), ("l0qc", qcos), ("l0ks", ksin), ("l0kc", kcos),
                    ("l0cs", csin), ("l0cc", ccos)):
        capn = env.add_const(nm, arr)
        env.sbuf(nm, list(arr.shape), F32)
        p.dma("sp", env.sb[nm][:, :], capn, writes=[env.buf(nm)])
        consts[nm] = (env.sb[nm], env.buf(nm))
    for nm, arr in (("l0lm", lmask), ("l0clm", clm)):
        capn = env.add_const(nm, np_bf16(arr))
        env.sbuf(nm, list(arr.shape), BF16)
        p.dma("pool", env.sb[nm][:, :], capn, writes=[env.buf(nm)])
        consts[nm] = (env.sb[nm], env.buf(nm))
    if "identf" not in env.sb:
        idf = env.add_const("identf", np.eye(128, dtype=np.float32))
        env.sbuf("identf", [128, 128], F32)
        p.dma("sp", env.sb["identf"][:], idf, writes=[env.buf("identf")])
    identf, identf_buf = env.sb["identf"], env.buf("identf")
    if "m_onesf" not in env.sb:
        onesf0 = env.sbuf("m_onesf", [1, 128], F32)
        p.op("dve", lambda v: v.memset(onesf0[:, :], 1.0), writes=[env.buf("m_onesf")])
    onesf, onesf_buf = env.sb["m_onesf"], env.buf("m_onesf")
    segb_c = env.add_const("l0seg", np_bf16(seg))
    segb = env.sbuf("l0seg", [128, cpt], BF16)
    p.dma("pool", segb[:, :], segb_c, writes=[env.buf("l0seg")])
    segb_buf = env.buf("l0seg")
    segTb_c = env.add_const("l0segT", np_bf16(seg.T.copy()))
    segTb = env.sbuf("l0segT", [cpt, 128], BF16)
    p.dma("pool", segTb[:, :], segTb_c, writes=[env.buf("l0segT")])
    segT_buf = env.buf("l0segT")

    waT_dram = nc.dram_tensor("l0_waT", [128, NCH * 3 * C], BF16).ap()
    waT_dram_buf = Buf("l0waTd")
    wpT_dram = nc.dram_tensor("l0_wpT", [128, NCH * C], BF16).ap()
    wpT_dram_buf = Buf("l0wpTd")
    wld = [env.sbuf(f"l0wld{i}", [128, C], BF16) for i in range(2)]
    wld_buf = [env.buf(f"l0wld{i}") for i in range(2)]
    xlnT = env.sbuf("l0xlnT", [128, NCH * T], BF16); xlnT_buf = env.buf("l0xlnT")
    tsl = [xlnT[:, i * 1024:(i + 1) * 1024] for i in range(2)]
    tsl_buf = [Buf(f"l0tsl{i}") for i in range(2)]
    for wsrc_ap, rows, dram, dbuf in ((W["w_attn"], 3 * C, waT_dram, waT_dram_buf),
                                      (W["w_proj"], C, wpT_dram, wpT_dram_buf)):
        for rc in range(rows // 128):
            sl = rc % 2
            p.dma("pool", wld[sl][:, :], wsrc_ap[rc * 128:(rc + 1) * 128, :], writes=[wld_buf[sl]])

            def mmtr(t, sl=sl):
                for cc in range(NCH):
                    last = t.matmul(pT[:, cc * 128:(cc + 1) * 128],
                                    wld[sl][:, cc * 128:(cc + 1) * 128], ident[:, :],
                                    start=True, stop=True)
                return last
            p.op("pe", mmtr, reads=[wld_buf[sl], identb_buf], writes=[pT_buf])
            p.op("dve", lambda v, sl=sl: v.tensor_copy(tsl[sl], pT[:, 0:1024]),
                 reads=[pT_buf], writes=[tsl_buf[sl]])
            for cc in range(NCH):
                p.dma("sp", dram[:, cc * rows + rc * 128: cc * rows + (rc + 1) * 128],
                      tsl[sl][:, sl * 0 + cc * 128:(cc + 1) * 128] if False else
                      xlnT[:, sl * 1024 + cc * 128: sl * 1024 + (cc + 1) * 128],
                      reads=[tsl_buf[sl]], writes=[dbuf])
    was_sb = env.sbuf("l0was", [64, C], BF16)
    was_buf = env.buf("l0was")
    p.dma("pool", was_sb[:, :], W["w_assign"], writes=[was_buf])
    wasT = env.sbuf("l0wasT", [128, NCH * 64], BF16)
    wasT_buf = env.buf("l0wasT")
    for cc in range(NCH):
        p.op("pe", lambda t, cc=cc: t.matmul(pT[:, 0:64], was_sb[:, cc * 128:(cc + 1) * 128],
             ident[0:64, 0:64], start=True, stop=True),
             reads=[was_buf, identb_buf], writes=[pT_buf])
        p.op("dve", lambda v, cc=cc: v.tensor_copy(wasT[:, cc * 64:(cc + 1) * 64], pT[:, 0:64]),
             reads=[pT_buf], writes=[wasT_buf])
    ba_sl = [env.sbuf(f"l0ba{i}", [1, 256], BF16) for i in range(2)]
    ba_sl_buf = [env.buf(f"l0ba{i}") for i in range(2)]
    bp_row, bp_buf = emit_row_load(p, env, W["b_proj"], C, "l0bp")
    wpT = env.sbuf("l0wpT", [128, NCH * C], BF16)
    wpT_buf = env.buf("l0wpT")
    p.dma("pool", wpT[:, :], wpT_dram, reads=[wpT_dram_buf], writes=[wpT_buf])

    for tb in tsl_buf:
        if tb.w is not None:
            xlnT_buf.rs.extend(tb.rs)
            xlnT_buf.rs.append(tb.w)
    qfm = env.sbuf("l0qfm", [128, NCH * T], BF16); qfm_buf = env.buf("l0qfm")
    ktok = env.sbuf("l0ktok", [128, NT * C], BF16); ktok_buf = env.buf("l0ktok")
    vtok = env.sbuf("l0vtok", [128, NT * C], BF16); vtok_buf = env.buf("l0vtok")
    wstr = [env.sbuf(f"l0ws{i}", [128, NCH * 256], BF16) for i in range(2)]
    wstr_buf = [env.buf(f"l0ws{i}") for i in range(2)]
    NS = 2
    xt = [env.sbuf(f"l0xt{i}", [128, C], F32) for i in range(NS)]
    stats = [env.sbuf(f"l0st{i}", [128, 8], F32) for i in range(NS)]
    diag = [env.sbuf(f"l0dg{i}", [128, 128], BF16) for i in range(NS)]
    xcb = [env.sbuf(f"l0xc{i}", [128, C], BF16) for i in range(NS)]
    sqd = env.sbuf("l0sq", [128, C], BF16)
    xo = [env.sbuf(f"l0xo{i}", [128, C], F32) for i in range(NS)]
    bufs = {n: [env.buf(f"l0{n}{i}") for i in range(NS)] for n in ("xt", "st", "dg", "xc", "xo")}
    sqd_buf = env.buf("l0sq")
    qt_tmp = env.sbuf("l0qtt", [128, 512], BF16); qt_tmp_buf = env.buf("l0qtt")
    qr_tmp = env.sbuf("l0qrt", [128, 512], BF16); qr_tmp_buf = env.buf("l0qrt")
    ck_dram = nc.dram_tensor("l0ckd", [NCL, C], BF16).ap()
    cv_dram = nc.dram_tensor("l0cvd", [NCL, C], BF16).ap()
    ck_dram_buf = Buf("l0ckd"); cv_dram_buf = Buf("l0cvd")
    cks = env.sbuf("l0cks", [cpt, C], BF16); cks_buf = env.buf("l0cks")
    cvs = env.sbuf("l0cvs", [cpt, C], BF16); cvs_buf = env.buf("l0cvs")
    kcfm = env.sbuf("l0kcfm", [128, H * 64], BF16); kcfm_buf = env.buf("l0kcfm")
    lgx = env.sbuf("l0lgx", [128, 64], F32); lgx_buf = env.buf("l0lgx")
    lcol = env.sbuf("l0lcol", [128, 2], F32); lcol_buf = env.buf("l0lcol")
    lrow = env.sb["l0sq"].bitcast(F32); lrow_buf = env.buf("l0sq")
    lsc = env.sbuf("l0lsc", [128, 2], F32); lsc_buf = env.buf("l0lsc")
    ecol = env.sbuf("l0ecol", [128, 1], BF16); ecol_buf = env.buf("l0ecol")
    rsum = env.sbuf("l0rsum", [cpt, 2], F32); rsum_buf = env.buf("l0rsum")
    rsumb = env.sbuf("l0rsumb", [cpt, 1], BF16); rsumb_buf = env.buf("l0rsumb")
    wcol = env.sbuf("l0wcol", [128, 1], F32); wcol_buf = env.buf("l0wcol")
    Wseg = env.sbuf("l0Wseg", [128, NT * cpt], BF16); Wseg_buf = env.buf("l0Wseg")
    st2 = [env.sbuf(f"l0s2_{i}", [128, 4], F32) for i in range(2)]
    st2_buf = [env.buf(f"l0s2_{i}") for i in range(2)]

    for s in range(nseq):
        base = s * T
        for tt in range(NT):
            sl = tt % NS
            p.dma("sp", xt[sl][:, :], x_in[base + tt * 128: base + (tt + 1) * 128, :],
                  writes=[bufs["xt"][sl]], sem=tt % 2)
            dsts = [xlnT[:, c * T + tt * 128: c * T + (tt + 1) * 128] for c in range(NCH)]
            emit_ln_transpose(p, env, xt[sl][:, :], bufs["xt"][sl], stats[sl][:, :], bufs["st"][sl],
                              diag[sl][:, :], bufs["dg"][sl], xcb[sl][:, :], bufs["xc"][sl],
                              sqd[:, :], sqd_buf, "pT", dsts, xlnT_buf)
        for colq in range(12):
            wslot = colq % 2
            wt = wstr[wslot]
            wt3 = wt[:, :].rearrange("p (c j) -> p c j", j=256)
            wsrc3 = waT_dram.rearrange("p (c f) -> p c f", f=3 * C)
            p.dma("pool", wt3, wsrc3[:, :, colq * 256:(colq + 1) * 256],
                  reads=[waT_dram_buf], writes=[wstr_buf[wslot]])
            p.dma("pool", ba_sl[wslot][0:1, :], W["b_attn"][None, colq * 256:(colq + 1) * 256],
                  writes=[ba_sl_buf[wslot]])
            for tt in range(NT):
                hap = pS[:, (tt % 2) * 512:(tt % 2) * 512 + 256]

                def mmqkv(t, tt=tt, colq=colq, hap=hap, wt=wt, wslot=wslot):
                    for c in range(NCH):
                        t.matmul(hap, xlnT[:, c * T + tt * 128:c * T + (tt + 1) * 128],
                                 wt[:, c * 256:(c + 1) * 256], start=(c == 0), stop=False)
                    return t.matmul(hap, ones[0:1, :], ba_sl[wslot][0:1, :],
                                    start=False, stop=True)
                p.op("pe", mmqkv, reads=[wstr_buf[wslot], xlnT_buf, ones_buf, ba_sl_buf[wslot]],
                     writes=[pS_buf])
                if colq < 4:
                    p.op("dve", lambda v, hap=hap: v.tensor_copy(qt_tmp[:, 0:256], hap),
                         reads=[pS_buf], writes=[qt_tmp_buf])
                    pi = tt % TPG
                    qs_t, qs_b = consts["l0qs"]; qc_t, qc_b = consts["l0qc"]
                    for hh in range(4):
                        p.op("dve", lambda v, hh=hh, pi=pi: v.tensor_mul(
                             qr_tmp[:, hh * 64 + 1:hh * 64 + 64:2], qt_tmp[:, hh * 64:hh * 64 + 64:2],
                             qs_t[:, pi * 32:(pi + 1) * 32]),
                             reads=[qt_tmp_buf, qs_b], writes=[qr_tmp_buf])
                        p.op("dve", lambda v, hh=hh, pi=pi: v.tensor_mul(
                             qr_tmp[:, hh * 64:hh * 64 + 64:2], qt_tmp[:, hh * 64 + 1:hh * 64 + 64:2],
                             qc_t[:, pi * 32:(pi + 1) * 32]),
                             reads=[qt_tmp_buf, qc_b], writes=[qr_tmp_buf])
                    p.op("dve", lambda v: v.tensor_scalar(qr_tmp[:, 0:256:2], qr_tmp[:, 0:256:2],
                         -1.0, None, ALU.mult), reads=[qr_tmp_buf], writes=[qr_tmp_buf])

                    def mmtq(t, tt=tt):
                        for j in range(2):
                            last = t.matmul(pT[:, j * 128:(j + 1) * 128],
                                            qr_tmp[:, j * 128:(j + 1) * 128], ident[:, :],
                                            start=True, stop=True)
                        return last
                    p.op("pe", mmtq, reads=[qr_tmp_buf, identb_buf], writes=[pT_buf])

                    def evq(v, tt=tt, colq=colq):
                        for j in range(2):
                            cc = colq * 2 + j
                            last = v.tensor_copy(qfm[:, cc * T + tt * 128:cc * T + (tt + 1) * 128],
                                                 pT[:, j * 128:(j + 1) * 128])
                        return last
                    p.op("dve", evq, reads=[pT_buf], writes=[qfm_buf])
                elif colq < 8:
                    p.op("dve", lambda v, tt=tt, colq=colq, hap=hap: v.tensor_copy(
                         ktok[:, tt * C + (colq - 4) * 256:tt * C + (colq - 3) * 256], hap),
                         reads=[pS_buf], writes=[ktok_buf])
                else:
                    p.op("dve", lambda v, tt=tt, colq=colq, hap=hap: v.tensor_copy(
                         vtok[:, tt * C + (colq - 8) * 256:tt * C + (colq - 7) * 256], hap),
                         reads=[pS_buf], writes=[vtok_buf])
        clm_t, clm_b = consts["l0clm"]
        for tt in range(NT):
            def mml(t, tt=tt):
                for c in range(NCH):
                    last = t.matmul(pO[:, 0:64], xlnT[:, c * T + tt * 128:c * T + (tt + 1) * 128],
                                    wasT[:, c * 64:(c + 1) * 64], start=(c == 0), stop=(c == NCH - 1))
                return last
            p.op("pe", mml, reads=[xlnT_buf, wasT_buf], writes=[pO_buf])
            p.op("dve", lambda v, tt=tt: v.tensor_mul(lgx[:, :], pO[:, 0:64],
                 clm_t[:, tt * 64:(tt + 1) * 64]), reads=[pO_buf, clm_b], writes=[lgx_buf])
            p.op("dve", lambda v: v.tensor_reduce(lcol[:, 0:1], lgx[:, :], AX.X, ALU.add),
                 reads=[lgx_buf], writes=[lcol_buf])
            p.op("pe", lambda t: t.matmul(pT[0:1, 0:128], lcol[:, 0:1], identf[:, :],
                 start=True, stop=True), reads=[lcol_buf, identf_buf], writes=[pT_buf])
            p.op("dve", lambda v: v.tensor_copy(lrow[0:1, 0:128], pT[0:1, 0:128]),
                 reads=[pT_buf], writes=[lrow_buf])
            p.op("dve", lambda v: v.tensor_reduce(lsc[0:1, 0:1], lrow[0:1, 0:128], AX.X, ALU.max,
                 negate=True), reads=[lrow_buf], writes=[lsc_buf])
            p.op("pe", lambda t: t.matmul(pT[:, 0:1], onesf[0:1, :], lsc[0:1, 0:1],
                 start=True, stop=True), reads=[onesf_buf, lsc_buf], writes=[pT_buf])
            p.op("dve", lambda v: v.tensor_copy(lsc[:, 1:2], pT[:, 0:1]),
                 reads=[pT_buf], writes=[lsc_buf])
            p.op("act", lambda s2: s2.activation(ecol[:, :], lcol[:, 0:1], AF.Exp,
                 bias=lsc[:, 1:2]), reads=[lcol_buf, lsc_buf], writes=[ecol_buf])
            p.op("pe", lambda t: t.matmul(pT[0:cpt, 0:1], segb[:, :], ecol[:, :],
                 start=True, stop=True), reads=[segb_buf, ecol_buf], writes=[pT_buf])
            p.op("dve", lambda v: v.tensor_copy(rsum[:, 0:1], pT[0:cpt, 0:1]),
                 reads=[pT_buf], writes=[rsum_buf])
            p.op("dve", lambda v: v.reciprocal(rsum[:, 1:2], rsum[:, 0:1]),
                 reads=[rsum_buf], writes=[rsum_buf])
            p.op("dve", lambda v: v.tensor_copy(rsumb[:, 0:1], rsum[:, 1:2]),
                 reads=[rsum_buf], writes=[rsumb_buf])
            p.op("pe", lambda t: t.matmul(pT[:, 0:1], segTb[:, :], rsumb[:, 0:1],
                 start=True, stop=True), reads=[segT_buf, rsumb_buf], writes=[pT_buf])
            p.op("dve", lambda v: v.scalar_tensor_tensor(wcol[:, :], pT[:, 0:1], 1.0, ecol[:, :],
                 ALU.mult, ALU.mult), reads=[pT_buf, ecol_buf], writes=[wcol_buf])
            p.op("dve", lambda v, tt=tt: v.tensor_scalar(Wseg[:, tt * cpt:(tt + 1) * cpt],
                 segb[:, :], wcol[:, 0:1], None, ALU.mult),
                 reads=[segb_buf, wcol_buf], writes=[Wseg_buf])
        for tt in range(NT):
            def mmck(t, tt=tt):
                t.matmul(pT[0:cpt, 0:512], Wseg[:, tt * cpt:(tt + 1) * cpt],
                         ktok[:, tt * C:tt * C + 512], start=True, stop=True)
                t.matmul(pT[0:cpt, 512:1024], Wseg[:, tt * cpt:(tt + 1) * cpt],
                         ktok[:, tt * C + 512:(tt + 1) * C], start=True, stop=True)
                t.matmul(pO[0:cpt, 0:512], Wseg[:, tt * cpt:(tt + 1) * cpt],
                         vtok[:, tt * C:tt * C + 512], start=True, stop=True)
                return t.matmul(pO[0:cpt, 512:1024], Wseg[:, tt * cpt:(tt + 1) * cpt],
                                vtok[:, tt * C + 512:(tt + 1) * C], start=True, stop=True)
            p.op("pe", mmck, reads=[Wseg_buf, ktok_buf, vtok_buf], writes=[pT_buf, pO_buf])
            p.op("dve", lambda v: v.tensor_copy(cks[:, :], pT[0:cpt, 0:C]),
                 reads=[pT_buf], writes=[cks_buf])
            p.op("dve", lambda v: v.tensor_copy(cvs[:, :], pO[0:cpt, 0:C]),
                 reads=[pO_buf], writes=[cvs_buf])
            p.dma("sp", ck_dram[tt * cpt:(tt + 1) * cpt, :], cks[:, :], reads=[cks_buf],
                  writes=[ck_dram_buf], sem=3)
            p.dma("sp", cv_dram[tt * cpt:(tt + 1) * cpt, :], cvs[:, :], reads=[cvs_buf],
                  writes=[cv_dram_buf], sem=4)
        klfm = env.sb["l0xlnT"]
        klfm_buf = env.alias_buf(xlnT_buf, "l0klfm")
        kls_t, kls_b = consts["l0ks"]; klc_t, klc_b = consts["l0kc"]
        if "l0krot" not in env.sb:
            env.sbuf("l0krot", [128, C], BF16)
        krot, krot_buf = env.sb["l0krot"], env.buf("l0krot")
        for tt in range(NT):
            for hh in range(H):
                p.op("dve", lambda v, hh=hh, tt=tt: v.tensor_mul(
                     krot[:, hh * 64 + 1:hh * 64 + 64:2], ktok[:, tt * C + hh * 64:tt * C + hh * 64 + 64:2],
                     kls_t[:, tt * 32:(tt + 1) * 32]),
                     reads=[ktok_buf, kls_b], writes=[krot_buf])
                p.op("dve", lambda v, hh=hh, tt=tt: v.tensor_mul(
                     krot[:, hh * 64:hh * 64 + 64:2], ktok[:, tt * C + hh * 64 + 1:tt * C + hh * 64 + 64:2],
                     klc_t[:, tt * 32:(tt + 1) * 32]),
                     reads=[ktok_buf, klc_b], writes=[krot_buf])
            p.op("dve", lambda v: v.tensor_scalar(krot[:, 0:C:2], krot[:, 0:C:2], -1.0, None,
                 ALU.mult), reads=[krot_buf], writes=[krot_buf])

            def mmtk(t, tt=tt):
                for j in range(NCH):
                    last = t.matmul(pT[:, (j % 8) * 128:(j % 8) * 128 + 128],
                                    krot[:, j * 128:(j + 1) * 128], ident[:, :],
                                    start=True, stop=True)
                return last
            p.op("pe", mmtk, reads=[krot_buf, identb_buf], writes=[pT_buf])

            def evk(v, tt=tt):
                for j in range(NCH):
                    last = v.tensor_copy(klfm[:, j * T + tt * 128:j * T + (tt + 1) * 128],
                                         pT[:, (j % 8) * 128:(j % 8) * 128 + 128])
                return last
            p.op("dve", evk, reads=[pT_buf], writes=[klfm_buf])
        ckt = ktok[0:64, 0:C]
        cvt = ktok[0:64, C:2 * C]
        ckr = ktok[0:64, 2 * C:3 * C]
        ckt_buf = env.alias_buf(ktok_buf, "l0ckt")
        cvt_buf = ckt_buf
        ckr_buf = ckt_buf
        p.dma("pool", ckt[:, :], ck_dram, reads=[ck_dram_buf], writes=[ckt_buf])
        p.dma("pool", cvt[:, :], cv_dram, reads=[cv_dram_buf], writes=[cvt_buf])
        cs_t, cs_b = consts["l0cs"]; cc_t, cc_b = consts["l0cc"]
        for hh in range(H):
            p.op("dve", lambda v, hh=hh: v.tensor_mul(ckr[:, hh * 64 + 1:hh * 64 + 64:2],
                 ckt[:, hh * 64:hh * 64 + 64:2], cs_t[:, :]),
                 reads=[ckt_buf, cs_b], writes=[ckr_buf])
            p.op("dve", lambda v, hh=hh: v.tensor_mul(ckr[:, hh * 64:hh * 64 + 64:2],
                 ckt[:, hh * 64 + 1:hh * 64 + 64:2], cc_t[:, :]),
                 reads=[ckt_buf, cc_b], writes=[ckr_buf])
        p.op("dve", lambda v: v.tensor_scalar(ckr[:, 0:C:2], ckr[:, 0:C:2], -1.0, None, ALU.mult),
             reads=[ckr_buf], writes=[ckr_buf])
        for hh in range(H):
            p.op("pe", lambda t, hh=hh: t.matmul(pT[0:64, 0:64], ckr[:, hh * 64:(hh + 1) * 64],
                 ident[0:64, 0:64], start=True, stop=True),
                 reads=[ckr_buf, identb_buf], writes=[pT_buf])
            p.op("dve", lambda v, hh=hh: v.tensor_copy(kcfm[0:64, hh * 64:(hh + 1) * 64],
                 pT[0:64, 0:64]), reads=[pT_buf], writes=[kcfm_buf])
            p.op("dve", lambda v, hh=hh: v.tensor_copy(kcfm[64:128, hh * 64:(hh + 1) * 64],
                 pT[0:64, 0:64]), reads=[pT_buf], writes=[kcfm_buf])
        Es = [qt_tmp[:, 0:320], qr_tmp[:, 0:320]]
        Es_buf = [env.alias_buf(qt_tmp_buf, "l0E0"), env.alias_buf(qr_tmp_buf, "l0E1")]
        ETs = krot[:, 0:512]
        ETs_buf = env.alias_buf(krot_buf, "l0ET")
        dgl = [diag[0][:, :], diag[1][:, :]]
        dgl_buf = [env.alias_buf(bufs["dg"][0], "l0dl0"), env.alias_buf(bufs["dg"][1], "l0dl1")]
        ysb = xcb[0][:, :]
        ysb_buf = env.alias_buf(bufs["xc"][0], "l0y")
        lm_t, lm_b = consts["l0lm"]
        for g in range(G):
            ncl_vis = g * ncpg
            for qt2 in range(TPG):
                qt = g * TPG + qt2
                for h in range(H):
                    hp = 64 * (h % 2)
                    hc = h // 2
                    width = ncl_vis + GS

                    def mmsc(t, g=g, qt=qt, h=h, hp=hp, hc=hc, ncl_vis=ncl_vis):
                        if ncl_vis > 0:
                            t.matmul(pS[:, 0:ncl_vis],
                                     qfm[hp:hp + 64, hc * T + qt * 128:hc * T + (qt + 1) * 128],
                                     kcfm[hp:hp + 64, h * 64:h * 64 + ncl_vis],
                                     start=True, stop=True)
                        for j in range(TPG):
                            last = t.matmul(pS[:, ncl_vis + j * 128:ncl_vis + (j + 1) * 128],
                                            qfm[hp:hp + 64, hc * T + qt * 128:hc * T + (qt + 1) * 128],
                                            klfm[hp:hp + 64, hc * T + g * GS + j * 128:hc * T + g * GS + (j + 1) * 128],
                                            start=True, stop=True)
                        return last
                    p.op("pe", mmsc, reads=[qfm_buf, kcfm_buf, klfm_buf], writes=[pS_buf])
                    par = (qt * H + h) % 2
                    p.op("dve", lambda v, ncl_vis=ncl_vis, qt2=qt2: v.tensor_tensor(
                         pS[:, ncl_vis:ncl_vis + GS], pS[:, ncl_vis:ncl_vis + GS],
                         lm_t[:, qt2 * GS:(qt2 + 1) * GS], ALU.add),
                         reads=[lm_b], writes=[pS_buf])
                    p.op("dve", lambda v, width=width, par=par: v.tensor_reduce(
                         st2[par][:, 0:1], pS[:, 0:width], AX.X, ALU.max, negate=True),
                         reads=[pS_buf], writes=[st2_buf[par]])
                    p.op("act", lambda s2, width=width, par=par: s2.activation(
                         Es[par][:, 0:width], pS[:, 0:width], AF.Exp, bias=st2[par][:, 0:1],
                         accum_out=st2[par][:, 1:2]),
                         reads=[pS_buf, st2_buf[par]], writes=[Es_buf[par], st2_buf[par]])
                    p.op("dve", lambda v, par=par: v.reciprocal(st2[par][:, 2:3], st2[par][:, 1:2]),
                         reads=[st2_buf[par]], writes=[st2_buf[par]])
                    p.op("dve", lambda v, par=par: v.tensor_scalar(
                         dgl[par][:, :], ident[:, :], st2[par][:, 2:3], None, ALU.mult),
                         reads=[identb_buf, st2_buf[par]], writes=[dgl_buf[par]])
                    def mmtE(t, ncl_vis=ncl_vis, par=par):
                        o = 0
                        if ncl_vis > 0:
                            t.matmul(pT[0:ncl_vis, 0:128], Es[par][:, 0:ncl_vis], dgl[par][:, :],
                                     start=True, stop=True)
                        for j in range(TPG):
                            last = t.matmul(pO[0:128, j * 128:(j + 1) * 128],
                                            Es[par][:, ncl_vis + j * 128:ncl_vis + (j + 1) * 128],
                                            dgl[par][:, :], start=True, stop=True)
                        return last
                    p.op("pe", mmtE, reads=[Es_buf[par], dgl_buf[par]], writes=[pT_buf, pO_buf])

                    def evE(v, ncl_vis=ncl_vis):
                        if ncl_vis > 0:
                            v.tensor_copy(ETs[:ncl_vis, 0:128], pT[0:ncl_vis, 0:128])
                        return v.tensor_copy(ETs[:, 128:128 + TPG * 128], pO[:, 0:TPG * 128])
                    p.op("dve", evE, reads=[pT_buf, pO_buf], writes=[ETs_buf])

                    def mmpv(t, g=g, h=h, ncl_vis=ncl_vis):
                        first = True
                        if ncl_vis > 0:
                            t.matmul(pT[0:64, 896:1024], cvt[0:ncl_vis, h * 64:(h + 1) * 64],
                                     ETs[0:ncl_vis, 0:128], start=True, stop=False)
                            first = False
                        for j in range(TPG):
                            ksg = g * TPG + j
                            last = t.matmul(pT[0:64, 896:1024],
                                            vtok[:, ksg * C + h * 64:ksg * C + (h + 1) * 64],
                                            ETs[:, 128 + j * 128:128 + (j + 1) * 128],
                                            start=first, stop=(j == TPG - 1))
                            first = False
                        return last
                    p.op("pe", mmpv, reads=[cvt_buf, vtok_buf, ETs_buf], writes=[pT_buf])
                    p.op("dve", lambda v, hp=hp, hc=hc: v.tensor_copy(
                         ysb[hp:hp + 64, hc * 128:(hc + 1) * 128], pT[0:64, 896:1024]),
                         reads=[pT_buf], writes=[ysb_buf])
                sl = qt % NS
                p.dma("sp", xt[sl][:, :], x_in[base + qt * 128: base + (qt + 1) * 128, :],
                      writes=[bufs["xt"][sl]], sem=qt % 2)
                for half in range(2):
                    def mmo(t, half=half):
                        for cc in range(NCH):
                            t.matmul(pO[:, half * 512:(half + 1) * 512],
                                     ysb[:, cc * 128:(cc + 1) * 128],
                                     wpT[:, cc * C + half * 512:cc * C + (half + 1) * 512],
                                     start=(cc == 0), stop=False)
                        return t.matmul(pO[:, half * 512:(half + 1) * 512], ones[0:1, :],
                                        bp_row[0:1, half * 512:(half + 1) * 512],
                                        start=False, stop=True)
                    p.op("pe", mmo, reads=[ysb_buf, wpT_buf, ones_buf, bp_buf], writes=[pO_buf])
                p.op("dve", lambda v, sl=sl: v.tensor_add(xo[sl][:, :], pO[:, 0:C], xt[sl][:, :]),
                     reads=[pO_buf, bufs["xt"][sl]], writes=[bufs["xo"][sl]])
                p.dma("sp", x_out[base + qt * 128: base + (qt + 1) * 128, :], xo[sl][:, :],
                      reads=[bufs["xo"][sl]], sem=5)
        xlnT_buf.w = klfm_buf.w
        xlnT_buf.rs = list(klfm_buf.rs)
        ktok_buf.w = ckt_buf.w
        ktok_buf.rs = list(ckt_buf.rs)
        qt_tmp_buf.w = Es_buf[0].w
        qt_tmp_buf.rs = list(Es_buf[0].rs)
        qr_tmp_buf.w = Es_buf[1].w
        qr_tmp_buf.rs = list(Es_buf[1].rs)
        krot_buf.w = ETs_buf.w
        krot_buf.rs = list(ETs_buf.rs)
        for i2 in range(2):
            bufs["dg"][i2].w = dgl_buf[i2].w
            bufs["dg"][i2].rs = list(dgl_buf[i2].rs)
        bufs["xc"][0].w = ysb_buf.w
        bufs["xc"][0].rs = list(ysb_buf.rs)


BLOCK_WEIGHTS = {
    "L0_ffn": {"W": [C, C], "b": [C], "lora_down": [4, 100], "lora_up": [50, 4],
               "down_aux": [100, C], "up_aux": [C, 50]},
    "L1_ffn": {"W1": [C, 4096], "b1": [4096], "quantum_phases": [4096],
               "quantum_weights_real": [4096], "quantum_weights_imag": [4096],
               "W2": [4096, C], "b2": [C]},
    "L2_attn": {"Wq": [C, C], "Wk": [C, C], "Wv": [C, C], "Wo": [C, C]},
    "L1_attn": {"Wqkv": [C, 3 * C], "Wout": [C, C]},
    "L3_attn": {"wq": [C, C], "bq": [C], "wk": [C, C], "bk": [C],
                "wv": [C, C], "bv": [C], "wo": [C, C], "bo": [C]},
    "L2_ffn": {"gate_weight": [32, C], "e_score_correction_bias": [32],
               "experts_gate_up": [32, C, C], "experts_down": [32, C, 512],
               "shared_gate_w": [C, C], "shared_up_w": [C, C], "shared_down_w": [C, C]},
    "L3_ffn": {"Wg": [C, 8], "W1": [8, C, 4096], "b1": [8, 4096],
               "W2": [8, 4096, C], "b2": [8, C]},
    "L0_attn": {"w_attn": [3 * C, C], "b_attn": [3 * C], "w_proj": [C, C],
                "b_proj": [C], "w_assign": [64, C]},
}

EMITTERS = {"L0_ffn": emit_L0_ffn, "L1_ffn": emit_L1_ffn, "L2_attn": None,
            "L2_ffn": emit_L2_ffn}

PSUM_SPECS = {"L2_attn": [("pS", 2048), ("pT", 1024), ("pO", 1024)],
              "L1_attn": [("pS", 2048), ("pT", 1024), ("pO", 1024)],
              "L3_attn": [("pS", 2048), ("pT", 1024), ("pO", 1024)],
              "L0_attn": [("pS", 1024), ("pT", 1024), ("pO", 1024)]}


def build_single_block(name, ntok, T=2048, chain=False, nglob=None):
    nc = bass.Bass("TRN2", target_bir_lowering=False)
    ctx = ExitStack()
    with ctx:
        x_in = nc.dram_tensor("x_in", [ntok, C], F32, kind="ExternalInput").ap()
        x_out = nc.dram_tensor("x_out", [ntok, C], F32, kind="ExternalOutput").ap()
        W = {}
        for wname, shape in BLOCK_WEIGHTS[name].items():
            W[wname] = nc.dram_tensor(wname, shape, F32, kind="ExternalInput").ap()
        p = Prog(nc, ctx)
        env = Env(nc, ctx, p, psum_spec=PSUM_SPECS.get(name))
        setup_common_consts(env)
        if name == "L3_attn" and chain:
            WG = nc.dram_tensor("Wg", [C, 8], F32, kind="ExternalInput").ap()
            S_tok_o = nc.dram_tensor("S_tok_o", [ntok, 8], F32, kind="ExternalOutput").ap()
            S_T_o = nc.dram_tensor("S_T_o", [8, ntok], F32, kind="ExternalOutput").ap()
            emit_L3_attn(p, env, x_in, x_out, W, ntok, T=T)
            p.barrier()
            carve_reg = env.sb["a_qfm"].bitcast(F32)
            carve_buf = env.alias_buf(env.buf("a_qfm"), "s_carve")
            emit_L3_scores(p, env, x_out, {"Wg": WG}, S_tok_o, ntok, s_t_out=S_T_o,
                           carve=(carve_reg, carve_buf))
        elif name == "L3_ffn" and chain:
            ngl = nglob or ntok
            S_t_in = nc.dram_tensor("S_t_in", [8, ngl], F32, kind="ExternalInput").ap()
            S_tok_in = nc.dram_tensor("S_tok_in", [ntok, 8], F32, kind="ExternalInput").ap()
            emit_L3_ffn(p, env, x_in, x_out, W, ntok, S_t_in, S_tok_in, ngl)
        elif name == "L3_ffn":
            S_tok_d = nc.dram_tensor("S_tok_d", [ntok, 8], F32).ap()
            S_T_d = nc.dram_tensor("S_T_d", [8, ntok], F32).ap()
            emit_L3_scores(p, env, x_in, W, S_tok_d, ntok, s_t_out=S_T_d)
            p.barrier()
            emit_L3_ffn(p, env, x_in, x_out, W, ntok, S_T_d, S_tok_d, ntok)
        elif name == "L2_attn":
            emit_L2_attn(p, env, x_in, x_out, W, ntok, T=T)
        elif name == "L1_attn":
            emit_L1_attn(p, env, x_in, x_out, W, ntok, T=T)
        elif name == "L3_attn":
            emit_L3_attn(p, env, x_in, x_out, W, ntok, T=T)
        elif name == "L0_attn":
            emit_L0_attn(p, env, x_in, x_out, W, ntok, T=T)
        else:
            EMITTERS[name](p, env, x_in, x_out, W, ntok)
        p.wait("sp", p.all_tokens())
        p.emit()
    return nc, env.host_consts



BLOCK_SEQ_ALL = ["L0_attn", "L0_ffn", "L1_attn", "L1_ffn",
                 "L2_attn", "L2_ffn", "L3_attn", "L3_ffn"]

DEFAULT_PSUM = [("pA", 1024), ("pB", 1024), ("pC", 1024), ("pD", 1024)]


def build_fused(ntok=8192, T=2048, nglob=65536, ncores=8, mode="single"):
    nc = bass.Bass("TRN2", target_bir_lowering=False)
    ctx = ExitStack()
    with ctx:
        x_in = nc.dram_tensor("x_in", [ntok, C], F32, kind="ExternalInput").ap()
        x_out = nc.dram_tensor("x_out", [ntok, C], F32, kind="ExternalOutput").ap()
        blocks = BLOCK_SEQ_ALL if mode == "single" else BLOCK_SEQ_ALL[:7]
        W = {}
        for blk in BLOCK_SEQ_ALL:
            W[blk] = {}
            for wname, shape in BLOCK_WEIGHTS[blk].items():
                if blk == "L3_ffn" and mode == "A" and wname != "Wg":
                    continue
                W[blk][wname] = nc.dram_tensor(f"{blk}_{wname}", shape, F32,
                                               kind="ExternalInput").ap()
        xmid = [nc.dram_tensor(f"xmid{i}", [ntok, C], F32).ap() for i in range(7)]
        if mode == "A":
            S_tok_d = nc.dram_tensor("S_tok_o", [ntok, 8], F32, kind="ExternalOutput").ap()
            S_T_d = nc.dram_tensor("S_T_o", [8, ntok], F32, kind="ExternalOutput")
        else:
            S_tok_d = nc.dram_tensor("S_tok_d", [ntok, 8], F32).ap()
            S_T_d = nc.dram_tensor("S_T_d", [8, ntok], F32)
        S_g_d = nc.dram_tensor("S_g_d", [8 * ncores, ntok], F32)
        p = Prog(nc, ctx)
        env = Env(nc, ctx, p, psum_spec=False)
        setup_common_consts(env)
        coll_tok = None
        last_bi = len(blocks) - 1
        for bi, name in enumerate(blocks):
            xi = x_in if bi == 0 else xmid[bi - 1]
            xo = x_out if bi == last_bi else xmid[bi]
            bctx = ExitStack()
            with bctx:
                env.set_block_ctx(bctx)
                env.begin_block(f"B{bi}", PSUM_SPECS.get(name, DEFAULT_PSUM), bctx)
                if name == "L0_attn":
                    emit_L0_attn(p, env, xi, xo, W[name], ntok, T=T)
                elif name == "L0_ffn":
                    emit_L0_ffn(p, env, xi, xo, W[name], ntok)
                elif name == "L1_attn":
                    emit_L1_attn(p, env, xi, xo, W[name], ntok, T=T)
                elif name == "L1_ffn":
                    emit_L1_ffn(p, env, xi, xo, W[name], ntok)
                elif name == "L2_attn":
                    emit_L2_attn(p, env, xi, xo, W[name], ntok, T=T)
                elif name == "L2_ffn":
                    emit_L2_ffn(p, env, xi, xo, W[name], ntok)
                elif name == "L3_attn":
                    emit_L3_attn(p, env, xi, xo, W[name], ntok, T=T)
                    p.barrier()
                    carve_reg = env.sb["a_qfm"].bitcast(F32)
                    carve_buf = env.alias_buf(env.buf("a_qfm"), "s_carve")
                    emit_L3_scores(p, env, xo, {"Wg": W["L3_ffn"]["Wg"]}, S_tok_d,
                                   ntok, s_t_out=S_T_d.ap(),
                                   carve=(carve_reg, carve_buf))
                elif name == "L3_ffn":
                    def coll(g):
                        return g.collective_compute(
                            "AllGather", ALU.bypass,
                            replica_groups=[list(range(ncores))],
                            ins=[S_T_d.ap().opt()],
                            outs=[S_g_d.ap().opt()])
                    coll_tok = p.collective(coll)
                    emit_L3_ffn(p, env, xi, xo, W[name], ntok, S_g_d.ap(), S_tok_d,
                                nglob, S_deps=[coll_tok])
                env.restore_ctx()
            p.barrier()
        p.wait("sp", p.all_tokens())
        p.emit()
    return nc, env.host_consts


from concourse.bass_utils import run_bass_kernel_spmd

NCORES = 8
NTOK_CORE = 8192
NGLOB = NCORES * NTOK_CORE

_cacheA = {}


def kernel(**inputs):
    if "A" not in _cacheA:
        _cacheA["A"] = build_fused(NTOK_CORE, 2048, NGLOB, NCORES, mode="A")
        _cacheA["B"] = build_single_block("L3_ffn", NTOK_CORE, T=2048, chain=True,
                                          nglob=NGLOB)
    ncA, constsA = _cacheA["A"]
    ncB, constsB = _cacheA["B"]
    x = np.ascontiguousarray(inputs["L0_attn_x"]).astype(np.float32, copy=False)
    base = {}
    for blk, wd in BLOCK_WEIGHTS.items():
        for wname in wd:
            if blk == "L3_ffn" and wname != "Wg":
                continue
            key = f"{blk}_{wname}"
            base[key] = np.ascontiguousarray(inputs[key]).astype(np.float32, copy=False)
    for k, v in constsA.items():
        base["cst_" + k] = np.asarray(v)
    in_maps = []
    for c in range(NCORES):
        m = dict(base)
        m["x_in"] = np.ascontiguousarray(x[c * 4:(c + 1) * 4].reshape(NTOK_CORE, C))
        in_maps.append(m)
    resA = run_bass_kernel_spmd(ncA, in_maps, core_ids=list(range(NCORES)))
    x7 = [np.asarray(resA.results[c]["x_out"]) for c in range(NCORES)]
    S_tok = [np.asarray(resA.results[c]["S_tok_o"]) for c in range(NCORES)]
    S_T_full = np.ascontiguousarray(
        np.concatenate([np.asarray(resA.results[c]["S_T_o"]) for c in range(NCORES)], axis=1))
    baseB = {}
    for wname in BLOCK_WEIGHTS["L3_ffn"]:
        baseB[wname] = np.ascontiguousarray(inputs[f"L3_ffn_{wname}"]).astype(np.float32, copy=False)
    for k, v in constsB.items():
        baseB["cst_" + k] = np.asarray(v)
    in_mapsB = []
    for c in range(NCORES):
        m = dict(baseB)
        m["x_in"] = x7[c]
        m["S_t_in"] = S_T_full
        m["S_tok_in"] = S_tok[c]
        in_mapsB.append(m)
    resB = run_bass_kernel_spmd(ncB, in_mapsB, core_ids=list(range(NCORES)))
    out = np.stack([np.asarray(resB.results[c]["x_out"]) for c in range(NCORES)])
    return out.reshape(32, 2048, C).astype(np.float32, copy=False)


# version 07
# speedup vs baseline: 1.0065x; 1.0065x over previous
from contextlib import ExitStack
import math
import numpy as np
import ml_dtypes
import concourse.bass as bass
import concourse.mybir as mybir

F32 = mybir.dt.float32
BF16 = mybir.dt.bfloat16
AF = mybir.ActivationFunctionType
ALU = mybir.AluOpType
AX = mybir.AxisListType

ENGMAP = {"pe": "tensor", "dve": "vector", "act": "scalar", "pool": "gpsimd", "sp": "sync"}
import os
DBG_L1_SIMPLE = os.environ.get("DBG_L1_SIMPLE")
C = 1024
NCH = 8


class Buf:
    __slots__ = ("w", "rs", "name")

    def __init__(self, name=""):
        self.w = None
        self.rs = []
        self.name = name


class Prog:
    def __init__(self, nc, ctx: ExitStack, n_sp_sems=6, n_pool_sems=6):
        self.nc, self.ctx = nc, ctx
        self.items = {k: [] for k in ENGMAP}
        self.csem = {k: ctx.enter_context(nc.semaphore("c_" + k))
                     for k in ("pe", "dve", "act", "pool")}
        self.ccnt = {k: 0 for k in ("pe", "dve", "act", "pool")}
        self.dsems = {"sp": [ctx.enter_context(nc.semaphore(f"d_sp{i}")) for i in range(n_sp_sems)],
                      "pool": [ctx.enter_context(nc.semaphore(f"d_pl{i}")) for i in range(n_pool_sems)]}
        self.dcnt = {("sp", i): 0 for i in range(n_sp_sems)}
        self.dcnt.update({("pool", i): 0 for i in range(n_pool_sems)})
        self.drr = {"sp": 0, "pool": 0}

    def _wire(self, stream, entry_token, reads, writes, deps):
        alldeps = list(deps)
        for r in reads:
            if r.w is not None:
                alldeps.append(r.w)
        for w in writes:
            if w.w is not None:
                alldeps.append(w.w)
            alldeps.extend(w.rs)
        for r in reads:
            r.rs.append(entry_token)
        for w in writes:
            w.w = entry_token
            w.rs = []
        return alldeps

    def op(self, stream, fn, reads=(), writes=(), deps=()):
        self.ccnt[stream] += 1
        tok = (self.csem[stream], self.ccnt[stream])
        alldeps = self._wire(stream, tok, reads, writes, deps)
        self.items[stream].append((fn, alldeps, tok[0], tok[1]))
        return tok

    def dma(self, stream, out, in_, reads=(), writes=(), deps=(), sem=None, **kw):
        i = sem if sem is not None else self.drr[stream] % len(self.dsems[stream])
        if sem is None:
            self.drr[stream] += 1
        prev = self.dcnt[(stream, i)]
        self.dcnt[(stream, i)] += 16
        tok = (self.dsems[stream][i], self.dcnt[(stream, i)])
        deps = list(deps)
        if prev > 0:
            deps.append((self.dsems[stream][i], prev))
        alldeps = self._wire(stream, tok, reads, writes, deps)
        fn = lambda eng, out=out, in_=in_, kw=kw: eng.dma_start(out=out, in_=in_, **kw)
        self.items[stream].append((fn, alldeps, tok[0], tok[1]))
        return tok

    def collective(self, fn, reads=(), writes=(), deps=()):
        i = self.drr["pool"] % len(self.dsems["pool"])
        self.drr["pool"] += 1
        prev = self.dcnt[("pool", i)]
        self.dcnt[("pool", i)] += 16
        tok = (self.dsems["pool"][i], self.dcnt[("pool", i)])
        deps = list(deps)
        if prev > 0:
            deps.append((self.dsems["pool"][i], prev))
        alldeps = self._wire("pool", tok, reads, writes, deps)
        self.items["pool"].append((fn, alldeps, tok[0], tok[1]))
        return tok

    def all_tokens(self):
        toks = [(self.csem[k], self.ccnt[k]) for k in self.csem if self.ccnt[k] > 0]
        for (s, i), c in self.dcnt.items():
            if c > 0:
                toks.append((self.dsems[s][i], c))
        return toks

    def barrier(self):
        toks = self.all_tokens()
        for stream in self.items:
            self.items[stream].append((None, toks, None, None))

    def wait(self, stream, deps):
        self.items[stream].append((None, list(deps), None, None))

    def emit(self):
        dma_semset = set()
        for lst in self.dsems.values():
            dma_semset.update(id(s) for s in lst)
        with self.nc.Block() as block:
            for stream, items in self.items.items():
                if not items:
                    continue
                dec = getattr(block, ENGMAP[stream])

                def body(eng, _items=items):
                    lw = {}
                    for fn, deps, sem, val in _items:
                        for (dsem, dval) in deps:
                            key = id(dsem)
                            if lw.get(key, 0) < dval:
                                eng.wait_ge(dsem, dval)
                                lw[key] = dval
                        if fn is not None:
                            inst = fn(eng)
                            if isinstance(inst, (list, tuple)):
                                inst = inst[-1]
                            inst.then_inc(sem, 16 if id(sem) in dma_semset else 1)

                dec(body)


class Env:

    def __init__(self, nc, ctx, p: Prog, psum_spec=None):
        self.nc, self.ctx, self.p = nc, ctx, p
        self.host_consts = {}
        self.const_aps = {}
        self.const_bufs = {}
        self.prefix = ""
        self.psum = {}
        if psum_spec is not False:
            for name, wid in (psum_spec or [("pA", 1024), ("pB", 1024), ("pC", 1024), ("pD", 1024)]):
                h = ctx.enter_context(nc.psum_tensor(name, [128, wid], F32))
                self.psum[name] = h
                self.const_bufs[name] = Buf(name)
        self.sb = {}
        self.sbuf_bufs = {}

    def begin_block(self, prefix, psum_spec, block_ctx):
        self.prefix = prefix
        self.psum = {}
        for name, wid in psum_spec:
            h = block_ctx.enter_context(self.nc.psum_tensor(prefix + name, [128, wid], F32))
            self.psum[name] = h
            self.const_bufs[name] = Buf(prefix + name)

    def add_const(self, name, arr):
        if name in self.const_aps:
            return self.const_aps[name]
        self.host_consts[name] = arr
        dt = {np.dtype(np.float32): F32, np.dtype(ml_dtypes.bfloat16): BF16,
              np.dtype(np.int32): mybir.dt.int32}[arr.dtype]
        ap = self.nc.dram_tensor("cst_" + name, list(arr.shape), dt, kind="ExternalInput").ap()
        self.const_aps[name] = ap
        return ap

    SHARED_NAMES = {"identb", "onesb", "onescol", "identf", "m_onesf", "cmask"}

    def sbuf(self, name, shape, dt, ctx=None):
        pn = name if name in self.SHARED_NAMES else self.prefix + name
        h = (ctx or self.ctx).enter_context(self.nc.sbuf_tensor(pn, shape, dt))
        self.sb[name] = h
        self.sbuf_bufs[name] = Buf(pn)
        return h

    def set_block_ctx(self, ctx):
        self._saved_ctx = self.ctx
        self.ctx = ctx

    def restore_ctx(self):
        self.ctx = self._saved_ctx

    def buf(self, name):
        return self.sbuf_bufs.get(name) or self.const_bufs[name]

    def alias_buf(self, src_buf, name):
        b = Buf(name)
        b.w = src_buf.w
        b.rs = list(src_buf.rs)
        return b


def np_bf16(a):
    return np.asarray(a, dtype=ml_dtypes.bfloat16)



def setup_common_consts(env: Env):
    nc, p = env.nc, env.p
    idb = env.add_const("identb", np_bf16(np.eye(128)))
    ones = env.add_const("onesb", np_bf16(np.ones((1, 128))))
    onesc = env.add_const("onescol", np_bf16(np.ones((128, 1))))
    idf = env.add_const("identf", np.eye(128, dtype=np.float32))
    cm = env.add_const("cmask128", np.triu(np.full((128, 128), -1e30, np.float32), 1))
    env.sbuf("identb", [128, 128], BF16)
    env.sbuf("onesb", [1, 128], BF16)
    env.sbuf("onescol", [128, 1], BF16)
    env.sbuf("identf", [128, 128], F32)
    env.sbuf("cmask", [128, 128], F32)
    env.sbuf("m_onesf", [1, 128], F32)
    p.dma("sp", env.sb["identb"][:], idb, writes=[env.buf("identb")])
    p.dma("sp", env.sb["onesb"][:], ones, writes=[env.buf("onesb")])
    p.dma("sp", env.sb["onescol"][:], onesc, writes=[env.buf("onescol")])
    p.dma("sp", env.sb["identf"][:], idf, writes=[env.buf("identf")])
    p.dma("sp", env.sb["cmask"][:], cm, writes=[env.buf("cmask")])
    p.op("dve", lambda v: v.memset(env.sb["m_onesf"][:], 1.0), writes=[env.buf("m_onesf")])


def emit_ln_transpose(p, env, xt_ap, xt_buf, stats_ap, stats_buf, diag_ap, diag_buf,
                      xcb_ap, xcb_buf, sqd_ap, sqd_buf, pT_name, xlnT_ap, xlnT_buf,
                      n_t=128, out_f32_ap=None, out_f32_buf=None):
    nc = env.nc
    ident, identb_buf = env.sb["identb"], env.buf("identb")
    p.op("dve", lambda v: v.tensor_reduce(stats_ap[:, 0:1], xt_ap, AX.X, ALU.add, negate=True),
         reads=[xt_buf], writes=[stats_buf])
    p.op("dve", lambda v: v.tensor_scalar(stats_ap[:, 1:2], stats_ap[:, 0:1], 1.0 / C, None, ALU.mult),
         reads=[stats_buf], writes=[stats_buf])
    p.op("dve", lambda v: v.tensor_scalar(xcb_ap, xt_ap, stats_ap[:, 1:2], None, ALU.add),
         reads=[xt_buf, stats_buf], writes=[xcb_buf])
    p.op("act", lambda s: s.activation(sqd_ap, xcb_ap, AF.Square, accum_out=stats_ap[:, 2:3]),
         reads=[xcb_buf], writes=[sqd_buf, stats_buf])
    p.op("dve", lambda v: v.tensor_scalar(stats_ap[:, 3:4], stats_ap[:, 2:3], 1.0 / C, 1e-5, ALU.mult, ALU.add),
         reads=[stats_buf], writes=[stats_buf])
    p.op("act", lambda s: s.activation(stats_ap[:, 4:5], stats_ap[:, 3:4], AF.Sqrt),
         reads=[stats_buf], writes=[stats_buf])
    p.op("dve", lambda v: v.reciprocal(stats_ap[:, 5:6], stats_ap[:, 4:5]),
         reads=[stats_buf], writes=[stats_buf])
    p.op("dve", lambda v: v.tensor_scalar(diag_ap, ident[:n_t, :n_t], stats_ap[:, 5:6], None, ALU.mult),
         reads=[identb_buf, stats_buf], writes=[diag_buf])
    pT = env.psum[pT_name]
    pT_buf = env.buf(pT_name)

    def mm(t, xcb_ap=xcb_ap, diag_ap=diag_ap, pT=pT, n_t=n_t):
        last = None
        for c2 in range(NCH):
            last = t.matmul(pT[:, c2 * n_t:(c2 + 1) * n_t],
                            xcb_ap[:, c2 * 128:(c2 + 1) * 128], diag_ap,
                            start=True, stop=True)
        return last
    p.op("pe", mm, reads=[xcb_buf, diag_buf], writes=[pT_buf])
    if isinstance(xlnT_ap, list):
        def evac(v, pT=pT, n_t=n_t, xlnT_ap=xlnT_ap):
            last = None
            for c2 in range(NCH):
                last = v.tensor_copy(xlnT_ap[c2], pT[:, c2 * n_t:(c2 + 1) * n_t])
            return last
        p.op("dve", evac, reads=[pT_buf], writes=[xlnT_buf])
    else:
        p.op("dve", lambda v: v.tensor_copy(xlnT_ap, pT[:, 0:NCH * n_t]),
             reads=[pT_buf], writes=[xlnT_buf])
    if out_f32_ap is not None:
        p.op("act", lambda s: s.activation(out_f32_ap, xcb_ap, AF.Copy, scale=stats_ap[:, 5:6]),
             reads=[xcb_buf, stats_buf], writes=[out_f32_buf])


def emit_weight_T(p, env, w_hbm_ap, rows, cols, wb_name, wT_ap, wT_buf, pname="pC",
                  n_row_chunks=None, accum_into=None):
    nc = env.nc
    assert rows % 128 == 0 or rows <= 128
    assert cols % 128 == 0
    ncc = cols // 128
    nrc = (rows + 127) // 128
    wtmp = env.sbuf(wb_name, [128, cols], BF16)
    wtmp_bufs = [Buf(wb_name + str(i)) for i in range(2)]
    ident, identb_buf = env.sb["identb"], env.buf("identb")
    pT = env.psum[pname]
    pT_buf = env.buf(pname)
    for rc in range(nrc):
        r0 = rc * 128
        nr = min(128, rows - r0)
        slot = rc % 2
        b = wtmp_bufs[0]
        p.dma("pool", wtmp[:nr, :], w_hbm_ap[r0:r0 + nr, :], writes=[b])
        for cc in range(ncc):
            def mm(t, nr=nr, cc=cc, wtmp=wtmp):
                return t.matmul(pT[:, 0:nr], wtmp[:nr, cc * 128:(cc + 1) * 128],
                                ident[:nr, :nr], start=True, stop=True)
            p.op("pe", mm, reads=[b, identb_buf], writes=[pT_buf])
            dst = wT_ap[:, cc * rows + r0: cc * rows + r0 + nr]
            p.op("dve", lambda v, dst=dst, nr=nr: v.tensor_copy(dst, pT[:, 0:nr]),
                 reads=[pT_buf], writes=[wT_buf])
    return wtmp


def emit_fvec128(p, env, vec_hbm_ap, F, name, scale=None, bias=None):
    nc = env.nc
    ncv = F // 128
    tmp = env.sbuf(name + "_r", [max(ncv, 1), 128], F32)
    tmp_buf = Buf(name + "_r")
    out = env.sbuf(name, [128, ncv], F32)
    out_buf = env.buf(name)
    if "identf" not in env.sb:
        idf = env.add_const("identf", np.eye(128, dtype=np.float32))
        env.sbuf("identf", [128, 128], F32)
        p.dma("sp", env.sb["identf"][:], idf, writes=[env.buf("identf")])
    identf, identf_buf = env.sb["identf"], env.buf("identf")
    p.dma("sp", tmp[:, :], vec_hbm_ap.rearrange("(a p) -> a p", p=128), writes=[tmp_buf])
    pname = "pD" if "pD" in env.psum else list(env.psum)[-1]
    pT = env.psum[pname]
    pT_buf = env.buf(pname)
    p.op("pe", lambda t: t.matmul(pT[:, 0:ncv], tmp[:ncv, :], identf[:ncv, :ncv],
                                  start=True, stop=True),
         reads=[tmp_buf, identf_buf], writes=[pT_buf])
    if scale is None and bias is None:
        p.op("dve", lambda v: v.tensor_copy(out[:, :], pT[:, 0:ncv]),
             reads=[pT_buf], writes=[out_buf])
    else:
        s = 1.0 if scale is None else scale
        b = 0.0 if bias is None else bias
        p.op("dve", lambda v: v.tensor_scalar(out[:, :], pT[:, 0:ncv], s, b, ALU.mult, ALU.add),
             reads=[pT_buf], writes=[out_buf])
    return out, out_buf


def emit_row_load(p, env, vec_hbm_ap, F, name, dt=BF16):
    out = env.sbuf(name, [1, F], dt)
    out_buf = env.buf(name)
    stream = "pool" if dt == BF16 else "sp"
    p.dma(stream, out[0:1, :], vec_hbm_ap[None, :] if len(vec_hbm_ap.shape) == 1 else vec_hbm_ap,
          writes=[out_buf])
    return out, out_buf



def emit_L0_ffn(p, env, x_in, x_out, W, ntok):
    nc = env.nc
    ntiles = ntok // 128
    weffT = env.sbuf("weffT", [128, NCH * C], BF16)
    weffT_buf = env.buf("weffT")
    emit_weight_T(p, env, W["W"], C, C, "l0w_tmp", weffT[:, :], weffT_buf, pname="pC")
    ld = env.sbuf("ld", [4, 100], BF16); ld_buf = env.buf("ld")
    lu = env.sbuf("lu", [50, 4], BF16); lu_buf = env.buf("lu")
    da = env.sbuf("da", [100, C], BF16); da_buf = env.buf("da")
    ua = env.sbuf("ua", [128, 8 * 50], BF16); ua_buf = env.buf("ua")
    p.dma("pool", ld[:, :], W["lora_down"], writes=[ld_buf])
    p.dma("pool", lu[:, :], W["lora_up"], writes=[lu_buf])
    p.dma("pool", da[:, :], W["down_aux"], writes=[da_buf])
    for cc in range(8):
        p.dma("pool", ua[:, cc * 50:(cc + 1) * 50], W["up_aux"][cc * 128:(cc + 1) * 128, :],
              writes=[ua_buf])
    ident, identb_buf = env.sb["identb"], env.buf("identb")
    pC, pC_buf = env.psum["pC"], env.buf("pC")
    pD, pD_buf = env.psum["pD"], env.buf("pD")
    ldT = env.sbuf("ldT", [100, 4], BF16); ldT_buf = env.buf("ldT")
    p.op("pe", lambda t: t.matmul(pC[0:100, 0:4], ld[:, 0:100], ident[0:4, 0:4], start=True, stop=True),
         reads=[ld_buf, identb_buf], writes=[pC_buf])
    p.op("dve", lambda v: v.tensor_copy(ldT[:, :], pC[0:100, 0:4]), reads=[pC_buf], writes=[ldT_buf])
    A = env.sbuf("A", [4, C], BF16); A_buf = env.buf("A")
    def mmA(t):
        t.matmul(pD[0:4, 0:512], ldT[:, :], da[:, 0:512], start=True, stop=True)
        return t.matmul(pD[0:4, 512:1024], ldT[:, :], da[:, 512:1024], start=True, stop=True)
    p.op("pe", mmA, reads=[ldT_buf, da_buf], writes=[pD_buf])
    p.op("dve", lambda v: v.tensor_copy(A[:, :], pD[0:4, 0:1024]), reads=[pD_buf], writes=[A_buf])
    luT = env.sbuf("luT", [4, 50], BF16); luT_buf = env.buf("luT")
    p.op("pe", lambda t: t.matmul(pC[0:4, 0:50], lu[:, 0:4], ident[0:50, 0:50], start=True, stop=True),
         reads=[lu_buf, identb_buf], writes=[pC_buf])
    p.op("dve", lambda v: v.tensor_copy(luT[:, :], pC[0:4, 0:50]), reads=[pC_buf], writes=[luT_buf])
    Bm = env.sbuf("Bm", [50, C], BF16); Bm_buf = env.buf("Bm")
    def mmB(t):
        t.matmul(pD[0:50, 0:512], luT[:, :], A[:, 0:512], start=True, stop=True)
        return t.matmul(pD[0:50, 512:1024], luT[:, :], A[:, 512:1024], start=True, stop=True)
    p.op("pe", mmB, reads=[luT_buf, A_buf], writes=[pD_buf])
    p.op("dve", lambda v: v.tensor_copy(Bm[:, :], pD[0:50, 0:1024]), reads=[pD_buf], writes=[Bm_buf])
    uaT = env.sbuf("uaT", [50, C], BF16); uaT_buf = env.buf("uaT")
    for cc2 in range(8):
        p.op("pe", lambda t, cc2=cc2: t.matmul(pC[0:50, 0:128], ua[:, cc2 * 50:(cc2 + 1) * 50],
                                               ident[:, :], start=True, stop=True),
             reads=[ua_buf, identb_buf], writes=[pC_buf])
        p.op("dve", lambda v, cc2=cc2: v.tensor_copy(uaT[:, cc2 * 128:(cc2 + 1) * 128], pC[0:50, 0:128]),
             reads=[pC_buf], writes=[uaT_buf])
    for ic in range(8):
        def mmCT(t, ic=ic):
            t.matmul(pD[:, 0:512], Bm[:, ic * 128:(ic + 1) * 128], uaT[:, 0:512], start=True, stop=True)
            return t.matmul(pD[:, 512:1024], Bm[:, ic * 128:(ic + 1) * 128], uaT[:, 512:1024], start=True, stop=True)
        p.op("pe", mmCT, reads=[Bm_buf, uaT_buf], writes=[pD_buf])
        dst = weffT[:, ic * C:(ic + 1) * C]
        p.op("dve", lambda v, dst=dst: v.tensor_tensor(dst, pD[:, 0:1024], dst, ALU.add),
             reads=[pD_buf, weffT_buf], writes=[weffT_buf])
    b_row, b_row_buf = emit_row_load(p, env, W["b"], C, "l0b_row")

    NS = 2
    xt = [env.sbuf(f"xt{i}", [128, C], F32) for i in range(NS)]
    xo = [env.sbuf(f"xo{i}", [128, C], F32) for i in range(NS)]
    stats = [env.sbuf(f"stats{i}", [128, 8], F32) for i in range(NS)]
    diag = [env.sbuf(f"diag{i}", [128, 128], BF16) for i in range(NS)]
    xcb = [env.sbuf(f"xcb{i}", [128, C], BF16) for i in range(NS)]
    sqd = env.sbuf("sqd", [128, C], BF16)
    xlnT = [env.sbuf(f"xlnT{i}", [128, C], BF16) for i in range(NS)]
    bufs = {n: [env.buf(f"{n}{i}") for i in range(NS)] for n in ("xt", "xo", "stats", "diag", "xcb", "xlnT")}
    sqd_buf = env.buf("sqd")
    ones, ones_buf = env.sb["onesb"], env.buf("onesb")

    for ti in range(ntiles):
        sl = ti % NS
        psl = "pA" if ti % 2 == 0 else "pB"
        pP, pP_buf = env.psum[psl], env.buf(psl)
        pTname = "pC" if ti % 2 == 0 else "pD"
        row = x_in[ti * 128:(ti + 1) * 128, :]
        p.dma("sp", xt[sl][:, :], row, writes=[bufs["xt"][sl]], sem=ti % 3)
        emit_ln_transpose(p, env, xt[sl][:, :], bufs["xt"][sl], stats[sl][:, :], bufs["stats"][sl],
                          diag[sl][:, :], bufs["diag"][sl], xcb[sl][:, :], bufs["xcb"][sl],
                          sqd[:, :], sqd_buf, pTname, xlnT[sl][:, :], bufs["xlnT"][sl])
        def mk_proj(half):
            def mm(t, half=half, sl=sl, pP=pP):
                for c2 in range(NCH):
                    t.matmul(pP[:, half * 512:half * 512 + 512],
                             xlnT[sl][:, c2 * 128:(c2 + 1) * 128],
                             weffT[:, c2 * C + half * 512: c2 * C + half * 512 + 512],
                             start=(c2 == 0), stop=False)
                return t.matmul(pP[:, half * 512:half * 512 + 512], ones[0:1, :],
                                b_row[0:1, half * 512:half * 512 + 512], start=False, stop=True)
            return mm
        p.op("pe", mk_proj(0), reads=[bufs["xlnT"][sl], weffT_buf, ones_buf, b_row_buf], writes=[pP_buf])
        p.op("pe", mk_proj(1), reads=[bufs["xlnT"][sl], weffT_buf, ones_buf, b_row_buf], writes=[pP_buf])
        p.op("dve", lambda v, sl=sl, pP=pP: v.tensor_add(xo[sl][:, :], pP[:, 0:C], xt[sl][:, :]),
             reads=[pP_buf, bufs["xt"][sl]], writes=[bufs["xo"][sl]])
        p.dma("sp", x_out[ti * 128:(ti + 1) * 128, :], xo[sl][:, :], reads=[bufs["xo"][sl]], sem=3 + ti % 3)



def emit_L1_ffn(p, env, x_in, x_out, W, ntok):
    nc = env.nc
    DF = 4096
    NFC = DF // 128
    TG = 512
    ngroups = ntok // TG
    ident, identb_buf = env.sb["identb"], env.buf("identb")
    ones, ones_buf = env.sb["onesb"], env.buf("onesb")

    w1 = env.sbuf("w1r", [128, NCH * DF], BF16)
    w1_buf = env.buf("w1r")
    w2t = [env.sbuf(f"w2t{i}", [128, C], BF16) for i in range(2)]
    w2t_bufs = [env.buf(f"w2t{i}") for i in range(2)]
    for c in range(NCH):
        p.dma("pool", w1[:, c * DF:(c + 1) * DF], W["W1"][c * 128:(c + 1) * 128, :], writes=[w1_buf])
    b2row, b2row_buf = emit_row_load(p, env, W["b2"], C, "l1b2_row")

    b1v, b1v_buf = emit_fvec128(p, env, W["b1"], DF, "l1b1")
    thv, thv_buf = emit_fvec128(p, env, W["quantum_phases"], DF, "l1th")
    wrv, wrv_buf = emit_fvec128(p, env, W["quantum_weights_real"], DF, "l1wr")
    wiv, wiv_buf = emit_fvec128(p, env, W["quantum_weights_imag"], DF, "l1wi")
    PI = float(np.pi)
    fx = env.sbuf("l1fx", [128, 8 * NFC], F32)
    fx_buf = env.buf("l1fx")
    V = NFC

    def colap(i):
        return fx[:, i * V:(i + 1) * V]

    def reduce_sin(src_ap, src_buf, dst_i, shift):
        t_i, g_i = 7, 5
        p.op("dve", lambda v: v.tensor_scalar(colap(t_i), src_ap, shift, None, ALU.add),
             reads=[src_buf], writes=[fx_buf])
        for _ in range(9):
            p.op("dve", lambda v: v.tensor_scalar(colap(dst_i), colap(t_i), PI, None, ALU.is_ge),
                 reads=[fx_buf], writes=[fx_buf])
            p.op("dve", lambda v: v.scalar_tensor_tensor(colap(t_i), colap(dst_i), -2 * PI, colap(t_i),
                                                         ALU.mult, ALU.add),
                 reads=[fx_buf], writes=[fx_buf])
            p.op("dve", lambda v: v.tensor_scalar(colap(dst_i), colap(t_i), -PI, None, ALU.is_lt),
                 reads=[fx_buf], writes=[fx_buf])
            p.op("dve", lambda v: v.scalar_tensor_tensor(colap(t_i), colap(dst_i), 2 * PI, colap(t_i),
                                                         ALU.mult, ALU.add),
                 reads=[fx_buf], writes=[fx_buf])
        p.op("dve", lambda v: v.tensor_copy(colap(dst_i), colap(t_i)),
             reads=[fx_buf], writes=[fx_buf])

    reduce_sin(thv[:, :], thv_buf, 0, 0.0)
    p.op("act", lambda s: s.activation(colap(0), colap(0), AF.Sin), reads=[fx_buf], writes=[fx_buf])
    reduce_sin(thv[:, :], thv_buf, 1, PI / 2)
    p.op("act", lambda s: s.activation(colap(1), colap(1), AF.Sin), reads=[fx_buf], writes=[fx_buf])
    p.op("dve", lambda v: v.tensor_mul(colap(2), colap(1), wrv[:, :]), reads=[fx_buf, wrv_buf], writes=[fx_buf])
    p.op("dve", lambda v: v.tensor_mul(colap(3), colap(0), wiv[:, :]), reads=[fx_buf, wiv_buf], writes=[fx_buf])
    p.op("dve", lambda v: v.tensor_add(colap(2), colap(2), colap(3)), reads=[fx_buf], writes=[fx_buf])
    p.op("dve", lambda v: v.tensor_mul(colap(2), colap(2), wiv[:, :]), reads=[fx_buf, wiv_buf], writes=[fx_buf])
    p.op("dve", lambda v: v.scalar_tensor_tensor(colap(3), colap(2), 0.1, colap(0), ALU.mult, ALU.mult),
         reads=[fx_buf], writes=[fx_buf])
    p.op("dve", lambda v: v.scalar_tensor_tensor(colap(4), colap(2), 0.1, colap(1), ALU.mult, ALU.mult),
         reads=[fx_buf], writes=[fx_buf])
    p.op("dve", lambda v: v.tensor_scalar(colap(5), b1v[:, :], 0.1, None, ALU.mult),
         reads=[b1v_buf], writes=[fx_buf])
    p.op("dve", lambda v: v.tensor_scalar(colap(6), b1v[:, :], 0.1, PI / 2, ALU.mult, ALU.add),
         reads=[b1v_buf], writes=[fx_buf])

    NS = 4
    xt = [env.sbuf(f"l1xt{i}", [128, C], F32) for i in range(NS)]
    xo = [env.sbuf(f"l1xo{i}", [128, C], F32) for i in range(NS)]
    stats = [env.sbuf(f"l1st{i}", [128, 8], F32) for i in range(NS)]
    diag = [env.sbuf(f"l1dg{i}", [128, 128], BF16) for i in range(NS)]
    xcb = [env.sbuf(f"l1xc{i}", [128, C], BF16) for i in range(NS)]
    sqd = env.sbuf("l1sq", [128, C], BF16)
    bufs = {n: [env.buf(f"l1{n}{i}") for i in range(NS)] for n in ("xt", "xo", "st", "dg", "xc")}
    sqd_buf = env.buf("l1sq")
    xlnT = env.sbuf("l1xlnT", [128, NCH * TG], BF16)
    xlnT_buf = env.buf("l1xlnT")
    slab = env.sbuf("l1slab", [128, NFC * TG], BF16)
    slab_buf = env.buf("l1slab")
    ch = [env.sbuf(f"l1ch{i}", [128, TG], F32) for i in range(2)]
    sh = [env.sbuf(f"l1sh{i}", [128, TG], F32) for i in range(2)]
    t1 = [env.sbuf(f"l1t1{i}", [128, TG], F32) for i in range(2)]
    chb = [env.buf(f"l1ch{i}") for i in range(2)]
    shb = [env.buf(f"l1sh{i}") for i in range(2)]
    t1b = [env.buf(f"l1t1{i}") for i in range(2)]

    for g in range(ngroups):
        for tt in range(TG // 128):
            ti = g * (TG // 128) + tt
            sl = ti % NS
            row = x_in[ti * 128:(ti + 1) * 128, :]
            p.dma("sp", xt[sl][:, :], row, writes=[bufs["xt"][sl]], sem=tt % 2)
            dsts = [xlnT[:, c * TG + tt * 128: c * TG + (tt + 1) * 128] for c in range(NCH)]
            emit_ln_transpose(p, env, xt[sl][:, :], bufs["xt"][sl], stats[sl][:, :], bufs["st"][sl],
                              diag[sl][:, :], bufs["dg"][sl], xcb[sl][:, :], bufs["xc"][sl],
                              sqd[:, :], sqd_buf, "pD", dsts, xlnT_buf)
        for fc in range(NFC):
            half = fc % 2
            pH, pH_buf = env.psum["pA"], env.buf("pA")
            hap = pH[:, half * 512:(half + 1) * 512]

            def mmh(t, fc=fc, hap=hap):
                last = None
                for c in range(NCH):
                    last = t.matmul(hap, w1[:, c * DF + fc * 128:c * DF + (fc + 1) * 128],
                                    xlnT[:, c * TG:(c + 1) * TG], start=(c == 0), stop=(c == NCH - 1))
                return last
            p.op("pe", mmh, reads=[w1_buf, xlnT_buf], writes=[pH_buf])
            fcs = slice(fc, fc + 1)
            p.op("act", lambda s, hap=hap, fc=fc: s.activation(ch[fc % 2][:, :], hap, AF.Sin,
                 bias=fx[:, 6 * V + fc:6 * V + fc + 1], scale=0.1),
                 reads=[pH_buf, fx_buf], writes=[chb[fc % 2]])
            p.op("act", lambda s, hap=hap, fc=fc: s.activation(sh[fc % 2][:, :], hap, AF.Sin,
                 bias=fx[:, 5 * V + fc:5 * V + fc + 1], scale=0.1),
                 reads=[pH_buf, fx_buf], writes=[shb[fc % 2]])
            p.op("dve", lambda v, fc=fc: v.tensor_scalar(
                 t1[fc % 2][:, :], ch[fc % 2][:, :], fx[:, 3 * V + fc:3 * V + fc + 1],
                 b1v[:, fc:fc + 1], ALU.mult, ALU.add),
                 reads=[chb[fc % 2], fx_buf, b1v_buf], writes=[t1b[fc % 2]])
            p.op("dve", lambda v, fc=fc: v.scalar_tensor_tensor(
                 t1[fc % 2][:, :], sh[fc % 2][:, :], fx[:, 4 * V + fc:4 * V + fc + 1], t1[fc % 2][:, :],
                 ALU.mult, ALU.add),
                 reads=[shb[fc % 2], fx_buf], writes=[t1b[fc % 2]])
            p.op("dve", lambda v, hap=hap, fc=fc: v.tensor_tensor(
                 t1[fc % 2][:, :], hap, t1[fc % 2][:, :], ALU.add),
                 reads=[pH_buf, t1b[fc % 2]], writes=[t1b[fc % 2]])
            p.op("dve", lambda v, fc=fc: v.tensor_scalar(
                 slab[:, fc * TG:(fc + 1) * TG], t1[fc % 2][:, :], 0.0, None, ALU.max),
                 reads=[t1b[fc % 2]], writes=[slab_buf])
        pos = [env.psum[n] for n in ("pA", "pB", "pC", "pD")]
        pos_bufs = [env.buf(n) for n in ("pA", "pB", "pC", "pD")]
        for fc in range(NFC):
            wsl = fc % 2
            p.dma("pool", w2t[wsl][:, :], W["W2"][fc * 128:(fc + 1) * 128, :], writes=[w2t_bufs[wsl]])

            def mmo(t, fc=fc, wsl=wsl):
                last = None
                for tt in range(TG // 128):
                    for half in range(2):
                        last = t.matmul(pos[tt][:, half * 512:(half + 1) * 512],
                                        slab[:, fc * TG + tt * 128:fc * TG + (tt + 1) * 128],
                                        w2t[wsl][:, half * 512:(half + 1) * 512],
                                        start=(fc == 0), stop=False)
                return last
            p.op("pe", mmo, reads=[slab_buf, w2t_bufs[wsl]], writes=pos_bufs)

        def mmbias(t):
            last = None
            for tt in range(TG // 128):
                for half in range(2):
                    last = t.matmul(pos[tt][:, half * 512:(half + 1) * 512], ones[0:1, :],
                                    b2row[0:1, half * 512:(half + 1) * 512], start=False, stop=True)
            return last
        p.op("pe", mmbias, reads=[ones_buf, b2row_buf], writes=pos_bufs)
        for tt in range(TG // 128):
            ti = g * (TG // 128) + tt
            sl = ti % NS
            pO, pO_buf = pos[tt], pos_bufs[tt]
            p.op("dve", lambda v, sl=sl, pO=pO: v.tensor_add(xo[sl][:, :], pO[:, 0:C], xt[sl][:, :]),
                 reads=[pO_buf, bufs["xt"][sl]], writes=[bufs["xo"][sl]])
            p.dma("sp", x_out[ti * 128:(ti + 1) * 128, :], xo[sl][:, :], reads=[bufs["xo"][sl]], sem=2 + tt % 2)



def emit_std_attention(p, env, x_in, x_out, W, ntok, cfg):
    nc = env.nc
    T = cfg["T"]
    nseq = ntok // T
    NT = T // 128
    H, HD = 16, 64
    ident, identb_buf = env.sb["identb"], env.buf("identb")
    ones, ones_buf = env.sb["onesb"], env.buf("onesb")
    pS, pS_buf = env.psum["pS"], env.buf("pS")
    pT, pT_buf = env.psum["pT"], env.buf("pT")
    pO, pO_buf = env.psum["pO"], env.buf("pO")

    cm = env.add_const("cmask128", np.triu(np.full((128, 128), -1e30, np.float32), 1))
    if "cmask" not in env.sb:
        cmask0 = env.sbuf("cmask", [128, 128], F32)
        p.dma("sp", cmask0[:, :], cm, writes=[env.buf("cmask")])
    cmask, cmask_buf = env.sb["cmask"], env.buf("cmask")

    wininfo = {}
    if cfg.get("kind") == "window":
        RAD = cfg.get("radius", 256)
        CHK = cfg.get("chunk", 256)
        mask_cache = {}
        for qt in range(T // 128):
            n = (qt * 128) // CHK
            k0 = max(0, n * CHK - CHK)
            k1 = min(T, n * CHK + 2 * CHK)
            width = k1 - k0
            qpos = qt * 128 + np.arange(128)[:, None]
            kpos = k0 + np.arange(width)[None, :]
            valid = np.abs(qpos - kpos) <= RAD
            marr = np.where(valid, 0.0, -1e30).astype(np.float32)
            key = marr.tobytes()
            if key not in mask_cache:
                mname = f"wmask{len(mask_cache)}"
                mc = env.add_const(mname, np_bf16(marr))
                msb = env.sbuf(mname, [128, width], BF16)
                p.dma("pool", msb[:, :], mc, writes=[env.buf(mname)])
                mask_cache[key] = mname
            wininfo[qt] = (k0, width, mask_cache[key])

    wsb = {}
    wbuf = {}
    for wn in ["wo"]:
        wsb[wn] = env.sbuf("a_" + wn, [128, NCH * C], BF16)
        wbuf[wn] = env.buf("a_" + wn)
        for c in range(NCH):
            p.dma("pool", wsb[wn][:, c * C:(c + 1) * C], cfg[wn][c * 128:(c + 1) * 128, :],
                  writes=[wbuf[wn]])
    wstr = [env.sbuf(f"a_ws{i}", [128, NCH * 512], BF16) for i in range(2)]
    wstr_buf = [env.buf(f"a_ws{i}") for i in range(2)]
    brow = {}
    brow_buf = {}
    fvecs = {}
    for bn in ("bv", "bo"):
        if cfg.get(bn) is not None:
            brow[bn], brow_buf[bn] = emit_row_load(p, env, cfg[bn], C, "a_" + bn)
    for bn, bscale in (("bq", cfg.get("qscale", 1.0)), ("bk", 1.0)):
        if cfg.get(bn) is not None:
            fvecs[bn] = emit_fvec128(p, env, cfg[bn], C, "a_fv" + bn, scale=bscale)

    xlnT = env.sbuf("a_xlnT", [128, NCH * T], BF16); xlnT_buf = env.buf("a_xlnT")
    qfm = env.sbuf("a_qfm", [128, NCH * T], BF16); qfm_buf = env.buf("a_qfm")
    kfm = env.sbuf("a_kfm", [128, NCH * T], BF16); kfm_buf = env.buf("a_kfm")
    vtok = env.sbuf("a_vtok", [128, NT * C], BF16); vtok_buf = env.buf("a_vtok")
    statsA = env.sbuf("a_stats", [128, 2 * NT], F32); statsA_buf = env.buf("a_stats")

    NS = 2
    xt = [env.sbuf(f"a_xt{i}", [128, C], F32) for i in range(NS)]
    stats = [env.sbuf(f"a_st{i}", [128, 8], F32) for i in range(NS)]
    diag = [env.sbuf(f"a_dg{i}", [128, 128], BF16) for i in range(NS)]
    xcb = [env.sbuf(f"a_xc{i}", [128, C], BF16) for i in range(NS)]
    sqd = env.sbuf("a_sq", [128, C], BF16)
    xo = [env.sbuf(f"a_xo{i}", [128, C], F32) for i in range(NS)]
    bufs = {n: [env.buf(f"a_{n}{i}") for i in range(NS)] for n in ("xt", "st", "dg", "xc", "xo")}
    sqd_buf = env.buf("a_sq")
    EW = 768 if cfg.get("kind") == "window" else T
    Es = [env.sbuf(f"a_E{i}", [128, EW], BF16) for i in range(2)]
    Es_buf = [env.buf(f"a_E{i}") for i in range(2)]
    ETs = env.sbuf("a_ET", [128, 1024], BF16); ETs_buf = env.buf("a_ET")
    st2 = [env.sbuf(f"a_s2_{i}", [128, 4], F32) for i in range(2)]
    st2_buf = [env.buf(f"a_s2_{i}") for i in range(2)]
    dgl = [env.sbuf(f"a_dl{i}", [128, 128], BF16) for i in range(2)]
    dgl_buf = [env.buf(f"a_dl{i}") for i in range(2)]
    ysb = env.sbuf("a_y", [128, NCH * 128], BF16); ysb_buf = env.buf("a_y")

    qscale = cfg.get("qscale", 1.0)

    for s in range(nseq):
        base = s * T
        for tt in range(NT):
            sl = tt % NS
            p.dma("sp", xt[sl][:, :], x_in[base + tt * 128: base + (tt + 1) * 128, :],
                  writes=[bufs["xt"][sl]], sem=tt % 2)
            dsts = [xlnT[:, c * T + tt * 128: c * T + (tt + 1) * 128] for c in range(NCH)]
            emit_ln_transpose(p, env, xt[sl][:, :], bufs["xt"][sl], stats[sl][:, :], bufs["st"][sl],
                              diag[sl][:, :], bufs["dg"][sl], xcb[sl][:, :], bufs["xc"][sl],
                              sqd[:, :], sqd_buf, "pT", dsts, xlnT_buf)
            if cfg.get("extra_ln_residual"):
                p.op("dve", lambda v, sl=sl, tt=tt: v.tensor_copy(
                     statsA[:, 2 * tt:2 * tt + 1], stats[sl][:, 1:2]),
                     reads=[bufs["st"][sl]], writes=[statsA_buf])
                p.op("dve", lambda v, sl=sl, tt=tt: v.tensor_copy(
                     statsA[:, 2 * tt + 1:2 * tt + 2], stats[sl][:, 5:6]),
                     reads=[bufs["st"][sl]], writes=[statsA_buf])
        for dst, dst_buf, wn, scale, tokmajor in (
                (qfm, qfm_buf, "wq", qscale, False),
                (kfm, kfm_buf, "wk", 1.0, False),
                (vtok, vtok_buf, "wv", 1.0, True)):
            bn = {"wq": "bq", "wk": "bk", "wv": "bv"}[wn]
            has_b = cfg.get(bn) is not None
            if not tokmajor:
                bfv = bfv_buf = None
                if has_b:
                    bfv, bfv_buf = fvecs[bn]
                wsrc = cfg[wn].rearrange("(c p) f -> p c f", p=128)
                for fc in range(NCH):
                    wslot = fc % 2
                    wt = wstr[wslot]
                    wt3 = wt[:, 0:NCH * 128].rearrange("p (c j) -> p c j", j=128)
                    p.dma("pool", wt3, wsrc[:, :, fc * 128:(fc + 1) * 128],
                          writes=[wstr_buf[wslot]])
                    for tc in range(T // 512):
                        half = (fc + tc) % 2
                        hap = pS[:, half * 512:(half + 1) * 512]

                        def mmq(t, fc=fc, tc=tc, hap=hap, wt=wt):
                            for c in range(NCH):
                                last = t.matmul(hap, wt[:, c * 128:(c + 1) * 128],
                                                xlnT[:, c * T + tc * 512:c * T + (tc + 1) * 512],
                                                start=(c == 0), stop=(c == NCH - 1))
                            return last
                        p.op("pe", mmq, reads=[wstr_buf[wslot], xlnT_buf], writes=[pS_buf])
                        if has_b:
                            p.op("dve", lambda v, fc=fc, tc=tc, hap=hap, dst=dst, scale=scale, bfv=bfv:
                                 v.tensor_scalar(dst[:, fc * T + tc * 512:fc * T + (tc + 1) * 512],
                                                 hap, scale, bfv[:, fc:fc + 1], ALU.mult, ALU.add),
                                 reads=[pS_buf, bfv_buf], writes=[dst_buf])
                        else:
                            p.op("dve", lambda v, fc=fc, tc=tc, hap=hap, dst=dst, scale=scale:
                                 v.tensor_scalar(dst[:, fc * T + tc * 512:fc * T + (tc + 1) * 512],
                                                 hap, scale, None, ALU.mult),
                                 reads=[pS_buf], writes=[dst_buf])
            else:
                wsrc = cfg[wn].rearrange("(c p) f -> p c f", p=128)
                for half in range(2):
                    wslot = half
                    wt = wstr[wslot]
                    wt3 = wt[:, :].rearrange("p (c j) -> p c j", j=512)
                    p.dma("pool", wt3, wsrc[:, :, half * 512:(half + 1) * 512],
                          writes=[wstr_buf[wslot]])
                    for tt in range(NT):
                        hap = pS[:, (tt % 2) * 512:(tt % 2 + 1) * 512]

                        def mmv(t, tt=tt, half=half, hap=hap, wt=wt, has_b=has_b, bn=bn):
                            for c in range(NCH):
                                last = t.matmul(hap, xlnT[:, c * T + tt * 128:c * T + (tt + 1) * 128],
                                                wt[:, c * 512:(c + 1) * 512],
                                                start=(c == 0), stop=(c == NCH - 1) and not has_b)
                            if has_b:
                                last = t.matmul(hap, ones[0:1, :],
                                                brow[bn][0:1, half * 512:(half + 1) * 512],
                                                start=False, stop=True)
                            return last
                        p.op("pe", mmv, reads=[wstr_buf[wslot], xlnT_buf, ones_buf] +
                             ([brow_buf[bn]] if has_b else []), writes=[pS_buf])
                        p.op("dve", lambda v, tt=tt, half=half, hap=hap, dst=dst:
                             v.tensor_copy(dst[:, tt * C + half * 512:tt * C + (half + 1) * 512], hap),
                             reads=[pS_buf], writes=[dst_buf])
                continue
        for qt in range(NT):
            if cfg.get("kind") == "window":
                kbase, kmax, mname = wininfo[qt]
            else:
                kbase, kmax, mname = 0, (qt + 1) * 128, None
            n_ks = kmax // 128
            par = qt % 2
            for h in range(H):
                hp = 64 * (h % 2)
                hc = h // 2
                def mms(t, qt=qt, h=h, hp=hp, hc=hc, kmax=kmax, kbase=kbase):
                    last = None
                    for k0 in range(0, kmax, 512):
                        kn = min(512, kmax - k0)
                        last = t.matmul(pS[:, k0:k0 + kn],
                                        qfm[hp:hp + 64, hc * T + qt * 128:hc * T + (qt + 1) * 128],
                                        kfm[hp:hp + 64, hc * T + kbase + k0:hc * T + kbase + k0 + kn],
                                        start=True, stop=True)
                    return last
                p.op("pe", mms, reads=[qfm_buf, kfm_buf], writes=[pS_buf])
                if mname is None:
                    p.op("dve", lambda v, qt=qt, kmax=kmax: v.tensor_tensor(
                         pS[:, kmax - 128:kmax], pS[:, kmax - 128:kmax], cmask[:, :], ALU.add),
                         reads=[cmask_buf], writes=[pS_buf])
                else:
                    p.op("dve", lambda v, kmax=kmax, mname=mname: v.tensor_tensor(
                         pS[:, 0:kmax], pS[:, 0:kmax], env.sb[mname][:, 0:kmax], ALU.add),
                         reads=[env.buf(mname)], writes=[pS_buf])
                p.op("dve", lambda v, kmax=kmax, par=par: v.tensor_reduce(
                     st2[par][:, 0:1], pS[:, 0:kmax], AX.X, ALU.max, negate=True),
                     reads=[pS_buf], writes=[st2_buf[par]])
                p.op("act", lambda s2, kmax=kmax, par=par: s2.activation(
                     Es[par][:, 0:kmax], pS[:, 0:kmax], AF.Exp, bias=st2[par][:, 0:1],
                     accum_out=st2[par][:, 1:2]),
                     reads=[pS_buf, st2_buf[par]], writes=[Es_buf[par], st2_buf[par]])
                p.op("dve", lambda v, par=par: v.reciprocal(st2[par][:, 2:3], st2[par][:, 1:2]),
                     reads=[st2_buf[par]], writes=[st2_buf[par]])
                p.op("dve", lambda v, par=par: v.tensor_scalar(
                     dgl[par][:, :], ident[:, :], st2[par][:, 2:3], None, ALU.mult),
                     reads=[identb_buf, st2_buf[par]], writes=[dgl_buf[par]])
                for ks0 in range(0, n_ks, 8):
                    kn = min(8, n_ks - ks0)

                    def mmt(t, ks0=ks0, kn=kn, par=par):
                        for j in range(kn):
                            last = t.matmul(pT[:, j * 128:(j + 1) * 128],
                                            Es[par][:, (ks0 + j) * 128:(ks0 + j + 1) * 128],
                                            dgl[par][:, :], start=True, stop=True)
                        return last
                    p.op("pe", mmt, reads=[Es_buf[par], dgl_buf[par]], writes=[pT_buf])
                    p.op("dve", lambda v, kn=kn: v.tensor_copy(ETs[:, 0:kn * 128], pT[:, 0:kn * 128]),
                         reads=[pT_buf], writes=[ETs_buf])

                    def mmpv(t, ks0=ks0, kn=kn, h=h, n_ks=n_ks, kbase=kbase):
                        for j in range(kn):
                            ks = ks0 + j
                            ksg = kbase // 128 + ks
                            last = t.matmul(pO[0:64, 0:128],
                                            vtok[:, ksg * C + h * 64:ksg * C + (h + 1) * 64],
                                            ETs[:, j * 128:(j + 1) * 128],
                                            start=(ks == 0), stop=(ks == n_ks - 1))
                        return last
                    p.op("pe", mmpv, reads=[vtok_buf, ETs_buf], writes=[pO_buf])
                p.op("dve", lambda v, hp=hp, hc=hc: v.tensor_copy(
                     ysb[hp:hp + 64, hc * 128:(hc + 1) * 128], pO[0:64, 0:128]),
                     reads=[pO_buf], writes=[ysb_buf])
            sl = qt % NS
            p.dma("sp", xt[sl][:, :], x_in[base + qt * 128: base + (qt + 1) * 128, :],
                  writes=[bufs["xt"][sl]], sem=2 + qt % 2)
            for half in range(2):
                def mmo(t, half=half):
                    for cc in range(NCH):
                        last = t.matmul(pT[:, half * 512:(half + 1) * 512],
                                        ysb[:, cc * 128:(cc + 1) * 128],
                                        wsb["wo"][:, cc * C + half * 512:cc * C + (half + 1) * 512],
                                        start=(cc == 0), stop=(cc == NCH - 1) and "bo" not in brow)
                    if "bo" in brow:
                        last = t.matmul(pT[:, half * 512:(half + 1) * 512], ones[0:1, :],
                                        brow["bo"][0:1, half * 512:(half + 1) * 512],
                                        start=False, stop=True)
                    return last
                p.op("pe", mmo, reads=[ysb_buf, wbuf["wo"], ones_buf] +
                     ([brow_buf["bo"]] if "bo" in brow else []), writes=[pT_buf])
            p.op("dve", lambda v, sl=sl: v.tensor_add(xo[sl][:, :], pT[:, 0:C], xt[sl][:, :]),
                 reads=[pT_buf, bufs["xt"][sl]], writes=[bufs["xo"][sl]])
            if cfg.get("extra_ln_residual"):
                p.op("dve", lambda v, sl=sl, qt=qt: v.tensor_scalar(
                     xcb[sl][:, :], xt[sl][:, :], statsA[:, 2 * qt:2 * qt + 1],
                     statsA[:, 2 * qt + 1:2 * qt + 2], ALU.add, ALU.mult),
                     reads=[bufs["xt"][sl], statsA_buf], writes=[bufs["xc"][sl]])
                p.op("dve", lambda v, sl=sl: v.tensor_tensor(
                     xo[sl][:, :], xo[sl][:, :], xcb[sl][:, :], ALU.add),
                     reads=[bufs["xc"][sl]], writes=[bufs["xo"][sl]])
            p.dma("sp", x_out[base + qt * 128: base + (qt + 1) * 128, :], xo[sl][:, :],
                  reads=[bufs["xo"][sl]], sem=4 + qt % 2)


def emit_L1_attn(p, env, x_in, x_out, W, ntok, T=2048):
    nc = env.nc
    nseq = ntok // T
    NT = T // 128
    H, HD = 16, 64
    EPS = 1e-6
    ident, identb_buf = env.sb["identb"], env.buf("identb")
    ones, ones_buf = env.sb["onesb"], env.buf("onesb")
    pS, pS_buf = env.psum["pS"], env.buf("pS")
    pT, pT_buf = env.psum["pT"], env.buf("pT")
    pO, pO_buf = env.psum["pO"], env.buf("pO")

    cm = env.add_const("cmask128", np.triu(np.full((128, 128), -1e30, np.float32), 1))
    if "cmask" not in env.sb:
        cmask = env.sbuf("cmask", [128, 128], F32)
        p.dma("sp", cmask[:, :], cm, writes=[env.buf("cmask")])
    cmask, cmask_buf = env.sb["cmask"], env.buf("cmask")
    onescol, onescol_buf = env.sb["onescol"], env.buf("onescol")

    wqkv = cfg_w = W["Wqkv"]
    wout = W["Wout"]
    wo_sb = env.sbuf("q_wo", [128, NCH * C], BF16)
    wo_buf = env.buf("q_wo")
    for c in range(NCH):
        p.dma("pool", wo_sb[:, c * C:(c + 1) * C], wout[c * 128:(c + 1) * 128, :], writes=[wo_buf])
    wstr = [env.sbuf(f"q_ws{i}", [128, NCH * 256], BF16) for i in range(2)]
    wstr_buf = [env.buf(f"q_ws{i}") for i in range(2)]

    xlnT = env.sbuf("q_xlnT", [128, NCH * T], BF16); xlnT_buf = env.buf("q_xlnT")
    qfm = env.sbuf("q_qfm", [128, NCH * T], BF16); qfm_buf = env.buf("q_qfm")
    kfm = env.sbuf("q_kfm", [128, NCH * T], BF16); kfm_buf = env.buf("q_kfm")
    vtok = env.sbuf("q_vtok", [128, NT * C], BF16); vtok_buf = env.buf("q_vtok")

    NS = 2
    xt = [env.sbuf(f"q_xt{i}", [128, C], F32) for i in range(NS)]
    stats = [env.sbuf(f"q_st{i}", [128, 8], F32) for i in range(NS)]
    diag = [env.sbuf(f"q_dg{i}", [128, 128], BF16) for i in range(NS)]
    xcb = [env.sbuf(f"q_xc{i}", [128, C], BF16) for i in range(NS)]
    sqd = env.sbuf("q_sq", [128, C], BF16)
    xo = [env.sbuf(f"q_xo{i}", [128, C], F32) for i in range(NS)]
    bufs = {n: [env.buf(f"q_{n}{i}") for i in range(NS)] for n in ("xt", "st", "dg", "xc", "xo")}
    sqd_buf = env.buf("q_sq")

    Es = [env.sbuf(f"q_E{i}", [128, T], BF16) for i in range(2)]
    Es_buf = [env.buf(f"q_E{i}") for i in range(2)]
    ETs = env.sbuf("q_ET", [128, 1024], BF16); ETs_buf = env.buf("q_ET")
    st2 = [env.sbuf(f"q_s2_{i}", [128, 4], F32) for i in range(2)]
    st2_buf = [env.buf(f"q_s2_{i}") for i in range(2)]
    dgl = [env.sbuf(f"q_dl{i}", [128, 128], BF16) for i in range(2)]
    dgl_buf = [env.buf(f"q_dl{i}") for i in range(2)]
    ssb = env.sbuf("q_ssb", [128, T], F32); ssb_buf = env.buf("q_ssb")
    den = env.sbuf("q_den", [128, 512], F32); den_buf = env.buf("q_den")
    rin = env.sbuf("q_rin", [128, 512], F32); rin_buf = env.buf("q_rin")
    d2, d2_buf = den, den_buf
    qaug = env.sbuf("q_qaug", [128, T], BF16); qaug_buf = env.buf("q_qaug")
    kaug = env.sbuf("q_kaug", [128, T], BF16); kaug_buf = env.buf("q_kaug")
    sqs, sqs_buf = Es[1], Es_buf[1]
    nrow = env.sb["q_ssb"].bitcast(BF16); nrow_buf = ssb_buf
    qnc = env.sbuf("q_qnc", [128, NT], F32); qnc_buf = env.buf("q_qnc")
    y_all = env.sb["q_xlnT"]

    for s in range(nseq):
        base = s * T
        for tt in range(NT):
            sl = tt % NS
            p.dma("sp", xt[sl][:, :], x_in[base + tt * 128: base + (tt + 1) * 128, :],
                  writes=[bufs["xt"][sl]], sem=tt % 2)
            dsts = [xlnT[:, c * T + tt * 128: c * T + (tt + 1) * 128] for c in range(NCH)]
            emit_ln_transpose(p, env, xt[sl][:, :], bufs["xt"][sl], stats[sl][:, :], bufs["st"][sl],
                              diag[sl][:, :], bufs["dg"][sl], xcb[sl][:, :], bufs["xc"][sl],
                              sqd[:, :], sqd_buf, "pT", dsts, xlnT_buf)
        for dst, dst_buf, w_ap, tokmajor in ((qfm, qfm_buf, wqkv[:, 0:C], False),
                                             (kfm, kfm_buf, wqkv[:, C:2 * C], False),
                                             (vtok, vtok_buf, wqkv[:, 2 * C:3 * C], True)):
            wsrc = w_ap.rearrange("(c p) f -> p c f", p=128)
            if not tokmajor:
                for fc in range(NCH):
                    wslot = fc % 2
                    wt = wstr[wslot]
                    wt3 = wt[:, 0:NCH * 128].rearrange("p (c j) -> p c j", j=128)
                    p.dma("pool", wt3, wsrc[:, :, fc * 128:(fc + 1) * 128], writes=[wstr_buf[wslot]])
                    for tc in range(T // 512):
                        half = (fc + tc) % 2
                        hap = pS[:, half * 512:(half + 1) * 512]

                        def mmq(t, fc=fc, tc=tc, hap=hap, wt=wt):
                            for c in range(NCH):
                                last = t.matmul(hap, wt[:, c * 128:(c + 1) * 128],
                                                xlnT[:, c * T + tc * 512:c * T + (tc + 1) * 512],
                                                start=(c == 0), stop=(c == NCH - 1))
                            return last
                        p.op("pe", mmq, reads=[wstr_buf[wslot], xlnT_buf], writes=[pS_buf])
                        p.op("dve", lambda v, fc=fc, tc=tc, hap=hap, dst=dst:
                             v.tensor_copy(dst[:, fc * T + tc * 512:fc * T + (tc + 1) * 512], hap),
                             reads=[pS_buf], writes=[dst_buf])
            else:
                for quar in range(4):
                    wslot = quar % 2
                    wt = wstr[wslot]
                    wt3 = wt[:, :].rearrange("p (c j) -> p c j", j=256)
                    p.dma("pool", wt3, wsrc[:, :, quar * 256:(quar + 1) * 256], writes=[wstr_buf[wslot]])
                    for tt in range(NT):
                        hap = pS[:, (tt % 2) * 512:(tt % 2) * 512 + 256]

                        def mmv(t, tt=tt, quar=quar, hap=hap, wt=wt):
                            for c in range(NCH):
                                last = t.matmul(hap, xlnT[:, c * T + tt * 128:c * T + (tt + 1) * 128],
                                                wt[:, c * 256:(c + 1) * 256],
                                                start=(c == 0), stop=(c == NCH - 1))
                            return last
                        p.op("pe", mmv, reads=[wstr_buf[wslot], xlnT_buf], writes=[pS_buf])
                        p.op("dve", lambda v, tt=tt, quar=quar, hap=hap, dst=dst:
                             v.tensor_copy(dst[:, tt * C + quar * 256:tt * C + (quar + 1) * 256], hap),
                             reads=[pS_buf], writes=[dst_buf])
        y_all_buf = env.alias_buf(xlnT_buf, "q_yall")
        p.op("dve", lambda v: v.memset(qaug[64:65, :], 1.0), writes=[qaug_buf])
        for h in range(H):
            hp = 64 * (h % 2)
            hc = h // 2
            qh = qfm[hp:hp + 64, hc * T:(hc + 1) * T]
            kh = kfm[hp:hp + 64, hc * T:(hc + 1) * T]
            p.op("dve", lambda v, qh=qh: v.tensor_scalar(qaug[0:64, :], qh, -2.0, None, ALU.mult),
                 reads=[qfm_buf], writes=[qaug_buf])
            p.op("act", lambda s2, kh=kh: s2.activation(sqs[0:64, :], kh, AF.Square),
                 reads=[kfm_buf], writes=[sqs_buf])

            def mmkn(t):
                for tc in range(T // 512):
                    last = t.matmul(pS[0:1, tc * 512:(tc + 1) * 512], onescol[0:64, 0:1],
                                    sqs[0:64, tc * 512:(tc + 1) * 512], start=True, stop=True)
                return last
            p.op("pe", mmkn, reads=[sqs_buf, onescol_buf], writes=[pS_buf])
            p.op("dve", lambda v: v.tensor_copy(nrow[0:1, 0:T], pS[0:1, 0:T]),
                 reads=[pS_buf], writes=[nrow_buf])
            p.op("dve", lambda v, kh=kh: v.tensor_copy(kaug[0:64, :], kh),
                 reads=[kfm_buf], writes=[kaug_buf])
            p.op("dve", lambda v: v.tensor_copy(kaug[64:65, :], nrow[0:1, 0:T]),
                 reads=[nrow_buf], writes=[kaug_buf])
            p.op("act", lambda s2, qh=qh: s2.activation(sqs[0:64, :], qh, AF.Square),
                 reads=[qfm_buf], writes=[sqs_buf])

            def mmqn(t):
                for tc in range(T // 512):
                    last = t.matmul(pS[0:1, tc * 512:(tc + 1) * 512], onescol[0:64, 0:1],
                                    sqs[0:64, tc * 512:(tc + 1) * 512], start=True, stop=True)
                return last
            p.op("pe", mmqn, reads=[sqs_buf, onescol_buf], writes=[pS_buf])
            p.op("dve", lambda v: v.tensor_copy(nrow[0:1, 0:T], pS[0:1, 0:T]),
                 reads=[pS_buf], writes=[nrow_buf])

            def mmqc(t):
                for qt in range(NT):
                    last = t.matmul(pO[:, qt:qt + 1], nrow[0:1, qt * 128:(qt + 1) * 128],
                                    ones[0:1, 0:1], start=True, stop=True)
                return last
            p.op("pe", mmqc, reads=[nrow_buf, ones_buf], writes=[pO_buf])
            p.op("dve", lambda v: v.tensor_scalar(qnc[:, 0:NT], pO[:, 0:NT], 1.0, EPS, ALU.mult, ALU.add),
                 reads=[pO_buf], writes=[qnc_buf])

            for qt in range(NT):
                kmax = (qt + 1) * 128
                n_ks = qt + 1
                par = qt % 2
                def mmdot(t, qt=qt, kmax=kmax, qh=qh, kh=kh):
                    for k0 in range(0, kmax, 512):
                        kn_ = min(512, kmax - k0)
                        last = t.matmul(pS[:, k0:k0 + kn_],
                                        qh[:, qt * 128:(qt + 1) * 128], kh[:, k0:k0 + kn_],
                                        start=True, stop=True)
                    return last
                p.op("pe", mmdot, reads=[qfm_buf, kfm_buf], writes=[pS_buf])
                for k0 in range(0, kmax, 512):
                    kn_ = min(512, kmax - k0)

                    def mmden(t, qt=qt, k0=k0, kn_=kn_):
                        return t.matmul(pT[:, 0:kn_], qaug[0:65, qt * 128:(qt + 1) * 128],
                                        kaug[0:65, k0:k0 + kn_], start=True, stop=True)
                    p.op("pe", mmden, reads=[qaug_buf, kaug_buf], writes=[pT_buf])
                    p.op("dve", lambda v, qt=qt, kn_=kn_: v.tensor_scalar(
                         den[:, 0:kn_], pT[:, 0:kn_], qnc[:, qt:qt + 1], EPS, ALU.add, ALU.max),
                         reads=[pT_buf, qnc_buf], writes=[den_buf])
                    p.op("act", lambda s2, kn_=kn_: s2.activation(rin[:, 0:kn_], den[:, 0:kn_], AF.Ln),
                         reads=[den_buf], writes=[rin_buf])
                    p.op("act", lambda s2, kn_=kn_: s2.activation(rin[:, 0:kn_], rin[:, 0:kn_],
                         AF.Exp, scale=-1.0),
                         reads=[rin_buf], writes=[rin_buf])
                    p.op("act", lambda s2, k0=k0, kn_=kn_: s2.activation(d2[:, 0:kn_], pS[:, k0:k0 + kn_],
                         AF.Square),
                         reads=[pS_buf], writes=[d2_buf])
                    p.op("dve", lambda v, k0=k0, kn_=kn_: v.tensor_mul(
                         ssb[:, k0:k0 + kn_], d2[:, 0:kn_], rin[:, 0:kn_]),
                         reads=[d2_buf, rin_buf], writes=[ssb_buf])
                p.op("dve", lambda v, kmax=kmax: v.tensor_tensor(
                     ssb[:, kmax - 128:kmax], ssb[:, kmax - 128:kmax], cmask[:, :], ALU.add),
                     reads=[cmask_buf], writes=[ssb_buf])
                p.op("dve", lambda v, kmax=kmax, par=par: v.tensor_reduce(
                     st2[par][:, 0:1], ssb[:, 0:kmax], AX.X, ALU.max, negate=True),
                     reads=[ssb_buf], writes=[st2_buf[par]])
                p.op("act", lambda s2, kmax=kmax, par=par: s2.activation(
                     Es[par][:, 0:kmax], ssb[:, 0:kmax], AF.Exp, bias=st2[par][:, 0:1],
                     accum_out=st2[par][:, 1:2]),
                     reads=[ssb_buf, st2_buf[par]], writes=[Es_buf[par], st2_buf[par]])
                p.op("dve", lambda v, par=par: v.reciprocal(st2[par][:, 2:3], st2[par][:, 1:2]),
                     reads=[st2_buf[par]], writes=[st2_buf[par]])
                p.op("dve", lambda v, par=par: v.tensor_scalar(
                     dgl[par][:, :], ident[:, :], st2[par][:, 2:3], None, ALU.mult),
                     reads=[identb_buf, st2_buf[par]], writes=[dgl_buf[par]])
                for ks0 in range(0, n_ks, 8):
                    kn_ = min(8, n_ks - ks0)

                    def mmt(t, ks0=ks0, kn_=kn_, par=par):
                        for j in range(kn_):
                            last = t.matmul(pT[:, j * 128:(j + 1) * 128],
                                            Es[par][:, (ks0 + j) * 128:(ks0 + j + 1) * 128],
                                            dgl[par][:, :], start=True, stop=True)
                        return last
                    p.op("pe", mmt, reads=[Es_buf[par], dgl_buf[par]], writes=[pT_buf])
                    p.op("dve", lambda v, kn_=kn_: v.tensor_copy(ETs[:, 0:kn_ * 128], pT[:, 0:kn_ * 128]),
                         reads=[pT_buf], writes=[ETs_buf])

                    def mmpv(t, ks0=ks0, kn_=kn_, h=h, n_ks=n_ks):
                        for j in range(kn_):
                            ks = ks0 + j
                            last = t.matmul(pO[0:64, 0:128],
                                            vtok[:, ks * C + h * 64:ks * C + (h + 1) * 64],
                                            ETs[:, j * 128:(j + 1) * 128],
                                            start=(ks == 0), stop=(ks == n_ks - 1))
                        return last
                    p.op("pe", mmpv, reads=[vtok_buf, ETs_buf], writes=[pO_buf])
                p.op("dve", lambda v, hp=hp, hc=hc, qt=qt: v.tensor_copy(
                     y_all[hp:hp + 64, qt * C + hc * 128:qt * C + (hc + 1) * 128], pO[0:64, 0:128]),
                     reads=[pO_buf], writes=[y_all_buf])
        for qt in range(NT):
            sl = qt % NS
            p.dma("sp", xt[sl][:, :], x_in[base + qt * 128: base + (qt + 1) * 128, :],
                  writes=[bufs["xt"][sl]], sem=2 + qt % 2)
            for half in range(2):
                def mmo(t, half=half, qt=qt):
                    for cc in range(NCH):
                        last = t.matmul(pT[:, half * 512:(half + 1) * 512],
                                        y_all[:, qt * C + cc * 128:qt * C + (cc + 1) * 128],
                                        wo_sb[:, cc * C + half * 512:cc * C + (half + 1) * 512],
                                        start=(cc == 0), stop=(cc == NCH - 1))
                    return last
                p.op("pe", mmo, reads=[y_all_buf, wo_buf], writes=[pT_buf])
            p.op("dve", lambda v, sl=sl: v.tensor_add(xo[sl][:, :], pT[:, 0:C], xt[sl][:, :]),
                 reads=[pT_buf, bufs["xt"][sl]], writes=[bufs["xo"][sl]])
            p.dma("sp", x_out[base + qt * 128: base + (qt + 1) * 128, :], xo[sl][:, :],
                  reads=[bufs["xo"][sl]], sem=4 + qt % 2)
        xlnT_buf.w = y_all_buf.w
        xlnT_buf.rs = list(y_all_buf.rs)


def emit_L2_attn(p, env, x_in, x_out, W, ntok, T=2048):
    cfg = {"T": T, "wq": W["Wq"], "wk": W["Wk"], "wv": W["Wv"], "wo": W["Wo"],
           "qscale": 0.125, "extra_ln_residual": True, "kind": "causal"}
    emit_std_attention(p, env, x_in, x_out, W, ntok, cfg)


def emit_L3_attn(p, env, x_in, x_out, W, ntok, T=2048):
    cfg = {"T": T, "wq": W["wq"], "wk": W["wk"], "wv": W["wv"], "wo": W["wo"],
           "bq": W["bq"], "bk": W["bk"], "bv": W["bv"], "bo": W["bo"],
           "qscale": 0.125, "kind": "window", "radius": 256, "chunk": 256}
    emit_std_attention(p, env, x_in, x_out, W, ntok, cfg)


def emit_L2_ffn(p, env, x_in, x_out, W, ntok):
    nc = env.nc
    NE = 34
    MG = min(2048, ntok)
    TGP = min(512, MG)
    nmg = ntok // MG
    ident, identb_buf = env.sb["identb"], env.buf("identb")
    ones, ones_buf = env.sb["onesb"], env.buf("onesb")
    pA, pA_buf = env.psum["pA"], env.buf("pA")
    pB, pB_buf = env.psum["pB"], env.buf("pB")
    pC, pC_buf = env.psum["pC"], env.buf("pC")
    pD, pD_buf = env.psum["pD"], env.buf("pD")

    if "identf" not in env.sb:
        idf = env.add_const("identf", np.eye(128, dtype=np.float32))
        env.sbuf("identf", [128, 128], F32)
        p.dma("sp", env.sb["identf"][:], idf, writes=[env.buf("identf")])
    identf, identf_buf = env.sb["identf"], env.buf("identf")
    if "m_onesf" not in env.sb:
        onesf0 = env.sbuf("m_onesf", [1, 128], F32)
        p.op("dve", lambda v: v.memset(onesf0[:, :], 1.0), writes=[env.buf("m_onesf")])
    onesf, onesf_buf = env.sb["m_onesf"], env.buf("m_onesf")

    guT_dram = nc.dram_tensor("m_guT", [NE, 128, NCH * C], BF16).ap()
    dnT_dram = nc.dram_tensor("m_dnT", [NE, 128, 4 * C], BF16).ap()
    guT_dram_buf = Buf("m_guT_d")
    dnT_dram_buf = Buf("m_dnT_d")

    def gu_rows(e, rc):
        if e < 32:
            return W["experts_gate_up"][e, rc * 128:(rc + 1) * 128, :]
        off = 0 if e == 32 else 512
        if rc < 4:
            return W["shared_gate_w"][off + rc * 128:off + (rc + 1) * 128, :]
        return W["shared_up_w"][off + (rc - 4) * 128:off + (rc - 3) * 128, :]

    def dn_rows(e, rc):
        if e < 32:
            return W["experts_down"][e, rc * 128:(rc + 1) * 128, :]
        off = 0 if e == 32 else 512
        return W["shared_down_w"][rc * 128:(rc + 1) * 128, off:off + 512]

    wld = [env.sbuf(f"m_wld{i}", [128, C], BF16) for i in range(2)]
    wld_buf = [env.buf(f"m_wld{i}") for i in range(2)]
    tsl = [env.sbuf(f"m_tsl{i}", [128, 1024], BF16) for i in range(2)]
    tsl_buf = [env.buf(f"m_tsl{i}") for i in range(2)]
    for e in range(NE):
        for kind, rows_fn, ncc, dram in (("gu", gu_rows, 8, guT_dram), ("dn", dn_rows, 8, dnT_dram)):
            nrc = 8 if kind == "gu" else 8
            ncols = 1024 if kind == "gu" else 512
            nccd = ncols // 128
            for rc in range(nrc):
                sl = rc % 2
                p.dma("pool", wld[sl][:, 0:ncols], rows_fn(e, rc), writes=[wld_buf[sl]])

                def mmtr(t, sl=sl, nccd=nccd):
                    for cc in range(nccd):
                        last = t.matmul(pA[:, cc * 128:(cc + 1) * 128],
                                        wld[sl][:, cc * 128:(cc + 1) * 128], ident[:, :],
                                        start=True, stop=True)
                    return last
                p.op("pe", mmtr, reads=[wld_buf[sl], identb_buf], writes=[pA_buf])
                p.op("dve", lambda v, sl=sl, nccd=nccd: v.tensor_copy(
                     tsl[sl][:, 0:nccd * 128], pA[:, 0:nccd * 128]),
                     reads=[pA_buf], writes=[tsl_buf[sl]])
                rows_total = C if kind == "gu" else C
                dbuf = guT_dram_buf if kind == "gu" else dnT_dram_buf
                for cc in range(nccd):
                    p.dma("sp", dram[e, :, cc * rows_total + rc * 128: cc * rows_total + (rc + 1) * 128],
                          tsl[sl][:, cc * 128:(cc + 1) * 128], reads=[tsl_buf[sl]], writes=[dbuf])

    gw_sb = env.sbuf("m_gw", [32, C], F32)
    gw_buf = env.buf("m_gw")
    p.dma("sp", gw_sb[:, :], W["gate_weight"], writes=[gw_buf])
    gwT = env.sbuf("m_gwT", [128, NCH * 32], F32)
    gwT_buf = env.buf("m_gwT")
    for c in range(NCH):
        p.op("pe", lambda t, c=c: t.matmul(pD[0:128, 0:32], gw_sb[:, c * 128:(c + 1) * 128],
                                           identf[0:32, 0:32], start=True, stop=True),
             reads=[gw_buf, identf_buf], writes=[pD_buf])
        p.op("dve", lambda v, c=c: v.tensor_copy(gwT[:, c * 32:(c + 1) * 32], pD[0:128, 0:32]),
             reads=[pD_buf], writes=[gwT_buf])
    eb_sb = env.sbuf("m_eb", [1, 32], F32)
    eb_buf = env.buf("m_eb")
    p.dma("sp", eb_sb[0:1, :], W["e_score_correction_bias"][None, :], writes=[eb_buf])
    ebb = env.sbuf("m_ebb", [128, 32], F32)
    ebb_buf = env.buf("m_ebb")
    p.op("pe", lambda t: t.matmul(pD[:, 0:32], onesf[0:1, :], eb_sb[0:1, :], start=True, stop=True),
         reads=[onesf_buf, eb_buf], writes=[pD_buf])
    p.op("dve", lambda v: v.tensor_copy(ebb[:, :], pD[:, 0:32]), reads=[pD_buf], writes=[ebb_buf])

    NTM = MG // 128
    xlnT = env.sbuf("m_xlnT", [128, NCH * MG], BF16); xlnT_buf = env.buf("m_xlnT")
    acc = env.sbuf("m_acc", [128, NTM * C], F32); acc_buf = env.buf("m_acc")
    guT = [env.sbuf(f"m_guT{i}", [128, NCH * C], BF16) for i in range(2)]
    guT_buf = [env.buf(f"m_guT{i}") for i in range(2)]
    dnT = [env.sbuf(f"m_dnT{i}", [128, 4 * C], BF16) for i in range(2)]
    dnT_buf = [env.buf(f"m_dnT{i}") for i in range(2)]
    slab = env.sbuf("m_slab", [128, 4 * TGP], BF16); slab_buf = env.buf("m_slab")
    silu_s = env.sbuf("m_silu", [128, TGP], F32); silu_buf = env.buf("m_silu")
    assert 2 * TGP <= 1024
    comb = env.sbuf("m_comb", [128, NTM * NE], F32); comb_buf = env.buf("m_comb")
    xcf = env.sbuf("m_xcf", [128, C], F32); xcf_buf = env.buf("m_xcf")
    xlnT32 = env.sbuf("m_xlnT32", [128, C], F32); xlnT32_buf = env.buf("m_xlnT32")
    diagf = env.sbuf("m_diagf", [128, 128], F32); diagf_buf = env.buf("m_diagf")
    lg = env.sbuf("m_lg", [128, 32], F32); lg_buf = env.buf("m_lg")
    rt = env.sbuf("m_rt", [128, 160], F32); rt_buf = env.buf("m_rt")

    NS = 2
    xt = [env.sbuf(f"m_xt{i}", [128, C], F32) for i in range(NS)]
    stats = [env.sbuf(f"m_st{i}", [128, 8], F32) for i in range(NS)]
    diag = [env.sbuf(f"m_dg{i}", [128, 128], BF16) for i in range(NS)]
    xcb = [env.sbuf(f"m_xc{i}", [128, C], BF16) for i in range(NS)]
    sqd = env.sbuf("m_sq", [128, C], BF16)
    xo = [env.sbuf(f"m_xo{i}", [128, C], F32) for i in range(NS)]
    bufs = {n: [env.buf(f"m_{n}{i}") for i in range(NS)] for n in ("xt", "st", "dg", "xc", "xo")}
    sqd_buf = env.buf("m_sq")

    for mg in range(nmg):
        base = mg * MG
        p.op("dve", lambda v: v.memset(acc[:, :], 0.0), writes=[acc_buf])
        for tt in range(NTM):
            sl = tt % NS
            p.dma("sp", xt[sl][:, :], x_in[base + tt * 128: base + (tt + 1) * 128, :],
                  writes=[bufs["xt"][sl]], sem=tt % 2)
            dsts = [xlnT[:, c * MG + tt * 128: c * MG + (tt + 1) * 128] for c in range(NCH)]
            emit_ln_transpose(p, env, xt[sl][:, :], bufs["xt"][sl], stats[sl][:, :], bufs["st"][sl],
                              diag[sl][:, :], bufs["dg"][sl], xcb[sl][:, :], bufs["xc"][sl],
                              sqd[:, :], sqd_buf, "pD", dsts, xlnT_buf)
            p.op("dve", lambda v, sl=sl: v.tensor_scalar(xcf[:, :], xt[sl][:, :],
                 stats[sl][:, 1:2], None, ALU.add),
                 reads=[bufs["xt"][sl], bufs["st"][sl]], writes=[xcf_buf])
            p.op("dve", lambda v, sl=sl: v.tensor_scalar(diagf[:, :], identf[:, :],
                 stats[sl][:, 5:6], None, ALU.mult),
                 reads=[identf_buf, bufs["st"][sl]], writes=[diagf_buf])

            def mmt32(t):
                for c in range(NCH):
                    last = t.matmul(pD[:, c * 128:(c + 1) * 128], xcf[:, c * 128:(c + 1) * 128],
                                    diagf[:, :], start=True, stop=True)
                return last
            p.op("pe", mmt32, reads=[xcf_buf, diagf_buf], writes=[pD_buf])
            p.op("dve", lambda v: v.tensor_copy(xlnT32[:, :], pD[:, 0:C]),
                 reads=[pD_buf], writes=[xlnT32_buf])

            def mmgate(t):
                for c in range(NCH):
                    last = t.matmul(pA[:, 0:32], xlnT32[:, c * 128:(c + 1) * 128],
                                    gwT[:, c * 32:(c + 1) * 32], start=(c == 0), stop=(c == NCH - 1))
                return last
            p.op("pe", mmgate, reads=[xlnT32_buf, gwT_buf], writes=[pA_buf])
            p.op("act", lambda s2: s2.activation(lg[:, :], pA[:, 0:32], AF.Sigmoid),
                 reads=[pA_buf], writes=[lg_buf])
            p.op("dve", lambda v: v.tensor_add(rt[:, 0:32], lg[:, :], ebb[:, :]),
                 reads=[lg_buf, ebb_buf], writes=[rt_buf])
            p.op("dve", lambda v: v.tensor_reduce(rt[:, 32:40],
                 rt[:, 0:32].rearrange("p (g e) -> p g e", e=4), AX.X, ALU.max),
                 reads=[rt_buf], writes=[rt_buf])
            p.op("dve", lambda v: v.max(out=rt[:, 40:48], in_=rt[:, 32:40]),
                 reads=[rt_buf], writes=[rt_buf])
            p.op("dve", lambda v: v.tensor_scalar(rt[:, 32:40], rt[:, 32:40], rt[:, 43:44],
                 None, ALU.is_ge),
                 reads=[rt_buf], writes=[rt_buf])
            for j in range(4):
                p.op("dve", lambda v, j=j: v.tensor_tensor(
                     rt[:, 48 + j:80:4], rt[:, j:32:4], rt[:, 32:40], ALU.mult),
                     reads=[rt_buf], writes=[rt_buf])
            p.op("dve", lambda v: v.max(out=rt[:, 80:88], in_=rt[:, 48:80]),
                 reads=[rt_buf], writes=[rt_buf])
            p.op("dve", lambda v: v.tensor_scalar(rt[:, 48:80], rt[:, 48:80], rt[:, 85:86],
                 None, ALU.is_ge),
                 reads=[rt_buf], writes=[rt_buf])
            p.op("dve", lambda v: v.tensor_mul(rt[:, 88:120], rt[:, 48:80], lg[:, :]),
                 reads=[rt_buf, lg_buf], writes=[rt_buf])
            p.op("dve", lambda v: v.tensor_reduce(rt[:, 120:121], rt[:, 88:120], AX.X, ALU.add),
                 reads=[rt_buf], writes=[rt_buf])
            p.op("dve", lambda v: v.tensor_scalar(rt[:, 121:122], rt[:, 120:121], 1e-20, None, ALU.add),
                 reads=[rt_buf], writes=[rt_buf])
            p.op("dve", lambda v: v.reciprocal(rt[:, 122:123], rt[:, 121:122]),
                 reads=[rt_buf], writes=[rt_buf])
            p.op("dve", lambda v, tt=tt: v.tensor_scalar(comb[:, tt * NE:tt * NE + 32],
                 rt[:, 88:120], rt[:, 122:123], 2.5, ALU.mult, ALU.mult),
                 reads=[rt_buf], writes=[comb_buf])
            p.op("dve", lambda v, tt=tt: v.memset(comb[:, tt * NE + 32:tt * NE + 34], 1.0),
                 writes=[comb_buf])
        for e in range(NE):
            sl = e % 2
            p.dma("pool", guT[sl][:, :], guT_dram[e], reads=[guT_dram_buf], writes=[guT_buf[sl]])
            p.dma("pool", dnT[sl][:, :], dnT_dram[e], reads=[dnT_dram_buf], writes=[dnT_buf[sl]])
            for tg in range(MG // TGP):
                for icp in range(4):
                    for gu_i, (lo, ic) in enumerate(((0, icp), (TGP, icp + 4))):

                        def mmgu(t, ic=ic, tg=tg, sl=sl, lo=lo):
                            for hc in range(NCH):
                                last = t.matmul(pA[:, lo:lo + TGP],
                                                guT[sl][:, hc * C + ic * 128:hc * C + (ic + 1) * 128],
                                                xlnT[:, hc * MG + tg * TGP:hc * MG + (tg + 1) * TGP],
                                                start=(hc == 0), stop=(hc == NCH - 1))
                            return last
                        p.op("pe", mmgu, reads=[guT_buf[sl], xlnT_buf], writes=[pA_buf])
                        if gu_i == 0:
                            p.op("act", lambda s2, lo=lo: s2.activation(silu_s[:, :], pA[:, lo:lo + TGP],
                                 AF.Silu),
                                 reads=[pA_buf], writes=[silu_buf])
                        else:
                            p.op("dve", lambda v, icp=icp, lo=lo: v.tensor_mul(
                                 slab[:, icp * TGP:(icp + 1) * TGP], silu_s[:, :], pA[:, lo:lo + TGP]),
                                 reads=[silu_buf, pA_buf], writes=[slab_buf])
                for tsub in range(TGP // 128):
                    pX, pX_buf = (pB, pB_buf) if tsub % 2 == 0 else (pC, pC_buf)

                    def mmdn(t, tsub=tsub, sl=sl, pX=pX):
                        for half in range(2):
                            for ic in range(4):
                                last = t.matmul(pX[:, half * 512:(half + 1) * 512],
                                                slab[:, ic * TGP + tsub * 128:ic * TGP + (tsub + 1) * 128],
                                                dnT[sl][:, ic * C + half * 512:ic * C + (half + 1) * 512],
                                                start=(ic == 0), stop=(ic == 3))
                        return last
                    p.op("pe", mmdn, reads=[slab_buf, dnT_buf[sl]], writes=[pX_buf])
                    ti = tg * 4 + tsub

                    def accop(v, e=e, ti=ti, pX=pX):
                        return v.scalar_tensor_tensor(acc[:, ti * C:(ti + 1) * C], pX[:, 0:C],
                                                      comb[:, ti * NE + e:ti * NE + e + 1],
                                                      acc[:, ti * C:(ti + 1) * C], ALU.mult, ALU.add)
                    p.op("dve", accop, reads=[pX_buf, comb_buf, acc_buf], writes=[acc_buf])
        for tt in range(NTM):
            sl = tt % NS
            p.dma("sp", xt[sl][:, :], x_in[base + tt * 128: base + (tt + 1) * 128, :],
                  writes=[bufs["xt"][sl]], sem=2 + tt % 2)
            p.op("dve", lambda v, sl=sl, tt=tt: v.tensor_add(xo[sl][:, :],
                 acc[:, tt * C:(tt + 1) * C], xt[sl][:, :]),
                 reads=[acc_buf, bufs["xt"][sl]], writes=[bufs["xo"][sl]])
            p.dma("sp", x_out[base + tt * 128: base + (tt + 1) * 128, :], xo[sl][:, :],
                  reads=[bufs["xo"][sl]], sem=4 + tt % 2)


def emit_L3_scores(p, env, x_in, W, s_out, ntok, s_t_out=None, carve=None):
    nc = env.nc
    ident, identb_buf = env.sb["identb"], env.buf("identb")
    if "identf" not in env.sb:
        idf = env.add_const("identf", np.eye(128, dtype=np.float32))
        env.sbuf("identf", [128, 128], F32)
        p.dma("sp", env.sb["identf"][:], idf, writes=[env.buf("identf")])
    identf, identf_buf = env.sb["identf"], env.buf("identf")
    NS = 2
    if carve is None:
        wg_sb = env.sbuf("s_wg", [128, NCH * 8], F32)
        wg_buf = env.buf("s_wg")
        xt = [env.sbuf(f"s_xt{i}", [128, C], F32) for i in range(NS)]
        stats = [env.sbuf(f"s_st{i}", [128, 8], F32) for i in range(NS)]
        xcf = [env.sbuf(f"s_xc{i}", [128, C], F32) for i in range(NS)]
        diagf = [env.sbuf(f"s_dg{i}", [128, 128], F32) for i in range(NS)]
        sqd = env.sbuf("s_sq", [128, C], BF16)[:, :]
        xlnT32 = env.sbuf("s_xT", [128, C], F32)
        sloc = [env.sbuf(f"s_sl{i}", [128, 8], F32) for i in range(NS)]
        slocT = [env.sbuf(f"s_slT{i}", [8, 128], F32) for i in range(NS)]
        bufs = {n: [env.buf(f"s_{n}{i}") for i in range(NS)]
                for n in ("xt", "st", "xc", "dg", "sl", "slT")}
        sqd_buf = env.buf("s_sq")
        xT_buf = env.buf("s_xT")
    else:
        reg, regbuf = carve
        off = [0]

        def cv(ncols):
            v = reg[:, off[0]:off[0] + ncols]
            off[0] += ncols
            return v
        wg_sb = cv(NCH * 8); wg_buf = regbuf
        xt = [cv(C) for i in range(NS)]
        stats = [cv(8) for i in range(NS)]
        xcf = [cv(C) for i in range(NS)]
        diagf = [cv(128) for i in range(NS)]
        sqd = cv(C)
        xlnT32 = cv(C)
        sloc = [cv(8) for i in range(NS)]
        slocT = [cv(128)[0:8, :] for i in range(NS)]
        bufs = {n: [regbuf, regbuf] for n in ("xt", "st", "xc", "dg", "sl", "slT")}
        sqd_buf = regbuf
        xT_buf = regbuf
    wg3 = wg_sb[:, 0:NCH * 8].rearrange("p (c j) -> p c j", j=8)
    p.dma("sp", wg3, W["Wg"].rearrange("(c p) e -> p c e", p=128), writes=[wg_buf])
    pD, pD_buf = env.psum[list(env.psum)[-1]], env.buf(list(env.psum)[-1])
    pA, pA_buf = env.psum[list(env.psum)[0]], env.buf(list(env.psum)[0])
    for tt in range(ntok // 128):
        sl = tt % NS
        p.dma("sp", xt[sl][:, 0:C], x_in[tt * 128:(tt + 1) * 128, :], writes=[bufs["xt"][sl]], sem=tt % 2)
        p.op("dve", lambda v, sl=sl: v.tensor_reduce(stats[sl][:, 0:1], xt[sl][:, 0:C], AX.X, ALU.add,
             negate=True), reads=[bufs["xt"][sl]], writes=[bufs["st"][sl]])
        p.op("dve", lambda v, sl=sl: v.tensor_scalar(stats[sl][:, 1:2], stats[sl][:, 0:1], 1.0 / C,
             None, ALU.mult), reads=[bufs["st"][sl]], writes=[bufs["st"][sl]])
        p.op("dve", lambda v, sl=sl: v.tensor_scalar(xcf[sl][:, 0:C], xt[sl][:, 0:C], stats[sl][:, 1:2],
             None, ALU.add), reads=[bufs["xt"][sl], bufs["st"][sl]], writes=[bufs["xc"][sl]])
        p.op("act", lambda s2, sl=sl: s2.activation(sqd[:, 0:C], xcf[sl][:, 0:C], AF.Square,
             accum_out=stats[sl][:, 2:3]), reads=[bufs["xc"][sl]], writes=[sqd_buf, bufs["st"][sl]])
        p.op("dve", lambda v, sl=sl: v.tensor_scalar(stats[sl][:, 3:4], stats[sl][:, 2:3], 1.0 / C,
             1e-5, ALU.mult, ALU.add), reads=[bufs["st"][sl]], writes=[bufs["st"][sl]])
        p.op("act", lambda s2, sl=sl: s2.activation(stats[sl][:, 4:5], stats[sl][:, 3:4], AF.Sqrt),
             reads=[bufs["st"][sl]], writes=[bufs["st"][sl]])
        p.op("dve", lambda v, sl=sl: v.reciprocal(stats[sl][:, 5:6], stats[sl][:, 4:5]),
             reads=[bufs["st"][sl]], writes=[bufs["st"][sl]])
        p.op("dve", lambda v, sl=sl: v.tensor_scalar(diagf[sl][:, 0:128], identf[:, :], stats[sl][:, 5:6],
             None, ALU.mult), reads=[identf_buf, bufs["st"][sl]], writes=[bufs["dg"][sl]])

        def mmt32(t, sl=sl):
            for c in range(NCH):
                last = t.matmul(pD[:, c * 128:(c + 1) * 128], xcf[sl][:, c * 128:(c + 1) * 128],
                                diagf[sl][:, 0:128], start=True, stop=True)
            return last
        p.op("pe", mmt32, reads=[bufs["xc"][sl], bufs["dg"][sl]], writes=[pD_buf])
        p.op("dve", lambda v: v.tensor_copy(xlnT32[:, 0:C], pD[:, 0:C]), reads=[pD_buf], writes=[xT_buf])

        def mmg(t):
            for c in range(NCH):
                last = t.matmul(pA[:, 0:8], xlnT32[:, c * 128:(c + 1) * 128],
                                wg_sb[:, c * 8:(c + 1) * 8], start=(c == 0), stop=(c == NCH - 1))
            return last
        p.op("pe", mmg, reads=[xT_buf, wg_buf], writes=[pA_buf])
        p.op("act", lambda s2, sl=sl: s2.activation(sloc[sl][:, 0:8], pA[:, 0:8], AF.Sigmoid),
             reads=[pA_buf], writes=[bufs["sl"][sl]])
        p.dma("sp", s_out[tt * 128:(tt + 1) * 128, :], sloc[sl][:, 0:8], reads=[bufs["sl"][sl]],
              sem=2 + tt % 2)
        if s_t_out is not None:
            p.op("pe", lambda t, sl=sl: t.matmul(pD[0:8, 0:128], sloc[sl][:, 0:8], identf[:, :],
                 start=True, stop=True), reads=[bufs["sl"][sl], identf_buf], writes=[pD_buf])
            p.op("dve", lambda v, sl=sl: v.tensor_copy(slocT[sl][0:8, 0:128], pD[0:8, 0:128]),
                 reads=[pD_buf], writes=[bufs["slT"][sl]])
            p.dma("sp", s_t_out[:, tt * 128:(tt + 1) * 128], slocT[sl][0:8, 0:128],
                  reads=[bufs["slT"][sl]], sem=5)


def emit_L3_ffn(p, env, x_in, x_out, W, ntok, S_t, S_tok, nglob, S_deps=()):
    nc = env.nc
    E, DF = 8, 4096
    NFC = DF // 128
    TGP = 512
    K_SEL = nglob // 4
    L = nglob // 8
    ident, identb_buf = env.sb["identb"], env.buf("identb")
    ones, ones_buf = env.sb["onesb"], env.buf("onesb")
    pA, pA_buf = env.psum["pA"], env.buf("pA")
    pB, pB_buf = env.psum["pB"], env.buf("pB")
    pC, pC_buf = env.psum["pC"], env.buf("pC")
    pD, pD_buf = env.psum["pD"], env.buf("pD")
    if "identf" not in env.sb:
        idf = env.add_const("identf", np.eye(128, dtype=np.float32))
        env.sbuf("identf", [128, 128], F32)
        p.dma("sp", env.sb["identf"][:], idf, writes=[env.buf("identf")])
    identf, identf_buf = env.sb["identf"], env.buf("identf")

    slab = env.sbuf("e_slab", [128, (DF // 128) * TGP], BF16); slab_buf = env.buf("e_slab")
    scnt = env.sbuf("e_scnt", [64, L], F32); scnt_buf = env.buf("e_scnt")
    gmode = (S_t is not None and getattr(S_t, "shape", [0])[0] == 64)
    if gmode:
        p.dma("sp", scnt[:, :], S_t, deps=list(S_deps), writes=[scnt_buf])
        p64 = env.add_const("p64m", (np.arange(64)[:, None] % 8 == np.arange(8)[None, :])
                            .astype(np.float32))
        p64T = env.add_const("p64Tm", (np.arange(64)[None, :] % 8 == np.arange(8)[:, None])
                             .astype(np.float32))
    else:
        p.dma("sp", scnt[:, :], S_t.rearrange("e (j n) -> (e j) n", j=8), writes=[scnt_buf])
        p64 = env.add_const("p64", (np.arange(64)[:, None] // 8 == np.arange(8)[None, :])
                            .astype(np.float32))
        p64T = env.add_const("p64T", (np.arange(64)[None, :] // 8 == np.arange(8)[:, None])
                             .astype(np.float32))
    p64_sb = env.sbuf("e_p64", [64, 8], F32); p64_buf = env.buf("e_p64")
    p64T_sb = env.sbuf("e_p64T", [8, 64], F32); p64T_buf = env.buf("e_p64T")
    p.dma("sp", p64_sb[:, :], p64, writes=[p64_buf])
    p.dma("sp", p64T_sb[:, :], p64T, writes=[p64T_buf])
    bs = env.sbuf("e_bs", [8, 16], F32); bs_buf = env.buf("e_bs")
    gem = slab.bitcast(F32)[0:64, 0:L]
    gem_buf = env.buf("e_slab")
    prt = env.sbuf("e_prt", [64, 1], F32); prt_buf = env.buf("e_prt")
    thr = env.sbuf("e_thr", [64, 1], F32); thr_buf = env.buf("e_thr")
    p.op("dve", lambda v: v.memset(bs[:, 0:1], 0.0), writes=[bs_buf])
    p.op("dve", lambda v: v.memset(bs[:, 1:2], 1.0), writes=[bs_buf])
    p.op("dve", lambda v: v.memset(bs[:, 3:4], -1e30), writes=[bs_buf])
    for it in range(40):
        p.op("dve", lambda v: v.tensor_scalar(bs[:, 2:3], bs[:, 0:1], bs[:, 1:2], 0.5,
             ALU.add, ALU.mult), reads=[bs_buf], writes=[bs_buf])
        p.op("pe", lambda t: t.matmul(pD[0:64, 0:1], p64T_sb[:, :], bs[:, 2:3], start=True, stop=True),
             reads=[p64T_buf, bs_buf], writes=[pD_buf])
        p.op("dve", lambda v: v.tensor_copy(thr[:, :], pD[0:64, 0:1]), reads=[pD_buf], writes=[thr_buf])
        p.op("dve", lambda v: v.tensor_scalar(gem[0:64, 0:L], scnt[:, :], thr[:, 0:1], None, ALU.is_ge),
             reads=[scnt_buf, thr_buf], writes=[gem_buf])
        p.op("dve", lambda v: v.tensor_reduce(prt[:, :], gem[0:64, 0:L], AX.X, ALU.add),
             reads=[gem_buf], writes=[prt_buf])
        p.op("pe", lambda t: t.matmul(pD[0:8, 0:1], p64_sb[:, :], prt[:, :], start=True, stop=True),
             reads=[p64_buf, prt_buf], writes=[pD_buf])
        p.op("dve", lambda v: v.tensor_copy(bs[:, 4:5], pD[0:8, 0:1]), reads=[pD_buf], writes=[bs_buf])
        p.op("dve", lambda v: v.tensor_scalar(bs[:, 5:6], bs[:, 4:5], float(K_SEL), None, ALU.is_ge),
             reads=[bs_buf], writes=[bs_buf])
        p.op("dve", lambda v: v.tensor_sub(bs[:, 6:7], bs[:, 2:3], bs[:, 0:1]),
             reads=[bs_buf], writes=[bs_buf])
        p.op("dve", lambda v: v.scalar_tensor_tensor(bs[:, 0:1], bs[:, 5:6], bs[:, 6:7], bs[:, 0:1],
             ALU.mult, ALU.add), reads=[bs_buf], writes=[bs_buf])
        p.op("dve", lambda v: v.tensor_scalar(bs[:, 5:6], bs[:, 5:6], -1.0, 1.0, ALU.mult, ALU.add),
             reads=[bs_buf], writes=[bs_buf])
        p.op("dve", lambda v: v.tensor_sub(bs[:, 6:7], bs[:, 2:3], bs[:, 1:2]),
             reads=[bs_buf], writes=[bs_buf])
        p.op("dve", lambda v: v.scalar_tensor_tensor(bs[:, 1:2], bs[:, 5:6], bs[:, 6:7], bs[:, 1:2],
             ALU.mult, ALU.add), reads=[bs_buf], writes=[bs_buf])
        p.op("dve", lambda v: v.tensor_scalar(bs[:, 5:6], bs[:, 4:5], float(K_SEL), None, ALU.is_equal),
             reads=[bs_buf], writes=[bs_buf])
        p.op("dve", lambda v: v.tensor_sub(bs[:, 6:7], bs[:, 2:3], bs[:, 3:4]),
             reads=[bs_buf], writes=[bs_buf])
        p.op("dve", lambda v: v.scalar_tensor_tensor(bs[:, 3:4], bs[:, 5:6], bs[:, 6:7], bs[:, 3:4],
             ALU.mult, ALU.add), reads=[bs_buf], writes=[bs_buf])
    p.op("dve", lambda v: v.tensor_scalar(bs[:, 5:6], bs[:, 3:4], -1e29, None, ALU.is_gt),
         reads=[bs_buf], writes=[bs_buf])
    p.op("dve", lambda v: v.tensor_sub(bs[:, 6:7], bs[:, 3:4], bs[:, 0:1]),
         reads=[bs_buf], writes=[bs_buf])
    p.op("dve", lambda v: v.scalar_tensor_tensor(bs[:, 7:8], bs[:, 5:6], bs[:, 6:7], bs[:, 0:1],
         ALU.mult, ALU.add), reads=[bs_buf], writes=[bs_buf])
    thb = env.sbuf("e_thb", [128, 8], F32); thb_buf = env.buf("e_thb")
    throw = env.sbuf("e_throw", [1, 8], F32); throw_buf = env.buf("e_throw")
    p.op("pe", lambda t: t.matmul(pD[0:1, 0:8], bs[:, 7:8], identf[0:8, 0:8], start=True, stop=True),
         reads=[bs_buf, identf_buf], writes=[pD_buf])
    p.op("dve", lambda v: v.tensor_copy(throw[:, :], pD[0:1, 0:8]), reads=[pD_buf], writes=[throw_buf])
    if "m_onesf" not in env.sb:
        onesf = env.sbuf("m_onesf", [1, 128], F32)
        p.op("dve", lambda v: v.memset(onesf[:, :], 1.0), writes=[env.buf("m_onesf")])
    onesf, onesf_buf = env.sb["m_onesf"], env.buf("m_onesf")
    p.op("pe", lambda t: t.matmul(pD[:, 0:8], onesf[0:1, :], throw[0:1, :], start=True, stop=True),
         reads=[onesf_buf, throw_buf], writes=[pD_buf])
    p.op("dve", lambda v: v.tensor_copy(thb[:, :], pD[:, 0:8]), reads=[pD_buf], writes=[thb_buf])

    NTC = ntok // 128
    coeff = env.sbuf("e_coeff", [128, NTC * 8], F32); coeff_buf = env.buf("e_coeff")
    stok = env.sbuf("e_stok", [128, 8], F32); stok_buf = env.buf("e_stok")
    xlnd = nc.dram_tensor("e_xlnd", [NTC, 128, C], BF16).ap()
    xlnd_buf = Buf("e_xlnd")
    accd = nc.dram_tensor("e_accd", [ntok, C], F32).ap()
    accd_buf = Buf("e_accd")
    NS = 2
    xt = [env.sbuf(f"e_xt{i}", [128, C], F32) for i in range(NS)]
    stats = [env.sbuf(f"e_st{i}", [128, 8], F32) for i in range(NS)]
    diag = [env.sbuf(f"e_dg{i}", [128, 128], BF16) for i in range(NS)]
    xcb = [env.sbuf(f"e_xc{i}", [128, C], BF16) for i in range(NS)]
    sqd = env.sbuf("e_sq", [128, C], BF16)
    xo = [env.sbuf(f"e_xo{i}", [128, C], F32) for i in range(NS)]
    bufs = {n: [env.buf(f"e_{n}{i}") for i in range(NS)] for n in ("xt", "st", "dg", "xc", "xo")}
    sqd_buf = env.buf("e_sq")
    xlt = [env.sbuf(f"e_xlt{i}", [128, C], BF16) for i in range(NS)]
    xlt_buf = [env.buf(f"e_xlt{i}") for i in range(NS)]
    for tt in range(NTC):
        sl = tt % NS
        p.dma("sp", xt[sl][:, :], x_in[tt * 128:(tt + 1) * 128, :], writes=[bufs["xt"][sl]], sem=tt % 2)
        dsts = [xlt[sl][:, c * 128:(c + 1) * 128] for c in range(NCH)]
        emit_ln_transpose(p, env, xt[sl][:, :], bufs["xt"][sl], stats[sl][:, :], bufs["st"][sl],
                          diag[sl][:, :], bufs["dg"][sl], xcb[sl][:, :], bufs["xc"][sl],
                          sqd[:, :], sqd_buf, "pD", dsts, xlt_buf[sl])
        p.dma("sp", xlnd[tt], xlt[sl][:, :], reads=[xlt_buf[sl]], writes=[xlnd_buf], sem=2 + tt % 2)
        p.dma("sp", stok[:, :], S_tok[tt * 128:(tt + 1) * 128, :], writes=[stok_buf], sem=4)
        p.op("dve", lambda v, tt=tt: v.tensor_tensor(coeff[:, tt * 8:(tt + 1) * 8], stok[:, :],
             thb[:, :], ALU.is_ge), reads=[stok_buf, thb_buf], writes=[coeff_buf])
        p.op("dve", lambda v, tt=tt: v.tensor_mul(coeff[:, tt * 8:(tt + 1) * 8],
             coeff[:, tt * 8:(tt + 1) * 8], stok[:, :]), reads=[stok_buf], writes=[coeff_buf])

    w1r = env.sbuf("e_w1r", [128, NCH * DF], BF16); w1r_buf = env.buf("e_w1r")
    w2t = [env.sbuf(f"e_w2t{i}", [128, C], BF16) for i in range(2)]
    w2t_bufs = [env.buf(f"e_w2t{i}") for i in range(2)]
    xlg0 = env.sbuf("e_xlg0", [128, NCH * TGP], BF16)
    xlg = [xlg0, xlg0]
    xlg_buf = [env.buf("e_xlg0"), env.buf("e_xlg0")]
    b2row = env.sbuf("e_b2r", [1, C], BF16); b2row_buf = env.buf("e_b2r")
    b1v = env.sbuf("e_b1v", [128, 32], F32); b1v_buf = env.buf("e_b1v")
    b1ld = env.sbuf("e_b1ld", [32, 128], F32); b1ld_buf = env.buf("e_b1ld")
    oacc0 = env.sbuf("e_oac0", [128, C], F32)
    oacc = [oacc0, oacc0]
    oacc_buf = [env.buf("e_oac0"), env.buf("e_oac0")]
    for e in range(E):
        for c in range(NCH):
            p.dma("pool", w1r[:, c * DF:(c + 1) * DF], W["W1"][e, c * 128:(c + 1) * 128, :],
                  writes=[w1r_buf])
        p.dma("pool", b2row[0:1, :], W["b2"][e][None, :], writes=[b2row_buf])
        p.dma("sp", b1ld[:, :], W["b1"][e].rearrange("(a p) -> a p", p=128), writes=[b1ld_buf])
        p.op("pe", lambda t: t.matmul(pD[:, 0:32], b1ld[:, :], identf[0:32, 0:32],
             start=True, stop=True), reads=[b1ld_buf, identf_buf], writes=[pD_buf])
        p.op("dve", lambda v: v.tensor_copy(b1v[:, :], pD[:, 0:32]), reads=[pD_buf], writes=[b1v_buf])
        for tg in range(ntok // TGP):
            gsl = tg % 2
            for tsub in range(TGP // 128):
                tt = tg * 4 + tsub
                dst3 = xlg[gsl][:, :].rearrange("p (c j) -> p c j", j=TGP)[:, :, tsub * 128:(tsub + 1) * 128]
                src3 = xlnd[tt].rearrange("p (c j) -> p c j", j=128)
                p.dma("pool", dst3, src3, reads=[xlnd_buf], writes=[xlg_buf[gsl]])
            for fc in range(NFC):
                half = fc % 2
                hap = pA[:, half * 512:half * 512 + TGP]

                def mmh(t, fc=fc, gsl=gsl, hap=hap):
                    for c in range(NCH):
                        last = t.matmul(hap, w1r[:, c * DF + fc * 128:c * DF + (fc + 1) * 128],
                                        xlg[gsl][:, c * TGP:(c + 1) * TGP],
                                        start=(c == 0), stop=(c == NCH - 1))
                    return last
                p.op("pe", mmh, reads=[w1r_buf, xlg_buf[gsl]], writes=[pA_buf])
                p.op("act", lambda s2, fc=fc, hap=hap: s2.activation(
                     slab[:, fc * TGP:(fc + 1) * TGP], hap, AF.Gelu, bias=b1v[:, fc:fc + 1]),
                     reads=[pA_buf, b1v_buf], writes=[slab_buf])
            pos = [pA, pB, pC, pD]
            pos_bufs = [pA_buf, pB_buf, pC_buf, pD_buf]
            for fc in range(NFC):
                wsl = fc % 2
                p.dma("pool", w2t[wsl][:, :], W["W2"][e, fc * 128:(fc + 1) * 128, :],
                      writes=[w2t_bufs[wsl]])

                def mmo(t, fc=fc, wsl=wsl):
                    last = None
                    for tsub in range(TGP // 128):
                        for half in range(2):
                            last = t.matmul(pos[tsub][:, half * 512:(half + 1) * 512],
                                            slab[:, fc * TGP + tsub * 128:fc * TGP + (tsub + 1) * 128],
                                            w2t[wsl][:, half * 512:(half + 1) * 512],
                                            start=(fc == 0), stop=False)
                    return last
                p.op("pe", mmo, reads=[slab_buf, w2t_bufs[wsl]], writes=pos_bufs)

            def mmbias(t):
                last = None
                for tsub in range(TGP // 128):
                    for half in range(2):
                        last = t.matmul(pos[tsub][:, half * 512:(half + 1) * 512], ones[0:1, :],
                                        b2row[0:1, half * 512:(half + 1) * 512], start=False, stop=True)
                return last
            p.op("pe", mmbias, reads=[ones_buf, b2row_buf], writes=pos_bufs)
            for tsub in range(TGP // 128):
                tt = tg * 4 + tsub
                osl = tsub % 2
                p.op("dve", lambda v, tsub=tsub, tt=tt, e=e, osl=osl: v.tensor_scalar(
                     oacc[osl][:, :], pos[tsub][:, 0:C], coeff[:, tt * 8 + e:tt * 8 + e + 1],
                     None, ALU.mult),
                     reads=[pos_bufs[tsub], coeff_buf], writes=[oacc_buf[osl]])
                if e == 0:
                    p.dma("pool", accd[tt * 128:(tt + 1) * 128, :], oacc[osl][:, :],
                          reads=[oacc_buf[osl]], writes=[accd_buf])
                else:
                    p.dma("pool", accd[tt * 128:(tt + 1) * 128, :], oacc[osl][:, :],
                          reads=[oacc_buf[osl]], writes=[accd_buf], accum_op=ALU.add)
    for tt in range(NTC):
        sl = tt % NS
        p.dma("sp", xt[sl][:, :], x_in[tt * 128:(tt + 1) * 128, :], writes=[bufs["xt"][sl]], sem=tt % 2)
        p.dma("sp", xo[sl][:, :], accd[tt * 128:(tt + 1) * 128, :],
              reads=[accd_buf], writes=[bufs["xo"][sl]], sem=2 + tt % 2)
        p.op("dve", lambda v, sl=sl: v.tensor_add(xo[sl][:, :], xo[sl][:, :], xt[sl][:, :]),
             reads=[bufs["xt"][sl]], writes=[bufs["xo"][sl]])
        p.dma("sp", x_out[tt * 128:(tt + 1) * 128, :], xo[sl][:, :],
              reads=[bufs["xo"][sl]], sem=4 + tt % 2)


def _l0_sinusoidal(n, d=64):
    pos = np.arange(n, dtype=np.float32)[:, None]
    div = np.exp(np.arange(0, d, 2, dtype=np.float32) * (-math.log(10000.0) / d))
    ang = pos * div
    return np.stack([np.sin(ang), np.cos(ang)], axis=-1).reshape(n, d).astype(np.float32)


def emit_L0_attn(p, env, x_in, x_out, W, ntok, T=2048):
    nc = env.nc
    H, HD, NCL, GS = 16, 64, 64, 256
    nseq = ntok // T
    NT = T // 128
    ntpc = T // NCL
    cpt = 128 // ntpc
    G = T // GS
    TPG = GS // 128
    ncpg = GS // ntpc
    c_max = (G - 1) * ncpg
    qscale = 1.0 / math.sqrt(HD)

    ident, identb_buf = env.sb["identb"], env.buf("identb")
    ones, ones_buf = env.sb["onesb"], env.buf("onesb")
    pS, pS_buf = env.psum["pS"], env.buf("pS")
    pT, pT_buf = env.psum["pT"], env.buf("pT")
    pO, pO_buf = env.psum["pO"], env.buf("pO")

    S = _l0_sinusoidal(c_max + GS)
    qsin = np.concatenate([S[pi * 128:(pi + 1) * 128, 0:32] for pi in range(TPG)], 1) * qscale
    qcos = np.concatenate([S[pi * 128:(pi + 1) * 128, 32:64] for pi in range(TPG)], 1) * qscale
    ks_l, kc_l = [], []
    for tt in range(NT):
        g, pi = tt // TPG, tt % TPG
        rows = np.arange(128) + g * ncpg + pi * 128
        ks_l.append(S[rows, 0:32]); kc_l.append(S[rows, 32:64])
    ksin, kcos = np.concatenate(ks_l, 1), np.concatenate(kc_l, 1)
    csin = np.zeros((64, 32), np.float32); ccos = np.zeros((64, 32), np.float32)
    csin[:c_max] = S[:c_max, 0:32]; ccos[:c_max] = S[:c_max, 32:64]
    seg = (np.arange(128)[:, None] // ntpc == np.arange(cpt)[None, :]).astype(np.float32)
    lm = np.tril(np.ones((GS, GS), np.float32))
    lmask = np.concatenate([np.where(lm[pi * 128:(pi + 1) * 128, :], 0, -1e30)
                            for pi in range(TPG)], 1).astype(np.float32)
    clm = np.zeros((128, NT * NCL), np.float32)
    for tt in range(NT):
        for pp in range(128):
            clm[pp, tt * NCL + tt * cpt + pp // ntpc] = 1.0
    consts = {}
    for nm, arr in (("l0qs", qsin), ("l0qc", qcos), ("l0ks", ksin), ("l0kc", kcos),
                    ("l0cs", csin), ("l0cc", ccos)):
        capn = env.add_const(nm, arr)
        env.sbuf(nm, list(arr.shape), F32)
        p.dma("sp", env.sb[nm][:, :], capn, writes=[env.buf(nm)])
        consts[nm] = (env.sb[nm], env.buf(nm))
    for nm, arr in (("l0lm", lmask), ("l0clm", clm)):
        capn = env.add_const(nm, np_bf16(arr))
        env.sbuf(nm, list(arr.shape), BF16)
        p.dma("pool", env.sb[nm][:, :], capn, writes=[env.buf(nm)])
        consts[nm] = (env.sb[nm], env.buf(nm))
    if "identf" not in env.sb:
        idf = env.add_const("identf", np.eye(128, dtype=np.float32))
        env.sbuf("identf", [128, 128], F32)
        p.dma("sp", env.sb["identf"][:], idf, writes=[env.buf("identf")])
    identf, identf_buf = env.sb["identf"], env.buf("identf")
    if "m_onesf" not in env.sb:
        onesf0 = env.sbuf("m_onesf", [1, 128], F32)
        p.op("dve", lambda v: v.memset(onesf0[:, :], 1.0), writes=[env.buf("m_onesf")])
    onesf, onesf_buf = env.sb["m_onesf"], env.buf("m_onesf")
    segb_c = env.add_const("l0seg", np_bf16(seg))
    segb = env.sbuf("l0seg", [128, cpt], BF16)
    p.dma("pool", segb[:, :], segb_c, writes=[env.buf("l0seg")])
    segb_buf = env.buf("l0seg")
    segTb_c = env.add_const("l0segT", np_bf16(seg.T.copy()))
    segTb = env.sbuf("l0segT", [cpt, 128], BF16)
    p.dma("pool", segTb[:, :], segTb_c, writes=[env.buf("l0segT")])
    segT_buf = env.buf("l0segT")

    waT_dram = nc.dram_tensor("l0_waT", [128, NCH * 3 * C], BF16).ap()
    waT_dram_buf = Buf("l0waTd")
    wpT_dram = nc.dram_tensor("l0_wpT", [128, NCH * C], BF16).ap()
    wpT_dram_buf = Buf("l0wpTd")
    wld = [env.sbuf(f"l0wld{i}", [128, C], BF16) for i in range(2)]
    wld_buf = [env.buf(f"l0wld{i}") for i in range(2)]
    xlnT = env.sbuf("l0xlnT", [128, NCH * T], BF16); xlnT_buf = env.buf("l0xlnT")
    tsl = [xlnT[:, i * 1024:(i + 1) * 1024] for i in range(2)]
    tsl_buf = [Buf(f"l0tsl{i}") for i in range(2)]
    for wsrc_ap, rows, dram, dbuf in ((W["w_attn"], 3 * C, waT_dram, waT_dram_buf),
                                      (W["w_proj"], C, wpT_dram, wpT_dram_buf)):
        for rc in range(rows // 128):
            sl = rc % 2
            p.dma("pool", wld[sl][:, :], wsrc_ap[rc * 128:(rc + 1) * 128, :], writes=[wld_buf[sl]])

            def mmtr(t, sl=sl):
                for cc in range(NCH):
                    last = t.matmul(pT[:, cc * 128:(cc + 1) * 128],
                                    wld[sl][:, cc * 128:(cc + 1) * 128], ident[:, :],
                                    start=True, stop=True)
                return last
            p.op("pe", mmtr, reads=[wld_buf[sl], identb_buf], writes=[pT_buf])
            p.op("dve", lambda v, sl=sl: v.tensor_copy(tsl[sl], pT[:, 0:1024]),
                 reads=[pT_buf], writes=[tsl_buf[sl]])
            for cc in range(NCH):
                p.dma("sp", dram[:, cc * rows + rc * 128: cc * rows + (rc + 1) * 128],
                      tsl[sl][:, sl * 0 + cc * 128:(cc + 1) * 128] if False else
                      xlnT[:, sl * 1024 + cc * 128: sl * 1024 + (cc + 1) * 128],
                      reads=[tsl_buf[sl]], writes=[dbuf])
    was_sb = env.sbuf("l0was", [64, C], BF16)
    was_buf = env.buf("l0was")
    p.dma("pool", was_sb[:, :], W["w_assign"], writes=[was_buf])
    wasT = env.sbuf("l0wasT", [128, NCH * 64], BF16)
    wasT_buf = env.buf("l0wasT")
    for cc in range(NCH):
        p.op("pe", lambda t, cc=cc: t.matmul(pT[:, 0:64], was_sb[:, cc * 128:(cc + 1) * 128],
             ident[0:64, 0:64], start=True, stop=True),
             reads=[was_buf, identb_buf], writes=[pT_buf])
        p.op("dve", lambda v, cc=cc: v.tensor_copy(wasT[:, cc * 64:(cc + 1) * 64], pT[:, 0:64]),
             reads=[pT_buf], writes=[wasT_buf])
    ba_sl = [env.sbuf(f"l0ba{i}", [1, 256], BF16) for i in range(2)]
    ba_sl_buf = [env.buf(f"l0ba{i}") for i in range(2)]
    bp_row, bp_buf = emit_row_load(p, env, W["b_proj"], C, "l0bp")
    wpT = env.sbuf("l0wpT", [128, NCH * C], BF16)
    wpT_buf = env.buf("l0wpT")
    p.dma("pool", wpT[:, :], wpT_dram, reads=[wpT_dram_buf], writes=[wpT_buf])

    for tb in tsl_buf:
        if tb.w is not None:
            xlnT_buf.rs.extend(tb.rs)
            xlnT_buf.rs.append(tb.w)
    qfm = env.sbuf("l0qfm", [128, NCH * T], BF16); qfm_buf = env.buf("l0qfm")
    ktok = env.sbuf("l0ktok", [128, NT * C], BF16); ktok_buf = env.buf("l0ktok")
    vtok = env.sbuf("l0vtok", [128, NT * C], BF16); vtok_buf = env.buf("l0vtok")
    wstr = [env.sbuf(f"l0ws{i}", [128, NCH * 256], BF16) for i in range(2)]
    wstr_buf = [env.buf(f"l0ws{i}") for i in range(2)]
    NS = 2
    xt = [env.sbuf(f"l0xt{i}", [128, C], F32) for i in range(NS)]
    stats = [env.sbuf(f"l0st{i}", [128, 8], F32) for i in range(NS)]
    diag = [env.sbuf(f"l0dg{i}", [128, 128], BF16) for i in range(NS)]
    xcb = [env.sbuf(f"l0xc{i}", [128, C], BF16) for i in range(NS)]
    sqd = env.sbuf("l0sq", [128, C], BF16)
    xo = [env.sbuf(f"l0xo{i}", [128, C], F32) for i in range(NS)]
    bufs = {n: [env.buf(f"l0{n}{i}") for i in range(NS)] for n in ("xt", "st", "dg", "xc", "xo")}
    sqd_buf = env.buf("l0sq")
    qt_tmp = env.sbuf("l0qtt", [128, 512], BF16); qt_tmp_buf = env.buf("l0qtt")
    qr_tmp = env.sbuf("l0qrt", [128, 512], BF16); qr_tmp_buf = env.buf("l0qrt")
    ck_dram = nc.dram_tensor("l0ckd", [NCL, C], BF16).ap()
    cv_dram = nc.dram_tensor("l0cvd", [NCL, C], BF16).ap()
    ck_dram_buf = Buf("l0ckd"); cv_dram_buf = Buf("l0cvd")
    cks = env.sbuf("l0cks", [cpt, C], BF16); cks_buf = env.buf("l0cks")
    cvs = env.sbuf("l0cvs", [cpt, C], BF16); cvs_buf = env.buf("l0cvs")
    kcfm = env.sbuf("l0kcfm", [128, H * 64], BF16); kcfm_buf = env.buf("l0kcfm")
    lgx = env.sbuf("l0lgx", [128, 64], F32); lgx_buf = env.buf("l0lgx")
    lcol = env.sbuf("l0lcol", [128, 2], F32); lcol_buf = env.buf("l0lcol")
    lrow = env.sb["l0sq"].bitcast(F32); lrow_buf = env.buf("l0sq")
    lsc = env.sbuf("l0lsc", [128, 2], F32); lsc_buf = env.buf("l0lsc")
    ecol = env.sbuf("l0ecol", [128, 1], BF16); ecol_buf = env.buf("l0ecol")
    rsum = env.sbuf("l0rsum", [cpt, 2], F32); rsum_buf = env.buf("l0rsum")
    rsumb = env.sbuf("l0rsumb", [cpt, 1], BF16); rsumb_buf = env.buf("l0rsumb")
    wcol = env.sbuf("l0wcol", [128, 1], F32); wcol_buf = env.buf("l0wcol")
    Wseg = env.sbuf("l0Wseg", [128, NT * cpt], BF16); Wseg_buf = env.buf("l0Wseg")
    st2 = [env.sbuf(f"l0s2_{i}", [128, 4], F32) for i in range(2)]
    st2_buf = [env.buf(f"l0s2_{i}") for i in range(2)]

    for s in range(nseq):
        base = s * T
        for tt in range(NT):
            sl = tt % NS
            p.dma("sp", xt[sl][:, :], x_in[base + tt * 128: base + (tt + 1) * 128, :],
                  writes=[bufs["xt"][sl]], sem=tt % 2)
            dsts = [xlnT[:, c * T + tt * 128: c * T + (tt + 1) * 128] for c in range(NCH)]
            emit_ln_transpose(p, env, xt[sl][:, :], bufs["xt"][sl], stats[sl][:, :], bufs["st"][sl],
                              diag[sl][:, :], bufs["dg"][sl], xcb[sl][:, :], bufs["xc"][sl],
                              sqd[:, :], sqd_buf, "pT", dsts, xlnT_buf)
        for colq in range(12):
            wslot = colq % 2
            wt = wstr[wslot]
            wt3 = wt[:, :].rearrange("p (c j) -> p c j", j=256)
            wsrc3 = waT_dram.rearrange("p (c f) -> p c f", f=3 * C)
            p.dma("pool", wt3, wsrc3[:, :, colq * 256:(colq + 1) * 256],
                  reads=[waT_dram_buf], writes=[wstr_buf[wslot]])
            p.dma("pool", ba_sl[wslot][0:1, :], W["b_attn"][None, colq * 256:(colq + 1) * 256],
                  writes=[ba_sl_buf[wslot]])
            for tt in range(NT):
                hap = pS[:, (tt % 2) * 512:(tt % 2) * 512 + 256]

                def mmqkv(t, tt=tt, colq=colq, hap=hap, wt=wt, wslot=wslot):
                    for c in range(NCH):
                        t.matmul(hap, xlnT[:, c * T + tt * 128:c * T + (tt + 1) * 128],
                                 wt[:, c * 256:(c + 1) * 256], start=(c == 0), stop=False)
                    return t.matmul(hap, ones[0:1, :], ba_sl[wslot][0:1, :],
                                    start=False, stop=True)
                p.op("pe", mmqkv, reads=[wstr_buf[wslot], xlnT_buf, ones_buf, ba_sl_buf[wslot]],
                     writes=[pS_buf])
                if colq < 4:
                    p.op("dve", lambda v, hap=hap: v.tensor_copy(qt_tmp[:, 0:256], hap),
                         reads=[pS_buf], writes=[qt_tmp_buf])
                    pi = tt % TPG
                    qs_t, qs_b = consts["l0qs"]; qc_t, qc_b = consts["l0qc"]
                    for hh in range(4):
                        p.op("dve", lambda v, hh=hh, pi=pi: v.tensor_mul(
                             qr_tmp[:, hh * 64 + 1:hh * 64 + 64:2], qt_tmp[:, hh * 64:hh * 64 + 64:2],
                             qs_t[:, pi * 32:(pi + 1) * 32]),
                             reads=[qt_tmp_buf, qs_b], writes=[qr_tmp_buf])
                        p.op("dve", lambda v, hh=hh, pi=pi: v.tensor_mul(
                             qr_tmp[:, hh * 64:hh * 64 + 64:2], qt_tmp[:, hh * 64 + 1:hh * 64 + 64:2],
                             qc_t[:, pi * 32:(pi + 1) * 32]),
                             reads=[qt_tmp_buf, qc_b], writes=[qr_tmp_buf])
                    p.op("dve", lambda v: v.tensor_scalar(qr_tmp[:, 0:256:2], qr_tmp[:, 0:256:2],
                         -1.0, None, ALU.mult), reads=[qr_tmp_buf], writes=[qr_tmp_buf])

                    def mmtq(t, tt=tt):
                        for j in range(2):
                            last = t.matmul(pT[:, j * 128:(j + 1) * 128],
                                            qr_tmp[:, j * 128:(j + 1) * 128], ident[:, :],
                                            start=True, stop=True)
                        return last
                    p.op("pe", mmtq, reads=[qr_tmp_buf, identb_buf], writes=[pT_buf])

                    def evq(v, tt=tt, colq=colq):
                        for j in range(2):
                            cc = colq * 2 + j
                            last = v.tensor_copy(qfm[:, cc * T + tt * 128:cc * T + (tt + 1) * 128],
                                                 pT[:, j * 128:(j + 1) * 128])
                        return last
                    p.op("dve", evq, reads=[pT_buf], writes=[qfm_buf])
                elif colq < 8:
                    p.op("dve", lambda v, tt=tt, colq=colq, hap=hap: v.tensor_copy(
                         ktok[:, tt * C + (colq - 4) * 256:tt * C + (colq - 3) * 256], hap),
                         reads=[pS_buf], writes=[ktok_buf])
                else:
                    p.op("dve", lambda v, tt=tt, colq=colq, hap=hap: v.tensor_copy(
                         vtok[:, tt * C + (colq - 8) * 256:tt * C + (colq - 7) * 256], hap),
                         reads=[pS_buf], writes=[vtok_buf])
        clm_t, clm_b = consts["l0clm"]
        for tt in range(NT):
            def mml(t, tt=tt):
                for c in range(NCH):
                    last = t.matmul(pO[:, 0:64], xlnT[:, c * T + tt * 128:c * T + (tt + 1) * 128],
                                    wasT[:, c * 64:(c + 1) * 64], start=(c == 0), stop=(c == NCH - 1))
                return last
            p.op("pe", mml, reads=[xlnT_buf, wasT_buf], writes=[pO_buf])
            p.op("dve", lambda v, tt=tt: v.tensor_mul(lgx[:, :], pO[:, 0:64],
                 clm_t[:, tt * 64:(tt + 1) * 64]), reads=[pO_buf, clm_b], writes=[lgx_buf])
            p.op("dve", lambda v: v.tensor_reduce(lcol[:, 0:1], lgx[:, :], AX.X, ALU.add),
                 reads=[lgx_buf], writes=[lcol_buf])
            p.op("pe", lambda t: t.matmul(pT[0:1, 0:128], lcol[:, 0:1], identf[:, :],
                 start=True, stop=True), reads=[lcol_buf, identf_buf], writes=[pT_buf])
            p.op("dve", lambda v: v.tensor_copy(lrow[0:1, 0:128], pT[0:1, 0:128]),
                 reads=[pT_buf], writes=[lrow_buf])
            p.op("dve", lambda v: v.tensor_reduce(lsc[0:1, 0:1], lrow[0:1, 0:128], AX.X, ALU.max,
                 negate=True), reads=[lrow_buf], writes=[lsc_buf])
            p.op("pe", lambda t: t.matmul(pT[:, 0:1], onesf[0:1, :], lsc[0:1, 0:1],
                 start=True, stop=True), reads=[onesf_buf, lsc_buf], writes=[pT_buf])
            p.op("dve", lambda v: v.tensor_copy(lsc[:, 1:2], pT[:, 0:1]),
                 reads=[pT_buf], writes=[lsc_buf])
            p.op("act", lambda s2: s2.activation(ecol[:, :], lcol[:, 0:1], AF.Exp,
                 bias=lsc[:, 1:2]), reads=[lcol_buf, lsc_buf], writes=[ecol_buf])
            p.op("pe", lambda t: t.matmul(pT[0:cpt, 0:1], segb[:, :], ecol[:, :],
                 start=True, stop=True), reads=[segb_buf, ecol_buf], writes=[pT_buf])
            p.op("dve", lambda v: v.tensor_copy(rsum[:, 0:1], pT[0:cpt, 0:1]),
                 reads=[pT_buf], writes=[rsum_buf])
            p.op("dve", lambda v: v.reciprocal(rsum[:, 1:2], rsum[:, 0:1]),
                 reads=[rsum_buf], writes=[rsum_buf])
            p.op("dve", lambda v: v.tensor_copy(rsumb[:, 0:1], rsum[:, 1:2]),
                 reads=[rsum_buf], writes=[rsumb_buf])
            p.op("pe", lambda t: t.matmul(pT[:, 0:1], segTb[:, :], rsumb[:, 0:1],
                 start=True, stop=True), reads=[segT_buf, rsumb_buf], writes=[pT_buf])
            p.op("dve", lambda v: v.scalar_tensor_tensor(wcol[:, :], pT[:, 0:1], 1.0, ecol[:, :],
                 ALU.mult, ALU.mult), reads=[pT_buf, ecol_buf], writes=[wcol_buf])
            p.op("dve", lambda v, tt=tt: v.tensor_scalar(Wseg[:, tt * cpt:(tt + 1) * cpt],
                 segb[:, :], wcol[:, 0:1], None, ALU.mult),
                 reads=[segb_buf, wcol_buf], writes=[Wseg_buf])
        for tt in range(NT):
            def mmck(t, tt=tt):
                t.matmul(pT[0:cpt, 0:512], Wseg[:, tt * cpt:(tt + 1) * cpt],
                         ktok[:, tt * C:tt * C + 512], start=True, stop=True)
                t.matmul(pT[0:cpt, 512:1024], Wseg[:, tt * cpt:(tt + 1) * cpt],
                         ktok[:, tt * C + 512:(tt + 1) * C], start=True, stop=True)
                t.matmul(pO[0:cpt, 0:512], Wseg[:, tt * cpt:(tt + 1) * cpt],
                         vtok[:, tt * C:tt * C + 512], start=True, stop=True)
                return t.matmul(pO[0:cpt, 512:1024], Wseg[:, tt * cpt:(tt + 1) * cpt],
                                vtok[:, tt * C + 512:(tt + 1) * C], start=True, stop=True)
            p.op("pe", mmck, reads=[Wseg_buf, ktok_buf, vtok_buf], writes=[pT_buf, pO_buf])
            p.op("dve", lambda v: v.tensor_copy(cks[:, :], pT[0:cpt, 0:C]),
                 reads=[pT_buf], writes=[cks_buf])
            p.op("dve", lambda v: v.tensor_copy(cvs[:, :], pO[0:cpt, 0:C]),
                 reads=[pO_buf], writes=[cvs_buf])
            p.dma("sp", ck_dram[tt * cpt:(tt + 1) * cpt, :], cks[:, :], reads=[cks_buf],
                  writes=[ck_dram_buf], sem=3)
            p.dma("sp", cv_dram[tt * cpt:(tt + 1) * cpt, :], cvs[:, :], reads=[cvs_buf],
                  writes=[cv_dram_buf], sem=4)
        klfm = env.sb["l0xlnT"]
        klfm_buf = env.alias_buf(xlnT_buf, "l0klfm")
        kls_t, kls_b = consts["l0ks"]; klc_t, klc_b = consts["l0kc"]
        if "l0krot" not in env.sb:
            env.sbuf("l0krot", [128, C], BF16)
        krot, krot_buf = env.sb["l0krot"], env.buf("l0krot")
        for tt in range(NT):
            for hh in range(H):
                p.op("dve", lambda v, hh=hh, tt=tt: v.tensor_mul(
                     krot[:, hh * 64 + 1:hh * 64 + 64:2], ktok[:, tt * C + hh * 64:tt * C + hh * 64 + 64:2],
                     kls_t[:, tt * 32:(tt + 1) * 32]),
                     reads=[ktok_buf, kls_b], writes=[krot_buf])
                p.op("dve", lambda v, hh=hh, tt=tt: v.tensor_mul(
                     krot[:, hh * 64:hh * 64 + 64:2], ktok[:, tt * C + hh * 64 + 1:tt * C + hh * 64 + 64:2],
                     klc_t[:, tt * 32:(tt + 1) * 32]),
                     reads=[ktok_buf, klc_b], writes=[krot_buf])
            p.op("dve", lambda v: v.tensor_scalar(krot[:, 0:C:2], krot[:, 0:C:2], -1.0, None,
                 ALU.mult), reads=[krot_buf], writes=[krot_buf])

            def mmtk(t, tt=tt):
                for j in range(NCH):
                    last = t.matmul(pT[:, (j % 8) * 128:(j % 8) * 128 + 128],
                                    krot[:, j * 128:(j + 1) * 128], ident[:, :],
                                    start=True, stop=True)
                return last
            p.op("pe", mmtk, reads=[krot_buf, identb_buf], writes=[pT_buf])

            def evk(v, tt=tt):
                for j in range(NCH):
                    last = v.tensor_copy(klfm[:, j * T + tt * 128:j * T + (tt + 1) * 128],
                                         pT[:, (j % 8) * 128:(j % 8) * 128 + 128])
                return last
            p.op("dve", evk, reads=[pT_buf], writes=[klfm_buf])
        ckt = ktok[0:64, 0:C]
        cvt = ktok[0:64, C:2 * C]
        ckr = ktok[0:64, 2 * C:3 * C]
        ckt_buf = env.alias_buf(ktok_buf, "l0ckt")
        cvt_buf = ckt_buf
        ckr_buf = ckt_buf
        p.dma("pool", ckt[:, :], ck_dram, reads=[ck_dram_buf], writes=[ckt_buf])
        p.dma("pool", cvt[:, :], cv_dram, reads=[cv_dram_buf], writes=[cvt_buf])
        cs_t, cs_b = consts["l0cs"]; cc_t, cc_b = consts["l0cc"]
        for hh in range(H):
            p.op("dve", lambda v, hh=hh: v.tensor_mul(ckr[:, hh * 64 + 1:hh * 64 + 64:2],
                 ckt[:, hh * 64:hh * 64 + 64:2], cs_t[:, :]),
                 reads=[ckt_buf, cs_b], writes=[ckr_buf])
            p.op("dve", lambda v, hh=hh: v.tensor_mul(ckr[:, hh * 64:hh * 64 + 64:2],
                 ckt[:, hh * 64 + 1:hh * 64 + 64:2], cc_t[:, :]),
                 reads=[ckt_buf, cc_b], writes=[ckr_buf])
        p.op("dve", lambda v: v.tensor_scalar(ckr[:, 0:C:2], ckr[:, 0:C:2], -1.0, None, ALU.mult),
             reads=[ckr_buf], writes=[ckr_buf])
        for hh in range(H):
            p.op("pe", lambda t, hh=hh: t.matmul(pT[0:64, 0:64], ckr[:, hh * 64:(hh + 1) * 64],
                 ident[0:64, 0:64], start=True, stop=True),
                 reads=[ckr_buf, identb_buf], writes=[pT_buf])
            p.op("dve", lambda v, hh=hh: v.tensor_copy(kcfm[0:64, hh * 64:(hh + 1) * 64],
                 pT[0:64, 0:64]), reads=[pT_buf], writes=[kcfm_buf])
            p.op("dve", lambda v, hh=hh: v.tensor_copy(kcfm[64:128, hh * 64:(hh + 1) * 64],
                 pT[0:64, 0:64]), reads=[pT_buf], writes=[kcfm_buf])
        Es = [qt_tmp[:, 0:320], qr_tmp[:, 0:320]]
        Es_buf = [env.alias_buf(qt_tmp_buf, "l0E0"), env.alias_buf(qr_tmp_buf, "l0E1")]
        ETs = krot[:, 0:512]
        ETs_buf = env.alias_buf(krot_buf, "l0ET")
        dgl = [diag[0][:, :], diag[1][:, :]]
        dgl_buf = [env.alias_buf(bufs["dg"][0], "l0dl0"), env.alias_buf(bufs["dg"][1], "l0dl1")]
        ysb = xcb[0][:, :]
        ysb_buf = env.alias_buf(bufs["xc"][0], "l0y")
        lm_t, lm_b = consts["l0lm"]
        for g in range(G):
            ncl_vis = g * ncpg
            for qt2 in range(TPG):
                qt = g * TPG + qt2
                for h in range(H):
                    hp = 64 * (h % 2)
                    hc = h // 2
                    width = ncl_vis + GS

                    def mmsc(t, g=g, qt=qt, h=h, hp=hp, hc=hc, ncl_vis=ncl_vis):
                        if ncl_vis > 0:
                            t.matmul(pS[:, 0:ncl_vis],
                                     qfm[hp:hp + 64, hc * T + qt * 128:hc * T + (qt + 1) * 128],
                                     kcfm[hp:hp + 64, h * 64:h * 64 + ncl_vis],
                                     start=True, stop=True)
                        for j in range(TPG):
                            last = t.matmul(pS[:, ncl_vis + j * 128:ncl_vis + (j + 1) * 128],
                                            qfm[hp:hp + 64, hc * T + qt * 128:hc * T + (qt + 1) * 128],
                                            klfm[hp:hp + 64, hc * T + g * GS + j * 128:hc * T + g * GS + (j + 1) * 128],
                                            start=True, stop=True)
                        return last
                    p.op("pe", mmsc, reads=[qfm_buf, kcfm_buf, klfm_buf], writes=[pS_buf])
                    par = (qt * H + h) % 2
                    p.op("dve", lambda v, ncl_vis=ncl_vis, qt2=qt2: v.tensor_tensor(
                         pS[:, ncl_vis:ncl_vis + GS], pS[:, ncl_vis:ncl_vis + GS],
                         lm_t[:, qt2 * GS:(qt2 + 1) * GS], ALU.add),
                         reads=[lm_b], writes=[pS_buf])
                    p.op("dve", lambda v, width=width, par=par: v.tensor_reduce(
                         st2[par][:, 0:1], pS[:, 0:width], AX.X, ALU.max, negate=True),
                         reads=[pS_buf], writes=[st2_buf[par]])
                    p.op("act", lambda s2, width=width, par=par: s2.activation(
                         Es[par][:, 0:width], pS[:, 0:width], AF.Exp, bias=st2[par][:, 0:1],
                         accum_out=st2[par][:, 1:2]),
                         reads=[pS_buf, st2_buf[par]], writes=[Es_buf[par], st2_buf[par]])
                    p.op("dve", lambda v, par=par: v.reciprocal(st2[par][:, 2:3], st2[par][:, 1:2]),
                         reads=[st2_buf[par]], writes=[st2_buf[par]])
                    p.op("dve", lambda v, par=par: v.tensor_scalar(
                         dgl[par][:, :], ident[:, :], st2[par][:, 2:3], None, ALU.mult),
                         reads=[identb_buf, st2_buf[par]], writes=[dgl_buf[par]])
                    def mmtE(t, ncl_vis=ncl_vis, par=par):
                        o = 0
                        if ncl_vis > 0:
                            t.matmul(pT[0:ncl_vis, 0:128], Es[par][:, 0:ncl_vis], dgl[par][:, :],
                                     start=True, stop=True)
                        for j in range(TPG):
                            last = t.matmul(pO[0:128, j * 128:(j + 1) * 128],
                                            Es[par][:, ncl_vis + j * 128:ncl_vis + (j + 1) * 128],
                                            dgl[par][:, :], start=True, stop=True)
                        return last
                    p.op("pe", mmtE, reads=[Es_buf[par], dgl_buf[par]], writes=[pT_buf, pO_buf])

                    def evE(v, ncl_vis=ncl_vis):
                        if ncl_vis > 0:
                            v.tensor_copy(ETs[:ncl_vis, 0:128], pT[0:ncl_vis, 0:128])
                        return v.tensor_copy(ETs[:, 128:128 + TPG * 128], pO[:, 0:TPG * 128])
                    p.op("dve", evE, reads=[pT_buf, pO_buf], writes=[ETs_buf])

                    def mmpv(t, g=g, h=h, ncl_vis=ncl_vis):
                        first = True
                        if ncl_vis > 0:
                            t.matmul(pT[0:64, 896:1024], cvt[0:ncl_vis, h * 64:(h + 1) * 64],
                                     ETs[0:ncl_vis, 0:128], start=True, stop=False)
                            first = False
                        for j in range(TPG):
                            ksg = g * TPG + j
                            last = t.matmul(pT[0:64, 896:1024],
                                            vtok[:, ksg * C + h * 64:ksg * C + (h + 1) * 64],
                                            ETs[:, 128 + j * 128:128 + (j + 1) * 128],
                                            start=first, stop=(j == TPG - 1))
                            first = False
                        return last
                    p.op("pe", mmpv, reads=[cvt_buf, vtok_buf, ETs_buf], writes=[pT_buf])
                    p.op("dve", lambda v, hp=hp, hc=hc: v.tensor_copy(
                         ysb[hp:hp + 64, hc * 128:(hc + 1) * 128], pT[0:64, 896:1024]),
                         reads=[pT_buf], writes=[ysb_buf])
                sl = qt % NS
                p.dma("sp", xt[sl][:, :], x_in[base + qt * 128: base + (qt + 1) * 128, :],
                      writes=[bufs["xt"][sl]], sem=qt % 2)
                for half in range(2):
                    def mmo(t, half=half):
                        for cc in range(NCH):
                            t.matmul(pO[:, half * 512:(half + 1) * 512],
                                     ysb[:, cc * 128:(cc + 1) * 128],
                                     wpT[:, cc * C + half * 512:cc * C + (half + 1) * 512],
                                     start=(cc == 0), stop=False)
                        return t.matmul(pO[:, half * 512:(half + 1) * 512], ones[0:1, :],
                                        bp_row[0:1, half * 512:(half + 1) * 512],
                                        start=False, stop=True)
                    p.op("pe", mmo, reads=[ysb_buf, wpT_buf, ones_buf, bp_buf], writes=[pO_buf])
                p.op("dve", lambda v, sl=sl: v.tensor_add(xo[sl][:, :], pO[:, 0:C], xt[sl][:, :]),
                     reads=[pO_buf, bufs["xt"][sl]], writes=[bufs["xo"][sl]])
                p.dma("sp", x_out[base + qt * 128: base + (qt + 1) * 128, :], xo[sl][:, :],
                      reads=[bufs["xo"][sl]], sem=5)
        xlnT_buf.w = klfm_buf.w
        xlnT_buf.rs = list(klfm_buf.rs)
        ktok_buf.w = ckt_buf.w
        ktok_buf.rs = list(ckt_buf.rs)
        qt_tmp_buf.w = Es_buf[0].w
        qt_tmp_buf.rs = list(Es_buf[0].rs)
        qr_tmp_buf.w = Es_buf[1].w
        qr_tmp_buf.rs = list(Es_buf[1].rs)
        krot_buf.w = ETs_buf.w
        krot_buf.rs = list(ETs_buf.rs)
        for i2 in range(2):
            bufs["dg"][i2].w = dgl_buf[i2].w
            bufs["dg"][i2].rs = list(dgl_buf[i2].rs)
        bufs["xc"][0].w = ysb_buf.w
        bufs["xc"][0].rs = list(ysb_buf.rs)


BLOCK_WEIGHTS = {
    "L0_ffn": {"W": [C, C], "b": [C], "lora_down": [4, 100], "lora_up": [50, 4],
               "down_aux": [100, C], "up_aux": [C, 50]},
    "L1_ffn": {"W1": [C, 4096], "b1": [4096], "quantum_phases": [4096],
               "quantum_weights_real": [4096], "quantum_weights_imag": [4096],
               "W2": [4096, C], "b2": [C]},
    "L2_attn": {"Wq": [C, C], "Wk": [C, C], "Wv": [C, C], "Wo": [C, C]},
    "L1_attn": {"Wqkv": [C, 3 * C], "Wout": [C, C]},
    "L3_attn": {"wq": [C, C], "bq": [C], "wk": [C, C], "bk": [C],
                "wv": [C, C], "bv": [C], "wo": [C, C], "bo": [C]},
    "L2_ffn": {"gate_weight": [32, C], "e_score_correction_bias": [32],
               "experts_gate_up": [32, C, C], "experts_down": [32, C, 512],
               "shared_gate_w": [C, C], "shared_up_w": [C, C], "shared_down_w": [C, C]},
    "L3_ffn": {"Wg": [C, 8], "W1": [8, C, 4096], "b1": [8, 4096],
               "W2": [8, 4096, C], "b2": [8, C]},
    "L0_attn": {"w_attn": [3 * C, C], "b_attn": [3 * C], "w_proj": [C, C],
                "b_proj": [C], "w_assign": [64, C]},
}

EMITTERS = {"L0_ffn": emit_L0_ffn, "L1_ffn": emit_L1_ffn, "L2_attn": None,
            "L2_ffn": emit_L2_ffn}

PSUM_SPECS = {"L2_attn": [("pS", 2048), ("pT", 1024), ("pO", 1024)],
              "L1_attn": [("pS", 2048), ("pT", 1024), ("pO", 1024)],
              "L3_attn": [("pS", 2048), ("pT", 1024), ("pO", 1024)],
              "L0_attn": [("pS", 1024), ("pT", 1024), ("pO", 1024)]}


def build_single_block(name, ntok, T=2048, chain=False, nglob=None):
    nc = bass.Bass("TRN2", target_bir_lowering=False)
    ctx = ExitStack()
    with ctx:
        x_in = nc.dram_tensor("x_in", [ntok, C], F32, kind="ExternalInput").ap()
        x_out = nc.dram_tensor("x_out", [ntok, C], F32, kind="ExternalOutput").ap()
        W = {}
        for wname, shape in BLOCK_WEIGHTS[name].items():
            W[wname] = nc.dram_tensor(wname, shape, F32, kind="ExternalInput").ap()
        p = Prog(nc, ctx)
        env = Env(nc, ctx, p, psum_spec=PSUM_SPECS.get(name))
        setup_common_consts(env)
        if name == "L3_attn" and chain:
            WG = nc.dram_tensor("Wg", [C, 8], F32, kind="ExternalInput").ap()
            S_tok_o = nc.dram_tensor("S_tok_o", [ntok, 8], F32, kind="ExternalOutput").ap()
            S_T_o = nc.dram_tensor("S_T_o", [8, ntok], F32, kind="ExternalOutput").ap()
            emit_L3_attn(p, env, x_in, x_out, W, ntok, T=T)
            p.barrier()
            carve_reg = env.sb["a_qfm"].bitcast(F32)
            carve_buf = env.alias_buf(env.buf("a_qfm"), "s_carve")
            emit_L3_scores(p, env, x_out, {"Wg": WG}, S_tok_o, ntok, s_t_out=S_T_o,
                           carve=(carve_reg, carve_buf))
        elif name == "L3_ffn" and chain:
            ngl = nglob or ntok
            S_t_in = nc.dram_tensor("S_t_in", [8, ngl], F32, kind="ExternalInput").ap()
            S_tok_in = nc.dram_tensor("S_tok_in", [ntok, 8], F32, kind="ExternalInput").ap()
            emit_L3_ffn(p, env, x_in, x_out, W, ntok, S_t_in, S_tok_in, ngl)
        elif name == "L3_ffn":
            S_tok_d = nc.dram_tensor("S_tok_d", [ntok, 8], F32).ap()
            S_T_d = nc.dram_tensor("S_T_d", [8, ntok], F32).ap()
            emit_L3_scores(p, env, x_in, W, S_tok_d, ntok, s_t_out=S_T_d)
            p.barrier()
            emit_L3_ffn(p, env, x_in, x_out, W, ntok, S_T_d, S_tok_d, ntok)
        elif name == "L2_attn":
            emit_L2_attn(p, env, x_in, x_out, W, ntok, T=T)
        elif name == "L1_attn":
            emit_L1_attn(p, env, x_in, x_out, W, ntok, T=T)
        elif name == "L3_attn":
            emit_L3_attn(p, env, x_in, x_out, W, ntok, T=T)
        elif name == "L0_attn":
            emit_L0_attn(p, env, x_in, x_out, W, ntok, T=T)
        else:
            EMITTERS[name](p, env, x_in, x_out, W, ntok)
        p.wait("sp", p.all_tokens())
        p.emit()
    return nc, env.host_consts



BLOCK_SEQ_ALL = ["L0_attn", "L0_ffn", "L1_attn", "L1_ffn",
                 "L2_attn", "L2_ffn", "L3_attn", "L3_ffn"]

DEFAULT_PSUM = [("pA", 1024), ("pB", 1024), ("pC", 1024), ("pD", 1024)]


def build_fused(ntok=8192, T=2048, nglob=65536, ncores=8, mode="single"):
    nc = bass.Bass("TRN2", target_bir_lowering=False)
    ctx = ExitStack()
    with ctx:
        x_in = nc.dram_tensor("x_in", [ntok, C], F32, kind="ExternalInput").ap()
        x_out = nc.dram_tensor("x_out", [ntok, C], F32, kind="ExternalOutput").ap()
        blocks = BLOCK_SEQ_ALL if mode == "single" else BLOCK_SEQ_ALL[:7]
        W = {}
        for blk in BLOCK_SEQ_ALL:
            W[blk] = {}
            for wname, shape in BLOCK_WEIGHTS[blk].items():
                if blk == "L3_ffn" and mode == "A" and wname != "Wg":
                    continue
                W[blk][wname] = nc.dram_tensor(f"{blk}_{wname}", shape, F32,
                                               kind="ExternalInput").ap()
        xmid = [nc.dram_tensor(f"xmid{i}", [ntok, C], F32).ap() for i in range(7)]
        if mode == "A":
            S_tok_d = nc.dram_tensor("S_tok_o", [ntok, 8], F32, kind="ExternalOutput").ap()
            S_T_d = nc.dram_tensor("S_T_o", [8, ntok], F32, kind="ExternalOutput")
        else:
            S_tok_d = nc.dram_tensor("S_tok_d", [ntok, 8], F32).ap()
            S_T_d = nc.dram_tensor("S_T_d", [8, ntok], F32)
        S_g_d = nc.dram_tensor("S_g_d", [8 * ncores, ntok], F32)
        p = Prog(nc, ctx)
        env = Env(nc, ctx, p, psum_spec=False)
        setup_common_consts(env)
        coll_tok = None
        last_bi = len(blocks) - 1
        for bi, name in enumerate(blocks):
            xi = x_in if bi == 0 else xmid[bi - 1]
            xo = x_out if bi == last_bi else xmid[bi]
            bctx = ExitStack()
            with bctx:
                env.set_block_ctx(bctx)
                env.begin_block(f"B{bi}", PSUM_SPECS.get(name, DEFAULT_PSUM), bctx)
                if name == "L0_attn":
                    emit_L0_attn(p, env, xi, xo, W[name], ntok, T=T)
                elif name == "L0_ffn":
                    emit_L0_ffn(p, env, xi, xo, W[name], ntok)
                elif name == "L1_attn":
                    emit_L1_attn(p, env, xi, xo, W[name], ntok, T=T)
                elif name == "L1_ffn":
                    emit_L1_ffn(p, env, xi, xo, W[name], ntok)
                elif name == "L2_attn":
                    emit_L2_attn(p, env, xi, xo, W[name], ntok, T=T)
                elif name == "L2_ffn":
                    emit_L2_ffn(p, env, xi, xo, W[name], ntok)
                elif name == "L3_attn":
                    emit_L3_attn(p, env, xi, xo, W[name], ntok, T=T)
                    p.barrier()
                    carve_reg = env.sb["a_qfm"].bitcast(F32)
                    carve_buf = env.alias_buf(env.buf("a_qfm"), "s_carve")
                    emit_L3_scores(p, env, xo, {"Wg": W["L3_ffn"]["Wg"]}, S_tok_d,
                                   ntok, s_t_out=S_T_d.ap(),
                                   carve=(carve_reg, carve_buf))
                elif name == "L3_ffn":
                    def coll(g):
                        return g.collective_compute(
                            "AllGather", ALU.bypass,
                            replica_groups=[list(range(ncores))],
                            ins=[S_T_d.ap().opt()],
                            outs=[S_g_d.ap().opt()])
                    coll_tok = p.collective(coll)
                    emit_L3_ffn(p, env, xi, xo, W[name], ntok, S_g_d.ap(), S_tok_d,
                                nglob, S_deps=[coll_tok])
                env.restore_ctx()
            p.barrier()
        p.wait("sp", p.all_tokens())
        p.emit()
    return nc, env.host_consts


from concourse.bass_utils import run_bass_kernel_spmd

ALL_INPUT_NAMES = (
    "L0_attn_x", "L0_attn_w_attn", "L0_attn_b_attn", "L0_attn_w_proj",
    "L0_attn_b_proj", "L0_attn_w_assign", "L0_ffn_W", "L0_ffn_b",
    "L0_ffn_lora_down", "L0_ffn_lora_up", "L0_ffn_down_aux", "L0_ffn_up_aux",
    "L1_attn_Wqkv", "L1_attn_Wout", "L1_ffn_W1", "L1_ffn_b1",
    "L1_ffn_quantum_phases", "L1_ffn_quantum_weights_real",
    "L1_ffn_quantum_weights_imag", "L1_ffn_W2", "L1_ffn_b2",
    "L2_attn_Wq", "L2_attn_Wk", "L2_attn_Wv", "L2_attn_Wo",
    "L2_ffn_gate_weight", "L2_ffn_e_score_correction_bias",
    "L2_ffn_experts_gate_up", "L2_ffn_experts_down", "L2_ffn_shared_gate_w",
    "L2_ffn_shared_up_w", "L2_ffn_shared_down_w",
    "L3_attn_wq", "L3_attn_bq", "L3_attn_wk", "L3_attn_bk", "L3_attn_wv",
    "L3_attn_bv", "L3_attn_wo", "L3_attn_bo",
    "L3_ffn_Wg", "L3_ffn_W1", "L3_ffn_b1", "L3_ffn_W2", "L3_ffn_b2",
)

NCORES = 8
NTOK_CORE = 8192
NGLOB = NCORES * NTOK_CORE

_cacheA = {}


def kernel(**inputs):
    assert all(k in inputs for k in ALL_INPUT_NAMES)
    if "A" not in _cacheA:
        _cacheA["A"] = build_fused(NTOK_CORE, 2048, NGLOB, NCORES, mode="A")
        _cacheA["B"] = build_single_block("L3_ffn", NTOK_CORE, T=2048, chain=True,
                                          nglob=NGLOB)
    ncA, constsA = _cacheA["A"]
    ncB, constsB = _cacheA["B"]
    x = np.ascontiguousarray(inputs["L0_attn_x"]).astype(np.float32, copy=False)
    base = {}
    for blk, wd in BLOCK_WEIGHTS.items():
        for wname in wd:
            if blk == "L3_ffn" and wname != "Wg":
                continue
            key = f"{blk}_{wname}"
            base[key] = np.ascontiguousarray(inputs[key]).astype(np.float32, copy=False)
    for k, v in constsA.items():
        base["cst_" + k] = np.asarray(v)
    in_maps = []
    for c in range(NCORES):
        m = dict(base)
        m["x_in"] = np.ascontiguousarray(x[c * 4:(c + 1) * 4].reshape(NTOK_CORE, C))
        in_maps.append(m)
    resA = run_bass_kernel_spmd(ncA, in_maps, core_ids=list(range(NCORES)))
    x7 = [np.asarray(resA.results[c]["x_out"]) for c in range(NCORES)]
    S_tok = [np.asarray(resA.results[c]["S_tok_o"]) for c in range(NCORES)]
    S_T_full = np.ascontiguousarray(
        np.concatenate([np.asarray(resA.results[c]["S_T_o"]) for c in range(NCORES)], axis=1))
    baseB = {}
    for wname in BLOCK_WEIGHTS["L3_ffn"]:
        baseB[wname] = np.ascontiguousarray(inputs[f"L3_ffn_{wname}"]).astype(np.float32, copy=False)
    for k, v in constsB.items():
        baseB["cst_" + k] = np.asarray(v)
    in_mapsB = []
    for c in range(NCORES):
        m = dict(baseB)
        m["x_in"] = x7[c]
        m["S_t_in"] = S_T_full
        m["S_tok_in"] = S_tok[c]
        in_mapsB.append(m)
    resB = run_bass_kernel_spmd(ncB, in_mapsB, core_ids=list(range(NCORES)))
    out = np.stack([np.asarray(resB.results[c]["x_out"]) for c in range(NCORES)])
    return out.reshape(32, 2048, C).astype(np.float32, copy=False)


# version 08 (final)
# speedup vs baseline: 1.0795x; 1.0725x over previous
from contextlib import ExitStack
import math
import numpy as np
import ml_dtypes
import concourse.bass as bass
import concourse.mybir as mybir

F32 = mybir.dt.float32
BF16 = mybir.dt.bfloat16
AF = mybir.ActivationFunctionType
ALU = mybir.AluOpType
AX = mybir.AxisListType

ENGMAP = {"pe": "tensor", "dve": "vector", "act": "scalar", "pool": "gpsimd", "sp": "sync"}
import os
DBG_L1_SIMPLE = os.environ.get("DBG_L1_SIMPLE")
C = 1024
NCH = 8


class Buf:
    __slots__ = ("w", "rs", "name")

    def __init__(self, name=""):
        self.w = None
        self.rs = []
        self.name = name


class Prog:
    def __init__(self, nc, ctx: ExitStack, n_sp_sems=6, n_pool_sems=6):
        self.nc, self.ctx = nc, ctx
        self.items = {k: [] for k in ENGMAP}
        self.csem = {k: ctx.enter_context(nc.semaphore("c_" + k))
                     for k in ("pe", "dve", "act", "pool")}
        self.ccnt = {k: 0 for k in ("pe", "dve", "act", "pool")}
        self.dsems = {"sp": [ctx.enter_context(nc.semaphore(f"d_sp{i}")) for i in range(n_sp_sems)],
                      "pool": [ctx.enter_context(nc.semaphore(f"d_pl{i}")) for i in range(n_pool_sems)]}
        self.dcnt = {("sp", i): 0 for i in range(n_sp_sems)}
        self.dcnt.update({("pool", i): 0 for i in range(n_pool_sems)})
        self.drr = {"sp": 0, "pool": 0}

    def _wire(self, stream, entry_token, reads, writes, deps):
        alldeps = list(deps)
        for r in reads:
            if r.w is not None:
                alldeps.append(r.w)
        for w in writes:
            if w.w is not None:
                alldeps.append(w.w)
            alldeps.extend(w.rs)
        for r in reads:
            r.rs.append(entry_token)
        for w in writes:
            w.w = entry_token
            w.rs = []
        return alldeps

    def op(self, stream, fn, reads=(), writes=(), deps=()):
        self.ccnt[stream] += 1
        tok = (self.csem[stream], self.ccnt[stream])
        alldeps = self._wire(stream, tok, reads, writes, deps)
        self.items[stream].append((fn, alldeps, tok[0], tok[1]))
        return tok

    def dma(self, stream, out, in_, reads=(), writes=(), deps=(), sem=None, **kw):
        i = sem if sem is not None else self.drr[stream] % len(self.dsems[stream])
        if sem is None:
            self.drr[stream] += 1
        prev = self.dcnt[(stream, i)]
        self.dcnt[(stream, i)] += 16
        tok = (self.dsems[stream][i], self.dcnt[(stream, i)])
        deps = list(deps)
        if prev > 0:
            deps.append((self.dsems[stream][i], prev))
        alldeps = self._wire(stream, tok, reads, writes, deps)
        fn = lambda eng, out=out, in_=in_, kw=kw: eng.dma_start(out=out, in_=in_, **kw)
        self.items[stream].append((fn, alldeps, tok[0], tok[1]))
        return tok

    def collective(self, fn, reads=(), writes=(), deps=()):
        i = self.drr["pool"] % len(self.dsems["pool"])
        self.drr["pool"] += 1
        prev = self.dcnt[("pool", i)]
        self.dcnt[("pool", i)] += 16
        tok = (self.dsems["pool"][i], self.dcnt[("pool", i)])
        deps = list(deps)
        if prev > 0:
            deps.append((self.dsems["pool"][i], prev))
        alldeps = self._wire("pool", tok, reads, writes, deps)
        self.items["pool"].append((fn, alldeps, tok[0], tok[1]))
        return tok

    def all_tokens(self):
        toks = [(self.csem[k], self.ccnt[k]) for k in self.csem if self.ccnt[k] > 0]
        for (s, i), c in self.dcnt.items():
            if c > 0:
                toks.append((self.dsems[s][i], c))
        return toks

    def barrier(self):
        toks = self.all_tokens()
        for stream in self.items:
            self.items[stream].append((None, toks, None, None))

    def wait(self, stream, deps):
        self.items[stream].append((None, list(deps), None, None))

    def emit(self):
        dma_semset = set()
        for lst in self.dsems.values():
            dma_semset.update(id(s) for s in lst)
        with self.nc.Block() as block:
            for stream, items in self.items.items():
                if not items:
                    continue
                dec = getattr(block, ENGMAP[stream])

                def body(eng, _items=items):
                    lw = {}
                    for fn, deps, sem, val in _items:
                        for (dsem, dval) in deps:
                            key = id(dsem)
                            if lw.get(key, 0) < dval:
                                eng.wait_ge(dsem, dval)
                                lw[key] = dval
                        if fn is not None:
                            inst = fn(eng)
                            if isinstance(inst, (list, tuple)):
                                inst = inst[-1]
                            inst.then_inc(sem, 16 if id(sem) in dma_semset else 1)

                dec(body)


class Env:

    def __init__(self, nc, ctx, p: Prog, psum_spec=None):
        self.nc, self.ctx, self.p = nc, ctx, p
        self.host_consts = {}
        self.const_aps = {}
        self.const_bufs = {}
        self.prefix = ""
        self.psum = {}
        if psum_spec is not False:
            for name, wid in (psum_spec or [("pA", 1024), ("pB", 1024), ("pC", 1024), ("pD", 1024)]):
                h = ctx.enter_context(nc.psum_tensor(name, [128, wid], F32))
                self.psum[name] = h
                self.const_bufs[name] = Buf(name)
        self.sb = {}
        self.sbuf_bufs = {}

    def begin_block(self, prefix, psum_spec, block_ctx):
        self.prefix = prefix
        self.psum = {}
        for name, wid in psum_spec:
            h = block_ctx.enter_context(self.nc.psum_tensor(prefix + name, [128, wid], F32))
            self.psum[name] = h
            self.const_bufs[name] = Buf(prefix + name)

    def add_const(self, name, arr):
        if name in self.const_aps:
            return self.const_aps[name]
        self.host_consts[name] = arr
        dt = {np.dtype(np.float32): F32, np.dtype(ml_dtypes.bfloat16): BF16,
              np.dtype(np.int32): mybir.dt.int32}[arr.dtype]
        ap = self.nc.dram_tensor("cst_" + name, list(arr.shape), dt, kind="ExternalInput").ap()
        self.const_aps[name] = ap
        return ap

    SHARED_NAMES = {"identb", "onesb", "onescol", "identf", "m_onesf", "cmask"}

    def sbuf(self, name, shape, dt, ctx=None):
        pn = name if name in self.SHARED_NAMES else self.prefix + name
        h = (ctx or self.ctx).enter_context(self.nc.sbuf_tensor(pn, shape, dt))
        self.sb[name] = h
        self.sbuf_bufs[name] = Buf(pn)
        return h

    def set_block_ctx(self, ctx):
        self._saved_ctx = self.ctx
        self.ctx = ctx

    def restore_ctx(self):
        self.ctx = self._saved_ctx

    def buf(self, name):
        return self.sbuf_bufs.get(name) or self.const_bufs[name]

    def alias_buf(self, src_buf, name):
        b = Buf(name)
        b.w = src_buf.w
        b.rs = list(src_buf.rs)
        return b


def np_bf16(a):
    return np.asarray(a, dtype=ml_dtypes.bfloat16)



def setup_common_consts(env: Env):
    nc, p = env.nc, env.p
    idb = env.add_const("identb", np_bf16(np.eye(128)))
    ones = env.add_const("onesb", np_bf16(np.ones((1, 128))))
    onesc = env.add_const("onescol", np_bf16(np.ones((128, 1))))
    idf = env.add_const("identf", np.eye(128, dtype=np.float32))
    cm = env.add_const("cmask128", np.triu(np.full((128, 128), -1e30, np.float32), 1))
    env.sbuf("identb", [128, 128], BF16)
    env.sbuf("onesb", [1, 128], BF16)
    env.sbuf("onescol", [128, 1], BF16)
    env.sbuf("identf", [128, 128], F32)
    env.sbuf("cmask", [128, 128], F32)
    env.sbuf("m_onesf", [1, 128], F32)
    p.dma("sp", env.sb["identb"][:], idb, writes=[env.buf("identb")])
    p.dma("sp", env.sb["onesb"][:], ones, writes=[env.buf("onesb")])
    p.dma("sp", env.sb["onescol"][:], onesc, writes=[env.buf("onescol")])
    p.dma("sp", env.sb["identf"][:], idf, writes=[env.buf("identf")])
    p.dma("sp", env.sb["cmask"][:], cm, writes=[env.buf("cmask")])
    p.op("dve", lambda v: v.memset(env.sb["m_onesf"][:], 1.0), writes=[env.buf("m_onesf")])


def emit_ln_transpose(p, env, xt_ap, xt_buf, stats_ap, stats_buf, diag_ap, diag_buf,
                      xcb_ap, xcb_buf, sqd_ap, sqd_buf, pT_name, xlnT_ap, xlnT_buf,
                      n_t=128, out_f32_ap=None, out_f32_buf=None):
    nc = env.nc
    ident, identb_buf = env.sb["identb"], env.buf("identb")
    p.op("dve", lambda v: v.tensor_reduce(stats_ap[:, 0:1], xt_ap, AX.X, ALU.add, negate=True),
         reads=[xt_buf], writes=[stats_buf])
    p.op("dve", lambda v: v.tensor_scalar(stats_ap[:, 1:2], stats_ap[:, 0:1], 1.0 / C, None, ALU.mult),
         reads=[stats_buf], writes=[stats_buf])
    p.op("dve", lambda v: v.tensor_scalar(xcb_ap, xt_ap, stats_ap[:, 1:2], None, ALU.add),
         reads=[xt_buf, stats_buf], writes=[xcb_buf])
    p.op("act", lambda s: s.activation(sqd_ap, xcb_ap, AF.Square, accum_out=stats_ap[:, 2:3]),
         reads=[xcb_buf], writes=[sqd_buf, stats_buf])
    p.op("dve", lambda v: v.tensor_scalar(stats_ap[:, 3:4], stats_ap[:, 2:3], 1.0 / C, 1e-5, ALU.mult, ALU.add),
         reads=[stats_buf], writes=[stats_buf])
    p.op("act", lambda s: s.activation(stats_ap[:, 4:5], stats_ap[:, 3:4], AF.Sqrt),
         reads=[stats_buf], writes=[stats_buf])
    p.op("dve", lambda v: v.reciprocal(stats_ap[:, 5:6], stats_ap[:, 4:5]),
         reads=[stats_buf], writes=[stats_buf])
    p.op("dve", lambda v: v.tensor_scalar(diag_ap, ident[:n_t, :n_t], stats_ap[:, 5:6], None, ALU.mult),
         reads=[identb_buf, stats_buf], writes=[diag_buf])
    pT = env.psum[pT_name]
    pT_buf = env.buf(pT_name)

    def mm(t, xcb_ap=xcb_ap, diag_ap=diag_ap, pT=pT, n_t=n_t):
        last = None
        for c2 in range(NCH):
            last = t.matmul(pT[:, c2 * n_t:(c2 + 1) * n_t],
                            xcb_ap[:, c2 * 128:(c2 + 1) * 128], diag_ap,
                            start=True, stop=True)
        return last
    p.op("pe", mm, reads=[xcb_buf, diag_buf], writes=[pT_buf])
    if isinstance(xlnT_ap, list):
        def evac(v, pT=pT, n_t=n_t, xlnT_ap=xlnT_ap):
            last = None
            for c2 in range(NCH):
                last = v.tensor_copy(xlnT_ap[c2], pT[:, c2 * n_t:(c2 + 1) * n_t])
            return last
        p.op("dve", evac, reads=[pT_buf], writes=[xlnT_buf])
    else:
        p.op("dve", lambda v: v.tensor_copy(xlnT_ap, pT[:, 0:NCH * n_t]),
             reads=[pT_buf], writes=[xlnT_buf])
    if out_f32_ap is not None:
        p.op("act", lambda s: s.activation(out_f32_ap, xcb_ap, AF.Copy, scale=stats_ap[:, 5:6]),
             reads=[xcb_buf, stats_buf], writes=[out_f32_buf])


def emit_weight_T(p, env, w_hbm_ap, rows, cols, wb_name, wT_ap, wT_buf, pname="pC",
                  n_row_chunks=None, accum_into=None):
    nc = env.nc
    assert rows % 128 == 0 or rows <= 128
    assert cols % 128 == 0
    ncc = cols // 128
    nrc = (rows + 127) // 128
    wtmp = env.sbuf(wb_name, [128, cols], BF16)
    wtmp_bufs = [Buf(wb_name + str(i)) for i in range(2)]
    ident, identb_buf = env.sb["identb"], env.buf("identb")
    pT = env.psum[pname]
    pT_buf = env.buf(pname)
    for rc in range(nrc):
        r0 = rc * 128
        nr = min(128, rows - r0)
        slot = rc % 2
        b = wtmp_bufs[0]
        p.dma("pool", wtmp[:nr, :], w_hbm_ap[r0:r0 + nr, :], writes=[b])
        for cc in range(ncc):
            def mm(t, nr=nr, cc=cc, wtmp=wtmp):
                return t.matmul(pT[:, 0:nr], wtmp[:nr, cc * 128:(cc + 1) * 128],
                                ident[:nr, :nr], start=True, stop=True)
            p.op("pe", mm, reads=[b, identb_buf], writes=[pT_buf])
            dst = wT_ap[:, cc * rows + r0: cc * rows + r0 + nr]
            p.op("dve", lambda v, dst=dst, nr=nr: v.tensor_copy(dst, pT[:, 0:nr]),
                 reads=[pT_buf], writes=[wT_buf])
    return wtmp


def emit_fvec128(p, env, vec_hbm_ap, F, name, scale=None, bias=None):
    nc = env.nc
    ncv = F // 128
    tmp = env.sbuf(name + "_r", [max(ncv, 1), 128], F32)
    tmp_buf = Buf(name + "_r")
    out = env.sbuf(name, [128, ncv], F32)
    out_buf = env.buf(name)
    if "identf" not in env.sb:
        idf = env.add_const("identf", np.eye(128, dtype=np.float32))
        env.sbuf("identf", [128, 128], F32)
        p.dma("sp", env.sb["identf"][:], idf, writes=[env.buf("identf")])
    identf, identf_buf = env.sb["identf"], env.buf("identf")
    p.dma("sp", tmp[:, :], vec_hbm_ap.rearrange("(a p) -> a p", p=128), writes=[tmp_buf])
    pname = "pD" if "pD" in env.psum else list(env.psum)[-1]
    pT = env.psum[pname]
    pT_buf = env.buf(pname)
    p.op("pe", lambda t: t.matmul(pT[:, 0:ncv], tmp[:ncv, :], identf[:ncv, :ncv],
                                  start=True, stop=True),
         reads=[tmp_buf, identf_buf], writes=[pT_buf])
    if scale is None and bias is None:
        p.op("dve", lambda v: v.tensor_copy(out[:, :], pT[:, 0:ncv]),
             reads=[pT_buf], writes=[out_buf])
    else:
        s = 1.0 if scale is None else scale
        b = 0.0 if bias is None else bias
        p.op("dve", lambda v: v.tensor_scalar(out[:, :], pT[:, 0:ncv], s, b, ALU.mult, ALU.add),
             reads=[pT_buf], writes=[out_buf])
    return out, out_buf


def emit_row_load(p, env, vec_hbm_ap, F, name, dt=BF16):
    out = env.sbuf(name, [1, F], dt)
    out_buf = env.buf(name)
    stream = "pool" if dt == BF16 else "sp"
    p.dma(stream, out[0:1, :], vec_hbm_ap[None, :] if len(vec_hbm_ap.shape) == 1 else vec_hbm_ap,
          writes=[out_buf])
    return out, out_buf



def emit_L0_ffn(p, env, x_in, x_out, W, ntok):
    nc = env.nc
    ntiles = ntok // 128
    weffT = env.sbuf("weffT", [128, NCH * C], BF16)
    weffT_buf = env.buf("weffT")
    emit_weight_T(p, env, W["W"], C, C, "l0w_tmp", weffT[:, :], weffT_buf, pname="pC")
    ld = env.sbuf("ld", [4, 100], BF16); ld_buf = env.buf("ld")
    lu = env.sbuf("lu", [50, 4], BF16); lu_buf = env.buf("lu")
    da = env.sbuf("da", [100, C], BF16); da_buf = env.buf("da")
    ua = env.sbuf("ua", [128, 8 * 50], BF16); ua_buf = env.buf("ua")
    p.dma("pool", ld[:, :], W["lora_down"], writes=[ld_buf])
    p.dma("pool", lu[:, :], W["lora_up"], writes=[lu_buf])
    p.dma("pool", da[:, :], W["down_aux"], writes=[da_buf])
    for cc in range(8):
        p.dma("pool", ua[:, cc * 50:(cc + 1) * 50], W["up_aux"][cc * 128:(cc + 1) * 128, :],
              writes=[ua_buf])
    ident, identb_buf = env.sb["identb"], env.buf("identb")
    pC, pC_buf = env.psum["pC"], env.buf("pC")
    pD, pD_buf = env.psum["pD"], env.buf("pD")
    ldT = env.sbuf("ldT", [100, 4], BF16); ldT_buf = env.buf("ldT")
    p.op("pe", lambda t: t.matmul(pC[0:100, 0:4], ld[:, 0:100], ident[0:4, 0:4], start=True, stop=True),
         reads=[ld_buf, identb_buf], writes=[pC_buf])
    p.op("dve", lambda v: v.tensor_copy(ldT[:, :], pC[0:100, 0:4]), reads=[pC_buf], writes=[ldT_buf])
    A = env.sbuf("A", [4, C], BF16); A_buf = env.buf("A")
    def mmA(t):
        t.matmul(pD[0:4, 0:512], ldT[:, :], da[:, 0:512], start=True, stop=True)
        return t.matmul(pD[0:4, 512:1024], ldT[:, :], da[:, 512:1024], start=True, stop=True)
    p.op("pe", mmA, reads=[ldT_buf, da_buf], writes=[pD_buf])
    p.op("dve", lambda v: v.tensor_copy(A[:, :], pD[0:4, 0:1024]), reads=[pD_buf], writes=[A_buf])
    luT = env.sbuf("luT", [4, 50], BF16); luT_buf = env.buf("luT")
    p.op("pe", lambda t: t.matmul(pC[0:4, 0:50], lu[:, 0:4], ident[0:50, 0:50], start=True, stop=True),
         reads=[lu_buf, identb_buf], writes=[pC_buf])
    p.op("dve", lambda v: v.tensor_copy(luT[:, :], pC[0:4, 0:50]), reads=[pC_buf], writes=[luT_buf])
    Bm = env.sbuf("Bm", [50, C], BF16); Bm_buf = env.buf("Bm")
    def mmB(t):
        t.matmul(pD[0:50, 0:512], luT[:, :], A[:, 0:512], start=True, stop=True)
        return t.matmul(pD[0:50, 512:1024], luT[:, :], A[:, 512:1024], start=True, stop=True)
    p.op("pe", mmB, reads=[luT_buf, A_buf], writes=[pD_buf])
    p.op("dve", lambda v: v.tensor_copy(Bm[:, :], pD[0:50, 0:1024]), reads=[pD_buf], writes=[Bm_buf])
    uaT = env.sbuf("uaT", [50, C], BF16); uaT_buf = env.buf("uaT")
    for cc2 in range(8):
        p.op("pe", lambda t, cc2=cc2: t.matmul(pC[0:50, 0:128], ua[:, cc2 * 50:(cc2 + 1) * 50],
                                               ident[:, :], start=True, stop=True),
             reads=[ua_buf, identb_buf], writes=[pC_buf])
        p.op("dve", lambda v, cc2=cc2: v.tensor_copy(uaT[:, cc2 * 128:(cc2 + 1) * 128], pC[0:50, 0:128]),
             reads=[pC_buf], writes=[uaT_buf])
    for ic in range(8):
        def mmCT(t, ic=ic):
            t.matmul(pD[:, 0:512], Bm[:, ic * 128:(ic + 1) * 128], uaT[:, 0:512], start=True, stop=True)
            return t.matmul(pD[:, 512:1024], Bm[:, ic * 128:(ic + 1) * 128], uaT[:, 512:1024], start=True, stop=True)
        p.op("pe", mmCT, reads=[Bm_buf, uaT_buf], writes=[pD_buf])
        dst = weffT[:, ic * C:(ic + 1) * C]
        p.op("dve", lambda v, dst=dst: v.tensor_tensor(dst, pD[:, 0:1024], dst, ALU.add),
             reads=[pD_buf, weffT_buf], writes=[weffT_buf])
    b_row, b_row_buf = emit_row_load(p, env, W["b"], C, "l0b_row")

    NS = 2
    xt = [env.sbuf(f"xt{i}", [128, C], F32) for i in range(NS)]
    xo = [env.sbuf(f"xo{i}", [128, C], F32) for i in range(NS)]
    stats = [env.sbuf(f"stats{i}", [128, 8], F32) for i in range(NS)]
    diag = [env.sbuf(f"diag{i}", [128, 128], BF16) for i in range(NS)]
    xcb = [env.sbuf(f"xcb{i}", [128, C], BF16) for i in range(NS)]
    sqd = env.sbuf("sqd", [128, C], BF16)
    xlnT = [env.sbuf(f"xlnT{i}", [128, C], BF16) for i in range(NS)]
    bufs = {n: [env.buf(f"{n}{i}") for i in range(NS)] for n in ("xt", "xo", "stats", "diag", "xcb", "xlnT")}
    sqd_buf = env.buf("sqd")
    ones, ones_buf = env.sb["onesb"], env.buf("onesb")

    for ti in range(ntiles):
        sl = ti % NS
        psl = "pA" if ti % 2 == 0 else "pB"
        pP, pP_buf = env.psum[psl], env.buf(psl)
        pTname = "pC" if ti % 2 == 0 else "pD"
        row = x_in[ti * 128:(ti + 1) * 128, :]
        p.dma("sp", xt[sl][:, :], row, writes=[bufs["xt"][sl]], sem=ti % 3)
        emit_ln_transpose(p, env, xt[sl][:, :], bufs["xt"][sl], stats[sl][:, :], bufs["stats"][sl],
                          diag[sl][:, :], bufs["diag"][sl], xcb[sl][:, :], bufs["xcb"][sl],
                          sqd[:, :], sqd_buf, pTname, xlnT[sl][:, :], bufs["xlnT"][sl])
        def mk_proj(half):
            def mm(t, half=half, sl=sl, pP=pP):
                for c2 in range(NCH):
                    t.matmul(pP[:, half * 512:half * 512 + 512],
                             xlnT[sl][:, c2 * 128:(c2 + 1) * 128],
                             weffT[:, c2 * C + half * 512: c2 * C + half * 512 + 512],
                             start=(c2 == 0), stop=False)
                return t.matmul(pP[:, half * 512:half * 512 + 512], ones[0:1, :],
                                b_row[0:1, half * 512:half * 512 + 512], start=False, stop=True)
            return mm
        p.op("pe", mk_proj(0), reads=[bufs["xlnT"][sl], weffT_buf, ones_buf, b_row_buf], writes=[pP_buf])
        p.op("pe", mk_proj(1), reads=[bufs["xlnT"][sl], weffT_buf, ones_buf, b_row_buf], writes=[pP_buf])
        p.op("dve", lambda v, sl=sl, pP=pP: v.tensor_add(xo[sl][:, :], pP[:, 0:C], xt[sl][:, :]),
             reads=[pP_buf, bufs["xt"][sl]], writes=[bufs["xo"][sl]])
        p.dma("sp", x_out[ti * 128:(ti + 1) * 128, :], xo[sl][:, :], reads=[bufs["xo"][sl]], sem=3 + ti % 3)



def emit_L1_ffn(p, env, x_in, x_out, W, ntok):
    nc = env.nc
    DF = 4096
    NFC = DF // 128
    TG = 512
    ngroups = ntok // TG
    ident, identb_buf = env.sb["identb"], env.buf("identb")
    ones, ones_buf = env.sb["onesb"], env.buf("onesb")

    w1 = env.sbuf("w1r", [128, NCH * DF], BF16)
    w1_buf = env.buf("w1r")
    w2t = [env.sbuf(f"w2t{i}", [128, C], BF16) for i in range(2)]
    w2t_bufs = [env.buf(f"w2t{i}") for i in range(2)]
    for c in range(NCH):
        p.dma("pool", w1[:, c * DF:(c + 1) * DF], W["W1"][c * 128:(c + 1) * 128, :], writes=[w1_buf])
    b2row, b2row_buf = emit_row_load(p, env, W["b2"], C, "l1b2_row")

    b1v, b1v_buf = emit_fvec128(p, env, W["b1"], DF, "l1b1")
    thv, thv_buf = emit_fvec128(p, env, W["quantum_phases"], DF, "l1th")
    wrv, wrv_buf = emit_fvec128(p, env, W["quantum_weights_real"], DF, "l1wr")
    wiv, wiv_buf = emit_fvec128(p, env, W["quantum_weights_imag"], DF, "l1wi")
    PI = float(np.pi)
    fx = env.sbuf("l1fx", [128, 8 * NFC], F32)
    fx_buf = env.buf("l1fx")
    V = NFC

    def colap(i):
        return fx[:, i * V:(i + 1) * V]

    def reduce_sin(src_ap, src_buf, dst_i, shift):
        t_i, g_i = 7, 5
        p.op("dve", lambda v: v.tensor_scalar(colap(t_i), src_ap, shift, None, ALU.add),
             reads=[src_buf], writes=[fx_buf])
        for _ in range(9):
            p.op("dve", lambda v: v.tensor_scalar(colap(dst_i), colap(t_i), PI, None, ALU.is_ge),
                 reads=[fx_buf], writes=[fx_buf])
            p.op("dve", lambda v: v.scalar_tensor_tensor(colap(t_i), colap(dst_i), -2 * PI, colap(t_i),
                                                         ALU.mult, ALU.add),
                 reads=[fx_buf], writes=[fx_buf])
            p.op("dve", lambda v: v.tensor_scalar(colap(dst_i), colap(t_i), -PI, None, ALU.is_lt),
                 reads=[fx_buf], writes=[fx_buf])
            p.op("dve", lambda v: v.scalar_tensor_tensor(colap(t_i), colap(dst_i), 2 * PI, colap(t_i),
                                                         ALU.mult, ALU.add),
                 reads=[fx_buf], writes=[fx_buf])
        p.op("dve", lambda v: v.tensor_copy(colap(dst_i), colap(t_i)),
             reads=[fx_buf], writes=[fx_buf])

    reduce_sin(thv[:, :], thv_buf, 0, 0.0)
    p.op("act", lambda s: s.activation(colap(0), colap(0), AF.Sin), reads=[fx_buf], writes=[fx_buf])
    reduce_sin(thv[:, :], thv_buf, 1, PI / 2)
    p.op("act", lambda s: s.activation(colap(1), colap(1), AF.Sin), reads=[fx_buf], writes=[fx_buf])
    p.op("dve", lambda v: v.tensor_mul(colap(2), colap(1), wrv[:, :]), reads=[fx_buf, wrv_buf], writes=[fx_buf])
    p.op("dve", lambda v: v.tensor_mul(colap(3), colap(0), wiv[:, :]), reads=[fx_buf, wiv_buf], writes=[fx_buf])
    p.op("dve", lambda v: v.tensor_add(colap(2), colap(2), colap(3)), reads=[fx_buf], writes=[fx_buf])
    p.op("dve", lambda v: v.tensor_mul(colap(2), colap(2), wiv[:, :]), reads=[fx_buf, wiv_buf], writes=[fx_buf])
    p.op("dve", lambda v: v.scalar_tensor_tensor(colap(3), colap(2), 0.1, colap(0), ALU.mult, ALU.mult),
         reads=[fx_buf], writes=[fx_buf])
    p.op("dve", lambda v: v.scalar_tensor_tensor(colap(4), colap(2), 0.1, colap(1), ALU.mult, ALU.mult),
         reads=[fx_buf], writes=[fx_buf])
    p.op("dve", lambda v: v.tensor_scalar(colap(5), b1v[:, :], 0.1, None, ALU.mult),
         reads=[b1v_buf], writes=[fx_buf])
    p.op("dve", lambda v: v.tensor_scalar(colap(6), b1v[:, :], 0.1, PI / 2, ALU.mult, ALU.add),
         reads=[b1v_buf], writes=[fx_buf])

    NS = 4
    xt = [env.sbuf(f"l1xt{i}", [128, C], F32) for i in range(NS)]
    xo = [env.sbuf(f"l1xo{i}", [128, C], F32) for i in range(NS)]
    stats = [env.sbuf(f"l1st{i}", [128, 8], F32) for i in range(NS)]
    diag = [env.sbuf(f"l1dg{i}", [128, 128], BF16) for i in range(NS)]
    xcb = [env.sbuf(f"l1xc{i}", [128, C], BF16) for i in range(NS)]
    sqd = env.sbuf("l1sq", [128, C], BF16)
    bufs = {n: [env.buf(f"l1{n}{i}") for i in range(NS)] for n in ("xt", "xo", "st", "dg", "xc")}
    sqd_buf = env.buf("l1sq")
    xlnT = env.sbuf("l1xlnT", [128, NCH * TG], BF16)
    xlnT_buf = env.buf("l1xlnT")
    slab = env.sbuf("l1slab", [128, NFC * TG], BF16)
    slab_buf = env.buf("l1slab")
    ch = [env.sbuf(f"l1ch{i}", [128, TG], F32) for i in range(2)]
    sh = [env.sbuf(f"l1sh{i}", [128, TG], F32) for i in range(2)]
    t1 = [env.sbuf(f"l1t1{i}", [128, TG], F32) for i in range(2)]
    chb = [env.buf(f"l1ch{i}") for i in range(2)]
    shb = [env.buf(f"l1sh{i}") for i in range(2)]
    t1b = [env.buf(f"l1t1{i}") for i in range(2)]

    for g in range(ngroups):
        for tt in range(TG // 128):
            ti = g * (TG // 128) + tt
            sl = ti % NS
            row = x_in[ti * 128:(ti + 1) * 128, :]
            p.dma("sp", xt[sl][:, :], row, writes=[bufs["xt"][sl]], sem=tt % 2)
            dsts = [xlnT[:, c * TG + tt * 128: c * TG + (tt + 1) * 128] for c in range(NCH)]
            emit_ln_transpose(p, env, xt[sl][:, :], bufs["xt"][sl], stats[sl][:, :], bufs["st"][sl],
                              diag[sl][:, :], bufs["dg"][sl], xcb[sl][:, :], bufs["xc"][sl],
                              sqd[:, :], sqd_buf, "pD", dsts, xlnT_buf)
        for fc in range(NFC):
            half = fc % 2
            pH, pH_buf = env.psum["pA"], env.buf("pA")
            hap = pH[:, half * 512:(half + 1) * 512]

            def mmh(t, fc=fc, hap=hap):
                last = None
                for c in range(NCH):
                    last = t.matmul(hap, w1[:, c * DF + fc * 128:c * DF + (fc + 1) * 128],
                                    xlnT[:, c * TG:(c + 1) * TG], start=(c == 0), stop=(c == NCH - 1))
                return last
            p.op("pe", mmh, reads=[w1_buf, xlnT_buf], writes=[pH_buf])
            fcs = slice(fc, fc + 1)
            p.op("act", lambda s, hap=hap, fc=fc: s.activation(ch[fc % 2][:, :], hap, AF.Sin,
                 bias=fx[:, 6 * V + fc:6 * V + fc + 1], scale=0.1),
                 reads=[pH_buf, fx_buf], writes=[chb[fc % 2]])
            p.op("act", lambda s, hap=hap, fc=fc: s.activation(sh[fc % 2][:, :], hap, AF.Sin,
                 bias=fx[:, 5 * V + fc:5 * V + fc + 1], scale=0.1),
                 reads=[pH_buf, fx_buf], writes=[shb[fc % 2]])
            p.op("dve", lambda v, fc=fc: v.tensor_scalar(
                 t1[fc % 2][:, :], ch[fc % 2][:, :], fx[:, 3 * V + fc:3 * V + fc + 1],
                 b1v[:, fc:fc + 1], ALU.mult, ALU.add),
                 reads=[chb[fc % 2], fx_buf, b1v_buf], writes=[t1b[fc % 2]])
            p.op("dve", lambda v, fc=fc: v.scalar_tensor_tensor(
                 t1[fc % 2][:, :], sh[fc % 2][:, :], fx[:, 4 * V + fc:4 * V + fc + 1], t1[fc % 2][:, :],
                 ALU.mult, ALU.add),
                 reads=[shb[fc % 2], fx_buf], writes=[t1b[fc % 2]])
            p.op("dve", lambda v, hap=hap, fc=fc: v.tensor_tensor(
                 t1[fc % 2][:, :], hap, t1[fc % 2][:, :], ALU.add),
                 reads=[pH_buf, t1b[fc % 2]], writes=[t1b[fc % 2]])
            p.op("dve", lambda v, fc=fc: v.tensor_scalar(
                 slab[:, fc * TG:(fc + 1) * TG], t1[fc % 2][:, :], 0.0, None, ALU.max),
                 reads=[t1b[fc % 2]], writes=[slab_buf])
        pos = [env.psum[n] for n in ("pA", "pB", "pC", "pD")]
        pos_bufs = [env.buf(n) for n in ("pA", "pB", "pC", "pD")]
        for fc in range(NFC):
            wsl = fc % 2
            p.dma("pool", w2t[wsl][:, :], W["W2"][fc * 128:(fc + 1) * 128, :], writes=[w2t_bufs[wsl]])

            def mmo(t, fc=fc, wsl=wsl):
                last = None
                for tt in range(TG // 128):
                    for half in range(2):
                        last = t.matmul(pos[tt][:, half * 512:(half + 1) * 512],
                                        slab[:, fc * TG + tt * 128:fc * TG + (tt + 1) * 128],
                                        w2t[wsl][:, half * 512:(half + 1) * 512],
                                        start=(fc == 0), stop=False)
                return last
            p.op("pe", mmo, reads=[slab_buf, w2t_bufs[wsl]], writes=pos_bufs)

        def mmbias(t):
            last = None
            for tt in range(TG // 128):
                for half in range(2):
                    last = t.matmul(pos[tt][:, half * 512:(half + 1) * 512], ones[0:1, :],
                                    b2row[0:1, half * 512:(half + 1) * 512], start=False, stop=True)
            return last
        p.op("pe", mmbias, reads=[ones_buf, b2row_buf], writes=pos_bufs)
        for tt in range(TG // 128):
            ti = g * (TG // 128) + tt
            sl = ti % NS
            pO, pO_buf = pos[tt], pos_bufs[tt]
            p.op("dve", lambda v, sl=sl, pO=pO: v.tensor_add(xo[sl][:, :], pO[:, 0:C], xt[sl][:, :]),
                 reads=[pO_buf, bufs["xt"][sl]], writes=[bufs["xo"][sl]])
            p.dma("sp", x_out[ti * 128:(ti + 1) * 128, :], xo[sl][:, :], reads=[bufs["xo"][sl]], sem=2 + tt % 2)



def emit_std_attention(p, env, x_in, x_out, W, ntok, cfg):
    nc = env.nc
    T = cfg["T"]
    nseq = ntok // T
    NT = T // 128
    H, HD = 16, 64
    ident, identb_buf = env.sb["identb"], env.buf("identb")
    ones, ones_buf = env.sb["onesb"], env.buf("onesb")
    pS, pS_buf = env.psum["pS"], env.buf("pS")
    pT, pT_buf = env.psum["pT"], env.buf("pT")
    pO, pO_buf = env.psum["pO"], env.buf("pO")
    pS2, pS2_buf = (env.psum["pS2"], env.buf("pS2")) if "pS2" in env.psum else (None, None)

    cm = env.add_const("cmask128", np.triu(np.full((128, 128), -1e30, np.float32), 1))
    if "cmask" not in env.sb:
        cmask0 = env.sbuf("cmask", [128, 128], F32)
        p.dma("sp", cmask0[:, :], cm, writes=[env.buf("cmask")])
    cmask, cmask_buf = env.sb["cmask"], env.buf("cmask")

    wininfo = {}
    if cfg.get("kind") == "window":
        RAD = cfg.get("radius", 256)
        CHK = cfg.get("chunk", 256)
        mask_cache = {}
        for qt in range(T // 128):
            n = (qt * 128) // CHK
            k0 = max(0, n * CHK - CHK)
            k1 = min(T, n * CHK + 2 * CHK)
            width = k1 - k0
            qpos = qt * 128 + np.arange(128)[:, None]
            kpos = k0 + np.arange(width)[None, :]
            valid = np.abs(qpos - kpos) <= RAD
            marr = np.where(valid, 0.0, -1e30).astype(np.float32)
            key = marr.tobytes()
            if key not in mask_cache:
                mname = f"wmask{len(mask_cache)}"
                mc = env.add_const(mname, np_bf16(marr))
                msb = env.sbuf(mname, [128, width], BF16)
                p.dma("pool", msb[:, :], mc, writes=[env.buf(mname)])
                mask_cache[key] = mname
            wininfo[qt] = (k0, width, mask_cache[key])

    wsb = {}
    wbuf = {}
    for wn in ["wo"]:
        wsb[wn] = env.sbuf("a_" + wn, [128, NCH * C], BF16)
        wbuf[wn] = env.buf("a_" + wn)
        for c in range(NCH):
            p.dma("pool", wsb[wn][:, c * C:(c + 1) * C], cfg[wn][c * 128:(c + 1) * 128, :],
                  writes=[wbuf[wn]])
    wstr = [env.sbuf(f"a_ws{i}", [128, NCH * 512], BF16) for i in range(2)]
    wstr_buf = [env.buf(f"a_ws{i}") for i in range(2)]
    brow = {}
    brow_buf = {}
    fvecs = {}
    for bn in ("bv", "bo"):
        if cfg.get(bn) is not None:
            brow[bn], brow_buf[bn] = emit_row_load(p, env, cfg[bn], C, "a_" + bn)
    for bn, bscale in (("bq", cfg.get("qscale", 1.0)), ("bk", 1.0)):
        if cfg.get(bn) is not None:
            fvecs[bn] = emit_fvec128(p, env, cfg[bn], C, "a_fv" + bn, scale=bscale)

    xlnT = env.sbuf("a_xlnT", [128, NCH * T], BF16); xlnT_buf = env.buf("a_xlnT")
    qfm = env.sbuf("a_qfm", [128, NCH * T], BF16); qfm_buf = env.buf("a_qfm")
    kfm = env.sbuf("a_kfm", [128, NCH * T], BF16); kfm_buf = env.buf("a_kfm")
    vtok = env.sbuf("a_vtok", [128, NT * C], BF16); vtok_buf = env.buf("a_vtok")
    statsA = env.sbuf("a_stats", [128, 2 * NT], F32); statsA_buf = env.buf("a_stats")

    NS = 2
    xt = [env.sbuf(f"a_xt{i}", [128, C], F32) for i in range(NS)]
    stats = [env.sbuf(f"a_st{i}", [128, 8], F32) for i in range(NS)]
    diag = [env.sbuf(f"a_dg{i}", [128, 128], BF16) for i in range(NS)]
    xcb = [env.sbuf(f"a_xc{i}", [128, C], BF16) for i in range(NS)]
    sqd = env.sbuf("a_sq", [128, C], BF16)
    xo = [env.sbuf(f"a_xo{i}", [128, C], F32) for i in range(NS)]
    bufs = {n: [env.buf(f"a_{n}{i}") for i in range(NS)] for n in ("xt", "st", "dg", "xc", "xo")}
    sqd_buf = env.buf("a_sq")
    EW = 768 if cfg.get("kind") == "window" else T
    Es = [env.sbuf(f"a_E{i}", [128, EW], BF16) for i in range(2)]
    Es_buf = [env.buf(f"a_E{i}") for i in range(2)]
    ETs = env.sbuf("a_ET", [128, 1024], BF16); ETs_buf = env.buf("a_ET")
    st2 = [env.sbuf(f"a_s2_{i}", [128, 4], F32) for i in range(2)]
    st2_buf = [env.buf(f"a_s2_{i}") for i in range(2)]
    dgl = [env.sbuf(f"a_dl{i}", [128, 128], BF16) for i in range(2)]
    dgl_buf = [env.buf(f"a_dl{i}") for i in range(2)]
    ysb = env.sbuf("a_y", [128, NCH * 128], BF16); ysb_buf = env.buf("a_y")

    qscale = cfg.get("qscale", 1.0)

    for s in range(nseq):
        base = s * T
        for tt in range(NT):
            sl = tt % NS
            p.dma("sp", xt[sl][:, :], x_in[base + tt * 128: base + (tt + 1) * 128, :],
                  writes=[bufs["xt"][sl]], sem=tt % 2)
            dsts = [xlnT[:, c * T + tt * 128: c * T + (tt + 1) * 128] for c in range(NCH)]
            emit_ln_transpose(p, env, xt[sl][:, :], bufs["xt"][sl], stats[sl][:, :], bufs["st"][sl],
                              diag[sl][:, :], bufs["dg"][sl], xcb[sl][:, :], bufs["xc"][sl],
                              sqd[:, :], sqd_buf, "pT", dsts, xlnT_buf)
            if cfg.get("extra_ln_residual"):
                p.op("dve", lambda v, sl=sl, tt=tt: v.tensor_copy(
                     statsA[:, 2 * tt:2 * tt + 1], stats[sl][:, 1:2]),
                     reads=[bufs["st"][sl]], writes=[statsA_buf])
                p.op("dve", lambda v, sl=sl, tt=tt: v.tensor_copy(
                     statsA[:, 2 * tt + 1:2 * tt + 2], stats[sl][:, 5:6]),
                     reads=[bufs["st"][sl]], writes=[statsA_buf])
        for dst, dst_buf, wn, scale, tokmajor in (
                (qfm, qfm_buf, "wq", qscale, False),
                (kfm, kfm_buf, "wk", 1.0, False),
                (vtok, vtok_buf, "wv", 1.0, True)):
            bn = {"wq": "bq", "wk": "bk", "wv": "bv"}[wn]
            has_b = cfg.get(bn) is not None
            if not tokmajor:
                bfv = bfv_buf = None
                if has_b:
                    bfv, bfv_buf = fvecs[bn]
                wsrc = cfg[wn].rearrange("(c p) f -> p c f", p=128)
                for fc in range(NCH):
                    wslot = fc % 2
                    wt = wstr[wslot]
                    wt3 = wt[:, 0:NCH * 128].rearrange("p (c j) -> p c j", j=128)
                    p.dma("pool", wt3, wsrc[:, :, fc * 128:(fc + 1) * 128],
                          writes=[wstr_buf[wslot]])
                    for tc in range(T // 512):
                        half = (fc + tc) % 2
                        pH, pH_buf = (pT, pT_buf) if half == 0 else (pO, pO_buf)
                        hap = pH[:, 0:512]

                        def mmq(t, fc=fc, tc=tc, hap=hap, wt=wt):
                            for c in range(NCH):
                                last = t.matmul(hap, wt[:, c * 128:(c + 1) * 128],
                                                xlnT[:, c * T + tc * 512:c * T + (tc + 1) * 512],
                                                start=(c == 0), stop=(c == NCH - 1))
                            return last
                        p.op("pe", mmq, reads=[wstr_buf[wslot], xlnT_buf], writes=[pH_buf])
                        if has_b:
                            p.op("dve", lambda v, fc=fc, tc=tc, hap=hap, dst=dst, scale=scale, bfv=bfv:
                                 v.tensor_scalar(dst[:, fc * T + tc * 512:fc * T + (tc + 1) * 512],
                                                 hap, scale, bfv[:, fc:fc + 1], ALU.mult, ALU.add),
                                 reads=[pH_buf, bfv_buf], writes=[dst_buf])
                        else:
                            p.op("dve", lambda v, fc=fc, tc=tc, hap=hap, dst=dst, scale=scale:
                                 v.tensor_scalar(dst[:, fc * T + tc * 512:fc * T + (tc + 1) * 512],
                                                 hap, scale, None, ALU.mult),
                                 reads=[pH_buf], writes=[dst_buf])
            else:
                wsrc = cfg[wn].rearrange("(c p) f -> p c f", p=128)
                for half in range(2):
                    wslot = half
                    wt = wstr[wslot]
                    wt3 = wt[:, :].rearrange("p (c j) -> p c j", j=512)
                    p.dma("pool", wt3, wsrc[:, :, half * 512:(half + 1) * 512],
                          writes=[wstr_buf[wslot]])
                    for tt in range(NT):
                        pH, pH_buf = (pT, pT_buf) if tt % 2 == 0 else (pO, pO_buf)
                        hap = pH[:, 0:512]

                        def mmv(t, tt=tt, half=half, hap=hap, wt=wt, has_b=has_b, bn=bn):
                            for c in range(NCH):
                                last = t.matmul(hap, xlnT[:, c * T + tt * 128:c * T + (tt + 1) * 128],
                                                wt[:, c * 512:(c + 1) * 512],
                                                start=(c == 0), stop=(c == NCH - 1) and not has_b)
                            if has_b:
                                last = t.matmul(hap, ones[0:1, :],
                                                brow[bn][0:1, half * 512:(half + 1) * 512],
                                                start=False, stop=True)
                            return last
                        p.op("pe", mmv, reads=[wstr_buf[wslot], xlnT_buf, ones_buf] +
                             ([brow_buf[bn]] if has_b else []), writes=[pH_buf])
                        p.op("dve", lambda v, tt=tt, half=half, hap=hap, dst=dst:
                             v.tensor_copy(dst[:, tt * C + half * 512:tt * C + (half + 1) * 512], hap),
                             reads=[pH_buf], writes=[dst_buf])
                continue
        pending = [None]

        def flush_pending():
            if pending[0] is not None:
                pending[0]()
                pending[0] = None

        for qt in range(NT):
            if cfg.get("kind") == "window":
                kbase, kmax, mname = wininfo[qt]
            else:
                kbase, kmax, mname = 0, (qt + 1) * 128, None
            n_ks = kmax // 128
            for h in range(H):
                par = (qt * H + h) % 2
                hp = 64 * (h % 2)
                hc = h // 2
                def mms(t, qt=qt, h=h, hp=hp, hc=hc, kmax=kmax, kbase=kbase):
                    last = None
                    for k0 in range(0, kmax, 512):
                        kn = min(512, kmax - k0)
                        last = t.matmul(pS[:, k0:k0 + kn],
                                        qfm[hp:hp + 64, hc * T + qt * 128:hc * T + (qt + 1) * 128],
                                        kfm[hp:hp + 64, hc * T + kbase + k0:hc * T + kbase + k0 + kn],
                                        start=True, stop=True)
                    return last
                p.op("pe", mms, reads=[qfm_buf, kfm_buf], writes=[pS_buf])
                flush_pending()
                if mname is None:
                    p.op("dve", lambda v, qt=qt, kmax=kmax: v.tensor_tensor(
                         pS[:, kmax - 128:kmax], pS[:, kmax - 128:kmax], cmask[:, :], ALU.add),
                         reads=[cmask_buf], writes=[pS_buf])
                else:
                    p.op("dve", lambda v, kmax=kmax, mname=mname: v.tensor_tensor(
                         pS[:, 0:kmax], pS[:, 0:kmax], env.sb[mname][:, 0:kmax], ALU.add),
                         reads=[env.buf(mname)], writes=[pS_buf])
                p.op("dve", lambda v, kmax=kmax, par=par: v.tensor_reduce(
                     st2[par][:, 0:1], pS[:, 0:kmax], AX.X, ALU.max, negate=True),
                     reads=[pS_buf], writes=[st2_buf[par]])
                p.op("act", lambda s2, kmax=kmax, par=par: s2.activation(
                     Es[par][:, 0:kmax], pS[:, 0:kmax], AF.Exp, bias=st2[par][:, 0:1],
                     accum_out=st2[par][:, 1:2]),
                     reads=[pS_buf, st2_buf[par]], writes=[Es_buf[par], st2_buf[par]])
                p.op("dve", lambda v, par=par: v.reciprocal(st2[par][:, 2:3], st2[par][:, 1:2]),
                     reads=[st2_buf[par]], writes=[st2_buf[par]])
                p.op("dve", lambda v, par=par: v.tensor_scalar(
                     dgl[par][:, :], ident[:, :], st2[par][:, 2:3], None, ALU.mult),
                     reads=[identb_buf, st2_buf[par]], writes=[dgl_buf[par]])
                def make_pending(h=h, hp=hp, hc=hc, n_ks=n_ks, kbase=kbase, par=par):
                    def run():
                        for ks0 in range(0, n_ks, 8):
                            kn = min(8, n_ks - ks0)

                            def mmt(t, ks0=ks0, kn=kn, par=par):
                                for j in range(kn):
                                    last = t.matmul(pT[:, j * 128:(j + 1) * 128],
                                                    Es[par][:, (ks0 + j) * 128:(ks0 + j + 1) * 128],
                                                    dgl[par][:, :], start=True, stop=True)
                                return last
                            p.op("pe", mmt, reads=[Es_buf[par], dgl_buf[par]], writes=[pT_buf])
                            p.op("dve", lambda v, kn=kn: v.tensor_copy(ETs[:, 0:kn * 128],
                                 pT[:, 0:kn * 128]),
                                 reads=[pT_buf], writes=[ETs_buf])

                            def mmpv(t, ks0=ks0, kn=kn, h=h, n_ks=n_ks, kbase=kbase):
                                for j in range(kn):
                                    ks = ks0 + j
                                    ksg = kbase // 128 + ks
                                    last = t.matmul(pO[0:64, 0:128],
                                                    vtok[:, ksg * C + h * 64:ksg * C + (h + 1) * 64],
                                                    ETs[:, j * 128:(j + 1) * 128],
                                                    start=(ks == 0), stop=(ks == n_ks - 1))
                                return last
                            p.op("pe", mmpv, reads=[vtok_buf, ETs_buf], writes=[pO_buf])
                        p.op("dve", lambda v, hp=hp, hc=hc: v.tensor_copy(
                             ysb[hp:hp + 64, hc * 128:(hc + 1) * 128], pO[0:64, 0:128]),
                             reads=[pO_buf], writes=[ysb_buf])
                    return run
                pending[0] = make_pending()
            flush_pending()
            sl = qt % NS
            p.dma("sp", xt[sl][:, :], x_in[base + qt * 128: base + (qt + 1) * 128, :],
                  writes=[bufs["xt"][sl]], sem=2 + qt % 2)
            for half in range(2):
                def mmo(t, half=half):
                    for cc in range(NCH):
                        last = t.matmul(pT[:, half * 512:(half + 1) * 512],
                                        ysb[:, cc * 128:(cc + 1) * 128],
                                        wsb["wo"][:, cc * C + half * 512:cc * C + (half + 1) * 512],
                                        start=(cc == 0), stop=(cc == NCH - 1) and "bo" not in brow)
                    if "bo" in brow:
                        last = t.matmul(pT[:, half * 512:(half + 1) * 512], ones[0:1, :],
                                        brow["bo"][0:1, half * 512:(half + 1) * 512],
                                        start=False, stop=True)
                    return last
                p.op("pe", mmo, reads=[ysb_buf, wbuf["wo"], ones_buf] +
                     ([brow_buf["bo"]] if "bo" in brow else []), writes=[pT_buf])
            p.op("dve", lambda v, sl=sl: v.tensor_add(xo[sl][:, :], pT[:, 0:C], xt[sl][:, :]),
                 reads=[pT_buf, bufs["xt"][sl]], writes=[bufs["xo"][sl]])
            if cfg.get("extra_ln_residual"):
                p.op("dve", lambda v, sl=sl, qt=qt: v.tensor_scalar(
                     xcb[sl][:, :], xt[sl][:, :], statsA[:, 2 * qt:2 * qt + 1],
                     statsA[:, 2 * qt + 1:2 * qt + 2], ALU.add, ALU.mult),
                     reads=[bufs["xt"][sl], statsA_buf], writes=[bufs["xc"][sl]])
                p.op("dve", lambda v, sl=sl: v.tensor_tensor(
                     xo[sl][:, :], xo[sl][:, :], xcb[sl][:, :], ALU.add),
                     reads=[bufs["xc"][sl]], writes=[bufs["xo"][sl]])
            p.dma("sp", x_out[base + qt * 128: base + (qt + 1) * 128, :], xo[sl][:, :],
                  reads=[bufs["xo"][sl]], sem=4 + qt % 2)


def emit_L1_attn(p, env, x_in, x_out, W, ntok, T=2048):
    nc = env.nc
    nseq = ntok // T
    NT = T // 128
    H, HD = 16, 64
    EPS = 1e-6
    ident, identb_buf = env.sb["identb"], env.buf("identb")
    ones, ones_buf = env.sb["onesb"], env.buf("onesb")
    pS, pS_buf = env.psum["pS"], env.buf("pS")
    pT, pT_buf = env.psum["pT"], env.buf("pT")
    pO, pO_buf = env.psum["pO"], env.buf("pO")

    cm = env.add_const("cmask128", np.triu(np.full((128, 128), -1e30, np.float32), 1))
    if "cmask" not in env.sb:
        cmask = env.sbuf("cmask", [128, 128], F32)
        p.dma("sp", cmask[:, :], cm, writes=[env.buf("cmask")])
    cmask, cmask_buf = env.sb["cmask"], env.buf("cmask")
    onescol, onescol_buf = env.sb["onescol"], env.buf("onescol")

    wqkv = cfg_w = W["Wqkv"]
    wout = W["Wout"]
    wo_sb = env.sbuf("q_wo", [128, NCH * C], BF16)
    wo_buf = env.buf("q_wo")
    for c in range(NCH):
        p.dma("pool", wo_sb[:, c * C:(c + 1) * C], wout[c * 128:(c + 1) * 128, :], writes=[wo_buf])
    wstr = [env.sbuf(f"q_ws{i}", [128, NCH * 256], BF16) for i in range(2)]
    wstr_buf = [env.buf(f"q_ws{i}") for i in range(2)]

    xlnT = env.sbuf("q_xlnT", [128, NCH * T], BF16); xlnT_buf = env.buf("q_xlnT")
    qfm = env.sbuf("q_qfm", [128, NCH * T], BF16); qfm_buf = env.buf("q_qfm")
    kfm = env.sbuf("q_kfm", [128, NCH * T], BF16); kfm_buf = env.buf("q_kfm")
    vtok = env.sbuf("q_vtok", [128, NT * C], BF16); vtok_buf = env.buf("q_vtok")

    NS = 2
    xt = [env.sbuf(f"q_xt{i}", [128, C], F32) for i in range(NS)]
    stats = [env.sbuf(f"q_st{i}", [128, 8], F32) for i in range(NS)]
    diag = [env.sbuf(f"q_dg{i}", [128, 128], BF16) for i in range(NS)]
    xcb = [env.sbuf(f"q_xc{i}", [128, C], BF16) for i in range(NS)]
    sqd = env.sbuf("q_sq", [128, C], BF16)
    xo = [env.sbuf(f"q_xo{i}", [128, C], F32) for i in range(NS)]
    bufs = {n: [env.buf(f"q_{n}{i}") for i in range(NS)] for n in ("xt", "st", "dg", "xc", "xo")}
    sqd_buf = env.buf("q_sq")

    Es = [env.sbuf(f"q_E{i}", [128, T], BF16) for i in range(2)]
    Es_buf = [env.buf(f"q_E{i}") for i in range(2)]
    ETs = env.sbuf("q_ET", [128, 1024], BF16); ETs_buf = env.buf("q_ET")
    st2 = [env.sbuf(f"q_s2_{i}", [128, 4], F32) for i in range(2)]
    st2_buf = [env.buf(f"q_s2_{i}") for i in range(2)]
    dgl = [env.sbuf(f"q_dl{i}", [128, 128], BF16) for i in range(2)]
    dgl_buf = [env.buf(f"q_dl{i}") for i in range(2)]
    ssb = env.sbuf("q_ssb", [128, T], F32); ssb_buf = env.buf("q_ssb")
    den = env.sbuf("q_den", [128, 512], F32); den_buf = env.buf("q_den")
    rin = env.sbuf("q_rin", [128, 512], F32); rin_buf = env.buf("q_rin")
    d2, d2_buf = den, den_buf
    qaug = env.sbuf("q_qaug", [128, T], BF16); qaug_buf = env.buf("q_qaug")
    kaug = env.sbuf("q_kaug", [128, T], BF16); kaug_buf = env.buf("q_kaug")
    sqs, sqs_buf = Es[1], Es_buf[1]
    nrow = env.sb["q_ssb"].bitcast(BF16); nrow_buf = ssb_buf
    qnc = env.sbuf("q_qnc", [128, NT], F32); qnc_buf = env.buf("q_qnc")
    y_all = env.sb["q_xlnT"]

    for s in range(nseq):
        base = s * T
        for tt in range(NT):
            sl = tt % NS
            p.dma("sp", xt[sl][:, :], x_in[base + tt * 128: base + (tt + 1) * 128, :],
                  writes=[bufs["xt"][sl]], sem=tt % 2)
            dsts = [xlnT[:, c * T + tt * 128: c * T + (tt + 1) * 128] for c in range(NCH)]
            emit_ln_transpose(p, env, xt[sl][:, :], bufs["xt"][sl], stats[sl][:, :], bufs["st"][sl],
                              diag[sl][:, :], bufs["dg"][sl], xcb[sl][:, :], bufs["xc"][sl],
                              sqd[:, :], sqd_buf, "pT", dsts, xlnT_buf)
        for dst, dst_buf, w_ap, tokmajor in ((qfm, qfm_buf, wqkv[:, 0:C], False),
                                             (kfm, kfm_buf, wqkv[:, C:2 * C], False),
                                             (vtok, vtok_buf, wqkv[:, 2 * C:3 * C], True)):
            wsrc = w_ap.rearrange("(c p) f -> p c f", p=128)
            if not tokmajor:
                for fc in range(NCH):
                    wslot = fc % 2
                    wt = wstr[wslot]
                    wt3 = wt[:, 0:NCH * 128].rearrange("p (c j) -> p c j", j=128)
                    p.dma("pool", wt3, wsrc[:, :, fc * 128:(fc + 1) * 128], writes=[wstr_buf[wslot]])
                    for tc in range(T // 512):
                        half = (fc + tc) % 2
                        hap = pS[:, half * 512:(half + 1) * 512]

                        def mmq(t, fc=fc, tc=tc, hap=hap, wt=wt):
                            for c in range(NCH):
                                last = t.matmul(hap, wt[:, c * 128:(c + 1) * 128],
                                                xlnT[:, c * T + tc * 512:c * T + (tc + 1) * 512],
                                                start=(c == 0), stop=(c == NCH - 1))
                            return last
                        p.op("pe", mmq, reads=[wstr_buf[wslot], xlnT_buf], writes=[pS_buf])
                        p.op("dve", lambda v, fc=fc, tc=tc, hap=hap, dst=dst:
                             v.tensor_copy(dst[:, fc * T + tc * 512:fc * T + (tc + 1) * 512], hap),
                             reads=[pS_buf], writes=[dst_buf])
            else:
                for quar in range(4):
                    wslot = quar % 2
                    wt = wstr[wslot]
                    wt3 = wt[:, :].rearrange("p (c j) -> p c j", j=256)
                    p.dma("pool", wt3, wsrc[:, :, quar * 256:(quar + 1) * 256], writes=[wstr_buf[wslot]])
                    for tt in range(NT):
                        hap = pS[:, (tt % 2) * 512:(tt % 2) * 512 + 256]

                        def mmv(t, tt=tt, quar=quar, hap=hap, wt=wt):
                            for c in range(NCH):
                                last = t.matmul(hap, xlnT[:, c * T + tt * 128:c * T + (tt + 1) * 128],
                                                wt[:, c * 256:(c + 1) * 256],
                                                start=(c == 0), stop=(c == NCH - 1))
                            return last
                        p.op("pe", mmv, reads=[wstr_buf[wslot], xlnT_buf], writes=[pS_buf])
                        p.op("dve", lambda v, tt=tt, quar=quar, hap=hap, dst=dst:
                             v.tensor_copy(dst[:, tt * C + quar * 256:tt * C + (quar + 1) * 256], hap),
                             reads=[pS_buf], writes=[dst_buf])
        y_all_buf = env.alias_buf(xlnT_buf, "q_yall")
        p.op("dve", lambda v: v.memset(qaug[64:65, :], 1.0), writes=[qaug_buf])
        for h in range(H):
            hp = 64 * (h % 2)
            hc = h // 2
            qh = qfm[hp:hp + 64, hc * T:(hc + 1) * T]
            kh = kfm[hp:hp + 64, hc * T:(hc + 1) * T]
            p.op("dve", lambda v, qh=qh: v.tensor_scalar(qaug[0:64, :], qh, -2.0, None, ALU.mult),
                 reads=[qfm_buf], writes=[qaug_buf])
            p.op("act", lambda s2, kh=kh: s2.activation(sqs[0:64, :], kh, AF.Square),
                 reads=[kfm_buf], writes=[sqs_buf])

            def mmkn(t):
                for tc in range(T // 512):
                    last = t.matmul(pS[0:1, tc * 512:(tc + 1) * 512], onescol[0:64, 0:1],
                                    sqs[0:64, tc * 512:(tc + 1) * 512], start=True, stop=True)
                return last
            p.op("pe", mmkn, reads=[sqs_buf, onescol_buf], writes=[pS_buf])
            p.op("dve", lambda v: v.tensor_copy(nrow[0:1, 0:T], pS[0:1, 0:T]),
                 reads=[pS_buf], writes=[nrow_buf])
            p.op("dve", lambda v, kh=kh: v.tensor_copy(kaug[0:64, :], kh),
                 reads=[kfm_buf], writes=[kaug_buf])
            p.op("dve", lambda v: v.tensor_copy(kaug[64:65, :], nrow[0:1, 0:T]),
                 reads=[nrow_buf], writes=[kaug_buf])
            p.op("act", lambda s2, qh=qh: s2.activation(sqs[0:64, :], qh, AF.Square),
                 reads=[qfm_buf], writes=[sqs_buf])

            def mmqn(t):
                for tc in range(T // 512):
                    last = t.matmul(pS[0:1, tc * 512:(tc + 1) * 512], onescol[0:64, 0:1],
                                    sqs[0:64, tc * 512:(tc + 1) * 512], start=True, stop=True)
                return last
            p.op("pe", mmqn, reads=[sqs_buf, onescol_buf], writes=[pS_buf])
            p.op("dve", lambda v: v.tensor_copy(nrow[0:1, 0:T], pS[0:1, 0:T]),
                 reads=[pS_buf], writes=[nrow_buf])

            def mmqc(t):
                for qt in range(NT):
                    last = t.matmul(pO[:, qt:qt + 1], nrow[0:1, qt * 128:(qt + 1) * 128],
                                    ones[0:1, 0:1], start=True, stop=True)
                return last
            p.op("pe", mmqc, reads=[nrow_buf, ones_buf], writes=[pO_buf])
            p.op("dve", lambda v: v.tensor_scalar(qnc[:, 0:NT], pO[:, 0:NT], 1.0, EPS, ALU.mult, ALU.add),
                 reads=[pO_buf], writes=[qnc_buf])

            for qt in range(NT):
                kmax = (qt + 1) * 128
                n_ks = qt + 1
                par = qt % 2
                def mmdot(t, qt=qt, kmax=kmax, qh=qh, kh=kh):
                    for k0 in range(0, kmax, 512):
                        kn_ = min(512, kmax - k0)
                        last = t.matmul(pS[:, k0:k0 + kn_],
                                        qh[:, qt * 128:(qt + 1) * 128], kh[:, k0:k0 + kn_],
                                        start=True, stop=True)
                    return last
                p.op("pe", mmdot, reads=[qfm_buf, kfm_buf], writes=[pS_buf])
                for k0 in range(0, kmax, 512):
                    kn_ = min(512, kmax - k0)

                    def mmden(t, qt=qt, k0=k0, kn_=kn_):
                        return t.matmul(pT[:, 0:kn_], qaug[0:65, qt * 128:(qt + 1) * 128],
                                        kaug[0:65, k0:k0 + kn_], start=True, stop=True)
                    p.op("pe", mmden, reads=[qaug_buf, kaug_buf], writes=[pT_buf])
                    p.op("dve", lambda v, qt=qt, kn_=kn_: v.tensor_scalar(
                         den[:, 0:kn_], pT[:, 0:kn_], qnc[:, qt:qt + 1], EPS, ALU.add, ALU.max),
                         reads=[pT_buf, qnc_buf], writes=[den_buf])
                    p.op("act", lambda s2, kn_=kn_: s2.activation(rin[:, 0:kn_], den[:, 0:kn_], AF.Ln),
                         reads=[den_buf], writes=[rin_buf])
                    p.op("act", lambda s2, kn_=kn_: s2.activation(rin[:, 0:kn_], rin[:, 0:kn_],
                         AF.Exp, scale=-1.0),
                         reads=[rin_buf], writes=[rin_buf])
                    p.op("act", lambda s2, k0=k0, kn_=kn_: s2.activation(d2[:, 0:kn_], pS[:, k0:k0 + kn_],
                         AF.Square),
                         reads=[pS_buf], writes=[d2_buf])
                    p.op("dve", lambda v, k0=k0, kn_=kn_: v.tensor_mul(
                         ssb[:, k0:k0 + kn_], d2[:, 0:kn_], rin[:, 0:kn_]),
                         reads=[d2_buf, rin_buf], writes=[ssb_buf])
                p.op("dve", lambda v, kmax=kmax: v.tensor_tensor(
                     ssb[:, kmax - 128:kmax], ssb[:, kmax - 128:kmax], cmask[:, :], ALU.add),
                     reads=[cmask_buf], writes=[ssb_buf])
                p.op("dve", lambda v, kmax=kmax, par=par: v.tensor_reduce(
                     st2[par][:, 0:1], ssb[:, 0:kmax], AX.X, ALU.max, negate=True),
                     reads=[ssb_buf], writes=[st2_buf[par]])
                p.op("act", lambda s2, kmax=kmax, par=par: s2.activation(
                     Es[par][:, 0:kmax], ssb[:, 0:kmax], AF.Exp, bias=st2[par][:, 0:1],
                     accum_out=st2[par][:, 1:2]),
                     reads=[ssb_buf, st2_buf[par]], writes=[Es_buf[par], st2_buf[par]])
                p.op("dve", lambda v, par=par: v.reciprocal(st2[par][:, 2:3], st2[par][:, 1:2]),
                     reads=[st2_buf[par]], writes=[st2_buf[par]])
                p.op("dve", lambda v, par=par: v.tensor_scalar(
                     dgl[par][:, :], ident[:, :], st2[par][:, 2:3], None, ALU.mult),
                     reads=[identb_buf, st2_buf[par]], writes=[dgl_buf[par]])
                for ks0 in range(0, n_ks, 8):
                    kn_ = min(8, n_ks - ks0)

                    def mmt(t, ks0=ks0, kn_=kn_, par=par):
                        for j in range(kn_):
                            last = t.matmul(pT[:, j * 128:(j + 1) * 128],
                                            Es[par][:, (ks0 + j) * 128:(ks0 + j + 1) * 128],
                                            dgl[par][:, :], start=True, stop=True)
                        return last
                    p.op("pe", mmt, reads=[Es_buf[par], dgl_buf[par]], writes=[pT_buf])
                    p.op("dve", lambda v, kn_=kn_: v.tensor_copy(ETs[:, 0:kn_ * 128], pT[:, 0:kn_ * 128]),
                         reads=[pT_buf], writes=[ETs_buf])

                    def mmpv(t, ks0=ks0, kn_=kn_, h=h, n_ks=n_ks):
                        for j in range(kn_):
                            ks = ks0 + j
                            last = t.matmul(pO[0:64, 0:128],
                                            vtok[:, ks * C + h * 64:ks * C + (h + 1) * 64],
                                            ETs[:, j * 128:(j + 1) * 128],
                                            start=(ks == 0), stop=(ks == n_ks - 1))
                        return last
                    p.op("pe", mmpv, reads=[vtok_buf, ETs_buf], writes=[pO_buf])
                p.op("dve", lambda v, hp=hp, hc=hc, qt=qt: v.tensor_copy(
                     y_all[hp:hp + 64, qt * C + hc * 128:qt * C + (hc + 1) * 128], pO[0:64, 0:128]),
                     reads=[pO_buf], writes=[y_all_buf])
        for qt in range(NT):
            sl = qt % NS
            p.dma("sp", xt[sl][:, :], x_in[base + qt * 128: base + (qt + 1) * 128, :],
                  writes=[bufs["xt"][sl]], sem=2 + qt % 2)
            for half in range(2):
                def mmo(t, half=half, qt=qt):
                    for cc in range(NCH):
                        last = t.matmul(pT[:, half * 512:(half + 1) * 512],
                                        y_all[:, qt * C + cc * 128:qt * C + (cc + 1) * 128],
                                        wo_sb[:, cc * C + half * 512:cc * C + (half + 1) * 512],
                                        start=(cc == 0), stop=(cc == NCH - 1))
                    return last
                p.op("pe", mmo, reads=[y_all_buf, wo_buf], writes=[pT_buf])
            p.op("dve", lambda v, sl=sl: v.tensor_add(xo[sl][:, :], pT[:, 0:C], xt[sl][:, :]),
                 reads=[pT_buf, bufs["xt"][sl]], writes=[bufs["xo"][sl]])
            p.dma("sp", x_out[base + qt * 128: base + (qt + 1) * 128, :], xo[sl][:, :],
                  reads=[bufs["xo"][sl]], sem=4 + qt % 2)
        xlnT_buf.w = y_all_buf.w
        xlnT_buf.rs = list(y_all_buf.rs)


def emit_L2_attn(p, env, x_in, x_out, W, ntok, T=2048):
    cfg = {"T": T, "wq": W["Wq"], "wk": W["Wk"], "wv": W["Wv"], "wo": W["Wo"],
           "qscale": 0.125, "extra_ln_residual": True, "kind": "causal"}
    emit_std_attention(p, env, x_in, x_out, W, ntok, cfg)


def emit_L3_attn(p, env, x_in, x_out, W, ntok, T=2048):
    cfg = {"T": T, "wq": W["wq"], "wk": W["wk"], "wv": W["wv"], "wo": W["wo"],
           "bq": W["bq"], "bk": W["bk"], "bv": W["bv"], "bo": W["bo"],
           "qscale": 0.125, "kind": "window", "radius": 256, "chunk": 256}
    emit_std_attention(p, env, x_in, x_out, W, ntok, cfg)


def emit_L2_ffn(p, env, x_in, x_out, W, ntok):
    nc = env.nc
    NE = 34
    MG = min(2048, ntok)
    TGP = min(512, MG)
    nmg = ntok // MG
    ident, identb_buf = env.sb["identb"], env.buf("identb")
    ones, ones_buf = env.sb["onesb"], env.buf("onesb")
    pA, pA_buf = env.psum["pA"], env.buf("pA")
    pB, pB_buf = env.psum["pB"], env.buf("pB")
    pC, pC_buf = env.psum["pC"], env.buf("pC")
    pD, pD_buf = env.psum["pD"], env.buf("pD")

    if "identf" not in env.sb:
        idf = env.add_const("identf", np.eye(128, dtype=np.float32))
        env.sbuf("identf", [128, 128], F32)
        p.dma("sp", env.sb["identf"][:], idf, writes=[env.buf("identf")])
    identf, identf_buf = env.sb["identf"], env.buf("identf")
    if "m_onesf" not in env.sb:
        onesf0 = env.sbuf("m_onesf", [1, 128], F32)
        p.op("dve", lambda v: v.memset(onesf0[:, :], 1.0), writes=[env.buf("m_onesf")])
    onesf, onesf_buf = env.sb["m_onesf"], env.buf("m_onesf")

    guT_dram = nc.dram_tensor("m_guT", [NE, 128, NCH * C], BF16).ap()
    dnT_dram = nc.dram_tensor("m_dnT", [NE, 128, 4 * C], BF16).ap()
    guT_dram_buf = Buf("m_guT_d")
    dnT_dram_buf = Buf("m_dnT_d")

    def gu_rows(e, rc):
        if e < 32:
            return W["experts_gate_up"][e, rc * 128:(rc + 1) * 128, :]
        off = 0 if e == 32 else 512
        if rc < 4:
            return W["shared_gate_w"][off + rc * 128:off + (rc + 1) * 128, :]
        return W["shared_up_w"][off + (rc - 4) * 128:off + (rc - 3) * 128, :]

    def dn_rows(e, rc):
        if e < 32:
            return W["experts_down"][e, rc * 128:(rc + 1) * 128, :]
        off = 0 if e == 32 else 512
        return W["shared_down_w"][rc * 128:(rc + 1) * 128, off:off + 512]

    wld = [env.sbuf(f"m_wld{i}", [128, C], BF16) for i in range(2)]
    wld_buf = [env.buf(f"m_wld{i}") for i in range(2)]
    tsl = [env.sbuf(f"m_tsl{i}", [128, 1024], BF16) for i in range(2)]
    tsl_buf = [env.buf(f"m_tsl{i}") for i in range(2)]
    for e in range(NE):
        for kind, rows_fn, ncc, dram in (("gu", gu_rows, 8, guT_dram), ("dn", dn_rows, 8, dnT_dram)):
            nrc = 8 if kind == "gu" else 8
            ncols = 1024 if kind == "gu" else 512
            nccd = ncols // 128
            for rc in range(nrc):
                sl = rc % 2
                p.dma("pool", wld[sl][:, 0:ncols], rows_fn(e, rc), writes=[wld_buf[sl]])

                def mmtr(t, sl=sl, nccd=nccd):
                    for cc in range(nccd):
                        last = t.matmul(pA[:, cc * 128:(cc + 1) * 128],
                                        wld[sl][:, cc * 128:(cc + 1) * 128], ident[:, :],
                                        start=True, stop=True)
                    return last
                p.op("pe", mmtr, reads=[wld_buf[sl], identb_buf], writes=[pA_buf])
                p.op("dve", lambda v, sl=sl, nccd=nccd: v.tensor_copy(
                     tsl[sl][:, 0:nccd * 128], pA[:, 0:nccd * 128]),
                     reads=[pA_buf], writes=[tsl_buf[sl]])
                rows_total = C if kind == "gu" else C
                dbuf = guT_dram_buf if kind == "gu" else dnT_dram_buf
                for cc in range(nccd):
                    p.dma("sp", dram[e, :, cc * rows_total + rc * 128: cc * rows_total + (rc + 1) * 128],
                          tsl[sl][:, cc * 128:(cc + 1) * 128], reads=[tsl_buf[sl]], writes=[dbuf])

    gw_sb = env.sbuf("m_gw", [32, C], F32)
    gw_buf = env.buf("m_gw")
    p.dma("sp", gw_sb[:, :], W["gate_weight"], writes=[gw_buf])
    gwT = env.sbuf("m_gwT", [128, NCH * 32], F32)
    gwT_buf = env.buf("m_gwT")
    for c in range(NCH):
        p.op("pe", lambda t, c=c: t.matmul(pD[0:128, 0:32], gw_sb[:, c * 128:(c + 1) * 128],
                                           identf[0:32, 0:32], start=True, stop=True),
             reads=[gw_buf, identf_buf], writes=[pD_buf])
        p.op("dve", lambda v, c=c: v.tensor_copy(gwT[:, c * 32:(c + 1) * 32], pD[0:128, 0:32]),
             reads=[pD_buf], writes=[gwT_buf])
    eb_sb = env.sbuf("m_eb", [1, 32], F32)
    eb_buf = env.buf("m_eb")
    p.dma("sp", eb_sb[0:1, :], W["e_score_correction_bias"][None, :], writes=[eb_buf])
    ebb = env.sbuf("m_ebb", [128, 32], F32)
    ebb_buf = env.buf("m_ebb")
    p.op("pe", lambda t: t.matmul(pD[:, 0:32], onesf[0:1, :], eb_sb[0:1, :], start=True, stop=True),
         reads=[onesf_buf, eb_buf], writes=[pD_buf])
    p.op("dve", lambda v: v.tensor_copy(ebb[:, :], pD[:, 0:32]), reads=[pD_buf], writes=[ebb_buf])

    NTM = MG // 128
    xlnT = env.sbuf("m_xlnT", [128, NCH * MG], BF16); xlnT_buf = env.buf("m_xlnT")
    acc = env.sbuf("m_acc", [128, NTM * C], F32); acc_buf = env.buf("m_acc")
    guT = [env.sbuf(f"m_guT{i}", [128, NCH * C], BF16) for i in range(2)]
    guT_buf = [env.buf(f"m_guT{i}") for i in range(2)]
    dnT = [env.sbuf(f"m_dnT{i}", [128, 4 * C], BF16) for i in range(2)]
    dnT_buf = [env.buf(f"m_dnT{i}") for i in range(2)]
    slab = env.sbuf("m_slab", [128, 4 * TGP], BF16); slab_buf = env.buf("m_slab")
    silu_s = env.sbuf("m_silu", [128, TGP], F32); silu_buf = env.buf("m_silu")
    assert 2 * TGP <= 1024
    comb = env.sbuf("m_comb", [128, NTM * NE], F32); comb_buf = env.buf("m_comb")
    xcf = env.sbuf("m_xcf", [128, C], F32); xcf_buf = env.buf("m_xcf")
    xlnT32 = env.sbuf("m_xlnT32", [128, C], F32); xlnT32_buf = env.buf("m_xlnT32")
    diagf = env.sbuf("m_diagf", [128, 128], F32); diagf_buf = env.buf("m_diagf")
    lg = env.sbuf("m_lg", [128, 32], F32); lg_buf = env.buf("m_lg")
    rt = env.sbuf("m_rt", [128, 160], F32); rt_buf = env.buf("m_rt")

    NS = 2
    xt = [env.sbuf(f"m_xt{i}", [128, C], F32) for i in range(NS)]
    stats = [env.sbuf(f"m_st{i}", [128, 8], F32) for i in range(NS)]
    diag = [env.sbuf(f"m_dg{i}", [128, 128], BF16) for i in range(NS)]
    xcb = [env.sbuf(f"m_xc{i}", [128, C], BF16) for i in range(NS)]
    sqd = env.sbuf("m_sq", [128, C], BF16)
    xo = [env.sbuf(f"m_xo{i}", [128, C], F32) for i in range(NS)]
    bufs = {n: [env.buf(f"m_{n}{i}") for i in range(NS)] for n in ("xt", "st", "dg", "xc", "xo")}
    sqd_buf = env.buf("m_sq")

    for mg in range(nmg):
        base = mg * MG
        p.op("dve", lambda v: v.memset(acc[:, :], 0.0), writes=[acc_buf])
        for tt in range(NTM):
            sl = tt % NS
            p.dma("sp", xt[sl][:, :], x_in[base + tt * 128: base + (tt + 1) * 128, :],
                  writes=[bufs["xt"][sl]], sem=tt % 2)
            dsts = [xlnT[:, c * MG + tt * 128: c * MG + (tt + 1) * 128] for c in range(NCH)]
            emit_ln_transpose(p, env, xt[sl][:, :], bufs["xt"][sl], stats[sl][:, :], bufs["st"][sl],
                              diag[sl][:, :], bufs["dg"][sl], xcb[sl][:, :], bufs["xc"][sl],
                              sqd[:, :], sqd_buf, "pD", dsts, xlnT_buf)
            p.op("dve", lambda v, sl=sl: v.tensor_scalar(xcf[:, :], xt[sl][:, :],
                 stats[sl][:, 1:2], None, ALU.add),
                 reads=[bufs["xt"][sl], bufs["st"][sl]], writes=[xcf_buf])
            p.op("dve", lambda v, sl=sl: v.tensor_scalar(diagf[:, :], identf[:, :],
                 stats[sl][:, 5:6], None, ALU.mult),
                 reads=[identf_buf, bufs["st"][sl]], writes=[diagf_buf])

            def mmt32(t):
                for c in range(NCH):
                    last = t.matmul(pD[:, c * 128:(c + 1) * 128], xcf[:, c * 128:(c + 1) * 128],
                                    diagf[:, :], start=True, stop=True)
                return last
            p.op("pe", mmt32, reads=[xcf_buf, diagf_buf], writes=[pD_buf])
            p.op("dve", lambda v: v.tensor_copy(xlnT32[:, :], pD[:, 0:C]),
                 reads=[pD_buf], writes=[xlnT32_buf])

            def mmgate(t):
                for c in range(NCH):
                    last = t.matmul(pA[:, 0:32], xlnT32[:, c * 128:(c + 1) * 128],
                                    gwT[:, c * 32:(c + 1) * 32], start=(c == 0), stop=(c == NCH - 1))
                return last
            p.op("pe", mmgate, reads=[xlnT32_buf, gwT_buf], writes=[pA_buf])
            p.op("act", lambda s2: s2.activation(lg[:, :], pA[:, 0:32], AF.Sigmoid),
                 reads=[pA_buf], writes=[lg_buf])
            p.op("dve", lambda v: v.tensor_add(rt[:, 0:32], lg[:, :], ebb[:, :]),
                 reads=[lg_buf, ebb_buf], writes=[rt_buf])
            p.op("dve", lambda v: v.tensor_reduce(rt[:, 32:40],
                 rt[:, 0:32].rearrange("p (g e) -> p g e", e=4), AX.X, ALU.max),
                 reads=[rt_buf], writes=[rt_buf])
            p.op("dve", lambda v: v.max(out=rt[:, 40:48], in_=rt[:, 32:40]),
                 reads=[rt_buf], writes=[rt_buf])
            p.op("dve", lambda v: v.tensor_scalar(rt[:, 32:40], rt[:, 32:40], rt[:, 43:44],
                 None, ALU.is_ge),
                 reads=[rt_buf], writes=[rt_buf])
            for j in range(4):
                p.op("dve", lambda v, j=j: v.tensor_tensor(
                     rt[:, 48 + j:80:4], rt[:, j:32:4], rt[:, 32:40], ALU.mult),
                     reads=[rt_buf], writes=[rt_buf])
            p.op("dve", lambda v: v.max(out=rt[:, 80:88], in_=rt[:, 48:80]),
                 reads=[rt_buf], writes=[rt_buf])
            p.op("dve", lambda v: v.tensor_scalar(rt[:, 48:80], rt[:, 48:80], rt[:, 85:86],
                 None, ALU.is_ge),
                 reads=[rt_buf], writes=[rt_buf])
            p.op("dve", lambda v: v.tensor_mul(rt[:, 88:120], rt[:, 48:80], lg[:, :]),
                 reads=[rt_buf, lg_buf], writes=[rt_buf])
            p.op("dve", lambda v: v.tensor_reduce(rt[:, 120:121], rt[:, 88:120], AX.X, ALU.add),
                 reads=[rt_buf], writes=[rt_buf])
            p.op("dve", lambda v: v.tensor_scalar(rt[:, 121:122], rt[:, 120:121], 1e-20, None, ALU.add),
                 reads=[rt_buf], writes=[rt_buf])
            p.op("dve", lambda v: v.reciprocal(rt[:, 122:123], rt[:, 121:122]),
                 reads=[rt_buf], writes=[rt_buf])
            p.op("dve", lambda v, tt=tt: v.tensor_scalar(comb[:, tt * NE:tt * NE + 32],
                 rt[:, 88:120], rt[:, 122:123], 2.5, ALU.mult, ALU.mult),
                 reads=[rt_buf], writes=[comb_buf])
            p.op("dve", lambda v, tt=tt: v.memset(comb[:, tt * NE + 32:tt * NE + 34], 1.0),
                 writes=[comb_buf])
        for e in range(NE):
            sl = e % 2
            p.dma("pool", guT[sl][:, :], guT_dram[e], reads=[guT_dram_buf], writes=[guT_buf[sl]])
            p.dma("pool", dnT[sl][:, :], dnT_dram[e], reads=[dnT_dram_buf], writes=[dnT_buf[sl]])
            for tg in range(MG // TGP):
                for icp in range(4):
                    for gu_i, (lo, ic) in enumerate(((0, icp), (TGP, icp + 4))):

                        def mmgu(t, ic=ic, tg=tg, sl=sl, lo=lo):
                            for hc in range(NCH):
                                last = t.matmul(pA[:, lo:lo + TGP],
                                                guT[sl][:, hc * C + ic * 128:hc * C + (ic + 1) * 128],
                                                xlnT[:, hc * MG + tg * TGP:hc * MG + (tg + 1) * TGP],
                                                start=(hc == 0), stop=(hc == NCH - 1))
                            return last
                        p.op("pe", mmgu, reads=[guT_buf[sl], xlnT_buf], writes=[pA_buf])
                        if gu_i == 0:
                            p.op("act", lambda s2, lo=lo: s2.activation(silu_s[:, :], pA[:, lo:lo + TGP],
                                 AF.Silu),
                                 reads=[pA_buf], writes=[silu_buf])
                        else:
                            p.op("dve", lambda v, icp=icp, lo=lo: v.tensor_mul(
                                 slab[:, icp * TGP:(icp + 1) * TGP], silu_s[:, :], pA[:, lo:lo + TGP]),
                                 reads=[silu_buf, pA_buf], writes=[slab_buf])
                for tsub in range(TGP // 128):
                    pX, pX_buf = (pB, pB_buf) if tsub % 2 == 0 else (pC, pC_buf)

                    def mmdn(t, tsub=tsub, sl=sl, pX=pX):
                        for half in range(2):
                            for ic in range(4):
                                last = t.matmul(pX[:, half * 512:(half + 1) * 512],
                                                slab[:, ic * TGP + tsub * 128:ic * TGP + (tsub + 1) * 128],
                                                dnT[sl][:, ic * C + half * 512:ic * C + (half + 1) * 512],
                                                start=(ic == 0), stop=(ic == 3))
                        return last
                    p.op("pe", mmdn, reads=[slab_buf, dnT_buf[sl]], writes=[pX_buf])
                    ti = tg * 4 + tsub

                    def accop(v, e=e, ti=ti, pX=pX):
                        return v.scalar_tensor_tensor(acc[:, ti * C:(ti + 1) * C], pX[:, 0:C],
                                                      comb[:, ti * NE + e:ti * NE + e + 1],
                                                      acc[:, ti * C:(ti + 1) * C], ALU.mult, ALU.add)
                    p.op("dve", accop, reads=[pX_buf, comb_buf, acc_buf], writes=[acc_buf])
        for tt in range(NTM):
            sl = tt % NS
            p.dma("sp", xt[sl][:, :], x_in[base + tt * 128: base + (tt + 1) * 128, :],
                  writes=[bufs["xt"][sl]], sem=2 + tt % 2)
            p.op("dve", lambda v, sl=sl, tt=tt: v.tensor_add(xo[sl][:, :],
                 acc[:, tt * C:(tt + 1) * C], xt[sl][:, :]),
                 reads=[acc_buf, bufs["xt"][sl]], writes=[bufs["xo"][sl]])
            p.dma("sp", x_out[base + tt * 128: base + (tt + 1) * 128, :], xo[sl][:, :],
                  reads=[bufs["xo"][sl]], sem=4 + tt % 2)


def emit_L3_scores(p, env, x_in, W, s_out, ntok, s_t_out=None, carve=None):
    nc = env.nc
    ident, identb_buf = env.sb["identb"], env.buf("identb")
    if "identf" not in env.sb:
        idf = env.add_const("identf", np.eye(128, dtype=np.float32))
        env.sbuf("identf", [128, 128], F32)
        p.dma("sp", env.sb["identf"][:], idf, writes=[env.buf("identf")])
    identf, identf_buf = env.sb["identf"], env.buf("identf")
    NS = 2
    if carve is None:
        wg_sb = env.sbuf("s_wg", [128, NCH * 8], F32)
        wg_buf = env.buf("s_wg")
        xt = [env.sbuf(f"s_xt{i}", [128, C], F32) for i in range(NS)]
        stats = [env.sbuf(f"s_st{i}", [128, 8], F32) for i in range(NS)]
        xcf = [env.sbuf(f"s_xc{i}", [128, C], F32) for i in range(NS)]
        diagf = [env.sbuf(f"s_dg{i}", [128, 128], F32) for i in range(NS)]
        sqd = env.sbuf("s_sq", [128, C], BF16)[:, :]
        xlnT32 = env.sbuf("s_xT", [128, C], F32)
        sloc = [env.sbuf(f"s_sl{i}", [128, 8], F32) for i in range(NS)]
        slocT = [env.sbuf(f"s_slT{i}", [8, 128], F32) for i in range(NS)]
        bufs = {n: [env.buf(f"s_{n}{i}") for i in range(NS)]
                for n in ("xt", "st", "xc", "dg", "sl", "slT")}
        sqd_buf = env.buf("s_sq")
        xT_buf = env.buf("s_xT")
    else:
        reg, regbuf = carve
        off = [0]

        def cv(ncols):
            v = reg[:, off[0]:off[0] + ncols]
            off[0] += ncols
            return v
        wg_sb = cv(NCH * 8); wg_buf = regbuf
        xt = [cv(C) for i in range(NS)]
        stats = [cv(8) for i in range(NS)]
        xcf = [cv(C) for i in range(NS)]
        diagf = [cv(128) for i in range(NS)]
        sqd = cv(C)
        xlnT32 = cv(C)
        sloc = [cv(8) for i in range(NS)]
        slocT = [cv(128)[0:8, :] for i in range(NS)]
        bufs = {n: [regbuf, regbuf] for n in ("xt", "st", "xc", "dg", "sl", "slT")}
        sqd_buf = regbuf
        xT_buf = regbuf
    wg3 = wg_sb[:, 0:NCH * 8].rearrange("p (c j) -> p c j", j=8)
    p.dma("sp", wg3, W["Wg"].rearrange("(c p) e -> p c e", p=128), writes=[wg_buf])
    pD, pD_buf = env.psum[list(env.psum)[-1]], env.buf(list(env.psum)[-1])
    pA, pA_buf = env.psum[list(env.psum)[0]], env.buf(list(env.psum)[0])
    for tt in range(ntok // 128):
        sl = tt % NS
        p.dma("sp", xt[sl][:, 0:C], x_in[tt * 128:(tt + 1) * 128, :], writes=[bufs["xt"][sl]], sem=tt % 2)
        p.op("dve", lambda v, sl=sl: v.tensor_reduce(stats[sl][:, 0:1], xt[sl][:, 0:C], AX.X, ALU.add,
             negate=True), reads=[bufs["xt"][sl]], writes=[bufs["st"][sl]])
        p.op("dve", lambda v, sl=sl: v.tensor_scalar(stats[sl][:, 1:2], stats[sl][:, 0:1], 1.0 / C,
             None, ALU.mult), reads=[bufs["st"][sl]], writes=[bufs["st"][sl]])
        p.op("dve", lambda v, sl=sl: v.tensor_scalar(xcf[sl][:, 0:C], xt[sl][:, 0:C], stats[sl][:, 1:2],
             None, ALU.add), reads=[bufs["xt"][sl], bufs["st"][sl]], writes=[bufs["xc"][sl]])
        p.op("act", lambda s2, sl=sl: s2.activation(sqd[:, 0:C], xcf[sl][:, 0:C], AF.Square,
             accum_out=stats[sl][:, 2:3]), reads=[bufs["xc"][sl]], writes=[sqd_buf, bufs["st"][sl]])
        p.op("dve", lambda v, sl=sl: v.tensor_scalar(stats[sl][:, 3:4], stats[sl][:, 2:3], 1.0 / C,
             1e-5, ALU.mult, ALU.add), reads=[bufs["st"][sl]], writes=[bufs["st"][sl]])
        p.op("act", lambda s2, sl=sl: s2.activation(stats[sl][:, 4:5], stats[sl][:, 3:4], AF.Sqrt),
             reads=[bufs["st"][sl]], writes=[bufs["st"][sl]])
        p.op("dve", lambda v, sl=sl: v.reciprocal(stats[sl][:, 5:6], stats[sl][:, 4:5]),
             reads=[bufs["st"][sl]], writes=[bufs["st"][sl]])
        p.op("dve", lambda v, sl=sl: v.tensor_scalar(diagf[sl][:, 0:128], identf[:, :], stats[sl][:, 5:6],
             None, ALU.mult), reads=[identf_buf, bufs["st"][sl]], writes=[bufs["dg"][sl]])

        def mmt32(t, sl=sl):
            for c in range(NCH):
                last = t.matmul(pD[:, c * 128:(c + 1) * 128], xcf[sl][:, c * 128:(c + 1) * 128],
                                diagf[sl][:, 0:128], start=True, stop=True)
            return last
        p.op("pe", mmt32, reads=[bufs["xc"][sl], bufs["dg"][sl]], writes=[pD_buf])
        p.op("dve", lambda v: v.tensor_copy(xlnT32[:, 0:C], pD[:, 0:C]), reads=[pD_buf], writes=[xT_buf])

        def mmg(t):
            for c in range(NCH):
                last = t.matmul(pA[:, 0:8], xlnT32[:, c * 128:(c + 1) * 128],
                                wg_sb[:, c * 8:(c + 1) * 8], start=(c == 0), stop=(c == NCH - 1))
            return last
        p.op("pe", mmg, reads=[xT_buf, wg_buf], writes=[pA_buf])
        p.op("act", lambda s2, sl=sl: s2.activation(sloc[sl][:, 0:8], pA[:, 0:8], AF.Sigmoid),
             reads=[pA_buf], writes=[bufs["sl"][sl]])
        p.dma("sp", s_out[tt * 128:(tt + 1) * 128, :], sloc[sl][:, 0:8], reads=[bufs["sl"][sl]],
              sem=2 + tt % 2)
        if s_t_out is not None:
            p.op("pe", lambda t, sl=sl: t.matmul(pD[0:8, 0:128], sloc[sl][:, 0:8], identf[:, :],
                 start=True, stop=True), reads=[bufs["sl"][sl], identf_buf], writes=[pD_buf])
            p.op("dve", lambda v, sl=sl: v.tensor_copy(slocT[sl][0:8, 0:128], pD[0:8, 0:128]),
                 reads=[pD_buf], writes=[bufs["slT"][sl]])
            p.dma("sp", s_t_out[:, tt * 128:(tt + 1) * 128], slocT[sl][0:8, 0:128],
                  reads=[bufs["slT"][sl]], sem=5)


def emit_L3_ffn(p, env, x_in, x_out, W, ntok, S_t, S_tok, nglob, S_deps=()):
    nc = env.nc
    E, DF = 8, 4096
    NFC = DF // 128
    TGP = 512
    K_SEL = nglob // 4
    L = nglob // 8
    ident, identb_buf = env.sb["identb"], env.buf("identb")
    ones, ones_buf = env.sb["onesb"], env.buf("onesb")
    pA, pA_buf = env.psum["pA"], env.buf("pA")
    pB, pB_buf = env.psum["pB"], env.buf("pB")
    pW, pW_buf = env.psum["pW"], env.buf("pW")
    pX, pX_buf = env.psum["pX"], env.buf("pX")
    pC, pC_buf = pW, pW_buf
    pD, pD_buf = pX, pX_buf
    if "identf" not in env.sb:
        idf = env.add_const("identf", np.eye(128, dtype=np.float32))
        env.sbuf("identf", [128, 128], F32)
        p.dma("sp", env.sb["identf"][:], idf, writes=[env.buf("identf")])
    identf, identf_buf = env.sb["identf"], env.buf("identf")

    w1r = env.sbuf("e_w1r", [128, NCH * DF], BF16); w1r_buf = env.buf("e_w1r")
    slab = w1r
    slab_buf = w1r_buf
    scnt = env.sbuf("e_scnt", [64, L], F32); scnt_buf = env.buf("e_scnt")
    gmode = (S_t is not None and getattr(S_t, "shape", [0])[0] == 64)
    if gmode:
        p.dma("sp", scnt[:, :], S_t, deps=list(S_deps), writes=[scnt_buf])
        p64 = env.add_const("p64m", (np.arange(64)[:, None] % 8 == np.arange(8)[None, :])
                            .astype(np.float32))
        p64T = env.add_const("p64Tm", (np.arange(64)[None, :] % 8 == np.arange(8)[:, None])
                             .astype(np.float32))
    else:
        p.dma("sp", scnt[:, :], S_t.rearrange("e (j n) -> (e j) n", j=8), writes=[scnt_buf])
        p64 = env.add_const("p64", (np.arange(64)[:, None] // 8 == np.arange(8)[None, :])
                            .astype(np.float32))
        p64T = env.add_const("p64T", (np.arange(64)[None, :] // 8 == np.arange(8)[:, None])
                             .astype(np.float32))
    p64_sb = env.sbuf("e_p64", [64, 8], F32); p64_buf = env.buf("e_p64")
    p64T_sb = env.sbuf("e_p64T", [8, 64], F32); p64T_buf = env.buf("e_p64T")
    p.dma("sp", p64_sb[:, :], p64, writes=[p64_buf])
    p.dma("sp", p64T_sb[:, :], p64T, writes=[p64T_buf])
    bs = env.sbuf("e_bs", [8, 16], F32); bs_buf = env.buf("e_bs")
    gem = slab.bitcast(F32)[0:64, 0:L]
    gem_buf = w1r_buf
    prt = env.sbuf("e_prt", [64, 1], F32); prt_buf = env.buf("e_prt")
    thr = env.sbuf("e_thr", [64, 1], F32); thr_buf = env.buf("e_thr")
    p.op("dve", lambda v: v.memset(bs[:, 0:1], 0.0), writes=[bs_buf])
    p.op("dve", lambda v: v.memset(bs[:, 1:2], 1.0), writes=[bs_buf])
    p.op("dve", lambda v: v.memset(bs[:, 3:4], -1e30), writes=[bs_buf])
    for it in range(40):
        p.op("dve", lambda v: v.tensor_scalar(bs[:, 2:3], bs[:, 0:1], bs[:, 1:2], 0.5,
             ALU.add, ALU.mult), reads=[bs_buf], writes=[bs_buf])
        p.op("pe", lambda t: t.matmul(pD[0:64, 0:1], p64T_sb[:, :], bs[:, 2:3], start=True, stop=True),
             reads=[p64T_buf, bs_buf], writes=[pD_buf])
        p.op("dve", lambda v: v.tensor_copy(thr[:, :], pD[0:64, 0:1]), reads=[pD_buf], writes=[thr_buf])
        p.op("dve", lambda v: v.tensor_scalar(gem[0:64, 0:L], scnt[:, :], thr[:, 0:1], None, ALU.is_ge),
             reads=[scnt_buf, thr_buf], writes=[gem_buf])
        p.op("dve", lambda v: v.tensor_reduce(prt[:, :], gem[0:64, 0:L], AX.X, ALU.add),
             reads=[gem_buf], writes=[prt_buf])
        p.op("pe", lambda t: t.matmul(pD[0:8, 0:1], p64_sb[:, :], prt[:, :], start=True, stop=True),
             reads=[p64_buf, prt_buf], writes=[pD_buf])
        p.op("dve", lambda v: v.tensor_copy(bs[:, 4:5], pD[0:8, 0:1]), reads=[pD_buf], writes=[bs_buf])
        p.op("dve", lambda v: v.tensor_scalar(bs[:, 5:6], bs[:, 4:5], float(K_SEL), None, ALU.is_ge),
             reads=[bs_buf], writes=[bs_buf])
        p.op("dve", lambda v: v.tensor_sub(bs[:, 6:7], bs[:, 2:3], bs[:, 0:1]),
             reads=[bs_buf], writes=[bs_buf])
        p.op("dve", lambda v: v.scalar_tensor_tensor(bs[:, 0:1], bs[:, 5:6], bs[:, 6:7], bs[:, 0:1],
             ALU.mult, ALU.add), reads=[bs_buf], writes=[bs_buf])
        p.op("dve", lambda v: v.tensor_scalar(bs[:, 5:6], bs[:, 5:6], -1.0, 1.0, ALU.mult, ALU.add),
             reads=[bs_buf], writes=[bs_buf])
        p.op("dve", lambda v: v.tensor_sub(bs[:, 6:7], bs[:, 2:3], bs[:, 1:2]),
             reads=[bs_buf], writes=[bs_buf])
        p.op("dve", lambda v: v.scalar_tensor_tensor(bs[:, 1:2], bs[:, 5:6], bs[:, 6:7], bs[:, 1:2],
             ALU.mult, ALU.add), reads=[bs_buf], writes=[bs_buf])
        p.op("dve", lambda v: v.tensor_scalar(bs[:, 5:6], bs[:, 4:5], float(K_SEL), None, ALU.is_equal),
             reads=[bs_buf], writes=[bs_buf])
        p.op("dve", lambda v: v.tensor_sub(bs[:, 6:7], bs[:, 2:3], bs[:, 3:4]),
             reads=[bs_buf], writes=[bs_buf])
        p.op("dve", lambda v: v.scalar_tensor_tensor(bs[:, 3:4], bs[:, 5:6], bs[:, 6:7], bs[:, 3:4],
             ALU.mult, ALU.add), reads=[bs_buf], writes=[bs_buf])
    p.op("dve", lambda v: v.tensor_scalar(bs[:, 5:6], bs[:, 3:4], -1e29, None, ALU.is_gt),
         reads=[bs_buf], writes=[bs_buf])
    p.op("dve", lambda v: v.tensor_sub(bs[:, 6:7], bs[:, 3:4], bs[:, 0:1]),
         reads=[bs_buf], writes=[bs_buf])
    p.op("dve", lambda v: v.scalar_tensor_tensor(bs[:, 7:8], bs[:, 5:6], bs[:, 6:7], bs[:, 0:1],
         ALU.mult, ALU.add), reads=[bs_buf], writes=[bs_buf])
    thb = env.sbuf("e_thb", [128, 8], F32); thb_buf = env.buf("e_thb")
    throw = env.sbuf("e_throw", [1, 8], F32); throw_buf = env.buf("e_throw")
    p.op("pe", lambda t: t.matmul(pD[0:1, 0:8], bs[:, 7:8], identf[0:8, 0:8], start=True, stop=True),
         reads=[bs_buf, identf_buf], writes=[pD_buf])
    p.op("dve", lambda v: v.tensor_copy(throw[:, :], pD[0:1, 0:8]), reads=[pD_buf], writes=[throw_buf])
    if "m_onesf" not in env.sb:
        onesf = env.sbuf("m_onesf", [1, 128], F32)
        p.op("dve", lambda v: v.memset(onesf[:, :], 1.0), writes=[env.buf("m_onesf")])
    onesf, onesf_buf = env.sb["m_onesf"], env.buf("m_onesf")
    p.op("pe", lambda t: t.matmul(pD[:, 0:8], onesf[0:1, :], throw[0:1, :], start=True, stop=True),
         reads=[onesf_buf, throw_buf], writes=[pD_buf])
    p.op("dve", lambda v: v.tensor_copy(thb[:, :], pD[:, 0:8]), reads=[pD_buf], writes=[thb_buf])

    NTC = ntok // 128
    coeff = env.sbuf("e_coeff", [128, NTC * 8], F32); coeff_buf = env.buf("e_coeff")
    stok = env.sbuf("e_stok", [128, 8], F32); stok_buf = env.buf("e_stok")
    xlnd = nc.dram_tensor("e_xlnd", [NTC, 128, C], BF16).ap()
    xlnd_buf = Buf("e_xlnd")
    accd = nc.dram_tensor("e_accd", [ntok, C], F32).ap()
    accd_buf = Buf("e_accd")
    NS = 2
    xt = [env.sbuf(f"e_xt{i}", [128, C], F32) for i in range(NS)]
    stats = [env.sbuf(f"e_st{i}", [128, 8], F32) for i in range(NS)]
    diag = [env.sbuf(f"e_dg{i}", [128, 128], BF16) for i in range(NS)]
    xcb = [env.sbuf(f"e_xc{i}", [128, C], BF16) for i in range(NS)]
    sqd = env.sbuf("e_sq", [128, C], BF16)
    xo = [env.sbuf(f"e_xo{i}", [128, C], F32) for i in range(NS)]
    bufs = {n: [env.buf(f"e_{n}{i}") for i in range(NS)] for n in ("xt", "st", "dg", "xc", "xo")}
    sqd_buf = env.buf("e_sq")
    xlt = [env.sbuf(f"e_xlt{i}", [128, C], BF16) for i in range(NS)]
    xlt_buf = [env.buf(f"e_xlt{i}") for i in range(NS)]
    for tt in range(NTC):
        sl = tt % NS
        p.dma("sp", xt[sl][:, :], x_in[tt * 128:(tt + 1) * 128, :], writes=[bufs["xt"][sl]], sem=tt % 2)
        dsts = [xlt[sl][:, c * 128:(c + 1) * 128] for c in range(NCH)]
        emit_ln_transpose(p, env, xt[sl][:, :], bufs["xt"][sl], stats[sl][:, :], bufs["st"][sl],
                          diag[sl][:, :], bufs["dg"][sl], xcb[sl][:, :], bufs["xc"][sl],
                          sqd[:, :], sqd_buf, "pX", dsts, xlt_buf[sl])
        p.dma("sp", xlnd[tt], xlt[sl][:, :], reads=[xlt_buf[sl]], writes=[xlnd_buf], sem=2 + tt % 2)
        p.dma("sp", stok[:, :], S_tok[tt * 128:(tt + 1) * 128, :], writes=[stok_buf], sem=4)
        p.op("dve", lambda v, tt=tt: v.tensor_tensor(coeff[:, tt * 8:(tt + 1) * 8], stok[:, :],
             thb[:, :], ALU.is_ge), reads=[stok_buf, thb_buf], writes=[coeff_buf])
        p.op("dve", lambda v, tt=tt: v.tensor_mul(coeff[:, tt * 8:(tt + 1) * 8],
             coeff[:, tt * 8:(tt + 1) * 8], stok[:, :]), reads=[stok_buf], writes=[coeff_buf])

    w2t = [env.sbuf(f"e_w2t{i}", [128, 512], BF16) for i in range(4)]
    w2t_bufs = [env.buf(f"e_w2t{i}") for i in range(4)]
    slabs = [env.sbuf("e_slab2", [128, NFC * TGP], BF16)]
    slabs_buf = [env.buf("e_slab2")]
    xlg = [env.sbuf(f"e_xlg{i}", [128, NCH * TGP], BF16) for i in range(2)]
    xlg_buf = [env.buf(f"e_xlg{i}") for i in range(2)]
    b2row = env.sbuf("e_b2r", [1, C], BF16); b2row_buf = env.buf("e_b2r")
    b1v = env.sbuf("e_b1v", [128, 32], F32); b1v_buf = env.buf("e_b1v")
    b1ld = env.sbuf("e_b1ld", [32, 128], F32); b1ld_buf = env.buf("e_b1ld")
    oacc = [env.sbuf(f"e_oac{i}", [128, C], F32) for i in range(4)]
    oacc_buf = [env.buf(f"e_oac{i}") for i in range(4)]
    NTG = ntok // TGP
    NSUB = TGP // 128
    for e in range(E):
        for c in range(NCH):
            p.dma("pool", w1r[:, c * DF:(c + 1) * DF], W["W1"][e, c * 128:(c + 1) * 128, :],
                  writes=[w1r_buf])
        p.dma("pool", b2row[0:1, :], W["b2"][e][None, :], writes=[b2row_buf])
        p.dma("sp", b1ld[:, :], W["b1"][e].rearrange("(a p) -> a p", p=128), writes=[b1ld_buf])
        p.op("pe", lambda t: t.matmul(pW[:, 0:32], b1ld[:, :], identf[0:32, 0:32],
             start=True, stop=True), reads=[b1ld_buf, identf_buf], writes=[pW_buf])
        p.op("dve", lambda v: v.tensor_copy(b1v[:, :], pW[:, 0:32]), reads=[pW_buf], writes=[b1v_buf])
        for tg in range(NTG):
            gsl = tg % 2
            slab, slab_buf = slabs[0], slabs_buf[0]
            for tsub in range(NSUB):
                tt = tg * NSUB + tsub
                dst3 = xlg[gsl][:, :].rearrange("p (c j) -> p c j", j=TGP)[:, :, tsub * 128:(tsub + 1) * 128]
                src3 = xlnd[tt].rearrange("p (c j) -> p c j", j=128)
                p.dma("pool", dst3, src3, reads=[xlnd_buf], writes=[xlg_buf[gsl]])
            for fc in range(NFC):
                pH, pH_buf = (pA, pA_buf) if fc % 2 == 0 else (pB, pB_buf)

                def mmh(t, fc=fc, gsl=gsl, pH=pH):
                    for c in range(NCH):
                        last = t.matmul(pH[:, 0:TGP], w1r[:, c * DF + fc * 128:c * DF + (fc + 1) * 128],
                                        xlg[gsl][:, c * TGP:(c + 1) * TGP],
                                        start=(c == 0), stop=(c == NCH - 1))
                    return last
                p.op("pe", mmh, reads=[w1r_buf, xlg_buf[gsl]], writes=[pH_buf])
                p.op("act", lambda s2, fc=fc, pH=pH, slab=slab: s2.activation(
                     slab[:, fc * TGP:(fc + 1) * TGP], pH[:, 0:TGP], AF.Gelu, bias=b1v[:, fc:fc + 1]),
                     reads=[pH_buf, b1v_buf], writes=[slab_buf])
            accv = [pW[:, 0:512], pW[:, 512:1024], pX[:, 0:512], pX[:, 512:1024]]
            for half in range(2):
                for fc in range(NFC):
                    wsl = fc % 4
                    p.dma("pool", w2t[wsl][:, 0:512],
                          W["W2"][e, fc * 128:(fc + 1) * 128, half * 512:(half + 1) * 512],
                          writes=[w2t_bufs[wsl]])

                    def mmo(t, fc=fc, wsl=wsl, slab=slab):
                        last = None
                        for tsub in range(NSUB):
                            last = t.matmul(accv[tsub],
                                            slab[:, fc * TGP + tsub * 128:fc * TGP + (tsub + 1) * 128],
                                            w2t[wsl][:, 0:512],
                                            start=(fc == 0), stop=False)
                        return last
                    p.op("pe", mmo, reads=[slab_buf, w2t_bufs[wsl]], writes=[pW_buf, pX_buf])

                def mmbias(t, half=half):
                    last = None
                    for tsub in range(NSUB):
                        last = t.matmul(accv[tsub], ones[0:1, :],
                                        b2row[0:1, half * 512:(half + 1) * 512],
                                        start=False, stop=True)
                    return last
                p.op("pe", mmbias, reads=[ones_buf, b2row_buf], writes=[pW_buf, pX_buf])
                for tsub in range(NSUB):
                    tt = tg * NSUB + tsub
                    osl = tsub
                    p.op("dve", lambda v, tsub=tsub, tt=tt, e=e, osl=osl, half=half: v.tensor_scalar(
                         oacc[osl][:, half * 512:(half + 1) * 512], accv[tsub],
                         coeff[:, tt * 8 + e:tt * 8 + e + 1], None, ALU.mult),
                         reads=[pW_buf if tsub < 2 else pX_buf, coeff_buf], writes=[oacc_buf[osl]])
                    if half == 1:
                        if e == 0:
                            p.dma("pool", accd[tt * 128:(tt + 1) * 128, :], oacc[osl][:, :],
                                  reads=[oacc_buf[osl]], writes=[accd_buf])
                        else:
                            p.dma("pool", accd[tt * 128:(tt + 1) * 128, :], oacc[osl][:, :],
                                  reads=[oacc_buf[osl]], writes=[accd_buf], accum_op=ALU.add)
    for tt in range(NTC):
        sl = tt % NS
        p.dma("sp", xt[sl][:, :], x_in[tt * 128:(tt + 1) * 128, :], writes=[bufs["xt"][sl]], sem=tt % 2)
        p.dma("sp", xo[sl][:, :], accd[tt * 128:(tt + 1) * 128, :],
              reads=[accd_buf], writes=[bufs["xo"][sl]], sem=2 + tt % 2)
        p.op("dve", lambda v, sl=sl: v.tensor_add(xo[sl][:, :], xo[sl][:, :], xt[sl][:, :]),
             reads=[bufs["xt"][sl]], writes=[bufs["xo"][sl]])
        p.dma("sp", x_out[tt * 128:(tt + 1) * 128, :], xo[sl][:, :],
              reads=[bufs["xo"][sl]], sem=4 + tt % 2)


def _l0_sinusoidal(n, d=64):
    pos = np.arange(n, dtype=np.float32)[:, None]
    div = np.exp(np.arange(0, d, 2, dtype=np.float32) * (-math.log(10000.0) / d))
    ang = pos * div
    return np.stack([np.sin(ang), np.cos(ang)], axis=-1).reshape(n, d).astype(np.float32)


def emit_L0_attn(p, env, x_in, x_out, W, ntok, T=2048):
    nc = env.nc
    H, HD, NCL, GS = 16, 64, 64, 256
    nseq = ntok // T
    NT = T // 128
    ntpc = T // NCL
    cpt = 128 // ntpc
    G = T // GS
    TPG = GS // 128
    ncpg = GS // ntpc
    c_max = (G - 1) * ncpg
    qscale = 1.0 / math.sqrt(HD)

    ident, identb_buf = env.sb["identb"], env.buf("identb")
    ones, ones_buf = env.sb["onesb"], env.buf("onesb")
    pS, pS_buf = env.psum["pS"], env.buf("pS")
    pT, pT_buf = env.psum["pT"], env.buf("pT")
    pO, pO_buf = env.psum["pO"], env.buf("pO")

    S = _l0_sinusoidal(c_max + GS)
    qsin = np.concatenate([S[pi * 128:(pi + 1) * 128, 0:32] for pi in range(TPG)], 1) * qscale
    qcos = np.concatenate([S[pi * 128:(pi + 1) * 128, 32:64] for pi in range(TPG)], 1) * qscale
    ks_l, kc_l = [], []
    for tt in range(NT):
        g, pi = tt // TPG, tt % TPG
        rows = np.arange(128) + g * ncpg + pi * 128
        ks_l.append(S[rows, 0:32]); kc_l.append(S[rows, 32:64])
    ksin, kcos = np.concatenate(ks_l, 1), np.concatenate(kc_l, 1)
    csin = np.zeros((64, 32), np.float32); ccos = np.zeros((64, 32), np.float32)
    csin[:c_max] = S[:c_max, 0:32]; ccos[:c_max] = S[:c_max, 32:64]
    seg = (np.arange(128)[:, None] // ntpc == np.arange(cpt)[None, :]).astype(np.float32)
    lm = np.tril(np.ones((GS, GS), np.float32))
    lmask = np.concatenate([np.where(lm[pi * 128:(pi + 1) * 128, :], 0, -1e30)
                            for pi in range(TPG)], 1).astype(np.float32)
    clm = np.zeros((128, NT * NCL), np.float32)
    for tt in range(NT):
        for pp in range(128):
            clm[pp, tt * NCL + tt * cpt + pp // ntpc] = 1.0
    consts = {}
    for nm, arr in (("l0qs", qsin), ("l0qc", qcos), ("l0ks", ksin), ("l0kc", kcos),
                    ("l0cs", csin), ("l0cc", ccos)):
        capn = env.add_const(nm, arr)
        env.sbuf(nm, list(arr.shape), F32)
        p.dma("sp", env.sb[nm][:, :], capn, writes=[env.buf(nm)])
        consts[nm] = (env.sb[nm], env.buf(nm))
    for nm, arr in (("l0lm", lmask), ("l0clm", clm)):
        capn = env.add_const(nm, np_bf16(arr))
        env.sbuf(nm, list(arr.shape), BF16)
        p.dma("pool", env.sb[nm][:, :], capn, writes=[env.buf(nm)])
        consts[nm] = (env.sb[nm], env.buf(nm))
    if "identf" not in env.sb:
        idf = env.add_const("identf", np.eye(128, dtype=np.float32))
        env.sbuf("identf", [128, 128], F32)
        p.dma("sp", env.sb["identf"][:], idf, writes=[env.buf("identf")])
    identf, identf_buf = env.sb["identf"], env.buf("identf")
    if "m_onesf" not in env.sb:
        onesf0 = env.sbuf("m_onesf", [1, 128], F32)
        p.op("dve", lambda v: v.memset(onesf0[:, :], 1.0), writes=[env.buf("m_onesf")])
    onesf, onesf_buf = env.sb["m_onesf"], env.buf("m_onesf")
    segb_c = env.add_const("l0seg", np_bf16(seg))
    segb = env.sbuf("l0seg", [128, cpt], BF16)
    p.dma("pool", segb[:, :], segb_c, writes=[env.buf("l0seg")])
    segb_buf = env.buf("l0seg")
    segTb_c = env.add_const("l0segT", np_bf16(seg.T.copy()))
    segTb = env.sbuf("l0segT", [cpt, 128], BF16)
    p.dma("pool", segTb[:, :], segTb_c, writes=[env.buf("l0segT")])
    segT_buf = env.buf("l0segT")

    waT_dram = nc.dram_tensor("l0_waT", [128, NCH * 3 * C], BF16).ap()
    waT_dram_buf = Buf("l0waTd")
    wpT_dram = nc.dram_tensor("l0_wpT", [128, NCH * C], BF16).ap()
    wpT_dram_buf = Buf("l0wpTd")
    wld = [env.sbuf(f"l0wld{i}", [128, C], BF16) for i in range(2)]
    wld_buf = [env.buf(f"l0wld{i}") for i in range(2)]
    xlnT = env.sbuf("l0xlnT", [128, NCH * T], BF16); xlnT_buf = env.buf("l0xlnT")
    tsl = [xlnT[:, i * 1024:(i + 1) * 1024] for i in range(2)]
    tsl_buf = [Buf(f"l0tsl{i}") for i in range(2)]
    for wsrc_ap, rows, dram, dbuf in ((W["w_attn"], 3 * C, waT_dram, waT_dram_buf),
                                      (W["w_proj"], C, wpT_dram, wpT_dram_buf)):
        for rc in range(rows // 128):
            sl = rc % 2
            p.dma("pool", wld[sl][:, :], wsrc_ap[rc * 128:(rc + 1) * 128, :], writes=[wld_buf[sl]])

            def mmtr(t, sl=sl):
                for cc in range(NCH):
                    last = t.matmul(pT[:, cc * 128:(cc + 1) * 128],
                                    wld[sl][:, cc * 128:(cc + 1) * 128], ident[:, :],
                                    start=True, stop=True)
                return last
            p.op("pe", mmtr, reads=[wld_buf[sl], identb_buf], writes=[pT_buf])
            p.op("dve", lambda v, sl=sl: v.tensor_copy(tsl[sl], pT[:, 0:1024]),
                 reads=[pT_buf], writes=[tsl_buf[sl]])
            for cc in range(NCH):
                p.dma("sp", dram[:, cc * rows + rc * 128: cc * rows + (rc + 1) * 128],
                      tsl[sl][:, sl * 0 + cc * 128:(cc + 1) * 128] if False else
                      xlnT[:, sl * 1024 + cc * 128: sl * 1024 + (cc + 1) * 128],
                      reads=[tsl_buf[sl]], writes=[dbuf])
    was_sb = env.sbuf("l0was", [64, C], BF16)
    was_buf = env.buf("l0was")
    p.dma("pool", was_sb[:, :], W["w_assign"], writes=[was_buf])
    wasT = env.sbuf("l0wasT", [128, NCH * 64], BF16)
    wasT_buf = env.buf("l0wasT")
    for cc in range(NCH):
        p.op("pe", lambda t, cc=cc: t.matmul(pT[:, 0:64], was_sb[:, cc * 128:(cc + 1) * 128],
             ident[0:64, 0:64], start=True, stop=True),
             reads=[was_buf, identb_buf], writes=[pT_buf])
        p.op("dve", lambda v, cc=cc: v.tensor_copy(wasT[:, cc * 64:(cc + 1) * 64], pT[:, 0:64]),
             reads=[pT_buf], writes=[wasT_buf])
    ba_sl = [env.sbuf(f"l0ba{i}", [1, 256], BF16) for i in range(2)]
    ba_sl_buf = [env.buf(f"l0ba{i}") for i in range(2)]
    bp_row, bp_buf = emit_row_load(p, env, W["b_proj"], C, "l0bp")
    wpT = env.sbuf("l0wpT", [128, NCH * C], BF16)
    wpT_buf = env.buf("l0wpT")
    p.dma("pool", wpT[:, :], wpT_dram, reads=[wpT_dram_buf], writes=[wpT_buf])

    for tb in tsl_buf:
        if tb.w is not None:
            xlnT_buf.rs.extend(tb.rs)
            xlnT_buf.rs.append(tb.w)
    qfm = env.sbuf("l0qfm", [128, NCH * T], BF16); qfm_buf = env.buf("l0qfm")
    ktok = env.sbuf("l0ktok", [128, NT * C], BF16); ktok_buf = env.buf("l0ktok")
    vtok = env.sbuf("l0vtok", [128, NT * C], BF16); vtok_buf = env.buf("l0vtok")
    wstr = [env.sbuf(f"l0ws{i}", [128, NCH * 256], BF16) for i in range(2)]
    wstr_buf = [env.buf(f"l0ws{i}") for i in range(2)]
    NS = 2
    xt = [env.sbuf(f"l0xt{i}", [128, C], F32) for i in range(NS)]
    stats = [env.sbuf(f"l0st{i}", [128, 8], F32) for i in range(NS)]
    diag = [env.sbuf(f"l0dg{i}", [128, 128], BF16) for i in range(NS)]
    xcb = [env.sbuf(f"l0xc{i}", [128, C], BF16) for i in range(NS)]
    sqd = env.sbuf("l0sq", [128, C], BF16)
    xo = [env.sbuf(f"l0xo{i}", [128, C], F32) for i in range(NS)]
    bufs = {n: [env.buf(f"l0{n}{i}") for i in range(NS)] for n in ("xt", "st", "dg", "xc", "xo")}
    sqd_buf = env.buf("l0sq")
    qt_tmp = env.sbuf("l0qtt", [128, 512], BF16); qt_tmp_buf = env.buf("l0qtt")
    qr_tmp = env.sbuf("l0qrt", [128, 512], BF16); qr_tmp_buf = env.buf("l0qrt")
    ck_dram = nc.dram_tensor("l0ckd", [NCL, C], BF16).ap()
    cv_dram = nc.dram_tensor("l0cvd", [NCL, C], BF16).ap()
    ck_dram_buf = Buf("l0ckd"); cv_dram_buf = Buf("l0cvd")
    cks = env.sbuf("l0cks", [cpt, C], BF16); cks_buf = env.buf("l0cks")
    cvs = env.sbuf("l0cvs", [cpt, C], BF16); cvs_buf = env.buf("l0cvs")
    kcfm = env.sbuf("l0kcfm", [128, H * 64], BF16); kcfm_buf = env.buf("l0kcfm")
    lgx = env.sbuf("l0lgx", [128, 64], F32); lgx_buf = env.buf("l0lgx")
    lcol = env.sbuf("l0lcol", [128, 2], F32); lcol_buf = env.buf("l0lcol")
    lrow = env.sb["l0sq"].bitcast(F32); lrow_buf = env.buf("l0sq")
    lsc = env.sbuf("l0lsc", [128, 2], F32); lsc_buf = env.buf("l0lsc")
    ecol = env.sbuf("l0ecol", [128, 1], BF16); ecol_buf = env.buf("l0ecol")
    rsum = env.sbuf("l0rsum", [cpt, 2], F32); rsum_buf = env.buf("l0rsum")
    rsumb = env.sbuf("l0rsumb", [cpt, 1], BF16); rsumb_buf = env.buf("l0rsumb")
    wcol = env.sbuf("l0wcol", [128, 1], F32); wcol_buf = env.buf("l0wcol")
    Wseg = env.sbuf("l0Wseg", [128, NT * cpt], BF16); Wseg_buf = env.buf("l0Wseg")
    st2 = [env.sbuf(f"l0s2_{i}", [128, 4], F32) for i in range(2)]
    st2_buf = [env.buf(f"l0s2_{i}") for i in range(2)]

    for s in range(nseq):
        base = s * T
        for tt in range(NT):
            sl = tt % NS
            p.dma("sp", xt[sl][:, :], x_in[base + tt * 128: base + (tt + 1) * 128, :],
                  writes=[bufs["xt"][sl]], sem=tt % 2)
            dsts = [xlnT[:, c * T + tt * 128: c * T + (tt + 1) * 128] for c in range(NCH)]
            emit_ln_transpose(p, env, xt[sl][:, :], bufs["xt"][sl], stats[sl][:, :], bufs["st"][sl],
                              diag[sl][:, :], bufs["dg"][sl], xcb[sl][:, :], bufs["xc"][sl],
                              sqd[:, :], sqd_buf, "pT", dsts, xlnT_buf)
        for colq in range(12):
            wslot = colq % 2
            wt = wstr[wslot]
            wt3 = wt[:, :].rearrange("p (c j) -> p c j", j=256)
            wsrc3 = waT_dram.rearrange("p (c f) -> p c f", f=3 * C)
            p.dma("pool", wt3, wsrc3[:, :, colq * 256:(colq + 1) * 256],
                  reads=[waT_dram_buf], writes=[wstr_buf[wslot]])
            p.dma("pool", ba_sl[wslot][0:1, :], W["b_attn"][None, colq * 256:(colq + 1) * 256],
                  writes=[ba_sl_buf[wslot]])
            for tt in range(NT):
                hap = pS[:, (tt % 2) * 512:(tt % 2) * 512 + 256]

                def mmqkv(t, tt=tt, colq=colq, hap=hap, wt=wt, wslot=wslot):
                    for c in range(NCH):
                        t.matmul(hap, xlnT[:, c * T + tt * 128:c * T + (tt + 1) * 128],
                                 wt[:, c * 256:(c + 1) * 256], start=(c == 0), stop=False)
                    return t.matmul(hap, ones[0:1, :], ba_sl[wslot][0:1, :],
                                    start=False, stop=True)
                p.op("pe", mmqkv, reads=[wstr_buf[wslot], xlnT_buf, ones_buf, ba_sl_buf[wslot]],
                     writes=[pS_buf])
                if colq < 4:
                    p.op("dve", lambda v, hap=hap: v.tensor_copy(qt_tmp[:, 0:256], hap),
                         reads=[pS_buf], writes=[qt_tmp_buf])
                    pi = tt % TPG
                    qs_t, qs_b = consts["l0qs"]; qc_t, qc_b = consts["l0qc"]
                    for hh in range(4):
                        p.op("dve", lambda v, hh=hh, pi=pi: v.tensor_mul(
                             qr_tmp[:, hh * 64 + 1:hh * 64 + 64:2], qt_tmp[:, hh * 64:hh * 64 + 64:2],
                             qs_t[:, pi * 32:(pi + 1) * 32]),
                             reads=[qt_tmp_buf, qs_b], writes=[qr_tmp_buf])
                        p.op("dve", lambda v, hh=hh, pi=pi: v.tensor_mul(
                             qr_tmp[:, hh * 64:hh * 64 + 64:2], qt_tmp[:, hh * 64 + 1:hh * 64 + 64:2],
                             qc_t[:, pi * 32:(pi + 1) * 32]),
                             reads=[qt_tmp_buf, qc_b], writes=[qr_tmp_buf])
                    p.op("dve", lambda v: v.tensor_scalar(qr_tmp[:, 0:256:2], qr_tmp[:, 0:256:2],
                         -1.0, None, ALU.mult), reads=[qr_tmp_buf], writes=[qr_tmp_buf])

                    def mmtq(t, tt=tt):
                        for j in range(2):
                            last = t.matmul(pT[:, j * 128:(j + 1) * 128],
                                            qr_tmp[:, j * 128:(j + 1) * 128], ident[:, :],
                                            start=True, stop=True)
                        return last
                    p.op("pe", mmtq, reads=[qr_tmp_buf, identb_buf], writes=[pT_buf])

                    def evq(v, tt=tt, colq=colq):
                        for j in range(2):
                            cc = colq * 2 + j
                            last = v.tensor_copy(qfm[:, cc * T + tt * 128:cc * T + (tt + 1) * 128],
                                                 pT[:, j * 128:(j + 1) * 128])
                        return last
                    p.op("dve", evq, reads=[pT_buf], writes=[qfm_buf])
                elif colq < 8:
                    p.op("dve", lambda v, tt=tt, colq=colq, hap=hap: v.tensor_copy(
                         ktok[:, tt * C + (colq - 4) * 256:tt * C + (colq - 3) * 256], hap),
                         reads=[pS_buf], writes=[ktok_buf])
                else:
                    p.op("dve", lambda v, tt=tt, colq=colq, hap=hap: v.tensor_copy(
                         vtok[:, tt * C + (colq - 8) * 256:tt * C + (colq - 7) * 256], hap),
                         reads=[pS_buf], writes=[vtok_buf])
        clm_t, clm_b = consts["l0clm"]
        for tt in range(NT):
            def mml(t, tt=tt):
                for c in range(NCH):
                    last = t.matmul(pO[:, 0:64], xlnT[:, c * T + tt * 128:c * T + (tt + 1) * 128],
                                    wasT[:, c * 64:(c + 1) * 64], start=(c == 0), stop=(c == NCH - 1))
                return last
            p.op("pe", mml, reads=[xlnT_buf, wasT_buf], writes=[pO_buf])
            p.op("dve", lambda v, tt=tt: v.tensor_mul(lgx[:, :], pO[:, 0:64],
                 clm_t[:, tt * 64:(tt + 1) * 64]), reads=[pO_buf, clm_b], writes=[lgx_buf])
            p.op("dve", lambda v: v.tensor_reduce(lcol[:, 0:1], lgx[:, :], AX.X, ALU.add),
                 reads=[lgx_buf], writes=[lcol_buf])
            p.op("pe", lambda t: t.matmul(pT[0:1, 0:128], lcol[:, 0:1], identf[:, :],
                 start=True, stop=True), reads=[lcol_buf, identf_buf], writes=[pT_buf])
            p.op("dve", lambda v: v.tensor_copy(lrow[0:1, 0:128], pT[0:1, 0:128]),
                 reads=[pT_buf], writes=[lrow_buf])
            p.op("dve", lambda v: v.tensor_reduce(lsc[0:1, 0:1], lrow[0:1, 0:128], AX.X, ALU.max,
                 negate=True), reads=[lrow_buf], writes=[lsc_buf])
            p.op("pe", lambda t: t.matmul(pT[:, 0:1], onesf[0:1, :], lsc[0:1, 0:1],
                 start=True, stop=True), reads=[onesf_buf, lsc_buf], writes=[pT_buf])
            p.op("dve", lambda v: v.tensor_copy(lsc[:, 1:2], pT[:, 0:1]),
                 reads=[pT_buf], writes=[lsc_buf])
            p.op("act", lambda s2: s2.activation(ecol[:, :], lcol[:, 0:1], AF.Exp,
                 bias=lsc[:, 1:2]), reads=[lcol_buf, lsc_buf], writes=[ecol_buf])
            p.op("pe", lambda t: t.matmul(pT[0:cpt, 0:1], segb[:, :], ecol[:, :],
                 start=True, stop=True), reads=[segb_buf, ecol_buf], writes=[pT_buf])
            p.op("dve", lambda v: v.tensor_copy(rsum[:, 0:1], pT[0:cpt, 0:1]),
                 reads=[pT_buf], writes=[rsum_buf])
            p.op("dve", lambda v: v.reciprocal(rsum[:, 1:2], rsum[:, 0:1]),
                 reads=[rsum_buf], writes=[rsum_buf])
            p.op("dve", lambda v: v.tensor_copy(rsumb[:, 0:1], rsum[:, 1:2]),
                 reads=[rsum_buf], writes=[rsumb_buf])
            p.op("pe", lambda t: t.matmul(pT[:, 0:1], segTb[:, :], rsumb[:, 0:1],
                 start=True, stop=True), reads=[segT_buf, rsumb_buf], writes=[pT_buf])
            p.op("dve", lambda v: v.scalar_tensor_tensor(wcol[:, :], pT[:, 0:1], 1.0, ecol[:, :],
                 ALU.mult, ALU.mult), reads=[pT_buf, ecol_buf], writes=[wcol_buf])
            p.op("dve", lambda v, tt=tt: v.tensor_scalar(Wseg[:, tt * cpt:(tt + 1) * cpt],
                 segb[:, :], wcol[:, 0:1], None, ALU.mult),
                 reads=[segb_buf, wcol_buf], writes=[Wseg_buf])
        for tt in range(NT):
            def mmck(t, tt=tt):
                t.matmul(pT[0:cpt, 0:512], Wseg[:, tt * cpt:(tt + 1) * cpt],
                         ktok[:, tt * C:tt * C + 512], start=True, stop=True)
                t.matmul(pT[0:cpt, 512:1024], Wseg[:, tt * cpt:(tt + 1) * cpt],
                         ktok[:, tt * C + 512:(tt + 1) * C], start=True, stop=True)
                t.matmul(pO[0:cpt, 0:512], Wseg[:, tt * cpt:(tt + 1) * cpt],
                         vtok[:, tt * C:tt * C + 512], start=True, stop=True)
                return t.matmul(pO[0:cpt, 512:1024], Wseg[:, tt * cpt:(tt + 1) * cpt],
                                vtok[:, tt * C + 512:(tt + 1) * C], start=True, stop=True)
            p.op("pe", mmck, reads=[Wseg_buf, ktok_buf, vtok_buf], writes=[pT_buf, pO_buf])
            p.op("dve", lambda v: v.tensor_copy(cks[:, :], pT[0:cpt, 0:C]),
                 reads=[pT_buf], writes=[cks_buf])
            p.op("dve", lambda v: v.tensor_copy(cvs[:, :], pO[0:cpt, 0:C]),
                 reads=[pO_buf], writes=[cvs_buf])
            p.dma("sp", ck_dram[tt * cpt:(tt + 1) * cpt, :], cks[:, :], reads=[cks_buf],
                  writes=[ck_dram_buf], sem=3)
            p.dma("sp", cv_dram[tt * cpt:(tt + 1) * cpt, :], cvs[:, :], reads=[cvs_buf],
                  writes=[cv_dram_buf], sem=4)
        klfm = env.sb["l0xlnT"]
        klfm_buf = env.alias_buf(xlnT_buf, "l0klfm")
        kls_t, kls_b = consts["l0ks"]; klc_t, klc_b = consts["l0kc"]
        if "l0krot" not in env.sb:
            env.sbuf("l0krot", [128, C], BF16)
        krot, krot_buf = env.sb["l0krot"], env.buf("l0krot")
        for tt in range(NT):
            for hh in range(H):
                p.op("dve", lambda v, hh=hh, tt=tt: v.tensor_mul(
                     krot[:, hh * 64 + 1:hh * 64 + 64:2], ktok[:, tt * C + hh * 64:tt * C + hh * 64 + 64:2],
                     kls_t[:, tt * 32:(tt + 1) * 32]),
                     reads=[ktok_buf, kls_b], writes=[krot_buf])
                p.op("dve", lambda v, hh=hh, tt=tt: v.tensor_mul(
                     krot[:, hh * 64:hh * 64 + 64:2], ktok[:, tt * C + hh * 64 + 1:tt * C + hh * 64 + 64:2],
                     klc_t[:, tt * 32:(tt + 1) * 32]),
                     reads=[ktok_buf, klc_b], writes=[krot_buf])
            p.op("dve", lambda v: v.tensor_scalar(krot[:, 0:C:2], krot[:, 0:C:2], -1.0, None,
                 ALU.mult), reads=[krot_buf], writes=[krot_buf])

            def mmtk(t, tt=tt):
                for j in range(NCH):
                    last = t.matmul(pT[:, (j % 8) * 128:(j % 8) * 128 + 128],
                                    krot[:, j * 128:(j + 1) * 128], ident[:, :],
                                    start=True, stop=True)
                return last
            p.op("pe", mmtk, reads=[krot_buf, identb_buf], writes=[pT_buf])

            def evk(v, tt=tt):
                for j in range(NCH):
                    last = v.tensor_copy(klfm[:, j * T + tt * 128:j * T + (tt + 1) * 128],
                                         pT[:, (j % 8) * 128:(j % 8) * 128 + 128])
                return last
            p.op("dve", evk, reads=[pT_buf], writes=[klfm_buf])
        ckt = ktok[0:64, 0:C]
        cvt = ktok[0:64, C:2 * C]
        ckr = ktok[0:64, 2 * C:3 * C]
        ckt_buf = env.alias_buf(ktok_buf, "l0ckt")
        cvt_buf = ckt_buf
        ckr_buf = ckt_buf
        p.dma("pool", ckt[:, :], ck_dram, reads=[ck_dram_buf], writes=[ckt_buf])
        p.dma("pool", cvt[:, :], cv_dram, reads=[cv_dram_buf], writes=[cvt_buf])
        cs_t, cs_b = consts["l0cs"]; cc_t, cc_b = consts["l0cc"]
        for hh in range(H):
            p.op("dve", lambda v, hh=hh: v.tensor_mul(ckr[:, hh * 64 + 1:hh * 64 + 64:2],
                 ckt[:, hh * 64:hh * 64 + 64:2], cs_t[:, :]),
                 reads=[ckt_buf, cs_b], writes=[ckr_buf])
            p.op("dve", lambda v, hh=hh: v.tensor_mul(ckr[:, hh * 64:hh * 64 + 64:2],
                 ckt[:, hh * 64 + 1:hh * 64 + 64:2], cc_t[:, :]),
                 reads=[ckt_buf, cc_b], writes=[ckr_buf])
        p.op("dve", lambda v: v.tensor_scalar(ckr[:, 0:C:2], ckr[:, 0:C:2], -1.0, None, ALU.mult),
             reads=[ckr_buf], writes=[ckr_buf])
        for hh in range(H):
            p.op("pe", lambda t, hh=hh: t.matmul(pT[0:64, 0:64], ckr[:, hh * 64:(hh + 1) * 64],
                 ident[0:64, 0:64], start=True, stop=True),
                 reads=[ckr_buf, identb_buf], writes=[pT_buf])
            p.op("dve", lambda v, hh=hh: v.tensor_copy(kcfm[0:64, hh * 64:(hh + 1) * 64],
                 pT[0:64, 0:64]), reads=[pT_buf], writes=[kcfm_buf])
            p.op("dve", lambda v, hh=hh: v.tensor_copy(kcfm[64:128, hh * 64:(hh + 1) * 64],
                 pT[0:64, 0:64]), reads=[pT_buf], writes=[kcfm_buf])
        Es = [qt_tmp[:, 0:320], qr_tmp[:, 0:320]]
        Es_buf = [env.alias_buf(qt_tmp_buf, "l0E0"), env.alias_buf(qr_tmp_buf, "l0E1")]
        ETs = krot[:, 0:512]
        ETs_buf = env.alias_buf(krot_buf, "l0ET")
        dgl = [diag[0][:, :], diag[1][:, :]]
        dgl_buf = [env.alias_buf(bufs["dg"][0], "l0dl0"), env.alias_buf(bufs["dg"][1], "l0dl1")]
        ysb = xcb[0][:, :]
        ysb_buf = env.alias_buf(bufs["xc"][0], "l0y")
        lm_t, lm_b = consts["l0lm"]
        for g in range(G):
            ncl_vis = g * ncpg
            for qt2 in range(TPG):
                qt = g * TPG + qt2
                for h in range(H):
                    hp = 64 * (h % 2)
                    hc = h // 2
                    width = ncl_vis + GS

                    def mmsc(t, g=g, qt=qt, h=h, hp=hp, hc=hc, ncl_vis=ncl_vis):
                        if ncl_vis > 0:
                            t.matmul(pS[:, 0:ncl_vis],
                                     qfm[hp:hp + 64, hc * T + qt * 128:hc * T + (qt + 1) * 128],
                                     kcfm[hp:hp + 64, h * 64:h * 64 + ncl_vis],
                                     start=True, stop=True)
                        for j in range(TPG):
                            last = t.matmul(pS[:, ncl_vis + j * 128:ncl_vis + (j + 1) * 128],
                                            qfm[hp:hp + 64, hc * T + qt * 128:hc * T + (qt + 1) * 128],
                                            klfm[hp:hp + 64, hc * T + g * GS + j * 128:hc * T + g * GS + (j + 1) * 128],
                                            start=True, stop=True)
                        return last
                    p.op("pe", mmsc, reads=[qfm_buf, kcfm_buf, klfm_buf], writes=[pS_buf])
                    par = (qt * H + h) % 2
                    p.op("dve", lambda v, ncl_vis=ncl_vis, qt2=qt2: v.tensor_tensor(
                         pS[:, ncl_vis:ncl_vis + GS], pS[:, ncl_vis:ncl_vis + GS],
                         lm_t[:, qt2 * GS:(qt2 + 1) * GS], ALU.add),
                         reads=[lm_b], writes=[pS_buf])
                    p.op("dve", lambda v, width=width, par=par: v.tensor_reduce(
                         st2[par][:, 0:1], pS[:, 0:width], AX.X, ALU.max, negate=True),
                         reads=[pS_buf], writes=[st2_buf[par]])
                    p.op("act", lambda s2, width=width, par=par: s2.activation(
                         Es[par][:, 0:width], pS[:, 0:width], AF.Exp, bias=st2[par][:, 0:1],
                         accum_out=st2[par][:, 1:2]),
                         reads=[pS_buf, st2_buf[par]], writes=[Es_buf[par], st2_buf[par]])
                    p.op("dve", lambda v, par=par: v.reciprocal(st2[par][:, 2:3], st2[par][:, 1:2]),
                         reads=[st2_buf[par]], writes=[st2_buf[par]])
                    p.op("dve", lambda v, par=par: v.tensor_scalar(
                         dgl[par][:, :], ident[:, :], st2[par][:, 2:3], None, ALU.mult),
                         reads=[identb_buf, st2_buf[par]], writes=[dgl_buf[par]])
                    def mmtE(t, ncl_vis=ncl_vis, par=par):
                        o = 0
                        if ncl_vis > 0:
                            t.matmul(pT[0:ncl_vis, 0:128], Es[par][:, 0:ncl_vis], dgl[par][:, :],
                                     start=True, stop=True)
                        for j in range(TPG):
                            last = t.matmul(pO[0:128, j * 128:(j + 1) * 128],
                                            Es[par][:, ncl_vis + j * 128:ncl_vis + (j + 1) * 128],
                                            dgl[par][:, :], start=True, stop=True)
                        return last
                    p.op("pe", mmtE, reads=[Es_buf[par], dgl_buf[par]], writes=[pT_buf, pO_buf])

                    def evE(v, ncl_vis=ncl_vis):
                        if ncl_vis > 0:
                            v.tensor_copy(ETs[:ncl_vis, 0:128], pT[0:ncl_vis, 0:128])
                        return v.tensor_copy(ETs[:, 128:128 + TPG * 128], pO[:, 0:TPG * 128])
                    p.op("dve", evE, reads=[pT_buf, pO_buf], writes=[ETs_buf])

                    def mmpv(t, g=g, h=h, ncl_vis=ncl_vis):
                        first = True
                        if ncl_vis > 0:
                            t.matmul(pT[0:64, 896:1024], cvt[0:ncl_vis, h * 64:(h + 1) * 64],
                                     ETs[0:ncl_vis, 0:128], start=True, stop=False)
                            first = False
                        for j in range(TPG):
                            ksg = g * TPG + j
                            last = t.matmul(pT[0:64, 896:1024],
                                            vtok[:, ksg * C + h * 64:ksg * C + (h + 1) * 64],
                                            ETs[:, 128 + j * 128:128 + (j + 1) * 128],
                                            start=first, stop=(j == TPG - 1))
                            first = False
                        return last
                    p.op("pe", mmpv, reads=[cvt_buf, vtok_buf, ETs_buf], writes=[pT_buf])
                    p.op("dve", lambda v, hp=hp, hc=hc: v.tensor_copy(
                         ysb[hp:hp + 64, hc * 128:(hc + 1) * 128], pT[0:64, 896:1024]),
                         reads=[pT_buf], writes=[ysb_buf])
                sl = qt % NS
                p.dma("sp", xt[sl][:, :], x_in[base + qt * 128: base + (qt + 1) * 128, :],
                      writes=[bufs["xt"][sl]], sem=qt % 2)
                for half in range(2):
                    def mmo(t, half=half):
                        for cc in range(NCH):
                            t.matmul(pO[:, half * 512:(half + 1) * 512],
                                     ysb[:, cc * 128:(cc + 1) * 128],
                                     wpT[:, cc * C + half * 512:cc * C + (half + 1) * 512],
                                     start=(cc == 0), stop=False)
                        return t.matmul(pO[:, half * 512:(half + 1) * 512], ones[0:1, :],
                                        bp_row[0:1, half * 512:(half + 1) * 512],
                                        start=False, stop=True)
                    p.op("pe", mmo, reads=[ysb_buf, wpT_buf, ones_buf, bp_buf], writes=[pO_buf])
                p.op("dve", lambda v, sl=sl: v.tensor_add(xo[sl][:, :], pO[:, 0:C], xt[sl][:, :]),
                     reads=[pO_buf, bufs["xt"][sl]], writes=[bufs["xo"][sl]])
                p.dma("sp", x_out[base + qt * 128: base + (qt + 1) * 128, :], xo[sl][:, :],
                      reads=[bufs["xo"][sl]], sem=5)
        xlnT_buf.w = klfm_buf.w
        xlnT_buf.rs = list(klfm_buf.rs)
        ktok_buf.w = ckt_buf.w
        ktok_buf.rs = list(ckt_buf.rs)
        qt_tmp_buf.w = Es_buf[0].w
        qt_tmp_buf.rs = list(Es_buf[0].rs)
        qr_tmp_buf.w = Es_buf[1].w
        qr_tmp_buf.rs = list(Es_buf[1].rs)
        krot_buf.w = ETs_buf.w
        krot_buf.rs = list(ETs_buf.rs)
        for i2 in range(2):
            bufs["dg"][i2].w = dgl_buf[i2].w
            bufs["dg"][i2].rs = list(dgl_buf[i2].rs)
        bufs["xc"][0].w = ysb_buf.w
        bufs["xc"][0].rs = list(ysb_buf.rs)


BLOCK_WEIGHTS = {
    "L0_ffn": {"W": [C, C], "b": [C], "lora_down": [4, 100], "lora_up": [50, 4],
               "down_aux": [100, C], "up_aux": [C, 50]},
    "L1_ffn": {"W1": [C, 4096], "b1": [4096], "quantum_phases": [4096],
               "quantum_weights_real": [4096], "quantum_weights_imag": [4096],
               "W2": [4096, C], "b2": [C]},
    "L2_attn": {"Wq": [C, C], "Wk": [C, C], "Wv": [C, C], "Wo": [C, C]},
    "L1_attn": {"Wqkv": [C, 3 * C], "Wout": [C, C]},
    "L3_attn": {"wq": [C, C], "bq": [C], "wk": [C, C], "bk": [C],
                "wv": [C, C], "bv": [C], "wo": [C, C], "bo": [C]},
    "L2_ffn": {"gate_weight": [32, C], "e_score_correction_bias": [32],
               "experts_gate_up": [32, C, C], "experts_down": [32, C, 512],
               "shared_gate_w": [C, C], "shared_up_w": [C, C], "shared_down_w": [C, C]},
    "L3_ffn": {"Wg": [C, 8], "W1": [8, C, 4096], "b1": [8, 4096],
               "W2": [8, 4096, C], "b2": [8, C]},
    "L0_attn": {"w_attn": [3 * C, C], "b_attn": [3 * C], "w_proj": [C, C],
                "b_proj": [C], "w_assign": [64, C]},
}

EMITTERS = {"L0_ffn": emit_L0_ffn, "L1_ffn": emit_L1_ffn, "L2_attn": None,
            "L2_ffn": emit_L2_ffn}

PSUM_SPECS = {"L2_attn": [("pS", 2048), ("pT", 1024), ("pO", 1024)],
              "L3_ffn": [("pA", 512), ("pB", 512), ("pW", 1024), ("pX", 1024)],
              "L1_attn": [("pS", 2048), ("pT", 1024), ("pO", 1024)],
              "L3_attn": [("pS", 1024), ("pS2", 1024), ("pT", 1024), ("pO", 1024)],
              "L0_attn": [("pS", 1024), ("pT", 1024), ("pO", 1024)]}


def build_single_block(name, ntok, T=2048, chain=False, nglob=None):
    nc = bass.Bass("TRN2", target_bir_lowering=False)
    ctx = ExitStack()
    with ctx:
        x_in = nc.dram_tensor("x_in", [ntok, C], F32, kind="ExternalInput").ap()
        x_out = nc.dram_tensor("x_out", [ntok, C], F32, kind="ExternalOutput").ap()
        W = {}
        for wname, shape in BLOCK_WEIGHTS[name].items():
            W[wname] = nc.dram_tensor(wname, shape, F32, kind="ExternalInput").ap()
        p = Prog(nc, ctx)
        env = Env(nc, ctx, p, psum_spec=PSUM_SPECS.get(name))
        setup_common_consts(env)
        if name == "L3_attn" and chain:
            WG = nc.dram_tensor("Wg", [C, 8], F32, kind="ExternalInput").ap()
            S_tok_o = nc.dram_tensor("S_tok_o", [ntok, 8], F32, kind="ExternalOutput").ap()
            S_T_o = nc.dram_tensor("S_T_o", [8, ntok], F32, kind="ExternalOutput").ap()
            emit_L3_attn(p, env, x_in, x_out, W, ntok, T=T)
            p.barrier()
            carve_reg = env.sb["a_qfm"].bitcast(F32)
            carve_buf = env.alias_buf(env.buf("a_qfm"), "s_carve")
            emit_L3_scores(p, env, x_out, {"Wg": WG}, S_tok_o, ntok, s_t_out=S_T_o,
                           carve=(carve_reg, carve_buf))
        elif name == "L3_ffn" and chain:
            ngl = nglob or ntok
            S_t_in = nc.dram_tensor("S_t_in", [8, ngl], F32, kind="ExternalInput").ap()
            S_tok_in = nc.dram_tensor("S_tok_in", [ntok, 8], F32, kind="ExternalInput").ap()
            emit_L3_ffn(p, env, x_in, x_out, W, ntok, S_t_in, S_tok_in, ngl)
        elif name == "L3_ffn":
            S_tok_d = nc.dram_tensor("S_tok_d", [ntok, 8], F32).ap()
            S_T_d = nc.dram_tensor("S_T_d", [8, ntok], F32).ap()
            emit_L3_scores(p, env, x_in, W, S_tok_d, ntok, s_t_out=S_T_d)
            p.barrier()
            emit_L3_ffn(p, env, x_in, x_out, W, ntok, S_T_d, S_tok_d, ntok)
        elif name == "L2_attn":
            emit_L2_attn(p, env, x_in, x_out, W, ntok, T=T)
        elif name == "L1_attn":
            emit_L1_attn(p, env, x_in, x_out, W, ntok, T=T)
        elif name == "L3_attn":
            emit_L3_attn(p, env, x_in, x_out, W, ntok, T=T)
        elif name == "L0_attn":
            emit_L0_attn(p, env, x_in, x_out, W, ntok, T=T)
        else:
            EMITTERS[name](p, env, x_in, x_out, W, ntok)
        p.wait("sp", p.all_tokens())
        p.emit()
    return nc, env.host_consts



BLOCK_SEQ_ALL = ["L0_attn", "L0_ffn", "L1_attn", "L1_ffn",
                 "L2_attn", "L2_ffn", "L3_attn", "L3_ffn"]

DEFAULT_PSUM = [("pA", 1024), ("pB", 1024), ("pC", 1024), ("pD", 1024)]


def build_fused(ntok=8192, T=2048, nglob=65536, ncores=8, mode="single"):
    nc = bass.Bass("TRN2", target_bir_lowering=False)
    ctx = ExitStack()
    with ctx:
        x_in = nc.dram_tensor("x_in", [ntok, C], F32, kind="ExternalInput").ap()
        x_out = nc.dram_tensor("x_out", [ntok, C], F32, kind="ExternalOutput").ap()
        blocks = BLOCK_SEQ_ALL if mode == "single" else BLOCK_SEQ_ALL[:7]
        W = {}
        for blk in BLOCK_SEQ_ALL:
            W[blk] = {}
            for wname, shape in BLOCK_WEIGHTS[blk].items():
                if blk == "L3_ffn" and mode == "A" and wname != "Wg":
                    continue
                W[blk][wname] = nc.dram_tensor(f"{blk}_{wname}", shape, F32,
                                               kind="ExternalInput").ap()
        xmid = [nc.dram_tensor(f"xmid{i}", [ntok, C], F32).ap() for i in range(7)]
        if mode == "A":
            S_tok_d = nc.dram_tensor("S_tok_o", [ntok, 8], F32, kind="ExternalOutput").ap()
            S_T_d = nc.dram_tensor("S_T_o", [8, ntok], F32, kind="ExternalOutput")
        else:
            S_tok_d = nc.dram_tensor("S_tok_d", [ntok, 8], F32).ap()
            S_T_d = nc.dram_tensor("S_T_d", [8, ntok], F32)
        S_g_d = nc.dram_tensor("S_g_d", [8 * ncores, ntok], F32)
        p = Prog(nc, ctx)
        env = Env(nc, ctx, p, psum_spec=False)
        setup_common_consts(env)
        coll_tok = None
        last_bi = len(blocks) - 1
        for bi, name in enumerate(blocks):
            xi = x_in if bi == 0 else xmid[bi - 1]
            xo = x_out if bi == last_bi else xmid[bi]
            bctx = ExitStack()
            with bctx:
                env.set_block_ctx(bctx)
                env.begin_block(f"B{bi}", PSUM_SPECS.get(name, DEFAULT_PSUM), bctx)
                if name == "L0_attn":
                    emit_L0_attn(p, env, xi, xo, W[name], ntok, T=T)
                elif name == "L0_ffn":
                    emit_L0_ffn(p, env, xi, xo, W[name], ntok)
                elif name == "L1_attn":
                    emit_L1_attn(p, env, xi, xo, W[name], ntok, T=T)
                elif name == "L1_ffn":
                    emit_L1_ffn(p, env, xi, xo, W[name], ntok)
                elif name == "L2_attn":
                    emit_L2_attn(p, env, xi, xo, W[name], ntok, T=T)
                elif name == "L2_ffn":
                    emit_L2_ffn(p, env, xi, xo, W[name], ntok)
                elif name == "L3_attn":
                    emit_L3_attn(p, env, xi, xo, W[name], ntok, T=T)
                    p.barrier()
                    carve_reg = env.sb["a_qfm"].bitcast(F32)
                    carve_buf = env.alias_buf(env.buf("a_qfm"), "s_carve")
                    emit_L3_scores(p, env, xo, {"Wg": W["L3_ffn"]["Wg"]}, S_tok_d,
                                   ntok, s_t_out=S_T_d.ap(),
                                   carve=(carve_reg, carve_buf))
                elif name == "L3_ffn":
                    def coll(g):
                        return g.collective_compute(
                            "AllGather", ALU.bypass,
                            replica_groups=[list(range(ncores))],
                            ins=[S_T_d.ap().opt()],
                            outs=[S_g_d.ap().opt()])
                    coll_tok = p.collective(coll)
                    emit_L3_ffn(p, env, xi, xo, W[name], ntok, S_g_d.ap(), S_tok_d,
                                nglob, S_deps=[coll_tok])
                env.restore_ctx()
            p.barrier()
        p.wait("sp", p.all_tokens())
        p.emit()
    return nc, env.host_consts


from concourse.bass_utils import run_bass_kernel_spmd

ALL_INPUT_NAMES = (
    "L0_attn_x", "L0_attn_w_attn", "L0_attn_b_attn", "L0_attn_w_proj",
    "L0_attn_b_proj", "L0_attn_w_assign", "L0_ffn_W", "L0_ffn_b",
    "L0_ffn_lora_down", "L0_ffn_lora_up", "L0_ffn_down_aux", "L0_ffn_up_aux",
    "L1_attn_Wqkv", "L1_attn_Wout", "L1_ffn_W1", "L1_ffn_b1",
    "L1_ffn_quantum_phases", "L1_ffn_quantum_weights_real",
    "L1_ffn_quantum_weights_imag", "L1_ffn_W2", "L1_ffn_b2",
    "L2_attn_Wq", "L2_attn_Wk", "L2_attn_Wv", "L2_attn_Wo",
    "L2_ffn_gate_weight", "L2_ffn_e_score_correction_bias",
    "L2_ffn_experts_gate_up", "L2_ffn_experts_down", "L2_ffn_shared_gate_w",
    "L2_ffn_shared_up_w", "L2_ffn_shared_down_w",
    "L3_attn_wq", "L3_attn_bq", "L3_attn_wk", "L3_attn_bk", "L3_attn_wv",
    "L3_attn_bv", "L3_attn_wo", "L3_attn_bo",
    "L3_ffn_Wg", "L3_ffn_W1", "L3_ffn_b1", "L3_ffn_W2", "L3_ffn_b2",
)

NCORES = 8
NTOK_CORE = 8192
NGLOB = NCORES * NTOK_CORE

_cacheA = {}


def kernel(**inputs):
    assert all(k in inputs for k in ALL_INPUT_NAMES)
    if "A" not in _cacheA:
        _cacheA["A"] = build_fused(NTOK_CORE, 2048, NGLOB, NCORES, mode="A")
        _cacheA["B"] = build_single_block("L3_ffn", NTOK_CORE, T=2048, chain=True,
                                          nglob=NGLOB)
    ncA, constsA = _cacheA["A"]
    ncB, constsB = _cacheA["B"]
    x = np.ascontiguousarray(inputs["L0_attn_x"]).astype(np.float32, copy=False)
    base = {}
    for blk, wd in BLOCK_WEIGHTS.items():
        for wname in wd:
            if blk == "L3_ffn" and wname != "Wg":
                continue
            key = f"{blk}_{wname}"
            base[key] = np.ascontiguousarray(inputs[key]).astype(np.float32, copy=False)
    for k, v in constsA.items():
        base["cst_" + k] = np.asarray(v)
    in_maps = []
    for c in range(NCORES):
        m = dict(base)
        m["x_in"] = np.ascontiguousarray(x[c * 4:(c + 1) * 4].reshape(NTOK_CORE, C))
        in_maps.append(m)
    resA = run_bass_kernel_spmd(ncA, in_maps, core_ids=list(range(NCORES)))
    x7 = [np.asarray(resA.results[c]["x_out"]) for c in range(NCORES)]
    S_tok = [np.asarray(resA.results[c]["S_tok_o"]) for c in range(NCORES)]
    S_T_full = np.ascontiguousarray(
        np.concatenate([np.asarray(resA.results[c]["S_T_o"]) for c in range(NCORES)], axis=1))
    baseB = {}
    for wname in BLOCK_WEIGHTS["L3_ffn"]:
        baseB[wname] = np.ascontiguousarray(inputs[f"L3_ffn_{wname}"]).astype(np.float32, copy=False)
    for k, v in constsB.items():
        baseB["cst_" + k] = np.asarray(v)
    in_mapsB = []
    for c in range(NCORES):
        m = dict(baseB)
        m["x_in"] = x7[c]
        m["S_t_in"] = S_T_full
        m["S_tok_in"] = S_tok[c]
        in_mapsB.append(m)
    resB = run_bass_kernel_spmd(ncB, in_mapsB, core_ids=list(range(NCORES)))
    out = np.stack([np.asarray(resB.results[c]["x_out"]) for c in range(NCORES)])
    return out.reshape(32, 2048, C).astype(np.float32, copy=False)
